# version 10
# baseline (speedup 1.0000x reference)
import numpy as np
import sys
sys.path.insert(0, '/opt/trn_rl_repo')
import concourse.bass as bass
import concourse.mybir as mybir
from concourse.bass_utils import run_bass_kernel_spmd

F32 = mybir.dt.float32
BF16 = mybir.dt.bfloat16
AF = mybir.ActivationFunctionType
ALU = mybir.AluOpType
AX = mybir.AxisListType

D = 1024
T = 2048
NCH = 8


def new_nc():
    return bass.Bass("TRN2", target_bir_lowering=False)


CONST_IDX = {}


def CONST(v):
    v = float(v)
    return CONST_IDX.setdefault(v, len(CONST_IDX))


def const_host_array():
    n = max(16, len(CONST_IDX))
    a = np.zeros((128, n), np.float32)
    for v, i in CONST_IDX.items():
        a[:, i] = v
    return a


def cc(b, v, p=128):
    return b['cc'][0:p, CONST(v):CONST(v) + 1]


class Group:
    __slots__ = ("eng", "fn", "waits", "inc", "tag")

    def __init__(self, eng, fn, waits, inc, tag):
        self.eng, self.fn, self.waits, self.inc, self.tag = eng, fn, waits, inc, tag


class Sched:
    ENGINES = ("sync", "tensor", "vector", "scalar", "gpsimd")

    NPOOL = 24

    def __init__(self, nc, stack):
        self.nc = nc
        self.sems = {e: stack.enter_context(nc.semaphore(f"S_{e}")) for e in self.ENGINES}
        for i in range(self.NPOOL):
            self.sems[f"D{i}"] = stack.enter_context(nc.semaphore(f"SD{i}"))
        self.count = {e: 0 for e in self.ENGINES}
        self.dcount = [0] * self.NPOOL
        self.dnext = 0
        self.groups = {e: [] for e in self.ENGINES}
        self.last_write = {}
        self.readers = {}

    def _deps(self, eng, reads, writes):
        need = {}

        def add(e2, c2):
            if e2 == eng and e2 == "tensor":
                return
            if need.get(e2, 0) < c2:
                need[e2] = c2
        self._add = add
        for k in reads:
            if k in self.last_write:
                add(*self.last_write[k])
        for k in writes:
            if k in self.last_write:
                add(*self.last_write[k])
            for e2, c2 in self.readers.get(k, {}).items():
                add(e2, c2)
        return need

    def _commit(self, eng, reads, writes, cnt_after):
        for k in reads:
            self.readers.setdefault(k, {})[eng] = cnt_after
        for k in writes:
            self.last_write[k] = (eng, cnt_after)
            self.readers[k] = {}

    def g(self, eng, fn, reads=(), writes=(), tag=""):
        need = self._deps(eng, reads, writes)
        self.count[eng] += 1
        self.groups[eng].append(Group(eng, fn, need, 1, tag))
        self._commit(eng, reads, writes, self.count[eng])
        return self.count[eng]

    def dma(self, eng, n, fn, reads=(), writes=(), tag=""):
        p = self.dnext
        self.dnext = (self.dnext + 1) % self.NPOOL
        need = self._deps(eng, reads, writes)
        dname = f"D{p}"
        gate = self.dcount[p]
        if gate > 0 and need.get(dname, 0) < gate:
            need[dname] = gate
        self.dcount[p] += 16 * n
        dcnt = self.dcount[p]
        sem = self.sems[dname]

        def wrapped(e, _unused, fn=fn, sem=sem):
            fn(e, sem)
        self.groups[eng].append(Group(eng, wrapped, need, 16 * n, tag))
        self._commit_dma(eng, dname, reads, writes, dcnt)
        return dcnt

    def _commit_dma(self, eng, dname, reads, writes, dcnt):
        for k in reads:
            self.readers.setdefault(k, {})[dname] = dcnt
        for k in writes:
            self.last_write[k] = (dname, dcnt)
            self.readers[k] = {}

    def barrier_all(self):
        snap = dict(self.count)
        dsnap = {f"D{i}": c for i, c in enumerate(self.dcount) if c > 0}
        for e in self.ENGINES:
            def fn(eng, _e=e):
                return eng.nop()
            need = {e2: c for e2, c in snap.items() if e2 != e and c > 0}
            need.update(dsnap)
            self.count[e] += 1
            self.groups[e].append(Group(e, fn, need, 1, "barrier"))

    def emit(self, block):
        sems = self.sems

        def run(eng_name):
            def body(eng):
                for g in self.groups[eng_name]:
                    for e2, c2 in sorted(g.waits.items()):
                        eng.wait_ge(sems[e2], c2)
                    if g.inc >= 16:
                        g.fn(eng, None)
                    else:
                        last = g.fn(eng)
                        if last is None:
                            raise RuntimeError(f"group {g.tag} on {eng_name} returned None")
                        last.then_inc(sems[eng_name], 1)
            return body
        block.sync(run("sync"))
        block.tensor(run("tensor"))
        block.vector(run("vector"))
        block.scalar(run("scalar"))
        block.gpsimd(run("gpsimd"))

    def drain_final(self):
        snap = dict(self.count)

        def fn(eng):
            return eng.nop()
        need = {e2: c for e2, c in snap.items() if e2 != "sync" and c > 0}
        self.count["sync"] += 1
        self.groups["sync"].append(Group("sync", fn, need, 1, "final"))

    def out_wait(self, eng="sync"):
        snap = dict(self.count)
        own = snap.pop(eng)

        def fn(e):
            if own > 0:
                e.wait_ge(self.sems[eng], own)
            return e.nop()
        need = {e2: c for e2, c in snap.items() if c > 0}
        need.update({f"D{i}": c for i, c in enumerate(self.dcount) if c > 0})
        self.count[eng] += 1
        self.groups[eng].append(Group(eng, fn, need, 1, "out_wait"))



def emit_ln(sch, b, xk, xak, onck, onrk, out_dram=None, src_dram=None,
            pre="ln", xn_bf=None):
    sq, rows = b['ln_sq'], b['ln_rows']
    mbc, rbc = b['ln_mbc'], b['ln_rbc']
    onc, onr = b[onck], b[onrk]
    psA, psB, psC, psM, psR = b['ln_psA'], b['ln_psB'], b['ln_psC'], b['ln_psM'], b['ln_psR']
    slots = [(psA, 0, 'ln_psA'), (psA, 32, 'ln_psA'), (psA, 64, 'ln_psA'), (psB, 0, 'ln_psB'),
             (psB, 32, 'ln_psB'), (psB, 64, 'ln_psB'), (psC, 0, 'ln_psC'), (psC, 32, 'ln_psC')]

    if src_dram is not None:
        for c in range(NCH):
            def fn(eng, sem, c=c):
                eng.dma_start(out=b[xk](c), in_=src_dram(c)).then_inc(sem, 16)
            sch.dma('sync', 1, fn, writes=[(xk, c)], tag=f"{pre}.ldx{c}")

    for c in range(NCH):
        def fn(eng, c=c):
            return eng.activation(out=sq[:, (c % 2) * T:(c % 2 + 1) * T], in_=b[xk](c), func=AF.Square)
        sch.g('scalar', fn, reads=[(xk, c)], writes=[('ln_sqb', c % 2)], tag=f"{pre}.sq{c}")

        def fn(eng, c=c):
            last = None
            for t in range(4):
                ps, r, _ = slots[t]
                last = eng.matmul(ps[r:r + 1, :], onc[:, :], b[xk](c)[:, t * 512:(t + 1) * 512],
                                  start=(c == 0), stop=(c == NCH - 1))
            return last
        sch.g('tensor', fn, reads=[(xk, c), 'ones'],
              writes=['ln_psA', 'ln_psB', 'ln_psC'], tag=f"{pre}.mmsum{c}")

        def fn(eng, c=c):
            last = None
            for t in range(4):
                ps, r, _ = slots[4 + t]
                last = eng.matmul(ps[r:r + 1, :], onc[:, :], sq[:, (c % 2) * T + t * 512:(c % 2) * T + (t + 1) * 512],
                                  start=(c == 0), stop=(c == NCH - 1))
            return last
        sch.g('tensor', fn, reads=[('ln_sqb', c % 2), 'ones'],
              writes=['ln_psA', 'ln_psB', 'ln_psC'], tag=f"{pre}.mmsq{c}")

    def fn(eng):
        last = None
        for t in range(4):
            ps, r, _ = slots[t]
            last = eng.mul(out=rows[0:1, t * 512:(t + 1) * 512], in_=ps[r:r + 1, :], mul=cc(b, 1.0 / 1024.0, 1))
        for t in range(4):
            ps, r, _ = slots[4 + t]
            last = eng.mul(out=rows[0:1, T + t * 512:T + (t + 1) * 512], in_=ps[r:r + 1, :], mul=cc(b, 1.0 / 1024.0, 1))
        return last
    sch.g('scalar', fn, reads=['ln_psA', 'ln_psB', 'ln_psC', 'consts'],
          writes=[('ln_rows', 'mean'), ('ln_rows', 'msq')], tag=f"{pre}.rows")

    def fn(eng):
        return eng.tensor_mul(rows[0:1, 2 * T:3 * T], rows[0:1, 0:T], rows[0:1, 0:T])
    sch.g('vector', fn, reads=[('ln_rows', 'mean')], writes=[('ln_rows', 'var')], tag=f"{pre}.var1")

    def fn(eng):
        return eng.tensor_sub(rows[0:1, 2 * T:3 * T], rows[0:1, T:2 * T], rows[0:1, 2 * T:3 * T])
    sch.g('vector', fn, reads=[('ln_rows', 'msq'), ('ln_rows', 'var')],
          writes=[('ln_rows', 'var')], tag=f"{pre}.var2")

    def fn(eng):
        return eng.activation(out=rows[0:1, 2 * T:3 * T], in_=rows[0:1, 2 * T:3 * T], func=AF.Ln, bias=cc(b, 1e-5, 1))
    sch.g('scalar', fn, reads=[('ln_rows', 'var'), 'consts'], writes=[('ln_rows', 'var')], tag=f"{pre}.rstd1")

    def fn(eng):
        return eng.activation(out=rows[0:1, 3 * T:4 * T], in_=rows[0:1, 2 * T:3 * T], func=AF.Exp, scale=cc(b, -0.5, 1))
    sch.g('scalar', fn, reads=[('ln_rows', 'var'), 'consts'], writes=[('ln_rows', 'rstd')], tag=f"{pre}.rstd2")

    for t in range(4):
        def fn(eng, t=t):
            eng.matmul(psM[:, :], onr[0:1, :], rows[0:1, t * 512:(t + 1) * 512], start=True, stop=True)
            return eng.matmul(psR[:, :], onr[0:1, :], rows[0:1, 3 * T + t * 512:3 * T + (t + 1) * 512],
                              start=True, stop=True)
        sch.g('tensor', fn, reads=[('ln_rows', 'mean'), ('ln_rows', 'rstd'), 'ones'],
              writes=['ln_psM', 'ln_psR'], tag=f"{pre}.bcmm{t}")

        def fn(eng, t=t):
            eng.tensor_copy(mbc[:, t * 512:(t + 1) * 512], psM[:, :])
            return eng.tensor_copy(rbc[:, t * 512:(t + 1) * 512], psR[:, :])
        sch.g('vector', fn, reads=['ln_psM', 'ln_psR'],
              writes=['ln_mbc', 'ln_rbc'], tag=f"{pre}.bcev{t}")

    for c in range(NCH):
        def fn(eng, c=c):
            return eng.tensor_sub(sq[:, 0:T], b[xk](c), mbc[:, :])
        sch.g('vector', fn, reads=[(xk, c), 'ln_mbc'], writes=[('ln_sqb', 0), ('ln_sqb', 1)],
              tag=f"{pre}.xns{c}")

        def fn(eng, c=c):
            last = None
            if xak is not None:
                last = eng.tensor_mul(b[xak](c), sq[:, 0:T], rbc[:, :])
            if xn_bf is not None:
                last = eng.tensor_mul(xn_bf(c), sq[:, 0:T], rbc[:, :])
            return last
        wr = []
        if xak is not None:
            wr.append((xak, c))
        if xn_bf is not None:
            wr.append(('xnbf', c))
        sch.g('vector', fn, reads=[('ln_sqb', 0), ('ln_sqb', 1), 'ln_rbc'], writes=wr,
              tag=f"{pre}.xn{c}")
        if out_dram is not None:
            def fn(eng, sem, c=c):
                eng.dma_start(out=out_dram(c), in_=b[xak](c)).then_inc(sem, 16)
            sch.dma('sync', 1, fn, reads=[(xak, c)], tag=f"{pre}.stx{c}")



def emit_attn0(sch, b, x0_tile, x1_store, pre="a0", load_wo=None, load_wq=None):
    wq, wk, wv, wo = b['wq'], b['wk'], b['wv'], b['wo']
    xn = b['xnbf']
    kT, v, qt, sc, at = b['kT'], b['v'], b['qt'], b['sc'], b['at']
    p0, p1 = b['a0_p0'], b['a0_p1']
    pS = [b['a0_s0'], b['a0_s1'], b['a0_s2'], b['a0_s3']]
    pav = b['a0_av']
    masks = b['masks']

    def fn(eng):
        return eng.activation(out=b['lng'][:, :], in_=b['gcols'][:, :], func=AF.Ln)
    sch.g('scalar', fn, reads=['gcols'], writes=['lng'], tag=f"{pre}.lng")

    for c in range(NCH):
        for tt in range(4):
            ps = p0 if (c * 4 + tt) % 2 == 0 else p1
            psk = 'a0_p0' if (c * 4 + tt) % 2 == 0 else 'a0_p1'

            def fn(eng, c=c, tt=tt, ps=ps):
                for k in range(NCH):
                    eng.matmul(ps[:, :], wk[:, k * 1024 + c * 128:k * 1024 + (c + 1) * 128],
                               xn(k)[:, tt * 512:(tt + 1) * 512],
                               start=(k == 0), stop=False)
                return eng.matmul(ps[:, :], b['bk_row'][0:1, c * 128:(c + 1) * 128],
                                  b['on512'][0:1, :], start=False, stop=True)
            sch.g('tensor', fn, reads=['wkT', 'bk_row', 'on512'] + [('xnbf', k) for k in range(NCH)],
                  writes=[psk], tag=f"{pre}.kp{c}.{tt}")

            def fn(eng, c=c, tt=tt, ps=ps):
                return eng.copy(out=kT[:, c * T + tt * 512:c * T + (tt + 1) * 512], in_=ps[:, :])
            sch.g('scalar', fn, reads=[psk], writes=[('kT', c)], tag=f"{pre}.ke{c}.{tt}")

    for jt in range(16):
        for h in range(2):
            ps = p0 if (jt * 2 + h) % 2 == 0 else p1
            psk = 'a0_p0' if (jt * 2 + h) % 2 == 0 else 'a0_p1'

            def fn(eng, jt=jt, h=h, ps=ps):
                for k in range(NCH):
                    eng.matmul(ps[:, :], xn(k)[:, jt * 128:(jt + 1) * 128],
                               wv[:, k * 1024 + h * 512:k * 1024 + (h + 1) * 512],
                               start=(k == 0), stop=False)
                return eng.matmul(ps[:, :], b['onr'][0:1, :], b['bv_row'][0:1, h * 512:(h + 1) * 512],
                                  start=False, stop=True)
            sch.g('tensor', fn, reads=['wvT', 'ones', 'bv_row'] + [('xnbf', k) for k in range(NCH)],
                  writes=[psk], tag=f"{pre}.vp{jt}.{h}")

            def fn(eng, jt=jt, h=h, ps=ps):
                return eng.copy(out=v[:, jt * 1024 + h * 512:jt * 1024 + (h + 1) * 512], in_=ps[:, :])
            sch.g('scalar', fn, reads=[psk], writes=[('v', jt)], tag=f"{pre}.ve{jt}.{h}")

    if load_wo is not None:
        load_wo(sch)
    if load_wq is not None:
        load_wq(sch)

    for t in range(4):
        nbj = 4 * t + 4
        for c in range(NCH):
            ps = p0 if c % 2 == 0 else p1
            psk = 'a0_p0' if c % 2 == 0 else 'a0_p1'

            def fn(eng, c=c, t=t, ps=ps):
                for k in range(NCH):
                    eng.matmul(ps[:, :], wq[:, k * 1024 + c * 128:k * 1024 + (c + 1) * 128],
                               xn(k)[:, t * 512:(t + 1) * 512],
                               start=(k == 0), stop=False)
                return eng.matmul(ps[:, :], b['bq_row'][0:1, c * 128:(c + 1) * 128],
                                  b['on512'][0:1, :], start=False, stop=True)
            sch.g('tensor', fn, reads=['wqT', 'bq_row', 'on512'] + [('xnbf', k) for k in range(NCH)],
                  writes=[psk], tag=f"{pre}.qp{t}.{c}")

            def fn(eng, c=c, ps=ps):
                return eng.copy(out=qt[:, c * 512:(c + 1) * 512], in_=ps[:, :])
            sch.g('scalar', fn, reads=[psk], writes=[('qt', c)], tag=f"{pre}.qe{t}.{c}")

        for c in range(NCH):
            for w0 in range(0, nbj, 8):
                wave = range(w0, min(w0 + 8, nbj))
                for bj in wave:
                    sA, sB = pS[bj % 2], pS[2 + bj % 2]
                    kA = ('a0_sA', bj % 2)
                    kB = ('a0_sB', bj % 2)
                    sl = bj % 8

                    def fn(eng, c=c, bj=bj, sA=sA, sB=sB):
                        eng.matmul(sA[:, :], kT[0:64, c * T + bj * 128:c * T + (bj + 1) * 128],
                                   qt[0:64, c * 512:(c + 1) * 512], start=True, stop=True,
                                   tile_position=(0, 0))
                        return eng.matmul(sB[:, :], kT[64:128, c * T + bj * 128:c * T + (bj + 1) * 128],
                                          qt[64:128, c * 512:(c + 1) * 512], start=True, stop=True,
                                          tile_position=(64, 0))
                    sch.g('tensor', fn, reads=[('kT', c), ('qt', c)], writes=[kA, kB],
                          tag=f"{pre}.sc{t}.{c}.{bj}")

                    r = bj - 4 * t
                    if r >= 0:
                        def fn(eng, sl=sl, r=r, sA=sA, sB=sB):
                            eng.tensor_mul(sc[:, (2 * sl) * 512:(2 * sl + 1) * 512], sA[:, :],
                                           masks[:, r * 512:(r + 1) * 512])
                            return eng.tensor_mul(sc[:, (2 * sl + 1) * 512:(2 * sl + 2) * 512], sB[:, :],
                                                  masks[:, r * 512:(r + 1) * 512])
                        sch.g('vector', fn, reads=[kA, kB, 'masks'],
                              writes=[('scb', sl)], tag=f"{pre}.se{t}.{c}.{bj}")
                    else:
                        def fn(eng, sl=sl, sA=sA, sB=sB):
                            eng.copy(out=sc[:, (2 * sl) * 512:(2 * sl + 1) * 512], in_=sA[:, :])
                            return eng.copy(out=sc[:, (2 * sl + 1) * 512:(2 * sl + 2) * 512], in_=sB[:, :])
                        sch.g('scalar', fn, reads=[kA, kB], writes=[('scb', sl)],
                              tag=f"{pre}.se{t}.{c}.{bj}")

                for bj in wave:
                    sl = bj % 8

                    def fn(eng, c=c, bj=bj, sl=sl, nbj=nbj):
                        eng.matmul(pav[0:64, :], v[:, bj * 1024 + c * 128:bj * 1024 + c * 128 + 64],
                                   sc[:, (2 * sl) * 512:(2 * sl + 1) * 512],
                                   start=(bj == 0), stop=(bj == nbj - 1), tile_position=(0, 0))
                        return eng.matmul(pav[64:128, :], v[:, bj * 1024 + c * 128 + 64:bj * 1024 + (c + 1) * 128],
                                          sc[:, (2 * sl + 1) * 512:(2 * sl + 2) * 512],
                                          start=(bj == 0), stop=(bj == nbj - 1), tile_position=(0, 64))
                    sch.g('tensor', fn, reads=[('scb', sl), ('v', bj)], writes=['a0_av'],
                          tag=f"{pre}.av{t}.{c}.{bj}")

            def fn(eng, c=c, t=t):
                return eng.tensor_scalar_mul(b['dkb'][:, 0:1], b['lng'][:, c:c + 1],
                                             cc(b, float(512 * t)))
            sch.g('vector', fn, reads=['lng', 'consts'], writes=['dkb'], tag=f"{pre}.dkb{t}.{c}")

            def fn(eng, c=c, t=t):
                return eng.activation(out=b['dk'][:, :], in_=b['iota512'][:, :],
                                      func=AF.Exp, scale=b['lng'][:, c:c + 1],
                                      bias=b['dkb'][:, 0:1])
            sch.g('scalar', fn, reads=['lng', 'iota512', 'dkb'], writes=['dk'], tag=f"{pre}.dk{t}.{c}")

            def fn(eng, c=c):
                return eng.tensor_mul(at[:, c * 512:(c + 1) * 512], pav[:, :], b['dk'][:, :])
            sch.g('vector', fn, reads=['a0_av', 'dk'], writes=[('at', c)], tag=f"{pre}.ae{t}.{c}")

        for o in range(NCH):
            ps = p0 if o % 2 == 0 else p1
            psk = 'a0_p0' if o % 2 == 0 else 'a0_p1'

            def fn(eng, o=o, ps=ps):
                last = None
                for k in range(NCH):
                    last = eng.matmul(ps[:, :], wo[:, k * 1024 + o * 128:k * 1024 + (o + 1) * 128],
                                      at[:, k * 512:(k + 1) * 512],
                                      start=(k == 0), stop=(k == NCH - 1))
                return last
            sch.g('tensor', fn, reads=['woT'] + [('at', k) for k in range(NCH)],
                  writes=[psk], tag=f"{pre}.wo{t}.{o}")

            x1_store(sch, o, t, ps, psk, b['bo_cols'][:, o:o + 1], x0_tile)


def const_host_array_padded(n=64):
    a = np.zeros((128, n), np.float32)
    s = const_host_array()
    a[:, :s.shape[1]] = s[:, :n]
    return a



def emit_moe(sch, b, load_w, first_write=True, pre="mo"):
    xn = b['xnbf']
    p0, p1, plg, ptr = b['mo_p0'], b['mo_p1'], b['mo_lg'], b['mo_tr']
    lgT, lgt = b['lgT'], b['lgt']
    sc_t, m_t = b['rt_sc'], b['rt_m']
    c1, c2, c3 = b['rt_c1'], b['rt_c2'], b['rt_c3']
    tmp = b['rt_tmp']
    scT, mT = b['scT'], b['mT']
    wb, mb = b['wb'], b['mb']
    xs, h, x3m, t1 = b['xs'], b['h'], b['x3m'], b['t1']
    w1, w2, w3 = b['w1'], b['w2'], b['w3']
    idn = b['id128']

    for tt in range(4):
        def fn(eng, tt=tt):
            last = None
            for k in range(NCH):
                last = eng.matmul(plg[0:8, :], b['gw'][:, k * 8:(k + 1) * 8],
                                  xn(k)[:, tt * 512:(tt + 1) * 512],
                                  start=(k == 0), stop=(k == NCH - 1))
            return last
        sch.g('tensor', fn, reads=['gw'] + [('xnbf', k) for k in range(NCH)],
              writes=['mo_lg'], tag=f"{pre}.lg{tt}")

        def fn(eng, tt=tt):
            return eng.copy(out=lgT[0:8, tt * 512:(tt + 1) * 512], in_=plg[0:8, :])
        sch.g('scalar', fn, reads=['mo_lg'], writes=['lgT'], tag=f"{pre}.lge{tt}")

    for i in range(16):
        def fn(eng, i=i):
            return eng.transpose(ptr[0:128, 0:8], lgT[0:8, i * 128:(i + 1) * 128], idn[0:8, 0:8])
        sch.g('tensor', fn, reads=['lgT', 'id128'], writes=['mo_tr'], tag=f"{pre}.tr{i}")

        def fn(eng, i=i):
            return eng.tensor_copy(lgt[:, i * 8:(i + 1) * 8], ptr[0:128, 0:8])
        sch.g('vector', fn, reads=['mo_tr'], writes=[('lgt', i)], tag=f"{pre}.tre{i}")

    for i in range(16):
        lg_i = lgt[:, i * 8:(i + 1) * 8]
        sc_i = sc_t[:, i * 8:(i + 1) * 8]
        m_i = m_t[:, i * 8:(i + 1) * 8]
        c1_i, c2_i, c3_i = c1[:, i:i + 1], c2[:, i:i + 1], c3[:, i:i + 1]

        sch.g('vector', lambda e, a=c1_i, x=lg_i: e.reduce_max(a, x, axis=AX.X, negate=True),
              reads=[('lgt', i)], writes=[('c1', i)], tag=f"{pre}.r1.{i}")
        sch.g('scalar', lambda e, o=sc_i, x=lg_i, a=c1_i: e.activation(out=o, in_=x, func=AF.Exp, bias=a),
              reads=[('lgt', i), ('c1', i)], writes=[('sc', i)], tag=f"{pre}.exp.{i}")
        sch.g('vector', lambda e, a=c2_i, x=sc_i: e.reduce_sum(a, x, axis=AX.X),
              reads=[('sc', i)], writes=[('c2', i)], tag=f"{pre}.sum.{i}")
        sch.g('vector', lambda e, a=c3_i, x=c2_i: e.reciprocal(a, x),
              reads=[('c2', i)], writes=[('c3', i)], tag=f"{pre}.rcp.{i}")
        sch.g('vector', lambda e, o=sc_i, a=c3_i: e.tensor_scalar(o, o, a, None, op0=ALU.mult),
              reads=[('sc', i), ('c3', i)], writes=[('sc', i)], tag=f"{pre}.nrm.{i}")
        sch.g('vector', lambda e, a=c1_i, x=sc_i: e.reduce_max(a, x, axis=AX.X, negate=True),
              reads=[('sc', i)], writes=[('c1', i)], tag=f"{pre}.m1.{i}")
        sch.g('vector', lambda e, o=m_i, x=sc_i, a=c1_i: e.tensor_scalar(o, x, a, None, op0=ALU.add),
              reads=[('sc', i), ('c1', i)], writes=[('m', i)], tag=f"{pre}.d1.{i}")
        sch.g('vector', lambda e, o=m_i, bb=b: e.tensor_single_scalar(o, o, cc(bb, 0.0), op=ALU.is_equal),
              reads=[('m', i), 'consts'], writes=[('m', i)], tag=f"{pre}.eq.{i}")
        sch.g('vector', lambda e, o=m_i, x=sc_i, bb=b: e.scalar_tensor_tensor(o, o, cc(bb, -1e30), x, op0=ALU.mult, op1=ALU.add),
              reads=[('m', i), ('sc', i), 'consts'], writes=[('m', i)], tag=f"{pre}.msk.{i}")
        sch.g('vector', lambda e, a=c2_i, x=m_i: e.reduce_max(a, x, axis=AX.X, negate=True),
              reads=[('m', i)], writes=[('c2', i)], tag=f"{pre}.m2.{i}")
        sch.g('vector', lambda e, o=m_i, x=sc_i, a=c2_i: e.tensor_scalar(o, x, a, None, op0=ALU.add),
              reads=[('sc', i), ('c2', i)], writes=[('m', i)], tag=f"{pre}.d2.{i}")
        sch.g('vector', lambda e, o=m_i, bb=b: e.tensor_single_scalar(o, o, cc(bb, 0.0), op=ALU.is_ge),
              reads=[('m', i), 'consts'], writes=[('m', i)], tag=f"{pre}.ge.{i}")

    for i in range(16):
        def fn(eng, i=i):
            return eng.transpose(ptr[0:8, 0:128], sc_t[:, i * 8:(i + 1) * 8], idn[:, :])
        sch.g('tensor', fn, reads=[('sc', i), 'id128'], writes=['mo_tr'], tag=f"{pre}.ts{i}")
        sch.g('scalar', lambda e, i=i: e.copy(out=scT[0:8, i * 128:(i + 1) * 128], in_=ptr[0:8, 0:128]),
              reads=['mo_tr'], writes=['scT'], tag=f"{pre}.tse{i}")

        def fn(eng, i=i):
            return eng.transpose(ptr[0:8, 0:128], m_t[:, i * 8:(i + 1) * 8], idn[:, :])
        sch.g('tensor', fn, reads=[('m', i), 'id128'], writes=['mo_tr'], tag=f"{pre}.tm{i}")
        sch.g('scalar', lambda e, i=i: e.copy(out=mT[0:8, i * 128:(i + 1) * 128], in_=ptr[0:8, 0:128]),
              reads=['mo_tr'], writes=['mT'], tag=f"{pre}.tme{i}")

    for e_ in range(8):
        load_w(sch, e_)

        for (src, srck, dst, dstk) in ((scT, 'scT', wb, 'wb'), (mT, 'mT', mb, 'mb')):
            for tt in range(4):
                ps = p0 if tt % 2 == 0 else p1
                psk = 'mo_p0' if tt % 2 == 0 else 'mo_p1'

                def fn(eng, e_=e_, tt=tt, src=src, ps=ps):
                    return eng.matmul(ps[:, :], b['sel8'][0:8, e_ * 128:(e_ + 1) * 128],
                                      src[0:8, tt * 512:(tt + 1) * 512], start=True, stop=True)
                sch.g('tensor', fn, reads=[srck, 'sel8'], writes=[psk], tag=f"{pre}.bc{dstk}{e_}.{tt}")

                def fn(eng, tt=tt, dst=dst, ps=ps):
                    return eng.copy(out=dst[:, tt * 512:(tt + 1) * 512], in_=ps[:, :])
                sch.g('scalar', fn, reads=[psk], writes=[dstk], tag=f"{pre}.bce{dstk}{e_}.{tt}")

        for qf in range(4):
            qo = qf * 512
            for k in range(NCH):
                def fn(eng, k=k, qo=qo):
                    return eng.tensor_mul(xs[:, k * 512:(k + 1) * 512],
                                          xn(k)[:, qo:qo + 512], wb[:, qo:qo + 512])
                sch.g('gpsimd', fn, reads=[('xnbf', k), 'wb'], writes=[('xs', k)],
                      tag=f"{pre}.xs{e_}.{qf}.{k}")

            for hc in range(NCH):
                ps = p0 if hc % 2 == 0 else p1
                psk = 'mo_p0' if hc % 2 == 0 else 'mo_p1'

                def fn(eng, hc=hc, ps=ps):
                    last = None
                    for k in range(NCH):
                        last = eng.matmul(ps[:, :], w1[:, k * 1024 + hc * 128:k * 1024 + (hc + 1) * 128],
                                          xs[:, k * 512:(k + 1) * 512],
                                          start=(k == 0), stop=(k == NCH - 1))
                    return last
                sch.g('tensor', fn, reads=['w1sb'] + [('xs', k) for k in range(NCH)],
                      writes=[psk], tag=f"{pre}.u1{e_}.{qf}.{hc}")

                def fn(eng, hc=hc, e_=e_, ps=ps):
                    return eng.activation(out=h[:, hc * 512:(hc + 1) * 512],
                                          in_=ps[:, :], func=AF.Silu,
                                          bias=b['b1c'][:, hc * 8 + e_:hc * 8 + e_ + 1])
                sch.g('scalar', fn, reads=[psk, 'b1c'], writes=[('h', hc)],
                      tag=f"{pre}.u1e{e_}.{qf}.{hc}")

            for oc in range(NCH):
                def fn(eng, oc=oc, ps=p0):
                    last = None
                    for k in range(NCH):
                        last = eng.matmul(ps[:, :], w3[:, k * 1024 + oc * 128:k * 1024 + (oc + 1) * 128],
                                          xs[:, k * 512:(k + 1) * 512],
                                          start=(k == 0), stop=(k == NCH - 1))
                    return last
                sch.g('tensor', fn, reads=['w3sb'] + [('xs', k) for k in range(NCH)],
                      writes=['mo_p0'], tag=f"{pre}.x3{e_}.{qf}.{oc}")

                def fn(eng, oc=oc, e_=e_, qo=qo):
                    return eng.scalar_tensor_tensor(
                        out=x3m[:, :], in0=p0[:, :],
                        scalar=b['b3c'][:, oc * 8 + e_:oc * 8 + e_ + 1],
                        in1=mb[:, qo:qo + 512],
                        op0=ALU.add, op1=ALU.mult)
                sch.g('vector', fn, reads=['mo_p0', 'b3c', 'mb'], writes=['x3m'],
                      tag=f"{pre}.x3e{e_}.{qf}.{oc}")

                def fn(eng, oc=oc, ps=p1):
                    last = None
                    for k in range(NCH):
                        last = eng.matmul(ps[:, :], w2[:, k * 1024 + oc * 128:k * 1024 + (oc + 1) * 128],
                                          h[:, k * 512:(k + 1) * 512],
                                          start=(k == 0), stop=(k == NCH - 1))
                    return last
                sch.g('tensor', fn, reads=['w2sb'] + [('h', k) for k in range(NCH)],
                      writes=['mo_p1'], tag=f"{pre}.x2{e_}.{qf}.{oc}")

                if e_ == 0 and first_write:
                    def fn(eng, oc=oc, e_=e_, qo=qo):
                        return eng.scalar_tensor_tensor(
                            out=b['y'](oc)[:, qo:qo + 512],
                            in0=p1[:, :], scalar=b['b2c'][:, oc * 8 + e_:oc * 8 + e_ + 1],
                            in1=x3m[:, :], op0=ALU.add, op1=ALU.mult)
                    sch.g('vector', fn, reads=['mo_p1', 'b2c', 'x3m'], writes=[('y', oc)],
                          tag=f"{pre}.x2e{e_}.{qf}.{oc}")
                else:
                    def fn(eng, oc=oc, e_=e_):
                        return eng.scalar_tensor_tensor(
                            out=t1[:, :], in0=p1[:, :],
                            scalar=b['b2c'][:, oc * 8 + e_:oc * 8 + e_ + 1],
                            in1=x3m[:, :], op0=ALU.add, op1=ALU.mult)
                    sch.g('vector', fn, reads=['mo_p1', 'b2c', 'x3m'], writes=['t1'],
                          tag=f"{pre}.x2e{e_}.{qf}.{oc}")

                    def fn(eng, oc=oc, qo=qo):
                        yap = b['y'](oc)[:, qo:qo + 512]
                        return eng.tensor_add(yap, yap, t1[:, :])
                    sch.g('gpsimd', fn, reads=['t1', ('y', oc)], writes=[('y', oc)],
                          tag=f"{pre}.acc{e_}.{qf}.{oc}")



def emit_shared(sch, b, z_store, pre="sh"):
    xn = b['xnbf']
    sw1, sw2, sw3 = b['sw1'], b['sw2'], b['sw3']
    sa, gg = b['sa'], b['gg']
    p0, p1 = b['sh_p0'], b['sh_p1']

    for hf in range(2):
        ho = hf * 1024
        for ic in range(16):
            for tt in range(2):
                ps, psk = (p0, 'sh_p0') if tt == 0 else (p1, 'sh_p1')

                def fn(eng, ic=ic, tt=tt, ho=ho, ps=ps):
                    last = None
                    for k in range(NCH):
                        last = eng.matmul(ps[:, :], sw1[:, k * T + ic * 128:k * T + (ic + 1) * 128],
                                          xn(k)[:, ho + tt * 512:ho + (tt + 1) * 512],
                                          start=(k == 0), stop=(k == NCH - 1))
                    return last
                sch.g('tensor', fn, reads=['sw1sb'] + [('xnbf', k) for k in range(NCH)],
                      writes=[psk], tag=f"{pre}.ua{hf}.{ic}.{tt}")

                def fn(eng, tt=tt, ps=ps):
                    return eng.activation(out=sa[:, tt * 512:(tt + 1) * 512], in_=ps[:, :], func=AF.Silu)
                sch.g('scalar', fn, reads=[psk], writes=[('sa', tt)], tag=f"{pre}.uae{hf}.{ic}.{tt}")

            for tt in range(2):
                ps, psk = (p0, 'sh_p0') if tt == 0 else (p1, 'sh_p1')

                def fn(eng, ic=ic, tt=tt, ho=ho, ps=ps):
                    last = None
                    for k in range(NCH):
                        last = eng.matmul(ps[:, :], sw2[:, k * T + ic * 128:k * T + (ic + 1) * 128],
                                          xn(k)[:, ho + tt * 512:ho + (tt + 1) * 512],
                                          start=(k == 0), stop=(k == NCH - 1))
                    return last
                sch.g('tensor', fn, reads=['sw2sb'] + [('xnbf', k) for k in range(NCH)],
                      writes=[psk], tag=f"{pre}.ub{hf}.{ic}.{tt}")

                def fn(eng, ic=ic, tt=tt, ps=ps):
                    return eng.tensor_mul(gg[:, ic * 1024 + tt * 512:ic * 1024 + (tt + 1) * 512],
                                          ps[:, :], sa[:, tt * 512:(tt + 1) * 512])
                sch.g('vector', fn, reads=[psk, ('sa', tt)], writes=[('gg', ic)],
                      tag=f"{pre}.gg{hf}.{ic}.{tt}")

        for oc in range(NCH):
            for tt in range(2):
                ps, psk = (p0, 'sh_p0') if tt == 0 else (p1, 'sh_p1')

                def fn(eng, oc=oc, tt=tt, ps=ps):
                    last = None
                    for ic in range(16):
                        last = eng.matmul(ps[:, :], sw3[:, ic * 1024 + oc * 128:ic * 1024 + (oc + 1) * 128],
                                          gg[:, ic * 1024 + tt * 512:ic * 1024 + (tt + 1) * 512],
                                          start=(ic == 0), stop=(ic == 15))
                    return last
                sch.g('tensor', fn, reads=['sw3sb'] + [('gg', ic) for ic in range(16)],
                      writes=[psk], tag=f"{pre}.z{hf}.{oc}.{tt}")

                z_store(sch, oc, hf, tt, psk, ps)



TAPC = [0.1 + 0.9 * k / 7.0 for k in range(8)]


def l1_cmat():
    C = np.zeros((8, 8), np.float32)
    for s in range(1, 8):
        w = min(s, 8)
        for dd in range(0, s):
            j = dd
            C[dd, s] = 0.1 + 0.9 * j / max(w - 1, 1) if w > 1 else 0.1
    return C


def emit_l1attn(sch, b, traj_store, g_dram, x_dram, pre="l1"):
    ND = T - 1
    df, x1c = b['df'], b['x1c']
    sq = b['l1sq']
    pd, pV, pB, pc, ptb = b['l1_pd'], b['l1_pV'], b['l1_pB'], b['l1_pc'], b['l1_ptb']
    pR0, pR1 = b['l1_pR0'], b['l1_pR1']
    l1c = b['l1c']
    rwa, rwb, rwc, rwd = b['rwa'], b['rwb'], b['rwc'], b['rwd']
    onc = b['onc']

    Gf = b['Gf32']

    def fn(eng, sem):
        for gc in range(4):
            eng.dma_start(out=Gf[:, gc * 1024:(gc + 1) * 1024],
                          in_=g_dram[gc * 128:(gc + 1) * 128, :]).then_inc(sem, 16)
    sch.dma('sync', 4, fn, writes=[('df', 0), ('df', 1), 'Gf32'], tag=f"{pre}.ldgf")

    for gc in range(4):
        def fn(eng, gc=gc):
            return eng.activation(out=sq[:, 0:1024], in_=Gf[:, gc * 1024:(gc + 1) * 1024], func=AF.Square)
        sch.g('scalar', fn, reads=['Gf32'], writes=['l1sq'], tag=f"{pre}.gsq{gc}")

        def fn(eng, gc=gc):
            return eng.reduce_sum(l1c[:, 48 + gc:49 + gc], sq[:, 0:1024], axis=AX.X)
        sch.g('vector', fn, reads=['l1sq'], writes=[('gn', gc)], tag=f"{pre}.gss{gc}")

    def fn(eng):
        return eng.tensor_scalar_max(l1c[:, 48:52], l1c[:, 48:52], cc(b, 1e-16))
    sch.g('vector', fn, reads=[('gn', g) for g in range(4)] + ['consts'], writes=['gnc'], tag=f"{pre}.gcl")

    def fn(eng):
        return eng.activation(out=l1c[:, 52:56], in_=l1c[:, 48:52], func=AF.Ln)
    sch.g('scalar', fn, reads=['gnc'], writes=['gnl'], tag=f"{pre}.gln")

    def fn(eng):
        return eng.activation(out=l1c[:, 48:52], in_=l1c[:, 52:56], func=AF.Exp, scale=cc(b, -0.5))
    sch.g('scalar', fn, reads=['gnl'], writes=['gni'], tag=f"{pre}.gexp")

    for gc in range(4):
        def fn(eng, gc=gc):
            return eng.transpose(pV[0:1, 0:128], l1c[:, 48 + gc:49 + gc], b['id128'][:, :])
        sch.g('tensor', fn, reads=['gni', 'id128'], writes=['l1_pV'], tag=f"{pre}.gtr{gc}")

        def fn(eng, gc=gc):
            return eng.tensor_copy(rwd[0:1, gc * 128:(gc + 1) * 128], pV[0:1, 0:128])
        sch.g('vector', fn, reads=['l1_pV'], writes=['gnr'], tag=f"{pre}.gte{gc}")

    def fn(eng):
        return eng.matmul(pB[:, :], b['onr'][0:1, :], rwd[0:1, 0:512], start=True, stop=True)
    sch.g('tensor', fn, reads=['gnr', 'ones'], writes=['l1_pB'], tag=f"{pre}.gbc")

    def fn(eng):
        return eng.copy(out=b['invgn'][:, :], in_=pB[:, :])
    sch.g('scalar', fn, reads=['l1_pB'], writes=['invgn'], tag=f"{pre}.gbce")

    for c in range(NCH):
        def fn(eng, sem, c=c):
            eng.dma_start(out=x1c(c), in_=x_dram(c)).then_inc(sem, 16)
        sch.dma('sync', 1, fn,
                reads=(['Gf32'] if c < 2 else []) + (['gnr'] if c == 7 else []),
                writes=[('x1c', c)] + (['rwd'] if c == 7 else []),
                tag=f"{pre}.ldx{c}")

    for c in range(NCH):
        def fn(eng, c=c):
            eng.tensor_sub(df(c)[:, 0:ND], x1c(c)[:, 1:T], x1c(c)[:, 0:ND])
            return eng.memset(df(c)[:, ND:T], 0.0)
        sch.g('vector', fn, reads=[('x1c', c)], writes=[('df', c)], tag=f"{pre}.df{c}")

    rslots = [(pR0, 0), (pR0, 32), (pR0, 64), (pR1, 0)]
    for c in range(NCH):
        def fn(eng, c=c):
            return eng.activation(out=sq[:, :], in_=df(c)[:, :], func=AF.Square)
        sch.g('scalar', fn, reads=[('df', c)], writes=['l1sq'], tag=f"{pre}.sq{c}")

        def fn(eng, c=c):
            last = None
            for st in range(16):
                last = eng.matmul(pc[:, st:st + 1], sq[:, st * 128:(st + 1) * 128], onc[:, :],
                                  start=(c == 0), stop=(c == NCH - 1))
            return last
        sch.g('tensor', fn, reads=['l1sq', 'ones'], writes=['l1_pc'], tag=f"{pre}.cmm{c}")

        def fn(eng, c=c):
            last = None
            for t4 in range(4):
                ps, r = rslots[t4]
                last = eng.matmul(ps[r:r + 1, :], onc[:, :], sq[:, t4 * 512:(t4 + 1) * 512],
                                  start=(c == 0), stop=(c == NCH - 1))
            return last
        sch.g('tensor', fn, reads=['l1sq', 'ones'], writes=['l1_pR'], tag=f"{pre}.rmm{c}")

    def fn(eng):
        return eng.tensor_copy(l1c[:, 0:16], pc[:, 0:16])
    sch.g('vector', fn, reads=['l1_pc'], writes=['scol'], tag=f"{pre}.cev")

    def fn(eng):
        last = None
        for t4 in range(4):
            ps, r = rslots[t4]
            last = eng.copy(out=rwa[0:1, t4 * 512:(t4 + 1) * 512], in_=ps[r:r + 1, :])
        return last
    sch.g('scalar', fn, reads=['l1_pR'] + [('x1c', c) for c in (4, 5, 6, 7)],
          writes=['rwa'], tag=f"{pre}.rev")

    def fn(eng):
        return eng.tensor_scalar_max(l1c[:, 0:16], l1c[:, 0:16], cc(b, 1e-12))
    sch.g('vector', fn, reads=['scol', 'consts'], writes=['scol'], tag=f"{pre}.ccl")

    def fn(eng):
        return eng.activation(out=l1c[:, 16:32], in_=l1c[:, 0:16], func=AF.Ln)
    sch.g('scalar', fn, reads=['scol'], writes=['lnc'], tag=f"{pre}.cln")

    def fn(eng):
        eng.activation(out=l1c[:, 0:16], in_=l1c[:, 16:32], func=AF.Exp, scale=cc(b, -0.5))
        return eng.activation(out=l1c[:, 32:48], in_=l1c[:, 16:32], func=AF.Exp, scale=cc(b, 0.5))
    sch.g('scalar', fn, reads=['lnc', 'consts'], writes=['invmc', 'magc'], tag=f"{pre}.cexp")

    def fn(eng):
        return eng.tensor_mul(l1c[:, 16:32], l1c[:, 0:16], l1c[:, 32:48])
    sch.g('vector', fn, reads=['invmc', 'magc'], writes=['anc'], tag=f"{pre}.can")

    def fn(eng):
        return eng.tensor_scalar_max(l1c[:, 16:32], l1c[:, 16:32], cc(b, 1e-8))
    sch.g('vector', fn, reads=['anc', 'consts'], writes=['anc'], tag=f"{pre}.crc1")

    def fn(eng):
        return eng.reciprocal(l1c[:, 16:32], l1c[:, 16:32])
    sch.g('vector', fn, reads=['anc'], writes=['anc'], tag=f"{pre}.crc2")

    def fn(eng):
        return eng.scalar_tensor_tensor(l1c[:, 16:32], l1c[:, 16:32], cc(b, 2.0), l1c[:, 0:16],
                                        op0=ALU.mult, op1=ALU.mult)
    sch.g('vector', fn, reads=['anc', 'invmc', 'consts'], writes=['rsc'], tag=f"{pre}.crs")

    def fn(eng):
        return eng.activation(out=l1c[:, 32:48], in_=l1c[:, 32:48], func=AF.Tanh, scale=cc(b, 2.0))
    sch.g('scalar', fn, reads=['magc', 'consts'], writes=['tnc'], tag=f"{pre}.ctn")

    def fn(eng):
        return eng.tensor_scalar_max(rwa[0:1, :], rwa[0:1, :], cc(b, 1e-12, 1))
    sch.g('vector', fn, reads=['rwa', 'consts'], writes=['rwa'], tag=f"{pre}.rcl")

    def fn(eng):
        return eng.activation(out=rwa[0:1, :], in_=rwa[0:1, :], func=AF.Ln)
    sch.g('scalar', fn, reads=['rwa'], writes=['rwa'], tag=f"{pre}.rln")

    def fn(eng):
        eng.activation(out=rwb[0:1, :], in_=rwa[0:1, :], func=AF.Exp, scale=cc(b, -0.5, 1))
        return eng.activation(out=rwc[0:1, :], in_=rwa[0:1, :], func=AF.Exp, scale=cc(b, 0.5, 1))
    sch.g('scalar', fn, reads=['rwa', 'consts'] + [('x1c', c) for c in (4, 5, 6, 7)],
          writes=['rwb', 'rwc'], tag=f"{pre}.rexp")

    def fn(eng):
        return eng.activation(out=rwc[0:1, :], in_=rwc[0:1, :], func=AF.Tanh, scale=cc(b, 2.0, 1))
    sch.g('scalar', fn, reads=['rwc', 'consts'], writes=['rwc'], tag=f"{pre}.rtn")

    def fn(eng):
        return eng.scalar_tensor_tensor(rwb[0:1, :], rwc[0:1, :], cc(b, 0.6, 1), rwb[0:1, :],
                                        op0=ALU.mult, op1=ALU.mult)
    sch.g('vector', fn, reads=['rwb', 'rwc', 'consts'], writes=['rwb'], tag=f"{pre}.rap")

    def fn(eng):
        return eng.tensor_scalar_mul(rwd[0:1, 8:T], rwc[0:1, 0:T - 8], cc(b, TAPC[0], 1))
    sch.g('vector', fn, reads=['rwc', 'consts', 'gnr'], writes=['rwd'], tag=f"{pre}.dtap0")
    for k in range(1, 8):
        def fn(eng, k=k):
            return eng.scalar_tensor_tensor(rwd[0:1, 8:T], rwc[0:1, k:T - 8 + k], cc(b, TAPC[k], 1),
                                            rwd[0:1, 8:T], op0=ALU.mult, op1=ALU.add)
        sch.g('vector', fn, reads=['rwc', 'rwd', 'consts'], writes=['rwd'], tag=f"{pre}.dtap{k}")

    def fn(eng):
        return eng.matmul(pV[0:1, 0:8], l1c[0:8, 32:33], b['Cmat'][0:8, 0:8], start=True, stop=True)
    sch.g('tensor', fn, reads=['tnc', 'Cmat'], writes=['l1_pV'], tag=f"{pre}.dsm")

    def fn(eng):
        return eng.tensor_copy(rwd[0:1, 0:8], pV[0:1, 0:8])
    sch.g('vector', fn, reads=['l1_pV'], writes=['rwd'], tag=f"{pre}.dsme")

    def fn(eng):
        return eng.tensor_scalar_max(rwa[0:1, :], rwd[0:1, :], cc(b, 1e-8, 1))
    sch.g('vector', fn, reads=['rwd', 'consts'], writes=['rwa'], tag=f"{pre}.dcl")

    def fn(eng):
        eng.activation(out=rwa[0:1, :], in_=rwa[0:1, :], func=AF.Ln)
        return eng.activation(out=rwa[0:1, :], in_=rwa[0:1, :], func=AF.Exp, scale=cc(b, -1.0, 1))
    sch.g('scalar', fn, reads=['rwa', 'consts'], writes=['rwa'], tag=f"{pre}.dinv")

    for st in range(16):
        def fn(eng, st=st):
            last = None
            for k in range(NCH):
                last = eng.tensor_copy(b['dfsl'][:, k * 128:(k + 1) * 128],
                                       df(k)[:, st * 128:(st + 1) * 128])
            return last
        sch.g('vector', fn, reads=[('df', k) for k in range(NCH)], writes=['dfsl'],
              tag=f"{pre}.cast{st}")

        def fn(eng, st=st):
            last = None
            for k in range(NCH):
                last = eng.matmul(pd[:, :], b['dfsl'][:, k * 128:(k + 1) * 128],
                                  b['GTbf'][:, k * 512:(k + 1) * 512],
                                  start=(k == 0), stop=(k == NCH - 1))
            return last
        sch.g('tensor', fn, reads=['dfsl', 'GTbf'], writes=['l1_pd'], tag=f"{pre}.dot{st}")

        def fn(eng, st=st):
            return eng.scalar_tensor_tensor(b['sims'][:, :], pd[:, :], l1c[:, 16 + st:17 + st],
                                            b['invgn'][:, :], op0=ALU.mult, op1=ALU.mult)
        sch.g('vector', fn, reads=['l1_pd', 'rsc', 'invgn'], writes=['sims'], tag=f"{pre}.sim{st}")

        def fn(eng, st=st):
            return eng.reduce_max(b['smx'][:, 0:1], b['sims'][:, :], axis=AX.X, negate=True)
        sch.g('vector', fn, reads=['sims'], writes=['smx'], tag=f"{pre}.smx{st}")

        def fn(eng, st=st):
            return eng.activation(out=b['sims'][:, :], in_=b['sims'][:, :], func=AF.Exp,
                                  bias=b['smx'][:, 0:1])
        sch.g('scalar', fn, reads=['sims', 'smx'], writes=['sims'], tag=f"{pre}.sex{st}")

        def fn(eng, st=st):
            return eng.reduce_sum(b['smx'][:, 1:2], b['sims'][:, :], axis=AX.X)
        sch.g('vector', fn, reads=['sims'], writes=['smx2'], tag=f"{pre}.ssm{st}")

        def fn(eng, st=st):
            return eng.reciprocal(b['smx'][:, 2:3], b['smx'][:, 1:2])
        sch.g('vector', fn, reads=['smx2'], writes=['smx3'], tag=f"{pre}.src{st}")

        def fn(eng, st=st):
            return eng.scalar_tensor_tensor(b['smx'][:, 3:4], b['smx'][:, 2:3], cc(b, 0.4),
                                            l1c[:, 32 + st:33 + st], op0=ALU.mult, op1=ALU.mult)
        sch.g('vector', fn, reads=['smx3', 'tnc', 'consts'], writes=['smx4'], tag=f"{pre}.snr{st}")

        def fn(eng, st=st):
            return eng.tensor_scalar(b['gwt'][:, :], b['sims'][:, :], b['smx'][:, 3:4], None,
                                     op0=ALU.mult)
        sch.g('vector', fn, reads=['sims', 'smx4'], writes=['gwt'], tag=f"{pre}.gw{st}")

        for j in range(4):
            def fn(eng, st=st, j=j):
                return eng.transpose(ptb[:, :], b['gwt'][:, j * 128:(j + 1) * 128], b['idbf'][:, :])
            sch.g('tensor', fn, reads=['gwt', 'idbf'], writes=['l1_ptb'], tag=f"{pre}.gt{st}.{j}")

            def fn(eng, st=st, j=j):
                return eng.tensor_copy(b['gwT'][:, j * T + st * 128:j * T + (st + 1) * 128], ptb[:, :])
            sch.g('vector', fn, reads=['l1_ptb'], writes=[('gwT', j)], tag=f"{pre}.gte{st}.{j}")

    for st2 in range(4):
        def fn(eng, st2=st2):
            return eng.matmul(pB[:, :], b['onr'][0:1, :], rwb[0:1, st2 * 512:(st2 + 1) * 512],
                              start=True, stop=True)
        sch.g('tensor', fn, reads=['rwb', 'ones'], writes=['l1_pB'], tag=f"{pre}.abc{st2}")

        def fn(eng, st2=st2):
            return eng.copy(out=b['apb'][:, :], in_=pB[:, :])
        sch.g('scalar', fn, reads=['l1_pB'], writes=['apb'], tag=f"{pre}.abce{st2}")

        for c in range(NCH):
            pu = pd if c % 2 == 0 else pV
            puk = 'l1_pd' if c % 2 == 0 else 'l1_pV'

            def fn(eng, st2=st2, c=c, pu=pu):
                last = None
                for gc in range(4):
                    last = eng.matmul(pu[:, :], b['Gbf'][:, gc * 1024 + c * 128:gc * 1024 + (c + 1) * 128],
                                      b['gwT'][:, gc * T + st2 * 512:gc * T + (st2 + 1) * 512],
                                      start=(gc == 0), stop=(gc == 3))
                return last
            sch.g('tensor', fn, reads=['Gbf'] + [('gwT', gc) for gc in range(4)],
                  writes=[puk], tag=f"{pre}.inf{st2}.{c}")

            def fn(eng, st2=st2, c=c):
                return eng.tensor_mul(df(c)[:, st2 * 512:(st2 + 1) * 512],
                                      df(c)[:, st2 * 512:(st2 + 1) * 512], b['apb'][:, :])
            sch.g('vector', fn, reads=[('df', c), 'apb'], writes=[('df', c)], tag=f"{pre}.ua{st2}.{c}")

            def fn(eng, st2=st2, c=c, pu=pu):
                return eng.tensor_add(df(c)[:, st2 * 512:(st2 + 1) * 512],
                                      df(c)[:, st2 * 512:(st2 + 1) * 512], pu[:, :])
            sch.g('vector', fn, reads=[('df', c), puk], writes=[('df', c)], tag=f"{pre}.ub{st2}.{c}")

    for st2 in range(4):
        def fn(eng, st2=st2):
            return eng.matmul(pB[:, :], b['onr'][0:1, :], rwa[0:1, st2 * 512:(st2 + 1) * 512],
                              start=True, stop=True)
        sch.g('tensor', fn, reads=['rwa', 'ones'], writes=['l1_pB'], tag=f"{pre}.ibc{st2}")

        def fn(eng, st2=st2):
            return eng.copy(out=b['ivb'][:, st2 * 512:(st2 + 1) * 512], in_=pB[:, :])
        sch.g('scalar', fn, reads=['l1_pB'] + ([('x1c', 1)] if st2 == 0 else []),
              writes=['ivb'], tag=f"{pre}.ibce{st2}")

    for c in range(NCH):
        ev = 'vector'
        acc = b['acc']

        def fn(eng, c=c):
            return eng.tensor_scalar_mul(acc[:, 8:T], df(c)[:, 0:T - 8], cc(b, TAPC[0]))
        sch.g(ev, fn, reads=[('df', c), 'consts', ('x1c', 0)], writes=['acc'], tag=f"{pre}.t0.{c}")
        for k in range(1, 8):
            def fn(eng, c=c, k=k):
                return eng.scalar_tensor_tensor(acc[:, 8:T], df(c)[:, k:T - 8 + k], cc(b, TAPC[k]),
                                                acc[:, 8:T], op0=ALU.mult, op1=ALU.add)
            sch.g(ev, fn, reads=[('df', c), 'acc', 'consts'], writes=['acc'], tag=f"{pre}.t{k}.{c}")

        def fn(eng, c=c):
            return eng.transpose(pV[0:8, 0:128], df(c)[:, 0:8], b['id128'][:, :])
        sch.g('tensor', fn, reads=[('df', c), 'id128'], writes=['l1_pV'], tag=f"{pre}.ut{c}")

        def fn(eng, c=c):
            return eng.tensor_copy(b['Usm'][0:8, :], pV[0:8, 0:128])
        sch.g('vector', fn, reads=['l1_pV'], writes=['Usm'], tag=f"{pre}.ute{c}")

        def fn(eng, c=c):
            return eng.matmul(pV[0:128, 0:8], b['Usm'][0:8, :], b['Cmat'][0:8, 0:8],
                              start=True, stop=True)
        sch.g('tensor', fn, reads=['Usm', 'Cmat'], writes=['l1_pV'], tag=f"{pre}.usm{c}")

        def fn(eng, c=c):
            return eng.tensor_copy(acc[:, 0:8], pV[0:128, 0:8])
        sch.g('vector', fn, reads=['l1_pV'], writes=['acc'], tag=f"{pre}.usme{c}")

        traj_store(sch, c)



def emit_tree(sch, b, out_store, pre="tf", skip=()):
    xn = b['xnbf']
    win, wout = b['ti_win'], b['ti_wout']
    ac, dc, ind, mT = b['ti_ac'], b['ti_dc'], b['ti_in'], b['ti_mT']
    p0, p1, ptb = b['tf_p0'], b['tf_p1'], b['tf_ptb']

    for q in range(4):
        for ti in range(4):
            tt = q * 4 + ti
            for ws in range(4):
                ps, psk = (p0, 'tf_p0') if ws % 2 == 0 else (p1, 'tf_p1')

                def fn(eng, tt=tt, ws=ws, ps=ps):
                    for k in range(NCH):
                        eng.matmul(ps[:, :], xn(k)[:, tt * 128:(tt + 1) * 128],
                                   win[:, k * T + ws * 512:k * T + (ws + 1) * 512],
                                   start=(k == 0), stop=False)
                    return eng.matmul(ps[:, :], b['onr'][0:1, :],
                                      b['ti_binr'][0:1, ws * 512:(ws + 1) * 512],
                                      start=False, stop=True)
                sch.g('tensor', fn, reads=['ti_win', 'ti_binr', 'ones'] + [('xnbf', k) for k in range(NCH)],
                      writes=[psk], tag=f"{pre}.lg{tt}.{ws}")

                def fn(eng, ti=ti, ws=ws, ps=ps):
                    return eng.activation(out=ac[:, ti * T + ws * 512:ti * T + (ws + 1) * 512],
                                          in_=ps[:, :], func=AF.Silu)
                sch.g('scalar', fn, reads=[psk], writes=[('ac', ti)], tag=f"{pre}.si{tt}.{ws}")

                def fn(eng, ti=ti, ws=ws, ps=ps, bb=b):
                    return eng.tensor_single_scalar(dc[:, ti * T + ws * 512:ti * T + (ws + 1) * 512],
                                                    ps[:, :], cc(bb, 0.0), op=ALU.is_gt)
                sch.g('vector', fn, reads=[psk, 'consts', ('ac', ti)], writes=[('dc', ti)], tag=f"{pre}.dc{tt}.{ws}")

        for ti in range(4):
            tt = q * 4 + ti
            io = ti * T
            ind3 = ind[:, io:io + 2040].rearrange("p (e n) -> p e n", n=255)
            dc3 = dc[:, io:io + 2040].rearrange("p (e n) -> p e n", n=255)

            if 'iz' in skip:
                pass
            else:
                def fn(eng, ti=ti, io=io):
                    return eng.memset(ind[:, io:io + T], 0.0)
                sch.g('vector', fn, reads=[], writes=[('ind', ti)], tag=f"{pre}.iz{tt}")

            if 'iz' not in skip:
                def fn(eng, ti=ti, io=io):
                    return eng.memset(ind[:, io:io + 2040].rearrange("p (e n) -> p e n", n=255)[:, :, 0:1], 1.0)
                sch.g('vector', fn, reads=[], writes=[('ind', ti)], tag=f"{pre}.iz2{tt}")

            for d in (() if 'walk' in skip else range(7)):
                a0 = (1 << d) - 1
                a1 = (1 << (d + 1)) - 1
                njd = 1 << d
                par = ind3[:, :, a0:a1]
                decp = dc3[:, :, a0:a1]
                chpair = ind3[:, :, a1:a1 + 2 * njd].rearrange("p e (i two) -> p e i two", two=2)
                right = chpair[:, :, :, 1:2].rearrange("p e i two -> p e (i two)")
                left = chpair[:, :, :, 0:1].rearrange("p e i two -> p e (i two)")

                def fn(eng, par=par, decp=decp, right=right):
                    return eng.tensor_mul(right, par, decp)
                sch.g('vector', fn, reads=[('ind', ti), ('dc', ti)], writes=[('ind', ti)],
                      tag=f"{pre}.wka{tt}.{d}")

                def fn(eng, par=par, right=right, left=left):
                    return eng.tensor_sub(left, par, right)
                sch.g('vector', fn, reads=[('ind', ti)], writes=[('ind', ti)],
                      tag=f"{pre}.wkb{tt}.{d}")

            if 'mk' not in skip:
                def fn(eng, ti=ti, io=io):
                    return eng.tensor_mul(ac[:, io:io + T], ac[:, io:io + T], ind[:, io:io + T])
                sch.g('vector', fn, reads=[('ac', ti), ('ind', ti)], writes=[('ac', ti)],
                      tag=f"{pre}.mk{tt}")

            for wc in (() if 'tr' in skip else range(16)):
                def fn(eng, ti=ti, io=io, wc=wc):
                    return eng.transpose(ptb[:, :], ac[:, io + wc * 128:io + (wc + 1) * 128],
                                         b['idbf'][:, :])
                sch.g('tensor', fn, reads=[('ac', ti), 'idbf'], writes=['tf_ptb'],
                      tag=f"{pre}.tr{tt}.{wc}")

                def fn(eng, ti=ti, wc=wc):
                    return eng.tensor_copy(mT[:, wc * 512 + ti * 128:wc * 512 + (ti + 1) * 128],
                                           ptb[:, :])
                sch.g('vector', fn, reads=['tf_ptb'], writes=[('mT', wc)], tag=f"{pre}.te{tt}.{wc}")

        for oc in range(NCH):
            ps, psk = (p0, 'tf_p0') if oc % 2 == 0 else (p1, 'tf_p1')

            def fn(eng, q=q, oc=oc, ps=ps):
                last = None
                for wc in range(16):
                    last = eng.matmul(ps[:, :], wout[:, wc * 1024 + oc * 128:wc * 1024 + (oc + 1) * 128],
                                      mT[:, wc * 512:(wc + 1) * 512],
                                      start=(wc == 0), stop=(wc == 15))
                return last
            sch.g('tensor', fn, reads=['ti_wout'] + [('mT', wc) for wc in range(16)],
                  writes=[psk], tag=f"{pre}.out{q}.{oc}")

            out_store(sch, q, oc, ps, psk)



def emit_attn0_v2(sch, b, wload_chunk, pre="a0"):
    xn = b['xnbf']
    kT, v, qt, sc, at = b['kT'], b['v'], b['qt'], b['sc'], b['at']
    PS = b['PS']
    wc = b['wc']
    masks = b['masks']

    def fn(eng):
        return eng.activation(out=b['lng'][:, :], in_=b['gcols'][:, :], func=AF.Ln)
    sch.g('scalar', fn, reads=['gcols'], writes=['lng'], tag=f"{pre}.lng")

    for tt in range(4):
        for k in range(NCH):
            wload_chunk(sch, 'WkT', k, k % 2)

            def fn(eng, tt=tt, k=k):
                last = None
                for c in range(NCH):
                    last = eng.matmul(PS[c][:, :], wc[:, (k % 2) * 1024 + c * 128:(k % 2) * 1024 + (c + 1) * 128],
                                      xn(k)[:, tt * 512:(tt + 1) * 512],
                                      start=(k == 0), stop=False)
                return last
            sch.g('tensor', fn, reads=[('wc', k % 2)] + [('xnbf', k)],
                  writes=[('P', c) for c in range(NCH)], tag=f"{pre}.kp{tt}.{k}")

        def fn(eng, tt=tt):
            last = None
            for c in range(NCH):
                last = eng.matmul(PS[c][:, :], b['bk_row'][0:1, c * 128:(c + 1) * 128],
                                  b['on512b'][0:1, :], start=False, stop=True)
            return last
        sch.g('tensor', fn, reads=['bk_row', 'on512b'], writes=[('P', c) for c in range(NCH)],
              tag=f"{pre}.kb{tt}")

        for c in range(NCH):
            def fn(eng, c=c, tt=tt):
                return eng.copy(out=kT[:, c * T + tt * 512:c * T + (tt + 1) * 512], in_=PS[c][:, :])
            sch.g('scalar', fn, reads=[('P', c)], writes=[('kT', c)], tag=f"{pre}.ke{tt}.{c}")

    for grp in range(4):
        for k in range(NCH):
            wload_chunk(sch, 'WvT', k, k % 2)

            def fn(eng, grp=grp, k=k):
                last = None
                for j in range(4):
                    jt = grp * 4 + j
                    for h in range(2):
                        last = eng.matmul(PS[j * 2 + h][:, :], xn(k)[:, jt * 128:(jt + 1) * 128],
                                          wc[:, (k % 2) * 1024 + h * 512:(k % 2) * 1024 + (h + 1) * 512],
                                          start=(k == 0), stop=False)
                return last
            sch.g('tensor', fn, reads=[('wc', k % 2), ('xnbf', k)],
                  writes=[('P', i) for i in range(8)], tag=f"{pre}.vp{grp}.{k}")

        def fn(eng, grp=grp):
            last = None
            for j in range(4):
                for h in range(2):
                    last = eng.matmul(PS[j * 2 + h][:, :], b['onrb'][0:1, :],
                                      b['bv_row'][0:1, h * 512:(h + 1) * 512], start=False, stop=True)
            return last
        sch.g('tensor', fn, reads=['onrb', 'bv_row'], writes=[('P', i) for i in range(8)],
              tag=f"{pre}.vb{grp}")

        for j in range(4):
            jt = grp * 4 + j
            for h in range(2):
                def fn(eng, jt=jt, j=j, h=h):
                    return eng.copy(out=v[:, jt * 1024 + h * 512:jt * 1024 + (h + 1) * 512],
                                    in_=PS[j * 2 + h][:, :])
                sch.g('scalar', fn, reads=[('P', j * 2 + h)], writes=[('v', jt)],
                      tag=f"{pre}.ve{jt}.{h}")

    for t in range(4):
        nbj = 4 * t + 4
        for k in range(NCH):
            wload_chunk(sch, 'WqT', k, k % 2)

            def fn(eng, t=t, k=k):
                last = None
                for c in range(NCH):
                    last = eng.matmul(PS[c][:, :], wc[:, (k % 2) * 1024 + c * 128:(k % 2) * 1024 + (c + 1) * 128],
                                      xn(k)[:, t * 512:(t + 1) * 512],
                                      start=(k == 0), stop=False)
                return last
            sch.g('tensor', fn, reads=[('wc', k % 2), ('xnbf', k)],
                  writes=[('P', c) for c in range(NCH)], tag=f"{pre}.qp{t}.{k}")

        def fn(eng, t=t):
            last = None
            for c in range(NCH):
                last = eng.matmul(PS[c][:, :], b['bq_row'][0:1, c * 128:(c + 1) * 128],
                                  b['on512b'][0:1, :], start=False, stop=True)
            return last
        sch.g('tensor', fn, reads=['bq_row', 'on512b'], writes=[('P', c) for c in range(NCH)],
              tag=f"{pre}.qb{t}")

        for c in range(NCH):
            def fn(eng, c=c):
                return eng.copy(out=qt[:, c * 512:(c + 1) * 512], in_=PS[c][:, :])
            sch.g('scalar', fn, reads=[('P', c)], writes=[('qt', c)], tag=f"{pre}.qe{t}.{c}")

        for c in range(NCH):
            for w0 in range(0, nbj, 4):
                wave = range(w0, min(w0 + 4, nbj))
                for bj in wave:
                    sA, sB = PS[bj % 3], PS[3 + bj % 3]
                    kA = ('P', bj % 3)
                    kB = ('P', 3 + bj % 3)
                    sl = bj % 4

                    def fn(eng, c=c, bj=bj, sA=sA, sB=sB):
                        eng.matmul(sA[:, :], kT[0:64, c * T + bj * 128:c * T + (bj + 1) * 128],
                                   qt[0:64, c * 512:(c + 1) * 512], start=True, stop=True,
                                   tile_position=(0, 0))
                        return eng.matmul(sB[:, :], kT[64:128, c * T + bj * 128:c * T + (bj + 1) * 128],
                                          qt[64:128, c * 512:(c + 1) * 512], start=True, stop=True,
                                          tile_position=(64, 0))
                    sch.g('tensor', fn, reads=[('kT', c), ('qt', c)], writes=[kA, kB],
                          tag=f"{pre}.sc{t}.{c}.{bj}")

                    r = bj - 4 * t
                    if r >= 0:
                        def fn(eng, sl=sl, r=r, sA=sA, sB=sB):
                            eng.tensor_mul(sc[:, (2 * sl) * 512:(2 * sl + 1) * 512], sA[:, :],
                                           masks[:, r * 512:(r + 1) * 512])
                            return eng.tensor_mul(sc[:, (2 * sl + 1) * 512:(2 * sl + 2) * 512], sB[:, :],
                                                  masks[:, r * 512:(r + 1) * 512])
                        sch.g('vector', fn, reads=[kA, kB, 'masks'],
                              writes=[('scb', sl)], tag=f"{pre}.se{t}.{c}.{bj}")
                    else:
                        def fn(eng, sl=sl, sA=sA, sB=sB):
                            eng.copy(out=sc[:, (2 * sl) * 512:(2 * sl + 1) * 512], in_=sA[:, :])
                            return eng.copy(out=sc[:, (2 * sl + 1) * 512:(2 * sl + 2) * 512], in_=sB[:, :])
                        sch.g('scalar', fn, reads=[kA, kB], writes=[('scb', sl)],
                              tag=f"{pre}.se{t}.{c}.{bj}")

                for bj in wave:
                    sl = bj % 4

                    def fn(eng, c=c, bj=bj, sl=sl, nbj=nbj):
                        eng.matmul(PS[6][0:64, :], v[:, bj * 1024 + c * 128:bj * 1024 + c * 128 + 64],
                                   sc[:, (2 * sl) * 512:(2 * sl + 1) * 512],
                                   start=(bj == 0), stop=(bj == nbj - 1), tile_position=(0, 0))
                        return eng.matmul(PS[6][64:128, :], v[:, bj * 1024 + c * 128 + 64:bj * 1024 + (c + 1) * 128],
                                          sc[:, (2 * sl + 1) * 512:(2 * sl + 2) * 512],
                                          start=(bj == 0), stop=(bj == nbj - 1), tile_position=(0, 64))
                    sch.g('tensor', fn, reads=[('scb', sl), ('v', bj)], writes=[('P', 6)],
                          tag=f"{pre}.av{t}.{c}.{bj}")

            def fn(eng, c=c, t=t):
                return eng.tensor_scalar_mul(b['dkb'][:, 0:1], b['lng'][:, c:c + 1],
                                             cc(b, float(512 * t)))
            sch.g('vector', fn, reads=['lng', 'consts'], writes=['dkb'], tag=f"{pre}.dkb{t}.{c}")

            def fn(eng, c=c, t=t):
                return eng.activation(out=b['dk'][:, :], in_=b['iota512'][:, :],
                                      func=AF.Exp, scale=b['lng'][:, c:c + 1],
                                      bias=b['dkb'][:, 0:1])
            sch.g('scalar', fn, reads=['lng', 'iota512', 'dkb'], writes=['dk'], tag=f"{pre}.dk{t}.{c}")

            def fn(eng, c=c):
                return eng.tensor_mul(at[:, c * 512:(c + 1) * 512], PS[6][:, :], b['dk'][:, :])
            sch.g('vector', fn, reads=[('P', 6), 'dk'], writes=[('at', c)], tag=f"{pre}.ae{t}.{c}")

        for k in range(NCH):
            wload_chunk(sch, 'WoT', k, k % 2)

            def fn(eng, t=t, k=k):
                last = None
                for o in range(NCH):
                    last = eng.matmul(PS[o][:, :], wc[:, (k % 2) * 1024 + o * 128:(k % 2) * 1024 + (o + 1) * 128],
                                      at[:, k * 512:(k + 1) * 512],
                                      start=(k == 0), stop=(k == NCH - 1))
                return last
            sch.g('tensor', fn, reads=[('wc', k % 2), ('at', k)],
                  writes=[('P', o) for o in range(NCH)], tag=f"{pre}.wo{t}.{k}")

        for o in range(NCH):
            def fn(eng, o=o, t=t):
                Rs = b['R'](o)[:, t * 512:(t + 1) * 512]
                return eng.scalar_tensor_tensor(out=Rs, in0=PS[o][:, :],
                                                scalar=b['bo_cols'][:, o:o + 1],
                                                in1=Rs, op0=ALU.add, op1=ALU.add)
            sch.g('vector', fn, reads=[('P', o), 'bo_cols', ('R', o)], writes=[('R', o)],
                  tag=f"{pre}.x1.{o}.{t}")



def emit_shared_v2(sch, b, ld_sw, ld_sw3c, pre="sh"):
    xn = b['xnbf']
    sw12, gg, sa, wc3 = b['sw12'], b['gg'], b['sa'], b['wc3']
    PS = b['PS']

    for th in range(2):
        to = th * 1024
        for ih in range(2):
            ld_sw(sch, 'sw1T', ih, 0)
            ld_sw(sch, 'sw2T', ih, 1)
            for ic8 in range(8):
                ic = ih * 8 + ic8
                for tt in range(2):
                    base = (ic8 % 2) * 4
                    pa, pb = PS[base + tt * 2], PS[base + tt * 2 + 1]
                    ka, kb = ('P', base + tt * 2), ('P', base + tt * 2 + 1)

                    def fn(eng, ic8=ic8, tt=tt, to=to, pa=pa):
                        last = None
                        for k in range(NCH):
                            last = eng.matmul(pa[:, :], sw12[:, k * 1024 + ic8 * 128:k * 1024 + (ic8 + 1) * 128],
                                              xn(k)[:, to + tt * 512:to + (tt + 1) * 512],
                                              start=(k == 0), stop=(k == NCH - 1))
                        return last
                    sch.g('tensor', fn, reads=[('sw12', 0)] + [('xnbf', k) for k in range(NCH)],
                          writes=[ka], tag=f"{pre}.ua{th}.{ic}.{tt}")

                    def fn(eng, tt=tt, pa=pa):
                        return eng.activation(out=sa[:, tt * 512:(tt + 1) * 512], in_=pa[:, :], func=AF.Silu)
                    sch.g('scalar', fn, reads=[ka], writes=[('sa', tt)], tag=f"{pre}.uae{th}.{ic}.{tt}")

                    def fn(eng, ic8=ic8, tt=tt, to=to, pb=pb):
                        last = None
                        for k in range(NCH):
                            last = eng.matmul(pb[:, :], sw12[:, 8192 + k * 1024 + ic8 * 128:8192 + k * 1024 + (ic8 + 1) * 128],
                                              xn(k)[:, to + tt * 512:to + (tt + 1) * 512],
                                              start=(k == 0), stop=(k == NCH - 1))
                        return last
                    sch.g('tensor', fn, reads=[('sw12', 1)] + [('xnbf', k) for k in range(NCH)],
                          writes=[kb], tag=f"{pre}.ub{th}.{ic}.{tt}")

                    def fn(eng, ic=ic, tt=tt, pb=pb):
                        return eng.tensor_mul(gg[:, ic * 1024 + tt * 512:ic * 1024 + (tt + 1) * 512],
                                              pb[:, :], sa[:, tt * 512:(tt + 1) * 512])
                    sch.g('vector', fn, reads=[kb, ('sa', tt)], writes=[('gg', ic)],
                          tag=f"{pre}.gg{th}.{ic}.{tt}")

        for tt in range(2):
            for wcI in range(16):
                ld_sw3c(sch, wcI, wcI % 2)

                def fn(eng, wcI=wcI, tt=tt):
                    last = None
                    for oc in range(NCH):
                        last = eng.matmul(PS[oc][:, :],
                                          wc3[:, (wcI % 2) * 1024 + oc * 128:(wcI % 2) * 1024 + (oc + 1) * 128],
                                          gg[:, wcI * 1024 + tt * 512:wcI * 1024 + (tt + 1) * 512],
                                          start=(wcI == 0), stop=(wcI == 15))
                    return last
                sch.g('tensor', fn, reads=[('wc3', wcI % 2), ('gg', wcI)],
                      writes=[('P', oc) for oc in range(NCH)], tag=f"{pre}.z{th}.{tt}.{wcI}")

            for oc in range(NCH):
                def fn(eng, oc=oc, th=th, tt=tt):
                    Rs = b['R'](oc)[:, th * 1024 + tt * 512:th * 1024 + (tt + 1) * 512]
                    return eng.tensor_add(Rs, Rs, PS[oc][:, :])
                sch.g('vector', fn, reads=[('P', oc), ('R', oc)], writes=[('R', oc)],
                      tag=f"{pre}.ze{th}.{tt}.{oc}")



def emit_l1attn_v2(sch, b, pre="l1"):
    ND = T - 1
    xn = b['xnbf']
    df = b['df']
    wsq = b['wsq']
    PS = b['PS']
    ptb = b['l1_ptb']
    l1c = b['l1c']
    rwb, rwc = b['rwb'], b['rwc']
    onc, onr = b['onc'], b['onr']
    pd, pV, pB, pc = PS[5], PS[6], PS[7], PS[3]
    kpd, kpV, kpB, kpc = ('P', 5), ('P', 6), ('P', 7), ('P', 3)
    rslots = [(PS[0], 0, ('P', 0)), (PS[0], 32, ('P', 0)), (PS[0], 64, ('P', 0)),
              (PS[1], 0, ('P', 1))]

    for gc in range(4):
        def fn(eng, gc=gc):
            return eng.activation(out=wsq[:, 0:1024], in_=b['Gbf'][:, gc * 1024:(gc + 1) * 1024],
                                  func=AF.Square)
        sch.g('scalar', fn, reads=['Gbf'], writes=['l1sq'], tag=f"{pre}.gsq{gc}")

        def fn(eng, gc=gc):
            return eng.reduce_sum(l1c[:, 48 + gc:49 + gc], wsq[:, 0:1024], axis=AX.X)
        sch.g('vector', fn, reads=['l1sq'], writes=[('gn', gc)], tag=f"{pre}.gss{gc}")

    def fn(eng):
        return eng.tensor_scalar_max(l1c[:, 48:52], l1c[:, 48:52], cc(b, 1e-16))
    sch.g('vector', fn, reads=[('gn', g) for g in range(4)] + ['consts'], writes=['gnc'], tag=f"{pre}.gcl")

    def fn(eng):
        return eng.activation(out=l1c[:, 52:56], in_=l1c[:, 48:52], func=AF.Ln)
    sch.g('scalar', fn, reads=['gnc'], writes=['gnl'], tag=f"{pre}.gln")

    def fn(eng):
        return eng.activation(out=l1c[:, 48:52], in_=l1c[:, 52:56], func=AF.Exp, scale=cc(b, -0.5))
    sch.g('scalar', fn, reads=['gnl', 'consts'], writes=['gni'], tag=f"{pre}.gexp")

    for gc in range(4):
        def fn(eng, gc=gc):
            return eng.transpose(pV[0:1, 0:128], l1c[:, 48 + gc:49 + gc], b['id128'][:, :])
        sch.g('tensor', fn, reads=['gni', 'id128'], writes=[kpV], tag=f"{pre}.gtr{gc}")

        def fn(eng, gc=gc):
            return eng.tensor_copy(rwc[0:1, gc * 128:(gc + 1) * 128], pV[0:1, 0:128])
        sch.g('vector', fn, reads=[kpV], writes=['gnr'], tag=f"{pre}.gte{gc}")

    def fn(eng):
        return eng.matmul(pB[:, :], onr[0:1, :], rwc[0:1, 0:512], start=True, stop=True)
    sch.g('tensor', fn, reads=['gnr', 'ones'], writes=[kpB], tag=f"{pre}.gbc")

    def fn(eng):
        return eng.copy(out=b['invgn'][:, :], in_=pB[:, :])
    sch.g('scalar', fn, reads=[kpB], writes=['invgn'], tag=f"{pre}.gbce")

    for c in range(NCH):
        def fn(eng, c=c):
            eng.tensor_sub(df(c)[:, 0:ND], xn(c)[:, 1:T], xn(c)[:, 0:ND])
            return eng.memset(df(c)[:, ND:T], 0.0)
        sch.g('vector', fn, reads=[('xnbf', c)], writes=[('df', c)], tag=f"{pre}.df{c}")

    for c in range(NCH):
        def fn(eng, c=c):
            return eng.activation(out=wsq[:, :], in_=df(c)[:, :], func=AF.Square)
        sch.g('scalar', fn, reads=[('df', c), 'gnr'], writes=['l1sq'], tag=f"{pre}.sq{c}")

        def fn(eng, c=c):
            last = None
            for st in range(16):
                last = eng.matmul(pc[:, st:st + 1], wsq[:, st * 128:(st + 1) * 128], onc[:, :],
                                  start=(c == 0), stop=(c == NCH - 1))
            return last
        sch.g('tensor', fn, reads=['l1sq', 'ones'], writes=[kpc], tag=f"{pre}.cmm{c}")

        def fn(eng, c=c):
            last = None
            for t4 in range(4):
                ps, r, _ = rslots[t4]
                last = eng.matmul(ps[r:r + 1, :], onc[:, :], wsq[:, t4 * 512:(t4 + 1) * 512],
                                  start=(c == 0), stop=(c == NCH - 1))
            return last
        sch.g('tensor', fn, reads=['l1sq', 'ones'], writes=[('P', 0), ('P', 1)], tag=f"{pre}.rmm{c}")

    def fn(eng):
        return eng.tensor_copy(l1c[:, 0:16], pc[:, 0:16])
    sch.g('vector', fn, reads=[kpc], writes=['scol'], tag=f"{pre}.cev")

    def fn(eng):
        last = None
        for t4 in range(4):
            ps, r, _ = rslots[t4]
            last = eng.copy(out=rwb[0:1, t4 * 512:(t4 + 1) * 512], in_=ps[r:r + 1, :])
        return last
    sch.g('scalar', fn, reads=[('P', 0), ('P', 1)], writes=['rwb'], tag=f"{pre}.rev")

    def fn(eng):
        return eng.tensor_scalar_max(l1c[:, 0:16], l1c[:, 0:16], cc(b, 1e-12))
    sch.g('vector', fn, reads=['scol', 'consts'], writes=['scol'], tag=f"{pre}.ccl")

    def fn(eng):
        return eng.activation(out=l1c[:, 16:32], in_=l1c[:, 0:16], func=AF.Ln)
    sch.g('scalar', fn, reads=['scol'], writes=['lnc'], tag=f"{pre}.cln")

    def fn(eng):
        eng.activation(out=l1c[:, 0:16], in_=l1c[:, 16:32], func=AF.Exp, scale=cc(b, -0.5))
        return eng.activation(out=l1c[:, 32:48], in_=l1c[:, 16:32], func=AF.Exp, scale=cc(b, 0.5))
    sch.g('scalar', fn, reads=['lnc', 'consts'], writes=['invmc', 'magc'], tag=f"{pre}.cexp")

    def fn(eng):
        return eng.tensor_mul(l1c[:, 16:32], l1c[:, 0:16], l1c[:, 32:48])
    sch.g('vector', fn, reads=['invmc', 'magc'], writes=['anc'], tag=f"{pre}.can")

    def fn(eng):
        return eng.tensor_scalar_max(l1c[:, 16:32], l1c[:, 16:32], cc(b, 1e-8))
    sch.g('vector', fn, reads=['anc', 'consts'], writes=['anc'], tag=f"{pre}.crc1")

    def fn(eng):
        return eng.reciprocal(l1c[:, 16:32], l1c[:, 16:32])
    sch.g('vector', fn, reads=['anc'], writes=['anc'], tag=f"{pre}.crc2")

    def fn(eng):
        return eng.scalar_tensor_tensor(l1c[:, 16:32], l1c[:, 16:32], cc(b, 2.0), l1c[:, 0:16],
                                        op0=ALU.mult, op1=ALU.mult)
    sch.g('vector', fn, reads=['anc', 'invmc', 'consts'], writes=['rsc'], tag=f"{pre}.crs")

    def fn(eng):
        return eng.activation(out=l1c[:, 32:48], in_=l1c[:, 32:48], func=AF.Tanh, scale=cc(b, 2.0))
    sch.g('scalar', fn, reads=['magc', 'consts'], writes=['tnc'], tag=f"{pre}.ctn")

    def fn(eng):
        return eng.tensor_scalar_max(rwb[0:1, :], rwb[0:1, :], cc(b, 1e-12, 1))
    sch.g('vector', fn, reads=['rwb', 'consts'], writes=['rwb'], tag=f"{pre}.rcl")

    def fn(eng):
        return eng.activation(out=rwb[0:1, :], in_=rwb[0:1, :], func=AF.Ln)
    sch.g('scalar', fn, reads=['rwb'], writes=['rwb'], tag=f"{pre}.rln")

    def fn(eng):
        return eng.activation(out=rwc[0:1, :], in_=rwb[0:1, :], func=AF.Exp, scale=cc(b, 0.5, 1))
    sch.g('scalar', fn, reads=['rwb', 'consts', 'gnr'], writes=['rwc'], tag=f"{pre}.rmag")

    def fn(eng):
        return eng.activation(out=rwc[0:1, :], in_=rwc[0:1, :], func=AF.Tanh, scale=cc(b, 2.0, 1))
    sch.g('scalar', fn, reads=['rwc', 'consts'], writes=['rwc'], tag=f"{pre}.rtn")

    def fn(eng):
        return eng.activation(out=rwb[0:1, :], in_=rwb[0:1, :], func=AF.Exp, scale=cc(b, -0.5, 1))
    sch.g('scalar', fn, reads=['rwb', 'consts'], writes=['rwb'], tag=f"{pre}.rinv")

    def fn(eng):
        return eng.scalar_tensor_tensor(rwb[0:1, :], rwc[0:1, :], cc(b, 0.6, 1), rwb[0:1, :],
                                        op0=ALU.mult, op1=ALU.mult)
    sch.g('vector', fn, reads=['rwb', 'rwc', 'consts'], writes=['rwb'], tag=f"{pre}.rap")

    for st in range(16):
        def fn(eng, st=st):
            last = None
            for k in range(NCH):
                last = eng.matmul(pd[:, :], df(k)[:, st * 128:(st + 1) * 128],
                                  b['GTbf'][:, k * 512:(k + 1) * 512],
                                  start=(k == 0), stop=(k == NCH - 1))
            return last
        sch.g('tensor', fn, reads=['GTbf'] + [('df', k) for k in range(NCH)],
              writes=[kpd], tag=f"{pre}.dot{st}")

        def fn(eng, st=st):
            return eng.scalar_tensor_tensor(b['sims'][:, :], pd[:, :], l1c[:, 16 + st:17 + st],
                                            b['invgn'][:, :], op0=ALU.mult, op1=ALU.mult)
        sch.g('vector', fn, reads=[kpd, 'rsc', 'invgn'], writes=['sims'], tag=f"{pre}.sim{st}")

        def fn(eng, st=st):
            return eng.reduce_max(b['smx'][:, 0:1], b['sims'][:, :], axis=AX.X, negate=True)
        sch.g('vector', fn, reads=['sims'], writes=['smx'], tag=f"{pre}.smx{st}")

        def fn(eng, st=st):
            return eng.activation(out=b['sims'][:, :], in_=b['sims'][:, :], func=AF.Exp,
                                  bias=b['smx'][:, 0:1])
        sch.g('scalar', fn, reads=['sims', 'smx'], writes=['sims'], tag=f"{pre}.sex{st}")

        def fn(eng, st=st):
            return eng.reduce_sum(b['smx'][:, 1:2], b['sims'][:, :], axis=AX.X)
        sch.g('vector', fn, reads=['sims'], writes=['smx2'], tag=f"{pre}.ssm{st}")

        def fn(eng, st=st):
            return eng.reciprocal(b['smx'][:, 2:3], b['smx'][:, 1:2])
        sch.g('vector', fn, reads=['smx2'], writes=['smx3'], tag=f"{pre}.src{st}")

        def fn(eng, st=st):
            return eng.scalar_tensor_tensor(b['smx'][:, 3:4], b['smx'][:, 2:3], cc(b, 0.4),
                                            l1c[:, 32 + st:33 + st], op0=ALU.mult, op1=ALU.mult)
        sch.g('vector', fn, reads=['smx3', 'tnc', 'consts'], writes=['smx4'], tag=f"{pre}.snr{st}")

        def fn(eng, st=st):
            return eng.tensor_scalar(b['gwt'][:, :], b['sims'][:, :], b['smx'][:, 3:4], None,
                                     op0=ALU.mult)
        sch.g('vector', fn, reads=['sims', 'smx4'], writes=['gwt'], tag=f"{pre}.gw{st}")

        for j in range(4):
            def fn(eng, st=st, j=j):
                return eng.transpose(ptb[:, :], b['gwt'][:, j * 128:(j + 1) * 128], b['idbf'][:, :])
            sch.g('tensor', fn, reads=['gwt', 'idbf'], writes=['l1_ptb'], tag=f"{pre}.gt{st}.{j}")

            def fn(eng, st=st, j=j):
                return eng.tensor_copy(b['gwT'][:, j * T + st * 128:j * T + (st + 1) * 128], ptb[:, :])
            sch.g('vector', fn, reads=['l1_ptb'], writes=[('gwT', j)], tag=f"{pre}.gte{st}.{j}")

    for st2 in range(4):
        def fn(eng, st2=st2):
            return eng.matmul(pB[:, :], onr[0:1, :], rwb[0:1, st2 * 512:(st2 + 1) * 512],
                              start=True, stop=True)
        sch.g('tensor', fn, reads=['rwb', 'ones'], writes=[kpB], tag=f"{pre}.abc{st2}")

        def fn(eng, st2=st2):
            return eng.copy(out=b['apb'][:, :], in_=pB[:, :])
        sch.g('scalar', fn, reads=[kpB], writes=['apb'], tag=f"{pre}.abce{st2}")

        for c in range(NCH):
            pu = pd if c % 2 == 0 else pV
            puk = kpd if c % 2 == 0 else kpV

            def fn(eng, st2=st2, c=c, pu=pu):
                last = None
                for gc in range(4):
                    last = eng.matmul(pu[:, :], b['Gbf'][:, gc * 1024 + c * 128:gc * 1024 + (c + 1) * 128],
                                      b['gwT'][:, gc * T + st2 * 512:gc * T + (st2 + 1) * 512],
                                      start=(gc == 0), stop=(gc == 3))
                return last
            sch.g('tensor', fn, reads=['Gbf'] + [('gwT', gc) for gc in range(4)],
                  writes=[puk], tag=f"{pre}.inf{st2}.{c}")

            def fn(eng, st2=st2, c=c):
                return eng.tensor_mul(df(c)[:, st2 * 512:(st2 + 1) * 512],
                                      df(c)[:, st2 * 512:(st2 + 1) * 512], b['apb'][:, :])
            sch.g('vector', fn, reads=[('df', c), 'apb'], writes=[('df', c)], tag=f"{pre}.ua{st2}.{c}")

            def fn(eng, st2=st2, c=c, pu=pu):
                return eng.tensor_add(df(c)[:, st2 * 512:(st2 + 1) * 512],
                                      df(c)[:, st2 * 512:(st2 + 1) * 512], pu[:, :])
            sch.g('vector', fn, reads=[('df', c), puk], writes=[('df', c)], tag=f"{pre}.ub{st2}.{c}")

    def fn(eng):
        return eng.tensor_scalar_mul(rwb[0:1, 8:T], rwc[0:1, 0:T - 8], cc(b, TAPC[0], 1))
    sch.g('vector', fn, reads=['rwc', 'consts'], writes=['rwb'], tag=f"{pre}.dtap0")
    for k in range(1, 8):
        def fn(eng, k=k):
            return eng.scalar_tensor_tensor(rwb[0:1, 8:T], rwc[0:1, k:T - 8 + k], cc(b, TAPC[k], 1),
                                            rwb[0:1, 8:T], op0=ALU.mult, op1=ALU.add)
        sch.g('vector', fn, reads=['rwc', 'rwb', 'consts'], writes=['rwb'], tag=f"{pre}.dtap{k}")

    def fn(eng):
        return eng.matmul(pV[0:1, 0:8], l1c[0:8, 32:33], b['Cmat'][0:8, 0:8], start=True, stop=True)
    sch.g('tensor', fn, reads=['tnc', 'Cmat'], writes=[kpV], tag=f"{pre}.dsm")

    def fn(eng):
        return eng.tensor_copy(rwb[0:1, 0:8], pV[0:1, 0:8])
    sch.g('vector', fn, reads=[kpV], writes=['rwb'], tag=f"{pre}.dsme")

    def fn(eng):
        return eng.tensor_scalar_max(rwb[0:1, :], rwb[0:1, :], cc(b, 1e-8, 1))
    sch.g('vector', fn, reads=['rwb', 'consts'], writes=['rwb'], tag=f"{pre}.dcl")

    def fn(eng):
        return eng.activation(out=rwb[0:1, :], in_=rwb[0:1, :], func=AF.Ln)
    sch.g('scalar', fn, reads=['rwb'], writes=['rwb'], tag=f"{pre}.dinv1")

    def fn(eng):
        return eng.activation(out=rwb[0:1, :], in_=rwb[0:1, :], func=AF.Exp, scale=cc(b, -1.0, 1))
    sch.g('scalar', fn, reads=['rwb', 'consts'], writes=['rwb'], tag=f"{pre}.dinv2")

    for st2 in range(4):
        def fn(eng, st2=st2):
            return eng.matmul(pB[:, :], onr[0:1, :], rwb[0:1, st2 * 512:(st2 + 1) * 512],
                              start=True, stop=True)
        sch.g('tensor', fn, reads=['rwb', 'ones'], writes=[kpB], tag=f"{pre}.ibc{st2}")

        def fn(eng, st2=st2):
            return eng.copy(out=b['ivb'][:, st2 * 512:(st2 + 1) * 512], in_=pB[:, :])
        sch.g('scalar', fn, reads=[kpB], writes=['ivb'], tag=f"{pre}.ibce{st2}")

    for c in range(NCH):
        acc = b['accb']

        def fn(eng, c=c):
            return eng.tensor_scalar_mul(acc[:, 8:T], df(c)[:, 0:T - 8], cc(b, TAPC[0]))
        sch.g('vector', fn, reads=[('df', c), 'consts'], writes=['acc'],
              tag=f"{pre}.t0.{c}")
        for k in range(1, 8):
            def fn(eng, c=c, k=k):
                return eng.scalar_tensor_tensor(acc[:, 8:T], df(c)[:, k:T - 8 + k], cc(b, TAPC[k]),
                                                acc[:, 8:T], op0=ALU.mult, op1=ALU.add)
            sch.g('vector', fn, reads=[('df', c), 'acc', 'consts'], writes=['acc'], tag=f"{pre}.t{k}.{c}")

        def fn(eng, c=c):
            return eng.transpose(ptb[0:8, 0:128], df(c)[:, 0:8], b['idbf'][:, :])
        sch.g('tensor', fn, reads=[('df', c), 'idbf'], writes=['l1_ptb'], tag=f"{pre}.ut{c}")

        def fn(eng, c=c):
            return eng.tensor_copy(b['Usm'][0:8, :], ptb[0:8, 0:128])
        sch.g('vector', fn, reads=['l1_ptb'], writes=['Usm'], tag=f"{pre}.ute{c}")

        def fn(eng, c=c):
            return eng.matmul(pV[0:128, 0:8], b['Usm'][0:8, :], b['Cmbf'][0:8, 0:8],
                              start=True, stop=True)
        sch.g('tensor', fn, reads=['Usm', 'Cmbf'], writes=[kpV], tag=f"{pre}.usm{c}")

        def fn(eng, c=c):
            return eng.tensor_copy(acc[:, 0:8], pV[0:128, 0:8])
        sch.g('vector', fn, reads=[kpV], writes=['acc'], tag=f"{pre}.usme{c}")

        for st2 in range(4):
            def fn(eng, c=c, st2=st2):
                return eng.tensor_mul(b['tjb'][:, :], acc[:, st2 * 512:(st2 + 1) * 512],
                                      b['ivb'][:, st2 * 512:(st2 + 1) * 512])
            sch.g('gpsimd', fn, reads=['acc', 'ivb'], writes=['tjb'], tag=f"{pre}.tj{c}.{st2}")

            def fn(eng, c=c, st2=st2):
                Rs = b['R'](c)[:, st2 * 512:(st2 + 1) * 512]
                return eng.tensor_add(Rs, Rs, b['tjb'][:, :])
            sch.g('gpsimd', fn, reads=['tjb', ('R', c)], writes=[('R', c)], tag=f"{pre}.tja{c}.{st2}")



def emit_tree_v2(sch, b, ld_wout_chunk, pre="tf"):
    xn = b['xnbf']
    win = b['ti_win']
    ac, dc, ind, mT = b['ti_ac'], b['ti_dc'], b['ti_in'], b['ti_mT']
    PS = b['PS']
    ptb = b['tf_ptb']
    wc3 = b['wc3']

    for q in range(4):
        for ti in range(4):
            tt = q * 4 + ti
            for ws in range(4):
                ps, psk = (PS[ws % 2], ('P', ws % 2))

                def fn(eng, tt=tt, ws=ws, ps=ps):
                    for k in range(NCH):
                        eng.matmul(ps[:, :], xn(k)[:, tt * 128:(tt + 1) * 128],
                                   win[:, k * T + ws * 512:k * T + (ws + 1) * 512],
                                   start=(k == 0), stop=False)
                    return eng.matmul(ps[:, :], b['onrb'][0:1, :],
                                      b['ti_binr'][0:1, ws * 512:(ws + 1) * 512],
                                      start=False, stop=True)
                sch.g('tensor', fn, reads=['ti_win', 'ti_binr', 'onrb'] + [('xnbf', k) for k in range(NCH)],
                      writes=[psk], tag=f"{pre}.lg{tt}.{ws}")

                def fn(eng, ti=ti, ws=ws, ps=ps):
                    return eng.activation(out=ac[:, ti * T + ws * 512:ti * T + (ws + 1) * 512],
                                          in_=ps[:, :], func=AF.Silu)
                sch.g('scalar', fn, reads=[psk], writes=[('ac', ti)], tag=f"{pre}.si{tt}.{ws}")

                def fn(eng, ti=ti, ws=ws, ps=ps, bb=b):
                    return eng.tensor_single_scalar(dc[:, ti * T + ws * 512:ti * T + (ws + 1) * 512],
                                                    ps[:, :], cc(bb, 0.0), op=ALU.is_gt)
                sch.g('vector', fn, reads=[psk, 'consts', ('ac', ti)], writes=[('dc', ti)],
                      tag=f"{pre}.dc{tt}.{ws}")

        for ti in range(4):
            tt = q * 4 + ti
            io = ti * T
            ind3 = ind[:, io:io + 2040].rearrange("p (e n) -> p e n", n=255)
            dc3 = dc[:, io:io + 2040].rearrange("p (e n) -> p e n", n=255)

            def fn(eng, ti=ti, io=io):
                return eng.memset(ind[:, io:io + T], 0.0)
            sch.g('gpsimd', fn, reads=[], writes=[('ind', ti)], tag=f"{pre}.iz{tt}")

            def fn(eng, ti=ti, io=io):
                return eng.memset(ind[:, io:io + 2040].rearrange("p (e n) -> p e n", n=255)[:, :, 0:1], 1.0)
            sch.g('gpsimd', fn, reads=[], writes=[('ind', ti)], tag=f"{pre}.iz2{tt}")

            for d in range(7):
                a0 = (1 << d) - 1
                a1 = (1 << (d + 1)) - 1
                njd = 1 << d
                par = ind3[:, :, a0:a1]
                decp = dc3[:, :, a0:a1]
                chpair = ind3[:, :, a1:a1 + 2 * njd].rearrange("p e (i two) -> p e i two", two=2)
                right = chpair[:, :, :, 1:2].rearrange("p e i two -> p e (i two)")
                left = chpair[:, :, :, 0:1].rearrange("p e i two -> p e (i two)")

                def fn(eng, par=par, decp=decp, right=right):
                    return eng.tensor_mul(right, par, decp)
                sch.g('gpsimd', fn, reads=[('ind', ti), ('dc', ti)], writes=[('ind', ti)],
                      tag=f"{pre}.wka{tt}.{d}")

                def fn(eng, par=par, right=right, left=left):
                    return eng.tensor_sub(left, par, right)
                sch.g('gpsimd', fn, reads=[('ind', ti)], writes=[('ind', ti)],
                      tag=f"{pre}.wkb{tt}.{d}")

            def fn(eng, ti=ti, io=io):
                return eng.tensor_mul(ac[:, io:io + T], ac[:, io:io + T], ind[:, io:io + T])
            sch.g('gpsimd', fn, reads=[('ac', ti), ('ind', ti)], writes=[('ac', ti)],
                  tag=f"{pre}.mk{tt}")

            for wcn in range(16):
                def fn(eng, ti=ti, io=io, wcn=wcn):
                    return eng.transpose(ptb[:, :], ac[:, io + wcn * 128:io + (wcn + 1) * 128],
                                         b['idbf'][:, :])
                sch.g('tensor', fn, reads=[('ac', ti), 'idbf'], writes=['tf_ptb'],
                      tag=f"{pre}.tr{tt}.{wcn}")

                def fn(eng, ti=ti, wcn=wcn):
                    return eng.copy(out=mT[:, wcn * 512 + ti * 128:wcn * 512 + (ti + 1) * 128],
                                    in_=ptb[:, :])
                sch.g('scalar', fn, reads=['tf_ptb'], writes=[('mT', wcn)], tag=f"{pre}.te{tt}.{wcn}")

        for oh in range(2):
            for wcn in range(16):
                ld_wout_chunk(sch, oh, wcn, wcn % 2)

                def fn(eng, q=q, oh=oh, wcn=wcn):
                    last = None
                    for o4 in range(4):
                        oc = oh * 4 + o4
                        last = eng.matmul(PS[o4][:, :],
                                          wc3[:, (wcn % 2) * 512 + o4 * 128:(wcn % 2) * 512 + (o4 + 1) * 128],
                                          mT[:, wcn * 512:(wcn + 1) * 512],
                                          start=(wcn == 0), stop=(wcn == 15))
                    return last
                sch.g('tensor', fn, reads=[('wc3', wcn % 2), ('mT', wcn)],
                      writes=[('P', o4) for o4 in range(4)], tag=f"{pre}.out{q}.{oh}.{wcn}")

            for o4 in range(4):
                oc = oh * 4 + o4

                def fn(eng, q=q, oc=oc, o4=o4):
                    Rs = b['R'](oc)[:, q * 512:(q + 1) * 512]
                    return eng.tensor_add(Rs, Rs, PS[o4][:, :])
                sch.g('vector', fn, reads=[('P', o4), ('R', oc)], writes=[('R', oc)],
                      tag=f"{pre}.oe{q}.{oc}")



def emit_moe_v2(sch, b, load_w, pre="mo"):
    xn = b['xnbf']
    p0, p1, p2, p3 = b['mo_p0'], b['mo_p1'], b['mo_p2'], b['mo_p3']
    plg, ptr = b['mo_lg'], b['mo_tr']
    lgT, lgt = b['lgT'], b['lgt']
    sc_t, m_t = b['rt_sc'], b['rt_m']
    c1, c2, c3 = b['rt_c1'], b['rt_c2'], b['rt_c3']
    scT, mT = b['scT'], b['mT']
    h, x3m, t1 = b['h'], b['x3m'], b['t1']
    w1, w2, w3 = b['w1'], b['w2'], b['w3']
    idn = b['id128']

    for tt in range(4):
        def fn(eng, tt=tt):
            last = None
            for k in range(NCH):
                last = eng.matmul(plg[0:8, :], b['gw'][:, k * 8:(k + 1) * 8],
                                  xn(k)[:, tt * 512:(tt + 1) * 512],
                                  start=(k == 0), stop=(k == NCH - 1))
            return last
        sch.g('tensor', fn, reads=['gw'] + [('xnbf', k) for k in range(NCH)],
              writes=['mo_lg'], tag=f"{pre}.lg{tt}")

        def fn(eng, tt=tt):
            return eng.copy(out=lgT[0:8, tt * 512:(tt + 1) * 512], in_=plg[0:8, :])
        sch.g('scalar', fn, reads=['mo_lg'], writes=['lgT'], tag=f"{pre}.lge{tt}")

    for i in range(16):
        def fn(eng, i=i):
            return eng.transpose(ptr[0:128, 0:8], lgT[0:8, i * 128:(i + 1) * 128], idn[0:8, 0:8])
        sch.g('tensor', fn, reads=['lgT', 'id128'], writes=['mo_tr'], tag=f"{pre}.tr{i}")

        def fn(eng, i=i):
            return eng.tensor_copy(lgt[:, i * 8:(i + 1) * 8], ptr[0:128, 0:8])
        sch.g('vector', fn, reads=['mo_tr'], writes=[('lgt', i)], tag=f"{pre}.tre{i}")

    for i in range(16):
        lg_i = lgt[:, i * 8:(i + 1) * 8]
        sc_i = sc_t[:, i * 8:(i + 1) * 8]
        m_i = m_t[:, i * 8:(i + 1) * 8]
        c1_i, c2_i, c3_i = c1[:, i:i + 1], c2[:, i:i + 1], c3[:, i:i + 1]

        sch.g('vector', lambda e, a=c1_i, x=lg_i: e.reduce_max(a, x, axis=AX.X, negate=True),
              reads=[('lgt', i)], writes=[('c1', i)], tag=f"{pre}.r1.{i}")
        sch.g('scalar', lambda e, o=sc_i, x=lg_i, a=c1_i: e.activation(out=o, in_=x, func=AF.Exp, bias=a),
              reads=[('lgt', i), ('c1', i)], writes=[('sc', i)], tag=f"{pre}.exp.{i}")
        sch.g('vector', lambda e, a=c2_i, x=sc_i: e.reduce_sum(a, x, axis=AX.X),
              reads=[('sc', i)], writes=[('c2', i)], tag=f"{pre}.sum.{i}")
        sch.g('vector', lambda e, a=c3_i, x=c2_i: e.reciprocal(a, x),
              reads=[('c2', i)], writes=[('c3', i)], tag=f"{pre}.rcp.{i}")
        sch.g('vector', lambda e, o=sc_i, a=c3_i: e.tensor_scalar(o, o, a, None, op0=ALU.mult),
              reads=[('sc', i), ('c3', i)], writes=[('sc', i)], tag=f"{pre}.nrm.{i}")
        sch.g('vector', lambda e, a=c1_i, x=sc_i: e.reduce_max(a, x, axis=AX.X, negate=True),
              reads=[('sc', i)], writes=[('c1', i)], tag=f"{pre}.m1.{i}")
        sch.g('vector', lambda e, o=m_i, x=sc_i, a=c1_i: e.tensor_scalar(o, x, a, None, op0=ALU.add),
              reads=[('sc', i), ('c1', i)], writes=[('m', i)], tag=f"{pre}.d1.{i}")
        sch.g('vector', lambda e, o=m_i, bb=b: e.tensor_single_scalar(o, o, cc(bb, 0.0), op=ALU.is_equal),
              reads=[('m', i), 'consts'], writes=[('m', i)], tag=f"{pre}.eq.{i}")
        sch.g('vector', lambda e, o=m_i, x=sc_i, bb=b: e.scalar_tensor_tensor(o, o, cc(bb, -1e30), x, op0=ALU.mult, op1=ALU.add),
              reads=[('m', i), ('sc', i), 'consts'], writes=[('m', i)], tag=f"{pre}.msk.{i}")
        sch.g('vector', lambda e, a=c2_i, x=m_i: e.reduce_max(a, x, axis=AX.X, negate=True),
              reads=[('m', i)], writes=[('c2', i)], tag=f"{pre}.m2.{i}")
        sch.g('vector', lambda e, o=m_i, x=sc_i, a=c2_i: e.tensor_scalar(o, x, a, None, op0=ALU.add),
              reads=[('sc', i), ('c2', i)], writes=[('m', i)], tag=f"{pre}.d2.{i}")
        sch.g('vector', lambda e, o=m_i, bb=b: e.tensor_single_scalar(o, o, cc(bb, 0.0), op=ALU.is_ge),
              reads=[('m', i), 'consts'], writes=[('m', i)], tag=f"{pre}.ge.{i}")

    for i in range(16):
        def fn(eng, i=i):
            return eng.transpose(ptr[0:8, 0:128], sc_t[:, i * 8:(i + 1) * 8], idn[:, :])
        sch.g('tensor', fn, reads=[('sc', i), 'id128'], writes=['mo_tr'], tag=f"{pre}.ts{i}")
        sch.g('scalar', lambda e, i=i: e.copy(out=scT[0:8, i * 128:(i + 1) * 128], in_=ptr[0:8, 0:128]),
              reads=['mo_tr'], writes=['scT'], tag=f"{pre}.tse{i}")

        def fn(eng, i=i):
            return eng.transpose(ptr[0:8, 0:128], m_t[:, i * 8:(i + 1) * 8], idn[:, :])
        sch.g('tensor', fn, reads=[('m', i), 'id128'], writes=['mo_tr'], tag=f"{pre}.tm{i}")
        sch.g('scalar', lambda e, i=i: e.copy(out=mT[0:8, i * 128:(i + 1) * 128], in_=ptr[0:8, 0:128]),
              reads=['mo_tr'], writes=['mT'], tag=f"{pre}.tme{i}")

    def prep_bcast(e_, src, srck, dst, dstkey):
        for tt in range(4):
            ps = p2 if tt % 2 == 0 else p3
            psk = 'mo_p2' if tt % 2 == 0 else 'mo_p3'

            def fn(eng, e_=e_, tt=tt, src=src, ps=ps):
                return eng.matmul(ps[:, :], b['sel8'][0:8, e_ * 128:(e_ + 1) * 128],
                                  src[0:8, tt * 512:(tt + 1) * 512], start=True, stop=True)
            sch.g('tensor', fn, reads=[srck, 'sel8'], writes=[psk], tag=f"{pre}.bc{dstkey}{e_}.{tt}")

            def fn(eng, tt=tt, dst=dst, ps=ps):
                return eng.copy(out=dst[:, tt * 512:(tt + 1) * 512], in_=ps[:, :])
            sch.g('scalar', fn, reads=[psk], writes=[dstkey], tag=f"{pre}.bce{dstkey}{e_}.{tt}")

    def prep_wbmb(e_):
        prep_bcast(e_, scT, 'scT', b['wb'](e_ % 2), ('wb', e_ % 2))

    def prep_xs(e_, qf):
        par = qf % 2
        wbp = b['wb'](e_ % 2)
        qo = qf * 512
        for k in range(NCH):
            def fn(eng, k=k, qo=qo, wbp=wbp, par=par):
                return eng.tensor_mul(b['xs'](par)[:, k * 512:(k + 1) * 512],
                                      xn(k)[:, qo:qo + 512], wbp[:, qo:qo + 512])
            sch.g('gpsimd', fn, reads=[('xnbf', k), ('wb', e_ % 2)], writes=[('xs', par, k)],
                  tag=f"{pre}.xs{e_}.{qf}.{k}")

    prep_wbmb(0)
    prep_xs(0, 0)

    for e_ in range(8):
        load_w(sch, e_)
        mbp = b['mb']
        prep_bcast(e_, mT, 'mT', mbp, 'mb')

        for qf in range(4):
            par = qf % 2
            xsp = b['xs'](par)
            if qf < 3:
                prep_xs(e_, qf + 1)
            elif e_ < 7:
                pass
            qo = qf * 512

            for hc in range(NCH):
                ps = p0 if hc % 2 == 0 else p1
                psk = 'mo_p0' if hc % 2 == 0 else 'mo_p1'

                def fn(eng, hc=hc, ps=ps, xsp=xsp):
                    last = None
                    for k in range(NCH):
                        last = eng.matmul(ps[:, :], w1[:, k * 1024 + hc * 128:k * 1024 + (hc + 1) * 128],
                                          xsp[:, k * 512:(k + 1) * 512],
                                          start=(k == 0), stop=(k == NCH - 1))
                    return last
                sch.g('tensor', fn, reads=['w1sb'] + [('xs', par, k) for k in range(NCH)],
                      writes=[psk], tag=f"{pre}.u1{e_}.{qf}.{hc}")

                def fn(eng, hc=hc, e_=e_, ps=ps):
                    return eng.activation(out=h[:, hc * 512:(hc + 1) * 512],
                                          in_=ps[:, :], func=AF.Silu,
                                          bias=b['b1c'][:, hc * 8 + e_:hc * 8 + e_ + 1])
                sch.g('scalar', fn, reads=[psk, 'b1c'], writes=[('h', hc)],
                      tag=f"{pre}.u1e{e_}.{qf}.{hc}")

            if qf == 1 and e_ < 7:
                prep_wbmb(e_ + 1)

            for oc in range(NCH):
                psa, pska = (p0, 'mo_p0') if oc % 2 == 0 else (p2, 'mo_p2')
                psb, pskb = (p1, 'mo_p1') if oc % 2 == 0 else (p3, 'mo_p3')

                def fn(eng, oc=oc, psa=psa, xsp=xsp):
                    last = None
                    for k in range(NCH):
                        last = eng.matmul(psa[:, :], w3[:, k * 1024 + oc * 128:k * 1024 + (oc + 1) * 128],
                                          xsp[:, k * 512:(k + 1) * 512],
                                          start=(k == 0), stop=(k == NCH - 1))
                    return last
                sch.g('tensor', fn, reads=['w3sb'] + [('xs', par, k) for k in range(NCH)],
                      writes=[pska], tag=f"{pre}.x3{e_}.{qf}.{oc}")

                def fn(eng, oc=oc, e_=e_, qo=qo, psa=psa, mbp=mbp):
                    return eng.scalar_tensor_tensor(
                        out=x3m[:, 0:512], in0=psa[:, :],
                        scalar=b['b3c'][:, oc * 8 + e_:oc * 8 + e_ + 1],
                        in1=mbp[:, qo:qo + 512],
                        op0=ALU.add, op1=ALU.mult)
                sch.g('vector', fn, reads=[pska, 'b3c', 'mb'], writes=[('x3m', oc % 2)],
                      tag=f"{pre}.x3e{e_}.{qf}.{oc}")

                def fn(eng, oc=oc, psb=psb):
                    last = None
                    for k in range(NCH):
                        last = eng.matmul(psb[:, :], w2[:, k * 1024 + oc * 128:k * 1024 + (oc + 1) * 128],
                                          h[:, k * 512:(k + 1) * 512],
                                          start=(k == 0), stop=(k == NCH - 1))
                    return last
                sch.g('tensor', fn, reads=['w2sb'] + [('h', k) for k in range(NCH)],
                      writes=[pskb], tag=f"{pre}.x2{e_}.{qf}.{oc}")

                def fn(eng, oc=oc, e_=e_, psb=psb):
                    return eng.scalar_tensor_tensor(
                        out=t1[:, 0:512], in0=psb[:, :],
                        scalar=b['b2c'][:, oc * 8 + e_:oc * 8 + e_ + 1],
                        in1=x3m[:, 0:512], op0=ALU.add, op1=ALU.mult)
                sch.g('vector', fn, reads=[pskb, 'b2c', ('x3m', oc % 2)], writes=[('t1', oc % 2)],
                      tag=f"{pre}.x2e{e_}.{qf}.{oc}")

                def fn(eng, oc=oc, qo=qo):
                    yap = b['y'](oc)[:, qo:qo + 512]
                    return eng.tensor_add(yap, yap, t1[:, 0:512])
                sch.g('vector', fn, reads=[('t1', oc % 2), ('y', oc)], writes=[('y', oc)],
                      tag=f"{pre}.acc{e_}.{qf}.{oc}")

        if e_ < 7:
            prep_xs(e_ + 1, 0)


from contextlib import ExitStack
import ml_dtypes

BF16NP = ml_dtypes.bfloat16


def build_fused_program():
    nc = new_nc()

    def din(name, shape, dt=F32):
        return nc.dram_tensor(name, shape, dt, kind="ExternalInput").ap()

    x0T = din("x0T", [D, T])
    WqT = din("WqT", [D, D]); WkT = din("WkT", [D, D]); WvT = din("WvT", [D, D]); WoT = din("WoT", [D, D])
    bq_row_d = din("bq_row", [1, D]); bk_row_d = din("bk_row", [1, D])
    bv_row_d = din("bv_row", [1, D])
    bo_cols_d = din("bo_cols", [128, 8])
    gcols_d = din("gcols", [128, 8])
    iota_d = din("iota", [128, T])
    masks_d = din("masks", [128, 4 * 512], BF16)
    gwT = din("gwT", [D, 8])
    w1T = din("w1T", [8 * D, D]); w2T = din("w2T", [8 * D, D]); w3T = din("w3T", [8 * D, D])
    b1c_d = din("b1c_d", [128, 64]); b2c_d = din("b2c_d", [128, 64]); b3c_d = din("b3c_d", [128, 64])
    sel8_d = din("sel8_d", [8, 1024], BF16)
    sw1T = din("sw1T", [D, T]); sw2T = din("sw2T", [D, T]); sw3T = din("sw3T", [T, D])
    G_d = din("G_d", [512, D]); GT_d = din("GT_d", [D, 512])
    Cm_d = din("Cm_d", [8, 8]); Cmb_d = din("Cmb_d", [8, 8], BF16)
    WinT = din("WinT", [D, T]); WoutT = din("WoutT", [T, D])
    binr_d = din("binr_d", [1, T])
    id_d = din("id_d", [128, 128]); idb_d = din("idb_d", [128, 128], BF16)
    oc_d = din("oc_d", [128, 1]); or_d = din("or_d", [1, 128]); orb_d = din("orb_d", [1, 128], BF16)
    on512b_d = din("on512b_d", [1, 512], BF16)
    cc_d = din("cc_d", [128, 64])
    outT = nc.dram_tensor("outT", [D, T], F32, kind="ExternalOutput").ap()
    import os
    FDBG = os.environ.get('FDBG', '') == '1'
    dbg_t = {}
    if FDBG:
        for nm in ('dbg_p1R', 'dbg_p2R', 'dbg_p3R', 'dbg_p4R', 'dbg_p6R'):
            dbg_t[nm] = nc.dram_tensor(nm, [D, T], F32, kind="ExternalOutput").ap()
        for nm in ('dbg_p1xn', 'dbg_p5xn'):
            dbg_t[nm] = nc.dram_tensor(nm, [D, T], BF16, kind="ExternalOutput").ap()

    with ExitStack() as stack:
        sch = Sched(nc, stack)
        g = {}

        def sbp(name, shape, dt):
            return stack.enter_context(nc.sbuf_tensor(name, shape, dt))
        R_t = sbp("g_R", [128, NCH * T], F32)
        xn_t = sbp("g_xn", [128, NCH * T], BF16)
        g['R'] = lambda c, _t=R_t: _t[:, c * T:(c + 1) * T]
        g['xnbf'] = lambda c, _t=xn_t: _t[:, c * T:(c + 1) * T]
        g['cc'] = sbp("g_cc", [128, 64], F32)
        g['onc'] = sbp("g_onc", [128, 1], F32)
        g['onr'] = sbp("g_onr", [1, 128], F32)
        g['onrb'] = sbp("g_onrb", [1, 128], BF16)
        g['on512b'] = sbp("g_on512b", [1, 512], BF16)
        g['id128'] = sbp("g_id128", [128, 128], F32)
        g['idbf'] = sbp("g_idbf", [128, 128], BF16)
        g['sel8'] = sbp("g_sel8", [8, 1024], BF16)
        g['Cmat'] = sbp("g_Cmat", [8, 8], F32)
        g['Cmbf'] = sbp("g_Cmbf", [8, 8], BF16)
        g['bo_cols'] = sbp("g_bo", [128, 8], F32)
        g['gcols'] = sbp("g_gc", [128, 8], F32)
        g['b1c'] = sbp("g_b1c", [128, 64], F32)
        g['b2c'] = sbp("g_b2c", [128, 64], F32)
        g['b3c'] = sbp("g_b3c", [128, 64], F32)
        g['lng'] = sbp("g_lng", [128, 8], F32)

        def fn(eng, sem):
            for (t_, d_) in ((g['cc'], cc_d), (g['onc'], oc_d), (g['onr'], or_d),
                             (g['id128'], id_d), (g['idbf'], idb_d), (g['sel8'], sel8_d),
                             (g['Cmat'], Cm_d), (g['Cmbf'], Cmb_d), (g['bo_cols'], bo_cols_d),
                             (g['gcols'], gcols_d), (g['b1c'], b1c_d), (g['b2c'], b2c_d),
                             (g['b3c'], b3c_d), (g['onrb'], orb_d), (g['on512b'], on512b_d)):
                eng.dma_start(out=t_[:, :], in_=d_[:, :]).then_inc(sem, 16)
        sch.dma('sync', 15, fn,
                writes=['consts', 'ones', 'id128', 'idbf', 'sel8', 'Cmat', 'Cmbf',
                        'bo_cols', 'gcols', 'b1c', 'b2c', 'b3c', 'onrb', 'on512b'], tag="gconsts")

        def dump_R(nm):
            if not FDBG:
                return
            for c in range(NCH):
                def fn(eng, sem, c=c, nm=nm):
                    eng.dma_start(out=dbg_t[nm][c * 128:(c + 1) * 128, :], in_=g['R'](c)).then_inc(sem, 16)
                sch.dma('sync', 1, fn, reads=[('R', c)], tag=f"{nm}.{c}")

        def dump_xn(nm):
            if not FDBG:
                return
            for c in range(NCH):
                def fn(eng, sem, c=c, nm=nm):
                    eng.dma_start(out=dbg_t[nm][c * 128:(c + 1) * 128, :], in_=g['xnbf'](c)).then_inc(sem, 16)
                sch.dma('sync', 1, fn, reads=[('xnbf', c)], tag=f"{nm}.{c}")

        def _phase1():
            with ExitStack() as ph:
                def sb(name, shape, dt):
                    return ph.enter_context(nc.sbuf_tensor(name, shape, dt))
                b = dict(g)
                b['ln_sq'] = sb("p1_sq", [128, 2 * T], F32)
                b['ln_rows'] = sb("p1_rows", [1, 4 * T], F32)
                b['ln_mbc'] = sb("p1_mbc", [128, T], F32)
                b['ln_rbc'] = sb("p1_rbc", [128, T], F32)
                for k in ('ln_psA', 'ln_psB', 'ln_psC', 'ln_psM', 'ln_psR'):
                    b[k] = ph.enter_context(nc.psum_tensor("p1_" + k, [128, 512], F32))

                for c in range(NCH):
                    def fn(eng, sem, c=c):
                        eng.dma_start(out=g['R'](c), in_=x0T[c * 128:(c + 1) * 128, :]).then_inc(sem, 16)
                    sch.dma('sync', 1, fn, writes=[('R', c)], tag=f"p1.ldx{c}")

                emit_ln(sch, b, 'R', None, 'onc', 'onr', pre="ln0", xn_bf=g['xnbf'])
        _phase1()
        dump_xn('dbg_p1xn')
        sch.barrier_all()

        def _phase2():
            with ExitStack() as ph:
                def sb(name, shape, dt):
                    return ph.enter_context(nc.sbuf_tensor(name, shape, dt))
                b = dict(g)
                b['kT'] = sb("p2_kT", [128, NCH * T], BF16)
                b['v'] = sb("p2_v", [128, 16 * 1024], BF16)
                b['qt'] = sb("p2_qt", [128, 8 * 512], BF16)
                b['sc'] = sb("p2_sc", [128, 8 * 512], BF16)
                b['at'] = sb("p2_at", [128, 8 * 512], BF16)
                b['wc'] = sb("p2_wc", [128, 2 * 1024], BF16)
                b['iota512'] = sb("p2_iota", [128, 512], F32)
                b['dk'] = sb("p2_dk", [128, 512], F32)
                b['dkb'] = sb("p2_dkb", [128, 1], F32)
                b['bq_row'] = sb("p2_bq", [1, D], BF16)
                b['bk_row'] = sb("p2_bk", [1, D], BF16)
                b['bv_row'] = sb("p2_bv", [1, D], BF16)
                b['masks'] = sb("p2_masks", [128, 4 * 512], BF16)
                b['PS'] = [ph.enter_context(nc.psum_tensor(f"p2_P{i}", [128, 512], F32))
                           for i in range(8)]

                def fn(eng, sem):
                    eng.dma_start(out=b['iota512'][:, :], in_=iota_d[:, 0:512]).then_inc(sem, 16)
                    eng.dma_start(out=b['masks'][:, :], in_=masks_d[:, :]).then_inc(sem, 16)
                sch.dma('sync', 2, fn, writes=['iota512', 'masks'], tag="p2consts")

                def fn(eng, sem):
                    eng.dma_start(out=b['bq_row'][:, :], in_=bq_row_d[:, :]).then_inc(sem, 16)
                    eng.dma_start(out=b['bk_row'][:, :], in_=bk_row_d[:, :]).then_inc(sem, 16)
                    eng.dma_start(out=b['bv_row'][:, :], in_=bv_row_d[:, :]).then_inc(sem, 16)
                sch.dma('gpsimd', 3, fn, writes=['bq_row', 'bk_row', 'bv_row'], tag="p2bias")

                wd = {'WqT': WqT, 'WkT': WkT, 'WvT': WvT, 'WoT': WoT}

                def wload_chunk(sch, which, k, slot):
                    def fn(eng, sem, which=which, k=k, slot=slot):
                        eng.dma_start(out=b['wc'][:, slot * 1024:(slot + 1) * 1024],
                                      in_=wd[which][k * 128:(k + 1) * 128, :]).then_inc(sem, 16)
                    sch.dma('gpsimd', 1, fn, writes=[('wc', slot)], tag=f"p2.ld{which}{k}")

                emit_attn0_v2(sch, b, wload_chunk)
        _phase2()
        dump_R('dbg_p2R')
        sch.barrier_all()

        def _phase3():
            with ExitStack() as ph:
                def sb(name, shape, dt):
                    return ph.enter_context(nc.sbuf_tensor(name, shape, dt))
                b = dict(g)
                b['y'] = g['R']
                b['gw'] = sb("p3_gw", [128, 64], BF16)
                b['lgT'] = sb("p3_lgT", [8, T], F32)
                b['lgt'] = sb("p3_lgt", [128, 16 * 8], F32)
                b['rt_sc'] = sb("p3_rt_sc", [128, 16 * 8], F32)
                b['rt_m'] = sb("p3_rt_m", [128, 16 * 8], F32)
                b['rt_c1'] = sb("p3_c1", [128, 16], F32)
                b['rt_c2'] = sb("p3_c2", [128, 16], F32)
                b['rt_c3'] = sb("p3_c3", [128, 16], F32)
                b['rt_tmp'] = sb("p3_tmp", [128, 8], F32)
                b['scT'] = sb("p3_scT", [8, T], BF16)
                b['mT'] = sb("p3_mT", [8, T], BF16)
                wb_t = sb("p3_wb", [128, 2 * T], BF16)
                b['mb'] = sb("p3_mb", [128, T], BF16)
                xs_t = sb("p3_xs", [128, 2 * 8 * 512], BF16)
                b['wb'] = lambda p, _t=wb_t: _t[:, p * T:(p + 1) * T]
                b['xs'] = lambda p, _t=xs_t: _t[:, p * 8 * 512:(p + 1) * 8 * 512]
                b['h'] = sb("p3_h", [128, 8 * 512], BF16)
                b['x3m'] = sb("p3_x3m", [128, 512], F32)
                b['t1'] = sb("p3_t1", [128, 512], F32)
                b['w1'] = sb("p3_w1", [128, 8 * 1024], BF16)
                b['w2'] = sb("p3_w2", [128, 8 * 1024], BF16)
                b['w3'] = sb("p3_w3", [128, 8 * 1024], BF16)
                b['mo_p0'] = ph.enter_context(nc.psum_tensor("p3_p0", [128, 512], F32))
                b['mo_p1'] = ph.enter_context(nc.psum_tensor("p3_p1", [128, 512], F32))
                b['mo_p2'] = ph.enter_context(nc.psum_tensor("p3_p2", [128, 512], F32))
                b['mo_p3'] = ph.enter_context(nc.psum_tensor("p3_p3", [128, 512], F32))
                b['mo_lg'] = ph.enter_context(nc.psum_tensor("p3_lg", [8, 512], F32))
                b['mo_tr'] = ph.enter_context(nc.psum_tensor("p3_tr", [128, 512], F32))

                def fn(eng, sem):
                    for k in range(NCH):
                        eng.dma_start(out=b['gw'][:, k * 8:(k + 1) * 8],
                                      in_=gwT[k * 128:(k + 1) * 128, :]).then_inc(sem, 16)
                sch.dma('gpsimd', NCH, fn, writes=['gw'], tag="ldgw")

                def load_w(sch, e_):
                    for (key, sbuf, dram) in (('w1sb', b['w1'], w1T), ('w3sb', b['w3'], w3T),
                                              ('w2sb', b['w2'], w2T)):
                        def fn(eng, sem, sbuf=sbuf, dram=dram, e_=e_):
                            for k in range(NCH):
                                eng.dma_start(out=sbuf[:, k * 1024:(k + 1) * 1024],
                                              in_=dram[e_ * 1024 + k * 128:e_ * 1024 + (k + 1) * 128, :]).then_inc(sem, 16)
                        sch.dma('gpsimd', NCH, fn, writes=[key], tag=f"ldw{key}{e_}")

                emit_moe_v2(sch, b, load_w)
        _phase3()
        dump_R('dbg_p3R')
        sch.barrier_all()

        def _phase4():
            with ExitStack() as ph:
                def sb(name, shape, dt):
                    return ph.enter_context(nc.sbuf_tensor(name, shape, dt))
                b = dict(g)
                b['sw12'] = sb("p4_sw12", [128, 2 * 8 * 1024], BF16)
                b['gg'] = sb("p4_gg", [128, 16 * 1024], BF16)
                b['sa'] = sb("p4_sa", [128, 1024], BF16)
                b['wc3'] = sb("p4_wc3", [128, 2 * 1024], BF16)
                b['PS'] = [ph.enter_context(nc.psum_tensor(f"p4_P{i}", [128, 512], F32))
                           for i in range(8)]
                swd = {'sw1T': sw1T, 'sw2T': sw2T}

                def ld_sw(sch, which, ihalf, dstslot):
                    def fn(eng, sem, which=which, ihalf=ihalf, dstslot=dstslot):
                        for k in range(NCH):
                            eng.dma_start(out=b['sw12'][:, dstslot * 8192 + k * 1024:dstslot * 8192 + (k + 1) * 1024],
                                          in_=swd[which][k * 128:(k + 1) * 128, ihalf * 1024:(ihalf + 1) * 1024]).then_inc(sem, 16)
                    sch.dma('gpsimd', NCH, fn, writes=[('sw12', dstslot)], tag=f"p4.ld{which}{ihalf}")

                def ld_sw3c(sch, wcI, slot):
                    def fn(eng, sem, wcI=wcI, slot=slot):
                        eng.dma_start(out=b['wc3'][:, slot * 1024:(slot + 1) * 1024],
                                      in_=sw3T[wcI * 128:(wcI + 1) * 128, :]).then_inc(sem, 16)
                    sch.dma('gpsimd', 1, fn, writes=[('wc3', slot)], tag=f"p4.ldsw3{wcI}")

                emit_shared_v2(sch, b, ld_sw, ld_sw3c)
        _phase4()
        dump_R('dbg_p4R')
        sch.barrier_all()

        def _phase5():
            with ExitStack() as ph:
                def sb(name, shape, dt):
                    return ph.enter_context(nc.sbuf_tensor(name, shape, dt))
                b = dict(g)
                b['ln_sq'] = sb("p5_sq", [128, 2 * T], F32)
                b['ln_rows'] = sb("p5_rows", [1, 4 * T], F32)
                b['ln_mbc'] = sb("p5_mbc", [128, T], F32)
                b['ln_rbc'] = sb("p5_rbc", [128, T], F32)
                for k in ('ln_psA', 'ln_psB', 'ln_psC', 'ln_psM', 'ln_psR'):
                    b[k] = ph.enter_context(nc.psum_tensor("p5_" + k, [128, 512], F32))
                emit_ln(sch, b, 'R', None, 'onc', 'onr', pre="ln1", xn_bf=g['xnbf'])
        _phase5()
        dump_xn('dbg_p5xn')
        sch.barrier_all()

        def _phase6():
            with ExitStack() as ph:
                def sb(name, shape, dt):
                    return ph.enter_context(nc.sbuf_tensor(name, shape, dt))
                b = dict(g)
                df_t = sb("p6_df", [128, NCH * T], BF16)
                b['df'] = lambda c, _t=df_t: _t[:, c * T:(c + 1) * T]
                b['wsq'] = sb("p6_wsq", [128, T], F32)
                b['rwb'] = sb("p6_rwb", [1, T], F32)
                b['rwc'] = sb("p6_rwc", [1, T], F32)
                b['GTbf'] = sb("p6_GTbf", [128, 8 * 512], BF16)
                b['Gbf'] = sb("p6_Gbf", [128, 4 * 1024], BF16)
                b['gwT'] = sb("p6_gwT", [128, 4 * T], BF16)
                b['gwt'] = sb("p6_gwt", [128, 512], BF16)
                b['sims'] = sb("p6_sims", [128, 512], BF16)
                b['smx'] = sb("p6_smx", [128, 4], F32)
                b['invgn'] = sb("p6_invgn", [128, 512], BF16)
                b['apb'] = sb("p6_apb", [128, 512], BF16)
                b['l1c'] = sb("p6_l1c", [128, 64], F32)
                b['Usm'] = sb("p6_Usm", [8, 128], BF16)
                b['ivb'] = sb("p6_ivb", [128, T], BF16)
                b['accb'] = sb("p6_accb", [128, T], BF16)
                b['tjb'] = sb("p6_tjb", [128, 512], BF16)
                PS = [None] * 8
                for i in (0, 1, 3, 5, 6, 7):
                    PS[i] = ph.enter_context(nc.psum_tensor(f"p6_P{i}", [128, 512], F32))
                b['PS'] = PS
                b['l1_ptb'] = ph.enter_context(nc.psum_tensor("p6_ptb", [128, 128], BF16))

                def fn(eng, sem):
                    for k in range(NCH):
                        eng.dma_start(out=b['GTbf'][:, k * 512:(k + 1) * 512],
                                      in_=GT_d[k * 128:(k + 1) * 128, :]).then_inc(sem, 16)
                    for gc in range(4):
                        eng.dma_start(out=b['Gbf'][:, gc * 1024:(gc + 1) * 1024],
                                      in_=G_d[gc * 128:(gc + 1) * 128, :]).then_inc(sem, 16)
                sch.dma('gpsimd', 12, fn, writes=['GTbf', 'Gbf'], tag="ldG")

                emit_l1attn_v2(sch, b)
        _phase6()
        dump_R('dbg_p6R')
        sch.barrier_all()

        def _phase7():
            with ExitStack() as ph:
                def sb(name, shape, dt):
                    return ph.enter_context(nc.sbuf_tensor(name, shape, dt))
                b = dict(g)
                b['ti_win'] = sb("p7_win", [128, NCH * T], BF16)
                b['ti_binr'] = sb("p7_binr", [1, T], BF16)
                b['ti_ac'] = sb("p7_ac", [128, 4 * T], BF16)
                b['ti_dc'] = sb("p7_dc", [128, 4 * T], BF16)
                b['ti_in'] = sb("p7_in", [128, 4 * T], BF16)
                b['ti_mT'] = sb("p7_mT", [128, 16 * 512], BF16)
                b['wc3'] = sb("p7_wc3", [128, 2 * 512], BF16)
                b['PS'] = [ph.enter_context(nc.psum_tensor(f"p7_P{i}", [128, 512], F32))
                           for i in range(4)]
                b['tf_ptb'] = ph.enter_context(nc.psum_tensor("p7_ptb", [128, 128], BF16))

                def fn(eng, sem):
                    eng.dma_start(out=b['ti_binr'][:, :], in_=binr_d[:, :]).then_inc(sem, 16)
                sch.dma('gpsimd', 1, fn, writes=['ti_binr'], tag="p7consts")

                def fn(eng, sem):
                    for k in range(NCH):
                        eng.dma_start(out=b['ti_win'][:, k * T:(k + 1) * T],
                                      in_=WinT[k * 128:(k + 1) * 128, :]).then_inc(sem, 16)
                sch.dma('gpsimd', NCH, fn, writes=['ti_win'], tag="p7ldw")

                def ld_wout_chunk(sch, oh, wcn, slot):
                    def fn(eng, sem, oh=oh, wcn=wcn, slot=slot):
                        eng.dma_start(out=b['wc3'][:, slot * 512:(slot + 1) * 512],
                                      in_=WoutT[wcn * 128:(wcn + 1) * 128, oh * 512:(oh + 1) * 512]).then_inc(sem, 16)
                    sch.dma('gpsimd', 1, fn, writes=[('wc3', slot)], tag=f"p7.ldwo{oh}.{wcn}")

                emit_tree_v2(sch, b, ld_wout_chunk)
        _phase7()
        sch.barrier_all()

        for c in range(NCH):
            def fn(eng, sem, c=c):
                eng.dma_start(out=outT[c * 128:(c + 1) * 128, :], in_=g['R'](c)).then_inc(sem, 16)
            sch.dma('sync', 1, fn, reads=[('R', c)], tag=f"stout{c}")
        sch.out_wait('sync')

        with nc.Block() as block:
            sch.emit(block)
    return nc



_PROG_CACHE = {}


def _prog(name, builder):
    if name not in _PROG_CACHE:
        _PROG_CACHE[name] = builder()
    return _PROG_CACHE[name]


def kernel(**inputs):
    NCORE = 8
    x0 = np.asarray(inputs['L0_attn_x'], dtype=np.float32)
    gamma = np.asarray(inputs['L0_attn_gamma'], dtype=np.float32)

    nc = _prog('fused', build_fused_program)

    ones_col = np.ones((128, 1), np.float32)
    ones_row = np.ones((1, 128), np.float32)
    id128 = np.eye(128, dtype=np.float32)
    idbf = np.eye(128, dtype=np.float32).astype(BF16NP)
    iota = np.broadcast_to(np.arange(T, dtype=np.float32), (128, T)).copy()
    masks = np.zeros((128, 4, 512), np.float32)
    for r in range(4):
        for j in range(128):
            masks[j, r, :] = (np.arange(512) >= j + 128 * r)
    masks = masks.reshape(128, 4 * 512).astype(BF16NP)
    sel8 = np.zeros((8, 1024), np.float32)
    for e in range(8):
        sel8[e, e * 128:(e + 1) * 128] = 1.0
    sel8 = sel8.astype(BF16NP)
    gcols = np.zeros((128, 8), np.float32)
    for c in range(8):
        for p in range(128):
            gcols[p, c] = gamma[2 * c + p // 64]
    ccarr = const_host_array_padded()
    WinT = np.zeros((1024, 2048), np.float32)
    WinT[:, :2040] = np.asarray(inputs['L1_ffn_W_in']).T
    WoutT = np.zeros((2048, 1024), np.float32)
    WoutT[:2040, :] = np.asarray(inputs['L1_ffn_W_out']).T
    binr = np.zeros((1, 2048), np.float32)
    binr[0, :2040] = np.asarray(inputs['L1_ffn_b_in'])
    G = np.asarray(inputs['L1_attn_guidance_trajectories'], dtype=np.float32)

    shared = {
        "WqT": np.ascontiguousarray(np.asarray(inputs['L0_attn_Wq']).T),
        "WkT": np.ascontiguousarray(np.asarray(inputs['L0_attn_Wk']).T),
        "WvT": np.ascontiguousarray(np.asarray(inputs['L0_attn_Wv']).T),
        "WoT": np.ascontiguousarray(np.asarray(inputs['L0_attn_Wo']).T),
        "bq_row": np.asarray(inputs['L0_attn_bq'], dtype=np.float32).reshape(1, D),
        "bk_row": np.asarray(inputs['L0_attn_bk'], dtype=np.float32).reshape(1, D),
        "bv_row": np.asarray(inputs['L0_attn_bv'], dtype=np.float32).reshape(1, D),
        "bo_cols": np.ascontiguousarray(np.asarray(inputs['L0_attn_bo']).reshape(8, 128).T),
        "gcols": gcols, "iota": iota, "masks": masks,
        "gwT": np.ascontiguousarray(np.asarray(inputs['L0_ffn_gate_w']).T),
        "w1T": np.ascontiguousarray(np.asarray(inputs['L0_ffn_w1']).transpose(0, 2, 1).reshape(8 * D, D)),
        "w2T": np.ascontiguousarray(np.asarray(inputs['L0_ffn_w2']).transpose(0, 2, 1).reshape(8 * D, D)),
        "w3T": np.ascontiguousarray(np.asarray(inputs['L0_ffn_w3']).transpose(0, 2, 1).reshape(8 * D, D)),
        "b1c_d": np.ascontiguousarray(np.asarray(inputs['L0_ffn_b1']).reshape(8, 8, 128).transpose(2, 1, 0).reshape(128, 64)),
        "b2c_d": np.ascontiguousarray(np.asarray(inputs['L0_ffn_b2']).reshape(8, 8, 128).transpose(2, 1, 0).reshape(128, 64)),
        "b3c_d": np.ascontiguousarray(np.asarray(inputs['L0_ffn_b3']).reshape(8, 8, 128).transpose(2, 1, 0).reshape(128, 64)),
        "sel8_d": sel8,
        "sw1T": np.ascontiguousarray(np.asarray(inputs['L0_ffn_sw1']).T),
        "sw2T": np.ascontiguousarray(np.asarray(inputs['L0_ffn_sw2']).T),
        "sw3T": np.ascontiguousarray(np.asarray(inputs['L0_ffn_sw3']).T),
        "G_d": G, "GT_d": np.ascontiguousarray(G.T), "Cm_d": l1_cmat(),
        "Cmb_d": l1_cmat().astype(BF16NP),
        "WinT": WinT, "WoutT": WoutT, "binr_d": binr,
        "id_d": id128, "idb_d": idbf, "oc_d": ones_col, "or_d": ones_row,
        "orb_d": ones_row.astype(BF16NP),
        "on512b_d": np.ones((1, 512), np.float32).astype(BF16NP),
        "cc_d": ccarr,
    }
    in_maps = [dict(shared, x0T=np.ascontiguousarray(x0[i].T)) for i in range(NCORE)]
    res = run_bass_kernel_spmd(nc, in_maps, core_ids=list(range(NCORE)))
    out = np.stack([np.ascontiguousarray(res.results[i]["outT"].T) for i in range(NCORE)], axis=0)
    return out.astype(np.float32)



# version 11 (final)
# speedup vs baseline: 1.0508x; 1.0508x over previous
import numpy as np
import sys
sys.path.insert(0, '/opt/trn_rl_repo')
import concourse.bass as bass
import concourse.mybir as mybir
from concourse.bass_utils import run_bass_kernel_spmd

F32 = mybir.dt.float32
BF16 = mybir.dt.bfloat16
AF = mybir.ActivationFunctionType
ALU = mybir.AluOpType
AX = mybir.AxisListType

D = 1024
T = 2048
NCH = 8


def new_nc():
    return bass.Bass("TRN2", target_bir_lowering=False)


CONST_IDX = {}


def CONST(v):
    v = float(v)
    return CONST_IDX.setdefault(v, len(CONST_IDX))


def const_host_array():
    n = max(16, len(CONST_IDX))
    a = np.zeros((128, n), np.float32)
    for v, i in CONST_IDX.items():
        a[:, i] = v
    return a


def cc(b, v, p=128):
    return b['cc'][0:p, CONST(v):CONST(v) + 1]


class Group:
    __slots__ = ("eng", "fn", "waits", "inc", "tag")

    def __init__(self, eng, fn, waits, inc, tag):
        self.eng, self.fn, self.waits, self.inc, self.tag = eng, fn, waits, inc, tag


class Sched:
    ENGINES = ("sync", "tensor", "vector", "scalar", "gpsimd")

    NPOOL = 24

    def __init__(self, nc, stack):
        self.nc = nc
        self.sems = {e: stack.enter_context(nc.semaphore(f"S_{e}")) for e in self.ENGINES}
        for i in range(self.NPOOL):
            self.sems[f"D{i}"] = stack.enter_context(nc.semaphore(f"SD{i}"))
        self.count = {e: 0 for e in self.ENGINES}
        self.dcount = [0] * self.NPOOL
        self.dnext = 0
        self.groups = {e: [] for e in self.ENGINES}
        self.last_write = {}
        self.readers = {}

    def _deps(self, eng, reads, writes):
        need = {}

        def add(e2, c2):
            if e2 == eng and e2 == "tensor":
                return
            if need.get(e2, 0) < c2:
                need[e2] = c2
        self._add = add
        for k in reads:
            if k in self.last_write:
                add(*self.last_write[k])
        for k in writes:
            if k in self.last_write:
                add(*self.last_write[k])
            for e2, c2 in self.readers.get(k, {}).items():
                add(e2, c2)
        return need

    def _commit(self, eng, reads, writes, cnt_after):
        for k in reads:
            self.readers.setdefault(k, {})[eng] = cnt_after
        for k in writes:
            self.last_write[k] = (eng, cnt_after)
            self.readers[k] = {}

    def g(self, eng, fn, reads=(), writes=(), tag=""):
        need = self._deps(eng, reads, writes)
        self.count[eng] += 1
        self.groups[eng].append(Group(eng, fn, need, 1, tag))
        self._commit(eng, reads, writes, self.count[eng])
        return self.count[eng]

    def dma(self, eng, n, fn, reads=(), writes=(), tag=""):
        p = self.dnext
        self.dnext = (self.dnext + 1) % self.NPOOL
        need = self._deps(eng, reads, writes)
        dname = f"D{p}"
        gate = self.dcount[p]
        if gate > 0 and need.get(dname, 0) < gate:
            need[dname] = gate
        self.dcount[p] += 16 * n
        dcnt = self.dcount[p]
        sem = self.sems[dname]

        def wrapped(e, _unused, fn=fn, sem=sem):
            fn(e, sem)
        self.groups[eng].append(Group(eng, wrapped, need, 16 * n, tag))
        self._commit_dma(eng, dname, reads, writes, dcnt)
        return dcnt

    def _commit_dma(self, eng, dname, reads, writes, dcnt):
        for k in reads:
            self.readers.setdefault(k, {})[dname] = dcnt
        for k in writes:
            self.last_write[k] = (dname, dcnt)
            self.readers[k] = {}

    def barrier_all(self):
        snap = dict(self.count)
        dsnap = {f"D{i}": c for i, c in enumerate(self.dcount) if c > 0}
        for e in self.ENGINES:
            def fn(eng, _e=e):
                return eng.nop()
            need = {e2: c for e2, c in snap.items() if e2 != e and c > 0}
            need.update(dsnap)
            self.count[e] += 1
            self.groups[e].append(Group(e, fn, need, 1, "barrier"))

    def emit(self, block):
        sems = self.sems

        def run(eng_name):
            def body(eng):
                for g in self.groups[eng_name]:
                    for e2, c2 in sorted(g.waits.items()):
                        eng.wait_ge(sems[e2], c2)
                    if g.inc >= 16:
                        g.fn(eng, None)
                    else:
                        last = g.fn(eng)
                        if last is None:
                            raise RuntimeError(f"group {g.tag} on {eng_name} returned None")
                        last.then_inc(sems[eng_name], 1)
            return body
        block.sync(run("sync"))
        block.tensor(run("tensor"))
        block.vector(run("vector"))
        block.scalar(run("scalar"))
        block.gpsimd(run("gpsimd"))

    def drain_final(self):
        snap = dict(self.count)

        def fn(eng):
            return eng.nop()
        need = {e2: c for e2, c in snap.items() if e2 != "sync" and c > 0}
        self.count["sync"] += 1
        self.groups["sync"].append(Group("sync", fn, need, 1, "final"))

    def out_wait(self, eng="sync"):
        snap = dict(self.count)
        own = snap.pop(eng)

        def fn(e):
            if own > 0:
                e.wait_ge(self.sems[eng], own)
            return e.nop()
        need = {e2: c for e2, c in snap.items() if c > 0}
        need.update({f"D{i}": c for i, c in enumerate(self.dcount) if c > 0})
        self.count[eng] += 1
        self.groups[eng].append(Group(eng, fn, need, 1, "out_wait"))



def emit_ln(sch, b, xk, xak, onck, onrk, out_dram=None, src_dram=None,
            pre="ln", xn_bf=None):
    sq, rows = b['ln_sq'], b['ln_rows']
    mbc, rbc = b['ln_mbc'], b['ln_rbc']
    onc, onr = b[onck], b[onrk]
    psA, psB, psC, psM, psR = b['ln_psA'], b['ln_psB'], b['ln_psC'], b['ln_psM'], b['ln_psR']
    slots = [(psA, 0, 'ln_psA'), (psA, 32, 'ln_psA'), (psA, 64, 'ln_psA'), (psB, 0, 'ln_psB'),
             (psB, 32, 'ln_psB'), (psB, 64, 'ln_psB'), (psC, 0, 'ln_psC'), (psC, 32, 'ln_psC')]

    if src_dram is not None:
        for c in range(NCH):
            def fn(eng, sem, c=c):
                eng.dma_start(out=b[xk](c), in_=src_dram(c)).then_inc(sem, 16)
            sch.dma('sync', 1, fn, writes=[(xk, c)], tag=f"{pre}.ldx{c}")

    for c in range(NCH):
        def fn(eng, c=c):
            return eng.activation(out=sq[:, (c % 2) * T:(c % 2 + 1) * T], in_=b[xk](c), func=AF.Square)
        sch.g('scalar', fn, reads=[(xk, c)], writes=[('ln_sqb', c % 2)], tag=f"{pre}.sq{c}")

        def fn(eng, c=c):
            last = None
            for t in range(4):
                ps, r, _ = slots[t]
                last = eng.matmul(ps[r:r + 1, :], onc[:, :], b[xk](c)[:, t * 512:(t + 1) * 512],
                                  start=(c == 0), stop=(c == NCH - 1))
            return last
        sch.g('tensor', fn, reads=[(xk, c), 'ones'],
              writes=['ln_psA', 'ln_psB', 'ln_psC'], tag=f"{pre}.mmsum{c}")

        def fn(eng, c=c):
            last = None
            for t in range(4):
                ps, r, _ = slots[4 + t]
                last = eng.matmul(ps[r:r + 1, :], onc[:, :], sq[:, (c % 2) * T + t * 512:(c % 2) * T + (t + 1) * 512],
                                  start=(c == 0), stop=(c == NCH - 1))
            return last
        sch.g('tensor', fn, reads=[('ln_sqb', c % 2), 'ones'],
              writes=['ln_psA', 'ln_psB', 'ln_psC'], tag=f"{pre}.mmsq{c}")

    def fn(eng):
        last = None
        for t in range(4):
            ps, r, _ = slots[t]
            last = eng.mul(out=rows[0:1, t * 512:(t + 1) * 512], in_=ps[r:r + 1, :], mul=cc(b, 1.0 / 1024.0, 1))
        for t in range(4):
            ps, r, _ = slots[4 + t]
            last = eng.mul(out=rows[0:1, T + t * 512:T + (t + 1) * 512], in_=ps[r:r + 1, :], mul=cc(b, 1.0 / 1024.0, 1))
        return last
    sch.g('scalar', fn, reads=['ln_psA', 'ln_psB', 'ln_psC', 'consts'],
          writes=[('ln_rows', 'mean'), ('ln_rows', 'msq')], tag=f"{pre}.rows")

    def fn(eng):
        return eng.tensor_mul(rows[0:1, 2 * T:3 * T], rows[0:1, 0:T], rows[0:1, 0:T])
    sch.g('vector', fn, reads=[('ln_rows', 'mean')], writes=[('ln_rows', 'var')], tag=f"{pre}.var1")

    def fn(eng):
        return eng.tensor_sub(rows[0:1, 2 * T:3 * T], rows[0:1, T:2 * T], rows[0:1, 2 * T:3 * T])
    sch.g('vector', fn, reads=[('ln_rows', 'msq'), ('ln_rows', 'var')],
          writes=[('ln_rows', 'var')], tag=f"{pre}.var2")

    def fn(eng):
        return eng.activation(out=rows[0:1, 2 * T:3 * T], in_=rows[0:1, 2 * T:3 * T], func=AF.Ln, bias=cc(b, 1e-5, 1))
    sch.g('scalar', fn, reads=[('ln_rows', 'var'), 'consts'], writes=[('ln_rows', 'var')], tag=f"{pre}.rstd1")

    def fn(eng):
        return eng.activation(out=rows[0:1, 3 * T:4 * T], in_=rows[0:1, 2 * T:3 * T], func=AF.Exp, scale=cc(b, -0.5, 1))
    sch.g('scalar', fn, reads=[('ln_rows', 'var'), 'consts'], writes=[('ln_rows', 'rstd')], tag=f"{pre}.rstd2")

    for t in range(4):
        def fn(eng, t=t):
            eng.matmul(psM[:, :], onr[0:1, :], rows[0:1, t * 512:(t + 1) * 512], start=True, stop=True)
            return eng.matmul(psR[:, :], onr[0:1, :], rows[0:1, 3 * T + t * 512:3 * T + (t + 1) * 512],
                              start=True, stop=True)
        sch.g('tensor', fn, reads=[('ln_rows', 'mean'), ('ln_rows', 'rstd'), 'ones'],
              writes=['ln_psM', 'ln_psR'], tag=f"{pre}.bcmm{t}")

        def fn(eng, t=t):
            eng.tensor_copy(mbc[:, t * 512:(t + 1) * 512], psM[:, :])
            return eng.tensor_copy(rbc[:, t * 512:(t + 1) * 512], psR[:, :])
        sch.g('vector', fn, reads=['ln_psM', 'ln_psR'],
              writes=['ln_mbc', 'ln_rbc'], tag=f"{pre}.bcev{t}")

    for c in range(NCH):
        def fn(eng, c=c):
            return eng.tensor_sub(sq[:, 0:T], b[xk](c), mbc[:, :])
        sch.g('vector', fn, reads=[(xk, c), 'ln_mbc'], writes=[('ln_sqb', 0), ('ln_sqb', 1)],
              tag=f"{pre}.xns{c}")

        def fn(eng, c=c):
            last = None
            if xak is not None:
                last = eng.tensor_mul(b[xak](c), sq[:, 0:T], rbc[:, :])
            if xn_bf is not None:
                last = eng.tensor_mul(xn_bf(c), sq[:, 0:T], rbc[:, :])
            return last
        wr = []
        if xak is not None:
            wr.append((xak, c))
        if xn_bf is not None:
            wr.append(('xnbf', c))
        sch.g('vector', fn, reads=[('ln_sqb', 0), ('ln_sqb', 1), 'ln_rbc'], writes=wr,
              tag=f"{pre}.xn{c}")
        if out_dram is not None:
            def fn(eng, sem, c=c):
                eng.dma_start(out=out_dram(c), in_=b[xak](c)).then_inc(sem, 16)
            sch.dma('sync', 1, fn, reads=[(xak, c)], tag=f"{pre}.stx{c}")



def emit_attn0(sch, b, x0_tile, x1_store, pre="a0", load_wo=None, load_wq=None):
    wq, wk, wv, wo = b['wq'], b['wk'], b['wv'], b['wo']
    xn = b['xnbf']
    kT, v, qt, sc, at = b['kT'], b['v'], b['qt'], b['sc'], b['at']
    p0, p1 = b['a0_p0'], b['a0_p1']
    pS = [b['a0_s0'], b['a0_s1'], b['a0_s2'], b['a0_s3']]
    pav = b['a0_av']
    masks = b['masks']

    def fn(eng):
        return eng.activation(out=b['lng'][:, :], in_=b['gcols'][:, :], func=AF.Ln)
    sch.g('scalar', fn, reads=['gcols'], writes=['lng'], tag=f"{pre}.lng")

    for c in range(NCH):
        for tt in range(4):
            ps = p0 if (c * 4 + tt) % 2 == 0 else p1
            psk = 'a0_p0' if (c * 4 + tt) % 2 == 0 else 'a0_p1'

            def fn(eng, c=c, tt=tt, ps=ps):
                for k in range(NCH):
                    eng.matmul(ps[:, :], wk[:, k * 1024 + c * 128:k * 1024 + (c + 1) * 128],
                               xn(k)[:, tt * 512:(tt + 1) * 512],
                               start=(k == 0), stop=False)
                return eng.matmul(ps[:, :], b['bk_row'][0:1, c * 128:(c + 1) * 128],
                                  b['on512'][0:1, :], start=False, stop=True)
            sch.g('tensor', fn, reads=['wkT', 'bk_row', 'on512'] + [('xnbf', k) for k in range(NCH)],
                  writes=[psk], tag=f"{pre}.kp{c}.{tt}")

            def fn(eng, c=c, tt=tt, ps=ps):
                return eng.copy(out=kT[:, c * T + tt * 512:c * T + (tt + 1) * 512], in_=ps[:, :])
            sch.g('scalar', fn, reads=[psk], writes=[('kT', c)], tag=f"{pre}.ke{c}.{tt}")

    for jt in range(16):
        for h in range(2):
            ps = p0 if (jt * 2 + h) % 2 == 0 else p1
            psk = 'a0_p0' if (jt * 2 + h) % 2 == 0 else 'a0_p1'

            def fn(eng, jt=jt, h=h, ps=ps):
                for k in range(NCH):
                    eng.matmul(ps[:, :], xn(k)[:, jt * 128:(jt + 1) * 128],
                               wv[:, k * 1024 + h * 512:k * 1024 + (h + 1) * 512],
                               start=(k == 0), stop=False)
                return eng.matmul(ps[:, :], b['onr'][0:1, :], b['bv_row'][0:1, h * 512:(h + 1) * 512],
                                  start=False, stop=True)
            sch.g('tensor', fn, reads=['wvT', 'ones', 'bv_row'] + [('xnbf', k) for k in range(NCH)],
                  writes=[psk], tag=f"{pre}.vp{jt}.{h}")

            def fn(eng, jt=jt, h=h, ps=ps):
                return eng.copy(out=v[:, jt * 1024 + h * 512:jt * 1024 + (h + 1) * 512], in_=ps[:, :])
            sch.g('scalar', fn, reads=[psk], writes=[('v', jt)], tag=f"{pre}.ve{jt}.{h}")

    if load_wo is not None:
        load_wo(sch)
    if load_wq is not None:
        load_wq(sch)

    for t in range(4):
        nbj = 4 * t + 4
        for c in range(NCH):
            ps = p0 if c % 2 == 0 else p1
            psk = 'a0_p0' if c % 2 == 0 else 'a0_p1'

            def fn(eng, c=c, t=t, ps=ps):
                for k in range(NCH):
                    eng.matmul(ps[:, :], wq[:, k * 1024 + c * 128:k * 1024 + (c + 1) * 128],
                               xn(k)[:, t * 512:(t + 1) * 512],
                               start=(k == 0), stop=False)
                return eng.matmul(ps[:, :], b['bq_row'][0:1, c * 128:(c + 1) * 128],
                                  b['on512'][0:1, :], start=False, stop=True)
            sch.g('tensor', fn, reads=['wqT', 'bq_row', 'on512'] + [('xnbf', k) for k in range(NCH)],
                  writes=[psk], tag=f"{pre}.qp{t}.{c}")

            def fn(eng, c=c, ps=ps):
                return eng.copy(out=qt[:, c * 512:(c + 1) * 512], in_=ps[:, :])
            sch.g('scalar', fn, reads=[psk], writes=[('qt', c)], tag=f"{pre}.qe{t}.{c}")

        for c in range(NCH):
            for w0 in range(0, nbj, 8):
                wave = range(w0, min(w0 + 8, nbj))
                for bj in wave:
                    sA, sB = pS[bj % 2], pS[2 + bj % 2]
                    kA = ('a0_sA', bj % 2)
                    kB = ('a0_sB', bj % 2)
                    sl = bj % 8

                    def fn(eng, c=c, bj=bj, sA=sA, sB=sB):
                        eng.matmul(sA[:, :], kT[0:64, c * T + bj * 128:c * T + (bj + 1) * 128],
                                   qt[0:64, c * 512:(c + 1) * 512], start=True, stop=True,
                                   tile_position=(0, 0))
                        return eng.matmul(sB[:, :], kT[64:128, c * T + bj * 128:c * T + (bj + 1) * 128],
                                          qt[64:128, c * 512:(c + 1) * 512], start=True, stop=True,
                                          tile_position=(64, 0))
                    sch.g('tensor', fn, reads=[('kT', c), ('qt', c)], writes=[kA, kB],
                          tag=f"{pre}.sc{t}.{c}.{bj}")

                    r = bj - 4 * t
                    if r >= 0:
                        def fn(eng, sl=sl, r=r, sA=sA, sB=sB):
                            eng.tensor_mul(sc[:, (2 * sl) * 512:(2 * sl + 1) * 512], sA[:, :],
                                           masks[:, r * 512:(r + 1) * 512])
                            return eng.tensor_mul(sc[:, (2 * sl + 1) * 512:(2 * sl + 2) * 512], sB[:, :],
                                                  masks[:, r * 512:(r + 1) * 512])
                        sch.g('vector', fn, reads=[kA, kB, 'masks'],
                              writes=[('scb', sl)], tag=f"{pre}.se{t}.{c}.{bj}")
                    else:
                        def fn(eng, sl=sl, sA=sA, sB=sB):
                            eng.copy(out=sc[:, (2 * sl) * 512:(2 * sl + 1) * 512], in_=sA[:, :])
                            return eng.copy(out=sc[:, (2 * sl + 1) * 512:(2 * sl + 2) * 512], in_=sB[:, :])
                        sch.g('scalar', fn, reads=[kA, kB], writes=[('scb', sl)],
                              tag=f"{pre}.se{t}.{c}.{bj}")

                for bj in wave:
                    sl = bj % 8

                    def fn(eng, c=c, bj=bj, sl=sl, nbj=nbj):
                        eng.matmul(pav[0:64, :], v[:, bj * 1024 + c * 128:bj * 1024 + c * 128 + 64],
                                   sc[:, (2 * sl) * 512:(2 * sl + 1) * 512],
                                   start=(bj == 0), stop=(bj == nbj - 1), tile_position=(0, 0))
                        return eng.matmul(pav[64:128, :], v[:, bj * 1024 + c * 128 + 64:bj * 1024 + (c + 1) * 128],
                                          sc[:, (2 * sl + 1) * 512:(2 * sl + 2) * 512],
                                          start=(bj == 0), stop=(bj == nbj - 1), tile_position=(0, 64))
                    sch.g('tensor', fn, reads=[('scb', sl), ('v', bj)], writes=['a0_av'],
                          tag=f"{pre}.av{t}.{c}.{bj}")

            def fn(eng, c=c, t=t):
                return eng.tensor_scalar_mul(b['dkb'][:, 0:1], b['lng'][:, c:c + 1],
                                             cc(b, float(512 * t)))
            sch.g('vector', fn, reads=['lng', 'consts'], writes=['dkb'], tag=f"{pre}.dkb{t}.{c}")

            def fn(eng, c=c, t=t):
                return eng.activation(out=b['dk'][:, :], in_=b['iota512'][:, :],
                                      func=AF.Exp, scale=b['lng'][:, c:c + 1],
                                      bias=b['dkb'][:, 0:1])
            sch.g('scalar', fn, reads=['lng', 'iota512', 'dkb'], writes=['dk'], tag=f"{pre}.dk{t}.{c}")

            def fn(eng, c=c):
                return eng.tensor_mul(at[:, c * 512:(c + 1) * 512], pav[:, :], b['dk'][:, :])
            sch.g('vector', fn, reads=['a0_av', 'dk'], writes=[('at', c)], tag=f"{pre}.ae{t}.{c}")

        for o in range(NCH):
            ps = p0 if o % 2 == 0 else p1
            psk = 'a0_p0' if o % 2 == 0 else 'a0_p1'

            def fn(eng, o=o, ps=ps):
                last = None
                for k in range(NCH):
                    last = eng.matmul(ps[:, :], wo[:, k * 1024 + o * 128:k * 1024 + (o + 1) * 128],
                                      at[:, k * 512:(k + 1) * 512],
                                      start=(k == 0), stop=(k == NCH - 1))
                return last
            sch.g('tensor', fn, reads=['woT'] + [('at', k) for k in range(NCH)],
                  writes=[psk], tag=f"{pre}.wo{t}.{o}")

            x1_store(sch, o, t, ps, psk, b['bo_cols'][:, o:o + 1], x0_tile)


def const_host_array_padded(n=64):
    a = np.zeros((128, n), np.float32)
    s = const_host_array()
    a[:, :s.shape[1]] = s[:, :n]
    return a



def emit_moe(sch, b, load_w, first_write=True, pre="mo"):
    xn = b['xnbf']
    p0, p1, plg, ptr = b['mo_p0'], b['mo_p1'], b['mo_lg'], b['mo_tr']
    lgT, lgt = b['lgT'], b['lgt']
    sc_t, m_t = b['rt_sc'], b['rt_m']
    c1, c2, c3 = b['rt_c1'], b['rt_c2'], b['rt_c3']
    tmp = b['rt_tmp']
    scT, mT = b['scT'], b['mT']
    wb, mb = b['wb'], b['mb']
    xs, h, x3m, t1 = b['xs'], b['h'], b['x3m'], b['t1']
    w1, w2, w3 = b['w1'], b['w2'], b['w3']
    idn = b['id128']

    for tt in range(4):
        def fn(eng, tt=tt):
            last = None
            for k in range(NCH):
                last = eng.matmul(plg[0:8, :], b['gw'][:, k * 8:(k + 1) * 8],
                                  xn(k)[:, tt * 512:(tt + 1) * 512],
                                  start=(k == 0), stop=(k == NCH - 1))
            return last
        sch.g('tensor', fn, reads=['gw'] + [('xnbf', k) for k in range(NCH)],
              writes=['mo_lg'], tag=f"{pre}.lg{tt}")

        def fn(eng, tt=tt):
            return eng.copy(out=lgT[0:8, tt * 512:(tt + 1) * 512], in_=plg[0:8, :])
        sch.g('scalar', fn, reads=['mo_lg'], writes=['lgT'], tag=f"{pre}.lge{tt}")

    for i in range(16):
        def fn(eng, i=i):
            return eng.transpose(ptr[0:128, 0:8], lgT[0:8, i * 128:(i + 1) * 128], idn[0:8, 0:8])
        sch.g('tensor', fn, reads=['lgT', 'id128'], writes=['mo_tr'], tag=f"{pre}.tr{i}")

        def fn(eng, i=i):
            return eng.tensor_copy(lgt[:, i * 8:(i + 1) * 8], ptr[0:128, 0:8])
        sch.g('vector', fn, reads=['mo_tr'], writes=[('lgt', i)], tag=f"{pre}.tre{i}")

    for i in range(16):
        lg_i = lgt[:, i * 8:(i + 1) * 8]
        sc_i = sc_t[:, i * 8:(i + 1) * 8]
        m_i = m_t[:, i * 8:(i + 1) * 8]
        c1_i, c2_i, c3_i = c1[:, i:i + 1], c2[:, i:i + 1], c3[:, i:i + 1]

        sch.g('vector', lambda e, a=c1_i, x=lg_i: e.reduce_max(a, x, axis=AX.X, negate=True),
              reads=[('lgt', i)], writes=[('c1', i)], tag=f"{pre}.r1.{i}")
        sch.g('scalar', lambda e, o=sc_i, x=lg_i, a=c1_i: e.activation(out=o, in_=x, func=AF.Exp, bias=a),
              reads=[('lgt', i), ('c1', i)], writes=[('sc', i)], tag=f"{pre}.exp.{i}")
        sch.g('vector', lambda e, a=c2_i, x=sc_i: e.reduce_sum(a, x, axis=AX.X),
              reads=[('sc', i)], writes=[('c2', i)], tag=f"{pre}.sum.{i}")
        sch.g('vector', lambda e, a=c3_i, x=c2_i: e.reciprocal(a, x),
              reads=[('c2', i)], writes=[('c3', i)], tag=f"{pre}.rcp.{i}")
        sch.g('vector', lambda e, o=sc_i, a=c3_i: e.tensor_scalar(o, o, a, None, op0=ALU.mult),
              reads=[('sc', i), ('c3', i)], writes=[('sc', i)], tag=f"{pre}.nrm.{i}")
        sch.g('vector', lambda e, a=c1_i, x=sc_i: e.reduce_max(a, x, axis=AX.X, negate=True),
              reads=[('sc', i)], writes=[('c1', i)], tag=f"{pre}.m1.{i}")
        sch.g('vector', lambda e, o=m_i, x=sc_i, a=c1_i: e.tensor_scalar(o, x, a, None, op0=ALU.add),
              reads=[('sc', i), ('c1', i)], writes=[('m', i)], tag=f"{pre}.d1.{i}")
        sch.g('vector', lambda e, o=m_i, bb=b: e.tensor_single_scalar(o, o, cc(bb, 0.0), op=ALU.is_equal),
              reads=[('m', i), 'consts'], writes=[('m', i)], tag=f"{pre}.eq.{i}")
        sch.g('vector', lambda e, o=m_i, x=sc_i, bb=b: e.scalar_tensor_tensor(o, o, cc(bb, -1e30), x, op0=ALU.mult, op1=ALU.add),
              reads=[('m', i), ('sc', i), 'consts'], writes=[('m', i)], tag=f"{pre}.msk.{i}")
        sch.g('vector', lambda e, a=c2_i, x=m_i: e.reduce_max(a, x, axis=AX.X, negate=True),
              reads=[('m', i)], writes=[('c2', i)], tag=f"{pre}.m2.{i}")
        sch.g('vector', lambda e, o=m_i, x=sc_i, a=c2_i: e.tensor_scalar(o, x, a, None, op0=ALU.add),
              reads=[('sc', i), ('c2', i)], writes=[('m', i)], tag=f"{pre}.d2.{i}")
        sch.g('vector', lambda e, o=m_i, bb=b: e.tensor_single_scalar(o, o, cc(bb, 0.0), op=ALU.is_ge),
              reads=[('m', i), 'consts'], writes=[('m', i)], tag=f"{pre}.ge.{i}")

    for i in range(16):
        def fn(eng, i=i):
            return eng.transpose(ptr[0:8, 0:128], sc_t[:, i * 8:(i + 1) * 8], idn[:, :])
        sch.g('tensor', fn, reads=[('sc', i), 'id128'], writes=['mo_tr'], tag=f"{pre}.ts{i}")
        sch.g('scalar', lambda e, i=i: e.copy(out=scT[0:8, i * 128:(i + 1) * 128], in_=ptr[0:8, 0:128]),
              reads=['mo_tr'], writes=['scT'], tag=f"{pre}.tse{i}")

        def fn(eng, i=i):
            return eng.transpose(ptr[0:8, 0:128], m_t[:, i * 8:(i + 1) * 8], idn[:, :])
        sch.g('tensor', fn, reads=[('m', i), 'id128'], writes=['mo_tr'], tag=f"{pre}.tm{i}")
        sch.g('scalar', lambda e, i=i: e.copy(out=mT[0:8, i * 128:(i + 1) * 128], in_=ptr[0:8, 0:128]),
              reads=['mo_tr'], writes=['mT'], tag=f"{pre}.tme{i}")

    for e_ in range(8):
        load_w(sch, e_)

        for (src, srck, dst, dstk) in ((scT, 'scT', wb, 'wb'), (mT, 'mT', mb, 'mb')):
            for tt in range(4):
                ps = p0 if tt % 2 == 0 else p1
                psk = 'mo_p0' if tt % 2 == 0 else 'mo_p1'

                def fn(eng, e_=e_, tt=tt, src=src, ps=ps):
                    return eng.matmul(ps[:, :], b['sel8'][0:8, e_ * 128:(e_ + 1) * 128],
                                      src[0:8, tt * 512:(tt + 1) * 512], start=True, stop=True)
                sch.g('tensor', fn, reads=[srck, 'sel8'], writes=[psk], tag=f"{pre}.bc{dstk}{e_}.{tt}")

                def fn(eng, tt=tt, dst=dst, ps=ps):
                    return eng.copy(out=dst[:, tt * 512:(tt + 1) * 512], in_=ps[:, :])
                sch.g('scalar', fn, reads=[psk], writes=[dstk], tag=f"{pre}.bce{dstk}{e_}.{tt}")

        for qf in range(4):
            qo = qf * 512
            for k in range(NCH):
                def fn(eng, k=k, qo=qo):
                    return eng.tensor_mul(xs[:, k * 512:(k + 1) * 512],
                                          xn(k)[:, qo:qo + 512], wb[:, qo:qo + 512])
                sch.g('gpsimd', fn, reads=[('xnbf', k), 'wb'], writes=[('xs', k)],
                      tag=f"{pre}.xs{e_}.{qf}.{k}")

            for hc in range(NCH):
                ps = p0 if hc % 2 == 0 else p1
                psk = 'mo_p0' if hc % 2 == 0 else 'mo_p1'

                def fn(eng, hc=hc, ps=ps):
                    last = None
                    for k in range(NCH):
                        last = eng.matmul(ps[:, :], w1[:, k * 1024 + hc * 128:k * 1024 + (hc + 1) * 128],
                                          xs[:, k * 512:(k + 1) * 512],
                                          start=(k == 0), stop=(k == NCH - 1))
                    return last
                sch.g('tensor', fn, reads=['w1sb'] + [('xs', k) for k in range(NCH)],
                      writes=[psk], tag=f"{pre}.u1{e_}.{qf}.{hc}")

                def fn(eng, hc=hc, e_=e_, ps=ps):
                    return eng.activation(out=h[:, hc * 512:(hc + 1) * 512],
                                          in_=ps[:, :], func=AF.Silu,
                                          bias=b['b1c'][:, hc * 8 + e_:hc * 8 + e_ + 1])
                sch.g('scalar', fn, reads=[psk, 'b1c'], writes=[('h', hc)],
                      tag=f"{pre}.u1e{e_}.{qf}.{hc}")

            for oc in range(NCH):
                def fn(eng, oc=oc, ps=p0):
                    last = None
                    for k in range(NCH):
                        last = eng.matmul(ps[:, :], w3[:, k * 1024 + oc * 128:k * 1024 + (oc + 1) * 128],
                                          xs[:, k * 512:(k + 1) * 512],
                                          start=(k == 0), stop=(k == NCH - 1))
                    return last
                sch.g('tensor', fn, reads=['w3sb'] + [('xs', k) for k in range(NCH)],
                      writes=['mo_p0'], tag=f"{pre}.x3{e_}.{qf}.{oc}")

                def fn(eng, oc=oc, e_=e_, qo=qo):
                    return eng.scalar_tensor_tensor(
                        out=x3m[:, :], in0=p0[:, :],
                        scalar=b['b3c'][:, oc * 8 + e_:oc * 8 + e_ + 1],
                        in1=mb[:, qo:qo + 512],
                        op0=ALU.add, op1=ALU.mult)
                sch.g('vector', fn, reads=['mo_p0', 'b3c', 'mb'], writes=['x3m'],
                      tag=f"{pre}.x3e{e_}.{qf}.{oc}")

                def fn(eng, oc=oc, ps=p1):
                    last = None
                    for k in range(NCH):
                        last = eng.matmul(ps[:, :], w2[:, k * 1024 + oc * 128:k * 1024 + (oc + 1) * 128],
                                          h[:, k * 512:(k + 1) * 512],
                                          start=(k == 0), stop=(k == NCH - 1))
                    return last
                sch.g('tensor', fn, reads=['w2sb'] + [('h', k) for k in range(NCH)],
                      writes=['mo_p1'], tag=f"{pre}.x2{e_}.{qf}.{oc}")

                if e_ == 0 and first_write:
                    def fn(eng, oc=oc, e_=e_, qo=qo):
                        return eng.scalar_tensor_tensor(
                            out=b['y'](oc)[:, qo:qo + 512],
                            in0=p1[:, :], scalar=b['b2c'][:, oc * 8 + e_:oc * 8 + e_ + 1],
                            in1=x3m[:, :], op0=ALU.add, op1=ALU.mult)
                    sch.g('vector', fn, reads=['mo_p1', 'b2c', 'x3m'], writes=[('y', oc)],
                          tag=f"{pre}.x2e{e_}.{qf}.{oc}")
                else:
                    def fn(eng, oc=oc, e_=e_):
                        return eng.scalar_tensor_tensor(
                            out=t1[:, :], in0=p1[:, :],
                            scalar=b['b2c'][:, oc * 8 + e_:oc * 8 + e_ + 1],
                            in1=x3m[:, :], op0=ALU.add, op1=ALU.mult)
                    sch.g('vector', fn, reads=['mo_p1', 'b2c', 'x3m'], writes=['t1'],
                          tag=f"{pre}.x2e{e_}.{qf}.{oc}")

                    def fn(eng, oc=oc, qo=qo):
                        yap = b['y'](oc)[:, qo:qo + 512]
                        return eng.tensor_add(yap, yap, t1[:, :])
                    sch.g('gpsimd', fn, reads=['t1', ('y', oc)], writes=[('y', oc)],
                          tag=f"{pre}.acc{e_}.{qf}.{oc}")



def emit_shared(sch, b, z_store, pre="sh"):
    xn = b['xnbf']
    sw1, sw2, sw3 = b['sw1'], b['sw2'], b['sw3']
    sa, gg = b['sa'], b['gg']
    p0, p1 = b['sh_p0'], b['sh_p1']

    for hf in range(2):
        ho = hf * 1024
        for ic in range(16):
            for tt in range(2):
                ps, psk = (p0, 'sh_p0') if tt == 0 else (p1, 'sh_p1')

                def fn(eng, ic=ic, tt=tt, ho=ho, ps=ps):
                    last = None
                    for k in range(NCH):
                        last = eng.matmul(ps[:, :], sw1[:, k * T + ic * 128:k * T + (ic + 1) * 128],
                                          xn(k)[:, ho + tt * 512:ho + (tt + 1) * 512],
                                          start=(k == 0), stop=(k == NCH - 1))
                    return last
                sch.g('tensor', fn, reads=['sw1sb'] + [('xnbf', k) for k in range(NCH)],
                      writes=[psk], tag=f"{pre}.ua{hf}.{ic}.{tt}")

                def fn(eng, tt=tt, ps=ps):
                    return eng.activation(out=sa[:, tt * 512:(tt + 1) * 512], in_=ps[:, :], func=AF.Silu)
                sch.g('scalar', fn, reads=[psk], writes=[('sa', tt)], tag=f"{pre}.uae{hf}.{ic}.{tt}")

            for tt in range(2):
                ps, psk = (p0, 'sh_p0') if tt == 0 else (p1, 'sh_p1')

                def fn(eng, ic=ic, tt=tt, ho=ho, ps=ps):
                    last = None
                    for k in range(NCH):
                        last = eng.matmul(ps[:, :], sw2[:, k * T + ic * 128:k * T + (ic + 1) * 128],
                                          xn(k)[:, ho + tt * 512:ho + (tt + 1) * 512],
                                          start=(k == 0), stop=(k == NCH - 1))
                    return last
                sch.g('tensor', fn, reads=['sw2sb'] + [('xnbf', k) for k in range(NCH)],
                      writes=[psk], tag=f"{pre}.ub{hf}.{ic}.{tt}")

                def fn(eng, ic=ic, tt=tt, ps=ps):
                    return eng.tensor_mul(gg[:, ic * 1024 + tt * 512:ic * 1024 + (tt + 1) * 512],
                                          ps[:, :], sa[:, tt * 512:(tt + 1) * 512])
                sch.g('vector', fn, reads=[psk, ('sa', tt)], writes=[('gg', ic)],
                      tag=f"{pre}.gg{hf}.{ic}.{tt}")

        for oc in range(NCH):
            for tt in range(2):
                ps, psk = (p0, 'sh_p0') if tt == 0 else (p1, 'sh_p1')

                def fn(eng, oc=oc, tt=tt, ps=ps):
                    last = None
                    for ic in range(16):
                        last = eng.matmul(ps[:, :], sw3[:, ic * 1024 + oc * 128:ic * 1024 + (oc + 1) * 128],
                                          gg[:, ic * 1024 + tt * 512:ic * 1024 + (tt + 1) * 512],
                                          start=(ic == 0), stop=(ic == 15))
                    return last
                sch.g('tensor', fn, reads=['sw3sb'] + [('gg', ic) for ic in range(16)],
                      writes=[psk], tag=f"{pre}.z{hf}.{oc}.{tt}")

                z_store(sch, oc, hf, tt, psk, ps)



TAPC = [0.1 + 0.9 * k / 7.0 for k in range(8)]


def l1_cmat():
    C = np.zeros((8, 8), np.float32)
    for s in range(1, 8):
        w = min(s, 8)
        for dd in range(0, s):
            j = dd
            C[dd, s] = 0.1 + 0.9 * j / max(w - 1, 1) if w > 1 else 0.1
    return C


def emit_l1attn(sch, b, traj_store, g_dram, x_dram, pre="l1"):
    ND = T - 1
    df, x1c = b['df'], b['x1c']
    sq = b['l1sq']
    pd, pV, pB, pc, ptb = b['l1_pd'], b['l1_pV'], b['l1_pB'], b['l1_pc'], b['l1_ptb']
    pR0, pR1 = b['l1_pR0'], b['l1_pR1']
    l1c = b['l1c']
    rwa, rwb, rwc, rwd = b['rwa'], b['rwb'], b['rwc'], b['rwd']
    onc = b['onc']

    Gf = b['Gf32']

    def fn(eng, sem):
        for gc in range(4):
            eng.dma_start(out=Gf[:, gc * 1024:(gc + 1) * 1024],
                          in_=g_dram[gc * 128:(gc + 1) * 128, :]).then_inc(sem, 16)
    sch.dma('sync', 4, fn, writes=[('df', 0), ('df', 1), 'Gf32'], tag=f"{pre}.ldgf")

    for gc in range(4):
        def fn(eng, gc=gc):
            return eng.activation(out=sq[:, 0:1024], in_=Gf[:, gc * 1024:(gc + 1) * 1024], func=AF.Square)
        sch.g('scalar', fn, reads=['Gf32'], writes=['l1sq'], tag=f"{pre}.gsq{gc}")

        def fn(eng, gc=gc):
            return eng.reduce_sum(l1c[:, 48 + gc:49 + gc], sq[:, 0:1024], axis=AX.X)
        sch.g('vector', fn, reads=['l1sq'], writes=[('gn', gc)], tag=f"{pre}.gss{gc}")

    def fn(eng):
        return eng.tensor_scalar_max(l1c[:, 48:52], l1c[:, 48:52], cc(b, 1e-16))
    sch.g('vector', fn, reads=[('gn', g) for g in range(4)] + ['consts'], writes=['gnc'], tag=f"{pre}.gcl")

    def fn(eng):
        return eng.activation(out=l1c[:, 52:56], in_=l1c[:, 48:52], func=AF.Ln)
    sch.g('scalar', fn, reads=['gnc'], writes=['gnl'], tag=f"{pre}.gln")

    def fn(eng):
        return eng.activation(out=l1c[:, 48:52], in_=l1c[:, 52:56], func=AF.Exp, scale=cc(b, -0.5))
    sch.g('scalar', fn, reads=['gnl'], writes=['gni'], tag=f"{pre}.gexp")

    for gc in range(4):
        def fn(eng, gc=gc):
            return eng.transpose(pV[0:1, 0:128], l1c[:, 48 + gc:49 + gc], b['id128'][:, :])
        sch.g('tensor', fn, reads=['gni', 'id128'], writes=['l1_pV'], tag=f"{pre}.gtr{gc}")

        def fn(eng, gc=gc):
            return eng.tensor_copy(rwd[0:1, gc * 128:(gc + 1) * 128], pV[0:1, 0:128])
        sch.g('vector', fn, reads=['l1_pV'], writes=['gnr'], tag=f"{pre}.gte{gc}")

    def fn(eng):
        return eng.matmul(pB[:, :], b['onr'][0:1, :], rwd[0:1, 0:512], start=True, stop=True)
    sch.g('tensor', fn, reads=['gnr', 'ones'], writes=['l1_pB'], tag=f"{pre}.gbc")

    def fn(eng):
        return eng.copy(out=b['invgn'][:, :], in_=pB[:, :])
    sch.g('scalar', fn, reads=['l1_pB'], writes=['invgn'], tag=f"{pre}.gbce")

    for c in range(NCH):
        def fn(eng, sem, c=c):
            eng.dma_start(out=x1c(c), in_=x_dram(c)).then_inc(sem, 16)
        sch.dma('sync', 1, fn,
                reads=(['Gf32'] if c < 2 else []) + (['gnr'] if c == 7 else []),
                writes=[('x1c', c)] + (['rwd'] if c == 7 else []),
                tag=f"{pre}.ldx{c}")

    for c in range(NCH):
        def fn(eng, c=c):
            eng.tensor_sub(df(c)[:, 0:ND], x1c(c)[:, 1:T], x1c(c)[:, 0:ND])
            return eng.memset(df(c)[:, ND:T], 0.0)
        sch.g('vector', fn, reads=[('x1c', c)], writes=[('df', c)], tag=f"{pre}.df{c}")

    rslots = [(pR0, 0), (pR0, 32), (pR0, 64), (pR1, 0)]
    for c in range(NCH):
        def fn(eng, c=c):
            return eng.activation(out=sq[:, :], in_=df(c)[:, :], func=AF.Square)
        sch.g('scalar', fn, reads=[('df', c)], writes=['l1sq'], tag=f"{pre}.sq{c}")

        def fn(eng, c=c):
            last = None
            for st in range(16):
                last = eng.matmul(pc[:, st:st + 1], sq[:, st * 128:(st + 1) * 128], onc[:, :],
                                  start=(c == 0), stop=(c == NCH - 1))
            return last
        sch.g('tensor', fn, reads=['l1sq', 'ones'], writes=['l1_pc'], tag=f"{pre}.cmm{c}")

        def fn(eng, c=c):
            last = None
            for t4 in range(4):
                ps, r = rslots[t4]
                last = eng.matmul(ps[r:r + 1, :], onc[:, :], sq[:, t4 * 512:(t4 + 1) * 512],
                                  start=(c == 0), stop=(c == NCH - 1))
            return last
        sch.g('tensor', fn, reads=['l1sq', 'ones'], writes=['l1_pR'], tag=f"{pre}.rmm{c}")

    def fn(eng):
        return eng.tensor_copy(l1c[:, 0:16], pc[:, 0:16])
    sch.g('vector', fn, reads=['l1_pc'], writes=['scol'], tag=f"{pre}.cev")

    def fn(eng):
        last = None
        for t4 in range(4):
            ps, r = rslots[t4]
            last = eng.copy(out=rwa[0:1, t4 * 512:(t4 + 1) * 512], in_=ps[r:r + 1, :])
        return last
    sch.g('scalar', fn, reads=['l1_pR'] + [('x1c', c) for c in (4, 5, 6, 7)],
          writes=['rwa'], tag=f"{pre}.rev")

    def fn(eng):
        return eng.tensor_scalar_max(l1c[:, 0:16], l1c[:, 0:16], cc(b, 1e-12))
    sch.g('vector', fn, reads=['scol', 'consts'], writes=['scol'], tag=f"{pre}.ccl")

    def fn(eng):
        return eng.activation(out=l1c[:, 16:32], in_=l1c[:, 0:16], func=AF.Ln)
    sch.g('scalar', fn, reads=['scol'], writes=['lnc'], tag=f"{pre}.cln")

    def fn(eng):
        eng.activation(out=l1c[:, 0:16], in_=l1c[:, 16:32], func=AF.Exp, scale=cc(b, -0.5))
        return eng.activation(out=l1c[:, 32:48], in_=l1c[:, 16:32], func=AF.Exp, scale=cc(b, 0.5))
    sch.g('scalar', fn, reads=['lnc', 'consts'], writes=['invmc', 'magc'], tag=f"{pre}.cexp")

    def fn(eng):
        return eng.tensor_mul(l1c[:, 16:32], l1c[:, 0:16], l1c[:, 32:48])
    sch.g('vector', fn, reads=['invmc', 'magc'], writes=['anc'], tag=f"{pre}.can")

    def fn(eng):
        return eng.tensor_scalar_max(l1c[:, 16:32], l1c[:, 16:32], cc(b, 1e-8))
    sch.g('vector', fn, reads=['anc', 'consts'], writes=['anc'], tag=f"{pre}.crc1")

    def fn(eng):
        return eng.reciprocal(l1c[:, 16:32], l1c[:, 16:32])
    sch.g('vector', fn, reads=['anc'], writes=['anc'], tag=f"{pre}.crc2")

    def fn(eng):
        return eng.scalar_tensor_tensor(l1c[:, 16:32], l1c[:, 16:32], cc(b, 2.0), l1c[:, 0:16],
                                        op0=ALU.mult, op1=ALU.mult)
    sch.g('vector', fn, reads=['anc', 'invmc', 'consts'], writes=['rsc'], tag=f"{pre}.crs")

    def fn(eng):
        return eng.activation(out=l1c[:, 32:48], in_=l1c[:, 32:48], func=AF.Tanh, scale=cc(b, 2.0))
    sch.g('scalar', fn, reads=['magc', 'consts'], writes=['tnc'], tag=f"{pre}.ctn")

    def fn(eng):
        return eng.tensor_scalar_max(rwa[0:1, :], rwa[0:1, :], cc(b, 1e-12, 1))
    sch.g('vector', fn, reads=['rwa', 'consts'], writes=['rwa'], tag=f"{pre}.rcl")

    def fn(eng):
        return eng.activation(out=rwa[0:1, :], in_=rwa[0:1, :], func=AF.Ln)
    sch.g('scalar', fn, reads=['rwa'], writes=['rwa'], tag=f"{pre}.rln")

    def fn(eng):
        eng.activation(out=rwb[0:1, :], in_=rwa[0:1, :], func=AF.Exp, scale=cc(b, -0.5, 1))
        return eng.activation(out=rwc[0:1, :], in_=rwa[0:1, :], func=AF.Exp, scale=cc(b, 0.5, 1))
    sch.g('scalar', fn, reads=['rwa', 'consts'] + [('x1c', c) for c in (4, 5, 6, 7)],
          writes=['rwb', 'rwc'], tag=f"{pre}.rexp")

    def fn(eng):
        return eng.activation(out=rwc[0:1, :], in_=rwc[0:1, :], func=AF.Tanh, scale=cc(b, 2.0, 1))
    sch.g('scalar', fn, reads=['rwc', 'consts'], writes=['rwc'], tag=f"{pre}.rtn")

    def fn(eng):
        return eng.scalar_tensor_tensor(rwb[0:1, :], rwc[0:1, :], cc(b, 0.6, 1), rwb[0:1, :],
                                        op0=ALU.mult, op1=ALU.mult)
    sch.g('vector', fn, reads=['rwb', 'rwc', 'consts'], writes=['rwb'], tag=f"{pre}.rap")

    def fn(eng):
        return eng.tensor_scalar_mul(rwd[0:1, 8:T], rwc[0:1, 0:T - 8], cc(b, TAPC[0], 1))
    sch.g('vector', fn, reads=['rwc', 'consts', 'gnr'], writes=['rwd'], tag=f"{pre}.dtap0")
    for k in range(1, 8):
        def fn(eng, k=k):
            return eng.scalar_tensor_tensor(rwd[0:1, 8:T], rwc[0:1, k:T - 8 + k], cc(b, TAPC[k], 1),
                                            rwd[0:1, 8:T], op0=ALU.mult, op1=ALU.add)
        sch.g('vector', fn, reads=['rwc', 'rwd', 'consts'], writes=['rwd'], tag=f"{pre}.dtap{k}")

    def fn(eng):
        return eng.matmul(pV[0:1, 0:8], l1c[0:8, 32:33], b['Cmat'][0:8, 0:8], start=True, stop=True)
    sch.g('tensor', fn, reads=['tnc', 'Cmat'], writes=['l1_pV'], tag=f"{pre}.dsm")

    def fn(eng):
        return eng.tensor_copy(rwd[0:1, 0:8], pV[0:1, 0:8])
    sch.g('vector', fn, reads=['l1_pV'], writes=['rwd'], tag=f"{pre}.dsme")

    def fn(eng):
        return eng.tensor_scalar_max(rwa[0:1, :], rwd[0:1, :], cc(b, 1e-8, 1))
    sch.g('vector', fn, reads=['rwd', 'consts'], writes=['rwa'], tag=f"{pre}.dcl")

    def fn(eng):
        eng.activation(out=rwa[0:1, :], in_=rwa[0:1, :], func=AF.Ln)
        return eng.activation(out=rwa[0:1, :], in_=rwa[0:1, :], func=AF.Exp, scale=cc(b, -1.0, 1))
    sch.g('scalar', fn, reads=['rwa', 'consts'], writes=['rwa'], tag=f"{pre}.dinv")

    for st in range(16):
        def fn(eng, st=st):
            last = None
            for k in range(NCH):
                last = eng.tensor_copy(b['dfsl'][:, k * 128:(k + 1) * 128],
                                       df(k)[:, st * 128:(st + 1) * 128])
            return last
        sch.g('vector', fn, reads=[('df', k) for k in range(NCH)], writes=['dfsl'],
              tag=f"{pre}.cast{st}")

        def fn(eng, st=st):
            last = None
            for k in range(NCH):
                last = eng.matmul(pd[:, :], b['dfsl'][:, k * 128:(k + 1) * 128],
                                  b['GTbf'][:, k * 512:(k + 1) * 512],
                                  start=(k == 0), stop=(k == NCH - 1))
            return last
        sch.g('tensor', fn, reads=['dfsl', 'GTbf'], writes=['l1_pd'], tag=f"{pre}.dot{st}")

        def fn(eng, st=st):
            return eng.scalar_tensor_tensor(b['sims'][:, :], pd[:, :], l1c[:, 16 + st:17 + st],
                                            b['invgn'][:, :], op0=ALU.mult, op1=ALU.mult)
        sch.g('vector', fn, reads=['l1_pd', 'rsc', 'invgn'], writes=['sims'], tag=f"{pre}.sim{st}")

        def fn(eng, st=st):
            return eng.reduce_max(b['smx'][:, 0:1], b['sims'][:, :], axis=AX.X, negate=True)
        sch.g('vector', fn, reads=['sims'], writes=['smx'], tag=f"{pre}.smx{st}")

        def fn(eng, st=st):
            return eng.activation(out=b['sims'][:, :], in_=b['sims'][:, :], func=AF.Exp,
                                  bias=b['smx'][:, 0:1])
        sch.g('scalar', fn, reads=['sims', 'smx'], writes=['sims'], tag=f"{pre}.sex{st}")

        def fn(eng, st=st):
            return eng.reduce_sum(b['smx'][:, 1:2], b['sims'][:, :], axis=AX.X)
        sch.g('vector', fn, reads=['sims'], writes=['smx2'], tag=f"{pre}.ssm{st}")

        def fn(eng, st=st):
            return eng.reciprocal(b['smx'][:, 2:3], b['smx'][:, 1:2])
        sch.g('vector', fn, reads=['smx2'], writes=['smx3'], tag=f"{pre}.src{st}")

        def fn(eng, st=st):
            return eng.scalar_tensor_tensor(b['smx'][:, 3:4], b['smx'][:, 2:3], cc(b, 0.4),
                                            l1c[:, 32 + st:33 + st], op0=ALU.mult, op1=ALU.mult)
        sch.g('vector', fn, reads=['smx3', 'tnc', 'consts'], writes=['smx4'], tag=f"{pre}.snr{st}")

        def fn(eng, st=st):
            return eng.tensor_scalar(b['gwt'][:, :], b['sims'][:, :], b['smx'][:, 3:4], None,
                                     op0=ALU.mult)
        sch.g('vector', fn, reads=['sims', 'smx4'], writes=['gwt'], tag=f"{pre}.gw{st}")

        for j in range(4):
            def fn(eng, st=st, j=j):
                return eng.transpose(ptb[:, :], b['gwt'][:, j * 128:(j + 1) * 128], b['idbf'][:, :])
            sch.g('tensor', fn, reads=['gwt', 'idbf'], writes=['l1_ptb'], tag=f"{pre}.gt{st}.{j}")

            def fn(eng, st=st, j=j):
                return eng.tensor_copy(b['gwT'][:, j * T + st * 128:j * T + (st + 1) * 128], ptb[:, :])
            sch.g('vector', fn, reads=['l1_ptb'], writes=[('gwT', j)], tag=f"{pre}.gte{st}.{j}")

    for st2 in range(4):
        def fn(eng, st2=st2):
            return eng.matmul(pB[:, :], b['onr'][0:1, :], rwb[0:1, st2 * 512:(st2 + 1) * 512],
                              start=True, stop=True)
        sch.g('tensor', fn, reads=['rwb', 'ones'], writes=['l1_pB'], tag=f"{pre}.abc{st2}")

        def fn(eng, st2=st2):
            return eng.copy(out=b['apb'][:, :], in_=pB[:, :])
        sch.g('scalar', fn, reads=['l1_pB'], writes=['apb'], tag=f"{pre}.abce{st2}")

        for c in range(NCH):
            pu = pd if c % 2 == 0 else pV
            puk = 'l1_pd' if c % 2 == 0 else 'l1_pV'

            def fn(eng, st2=st2, c=c, pu=pu):
                last = None
                for gc in range(4):
                    last = eng.matmul(pu[:, :], b['Gbf'][:, gc * 1024 + c * 128:gc * 1024 + (c + 1) * 128],
                                      b['gwT'][:, gc * T + st2 * 512:gc * T + (st2 + 1) * 512],
                                      start=(gc == 0), stop=(gc == 3))
                return last
            sch.g('tensor', fn, reads=['Gbf'] + [('gwT', gc) for gc in range(4)],
                  writes=[puk], tag=f"{pre}.inf{st2}.{c}")

            def fn(eng, st2=st2, c=c):
                return eng.tensor_mul(df(c)[:, st2 * 512:(st2 + 1) * 512],
                                      df(c)[:, st2 * 512:(st2 + 1) * 512], b['apb'][:, :])
            sch.g('vector', fn, reads=[('df', c), 'apb'], writes=[('df', c)], tag=f"{pre}.ua{st2}.{c}")

            def fn(eng, st2=st2, c=c, pu=pu):
                return eng.tensor_add(df(c)[:, st2 * 512:(st2 + 1) * 512],
                                      df(c)[:, st2 * 512:(st2 + 1) * 512], pu[:, :])
            sch.g('vector', fn, reads=[('df', c), puk], writes=[('df', c)], tag=f"{pre}.ub{st2}.{c}")

    for st2 in range(4):
        def fn(eng, st2=st2):
            return eng.matmul(pB[:, :], b['onr'][0:1, :], rwa[0:1, st2 * 512:(st2 + 1) * 512],
                              start=True, stop=True)
        sch.g('tensor', fn, reads=['rwa', 'ones'], writes=['l1_pB'], tag=f"{pre}.ibc{st2}")

        def fn(eng, st2=st2):
            return eng.copy(out=b['ivb'][:, st2 * 512:(st2 + 1) * 512], in_=pB[:, :])
        sch.g('scalar', fn, reads=['l1_pB'] + ([('x1c', 1)] if st2 == 0 else []),
              writes=['ivb'], tag=f"{pre}.ibce{st2}")

    for c in range(NCH):
        ev = 'vector'
        acc = b['acc']

        def fn(eng, c=c):
            return eng.tensor_scalar_mul(acc[:, 8:T], df(c)[:, 0:T - 8], cc(b, TAPC[0]))
        sch.g(ev, fn, reads=[('df', c), 'consts', ('x1c', 0)], writes=['acc'], tag=f"{pre}.t0.{c}")
        for k in range(1, 8):
            def fn(eng, c=c, k=k):
                return eng.scalar_tensor_tensor(acc[:, 8:T], df(c)[:, k:T - 8 + k], cc(b, TAPC[k]),
                                                acc[:, 8:T], op0=ALU.mult, op1=ALU.add)
            sch.g(ev, fn, reads=[('df', c), 'acc', 'consts'], writes=['acc'], tag=f"{pre}.t{k}.{c}")

        def fn(eng, c=c):
            return eng.transpose(pV[0:8, 0:128], df(c)[:, 0:8], b['id128'][:, :])
        sch.g('tensor', fn, reads=[('df', c), 'id128'], writes=['l1_pV'], tag=f"{pre}.ut{c}")

        def fn(eng, c=c):
            return eng.tensor_copy(b['Usm'][0:8, :], pV[0:8, 0:128])
        sch.g('vector', fn, reads=['l1_pV'], writes=['Usm'], tag=f"{pre}.ute{c}")

        def fn(eng, c=c):
            return eng.matmul(pV[0:128, 0:8], b['Usm'][0:8, :], b['Cmat'][0:8, 0:8],
                              start=True, stop=True)
        sch.g('tensor', fn, reads=['Usm', 'Cmat'], writes=['l1_pV'], tag=f"{pre}.usm{c}")

        def fn(eng, c=c):
            return eng.tensor_copy(acc[:, 0:8], pV[0:128, 0:8])
        sch.g('vector', fn, reads=['l1_pV'], writes=['acc'], tag=f"{pre}.usme{c}")

        traj_store(sch, c)



def emit_tree(sch, b, out_store, pre="tf", skip=()):
    xn = b['xnbf']
    win, wout = b['ti_win'], b['ti_wout']
    ac, dc, ind, mT = b['ti_ac'], b['ti_dc'], b['ti_in'], b['ti_mT']
    p0, p1, ptb = b['tf_p0'], b['tf_p1'], b['tf_ptb']

    for q in range(4):
        for ti in range(4):
            tt = q * 4 + ti
            for ws in range(4):
                ps, psk = (p0, 'tf_p0') if ws % 2 == 0 else (p1, 'tf_p1')

                def fn(eng, tt=tt, ws=ws, ps=ps):
                    for k in range(NCH):
                        eng.matmul(ps[:, :], xn(k)[:, tt * 128:(tt + 1) * 128],
                                   win[:, k * T + ws * 512:k * T + (ws + 1) * 512],
                                   start=(k == 0), stop=False)
                    return eng.matmul(ps[:, :], b['onr'][0:1, :],
                                      b['ti_binr'][0:1, ws * 512:(ws + 1) * 512],
                                      start=False, stop=True)
                sch.g('tensor', fn, reads=['ti_win', 'ti_binr', 'ones'] + [('xnbf', k) for k in range(NCH)],
                      writes=[psk], tag=f"{pre}.lg{tt}.{ws}")

                def fn(eng, ti=ti, ws=ws, ps=ps):
                    return eng.activation(out=ac[:, ti * T + ws * 512:ti * T + (ws + 1) * 512],
                                          in_=ps[:, :], func=AF.Silu)
                sch.g('scalar', fn, reads=[psk], writes=[('ac', ti)], tag=f"{pre}.si{tt}.{ws}")

                def fn(eng, ti=ti, ws=ws, ps=ps, bb=b):
                    return eng.tensor_single_scalar(dc[:, ti * T + ws * 512:ti * T + (ws + 1) * 512],
                                                    ps[:, :], cc(bb, 0.0), op=ALU.is_gt)
                sch.g('vector', fn, reads=[psk, 'consts', ('ac', ti)], writes=[('dc', ti)], tag=f"{pre}.dc{tt}.{ws}")

        for ti in range(4):
            tt = q * 4 + ti
            io = ti * T
            ind3 = ind[:, io:io + 2040].rearrange("p (e n) -> p e n", n=255)
            dc3 = dc[:, io:io + 2040].rearrange("p (e n) -> p e n", n=255)

            if 'iz' in skip:
                pass
            else:
                def fn(eng, ti=ti, io=io):
                    return eng.memset(ind[:, io:io + T], 0.0)
                sch.g('vector', fn, reads=[], writes=[('ind', ti)], tag=f"{pre}.iz{tt}")

            if 'iz' not in skip:
                def fn(eng, ti=ti, io=io):
                    return eng.memset(ind[:, io:io + 2040].rearrange("p (e n) -> p e n", n=255)[:, :, 0:1], 1.0)
                sch.g('vector', fn, reads=[], writes=[('ind', ti)], tag=f"{pre}.iz2{tt}")

            for d in (() if 'walk' in skip else range(7)):
                a0 = (1 << d) - 1
                a1 = (1 << (d + 1)) - 1
                njd = 1 << d
                par = ind3[:, :, a0:a1]
                decp = dc3[:, :, a0:a1]
                chpair = ind3[:, :, a1:a1 + 2 * njd].rearrange("p e (i two) -> p e i two", two=2)
                right = chpair[:, :, :, 1:2].rearrange("p e i two -> p e (i two)")
                left = chpair[:, :, :, 0:1].rearrange("p e i two -> p e (i two)")

                def fn(eng, par=par, decp=decp, right=right):
                    return eng.tensor_mul(right, par, decp)
                sch.g('vector', fn, reads=[('ind', ti), ('dc', ti)], writes=[('ind', ti)],
                      tag=f"{pre}.wka{tt}.{d}")

                def fn(eng, par=par, right=right, left=left):
                    return eng.tensor_sub(left, par, right)
                sch.g('vector', fn, reads=[('ind', ti)], writes=[('ind', ti)],
                      tag=f"{pre}.wkb{tt}.{d}")

            if 'mk' not in skip:
                def fn(eng, ti=ti, io=io):
                    return eng.tensor_mul(ac[:, io:io + T], ac[:, io:io + T], ind[:, io:io + T])
                sch.g('vector', fn, reads=[('ac', ti), ('ind', ti)], writes=[('ac', ti)],
                      tag=f"{pre}.mk{tt}")

            for wc in (() if 'tr' in skip else range(16)):
                def fn(eng, ti=ti, io=io, wc=wc):
                    return eng.transpose(ptb[:, :], ac[:, io + wc * 128:io + (wc + 1) * 128],
                                         b['idbf'][:, :])
                sch.g('tensor', fn, reads=[('ac', ti), 'idbf'], writes=['tf_ptb'],
                      tag=f"{pre}.tr{tt}.{wc}")

                def fn(eng, ti=ti, wc=wc):
                    return eng.tensor_copy(mT[:, wc * 512 + ti * 128:wc * 512 + (ti + 1) * 128],
                                           ptb[:, :])
                sch.g('vector', fn, reads=['tf_ptb'], writes=[('mT', wc)], tag=f"{pre}.te{tt}.{wc}")

        for oc in range(NCH):
            ps, psk = (p0, 'tf_p0') if oc % 2 == 0 else (p1, 'tf_p1')

            def fn(eng, q=q, oc=oc, ps=ps):
                last = None
                for wc in range(16):
                    last = eng.matmul(ps[:, :], wout[:, wc * 1024 + oc * 128:wc * 1024 + (oc + 1) * 128],
                                      mT[:, wc * 512:(wc + 1) * 512],
                                      start=(wc == 0), stop=(wc == 15))
                return last
            sch.g('tensor', fn, reads=['ti_wout'] + [('mT', wc) for wc in range(16)],
                  writes=[psk], tag=f"{pre}.out{q}.{oc}")

            out_store(sch, q, oc, ps, psk)



def emit_attn0_v2(sch, b, wload_chunk, pre="a0"):
    xn = b['xnbf']
    kT, v, qt, sc, at = b['kT'], b['v'], b['qt'], b['sc'], b['at']
    PS = b['PS']
    wc = b['wc']
    masks = b['masks']

    def fn(eng):
        return eng.activation(out=b['lng'][:, :], in_=b['gcols'][:, :], func=AF.Ln)
    sch.g('scalar', fn, reads=['gcols'], writes=['lng'], tag=f"{pre}.lng")

    for tt in range(4):
        for k in range(NCH):
            wload_chunk(sch, 'WkT', k, k % 2)

            def fn(eng, tt=tt, k=k):
                last = None
                for c in range(NCH):
                    last = eng.matmul(PS[c][:, :], wc[:, (k % 2) * 1024 + c * 128:(k % 2) * 1024 + (c + 1) * 128],
                                      xn(k)[:, tt * 512:(tt + 1) * 512],
                                      start=(k == 0), stop=False)
                return last
            sch.g('tensor', fn, reads=[('wc', k % 2)] + [('xnbf', k)],
                  writes=[('P', c) for c in range(NCH)], tag=f"{pre}.kp{tt}.{k}")

        def fn(eng, tt=tt):
            last = None
            for c in range(NCH):
                last = eng.matmul(PS[c][:, :], b['bk_row'][0:1, c * 128:(c + 1) * 128],
                                  b['on512b'][0:1, :], start=False, stop=True)
            return last
        sch.g('tensor', fn, reads=['bk_row', 'on512b'], writes=[('P', c) for c in range(NCH)],
              tag=f"{pre}.kb{tt}")

        for c in range(NCH):
            def fn(eng, c=c, tt=tt):
                return eng.copy(out=kT[:, c * T + tt * 512:c * T + (tt + 1) * 512], in_=PS[c][:, :])
            sch.g('scalar', fn, reads=[('P', c)], writes=[('kT', c)], tag=f"{pre}.ke{tt}.{c}")

    for grp in range(4):
        for k in range(NCH):
            wload_chunk(sch, 'WvT', k, k % 2)

            def fn(eng, grp=grp, k=k):
                last = None
                for j in range(4):
                    jt = grp * 4 + j
                    for h in range(2):
                        last = eng.matmul(PS[j * 2 + h][:, :], xn(k)[:, jt * 128:(jt + 1) * 128],
                                          wc[:, (k % 2) * 1024 + h * 512:(k % 2) * 1024 + (h + 1) * 512],
                                          start=(k == 0), stop=False)
                return last
            sch.g('tensor', fn, reads=[('wc', k % 2), ('xnbf', k)],
                  writes=[('P', i) for i in range(8)], tag=f"{pre}.vp{grp}.{k}")

        def fn(eng, grp=grp):
            last = None
            for j in range(4):
                for h in range(2):
                    last = eng.matmul(PS[j * 2 + h][:, :], b['onrb'][0:1, :],
                                      b['bv_row'][0:1, h * 512:(h + 1) * 512], start=False, stop=True)
            return last
        sch.g('tensor', fn, reads=['onrb', 'bv_row'], writes=[('P', i) for i in range(8)],
              tag=f"{pre}.vb{grp}")

        for j in range(4):
            jt = grp * 4 + j
            for h in range(2):
                def fn(eng, jt=jt, j=j, h=h):
                    return eng.copy(out=v[:, jt * 1024 + h * 512:jt * 1024 + (h + 1) * 512],
                                    in_=PS[j * 2 + h][:, :])
                sch.g('scalar', fn, reads=[('P', j * 2 + h)], writes=[('v', jt)],
                      tag=f"{pre}.ve{jt}.{h}")

    for t in range(4):
        nbj = 4 * t + 4
        for k in range(NCH):
            wload_chunk(sch, 'WqT', k, k % 2)

            def fn(eng, t=t, k=k):
                last = None
                for c in range(NCH):
                    last = eng.matmul(PS[c][:, :], wc[:, (k % 2) * 1024 + c * 128:(k % 2) * 1024 + (c + 1) * 128],
                                      xn(k)[:, t * 512:(t + 1) * 512],
                                      start=(k == 0), stop=False)
                return last
            sch.g('tensor', fn, reads=[('wc', k % 2), ('xnbf', k)],
                  writes=[('P', c) for c in range(NCH)], tag=f"{pre}.qp{t}.{k}")

        def fn(eng, t=t):
            last = None
            for c in range(NCH):
                last = eng.matmul(PS[c][:, :], b['bq_row'][0:1, c * 128:(c + 1) * 128],
                                  b['on512b'][0:1, :], start=False, stop=True)
            return last
        sch.g('tensor', fn, reads=['bq_row', 'on512b'], writes=[('P', c) for c in range(NCH)],
              tag=f"{pre}.qb{t}")

        for c in range(NCH):
            def fn(eng, c=c):
                return eng.copy(out=qt[:, c * 512:(c + 1) * 512], in_=PS[c][:, :])
            sch.g('scalar', fn, reads=[('P', c)], writes=[('qt', c)], tag=f"{pre}.qe{t}.{c}")

        for c in range(NCH):
            for w0 in range(0, nbj, 4):
                wave = range(w0, min(w0 + 4, nbj))
                for bj in wave:
                    sA, sB = PS[bj % 3], PS[3 + bj % 3]
                    kA = ('P', bj % 3)
                    kB = ('P', 3 + bj % 3)
                    sl = bj % 4

                    def fn(eng, c=c, bj=bj, sA=sA, sB=sB):
                        eng.matmul(sA[:, :], kT[0:64, c * T + bj * 128:c * T + (bj + 1) * 128],
                                   qt[0:64, c * 512:(c + 1) * 512], start=True, stop=True,
                                   tile_position=(0, 0))
                        return eng.matmul(sB[:, :], kT[64:128, c * T + bj * 128:c * T + (bj + 1) * 128],
                                          qt[64:128, c * 512:(c + 1) * 512], start=True, stop=True,
                                          tile_position=(64, 0))
                    sch.g('tensor', fn, reads=[('kT', c), ('qt', c)], writes=[kA, kB],
                          tag=f"{pre}.sc{t}.{c}.{bj}")

                    r = bj - 4 * t
                    if r >= 0:
                        def fn(eng, sl=sl, r=r, sA=sA, sB=sB):
                            eng.tensor_mul(sc[:, (2 * sl) * 512:(2 * sl + 1) * 512], sA[:, :],
                                           masks[:, r * 512:(r + 1) * 512])
                            return eng.tensor_mul(sc[:, (2 * sl + 1) * 512:(2 * sl + 2) * 512], sB[:, :],
                                                  masks[:, r * 512:(r + 1) * 512])
                        sch.g('vector', fn, reads=[kA, kB, 'masks'],
                              writes=[('scb', sl)], tag=f"{pre}.se{t}.{c}.{bj}")
                    else:
                        def fn(eng, sl=sl, sA=sA, sB=sB):
                            eng.copy(out=sc[:, (2 * sl) * 512:(2 * sl + 1) * 512], in_=sA[:, :])
                            return eng.copy(out=sc[:, (2 * sl + 1) * 512:(2 * sl + 2) * 512], in_=sB[:, :])
                        sch.g('scalar', fn, reads=[kA, kB], writes=[('scb', sl)],
                              tag=f"{pre}.se{t}.{c}.{bj}")

                for bj in wave:
                    sl = bj % 4

                    def fn(eng, c=c, bj=bj, sl=sl, nbj=nbj):
                        eng.matmul(PS[6][0:64, :], v[:, bj * 1024 + c * 128:bj * 1024 + c * 128 + 64],
                                   sc[:, (2 * sl) * 512:(2 * sl + 1) * 512],
                                   start=(bj == 0), stop=(bj == nbj - 1), tile_position=(0, 0))
                        return eng.matmul(PS[6][64:128, :], v[:, bj * 1024 + c * 128 + 64:bj * 1024 + (c + 1) * 128],
                                          sc[:, (2 * sl + 1) * 512:(2 * sl + 2) * 512],
                                          start=(bj == 0), stop=(bj == nbj - 1), tile_position=(0, 64))
                    sch.g('tensor', fn, reads=[('scb', sl), ('v', bj)], writes=[('P', 6)],
                          tag=f"{pre}.av{t}.{c}.{bj}")

            def fn(eng, c=c, t=t):
                return eng.tensor_scalar_mul(b['dkb'][:, 0:1], b['lng'][:, c:c + 1],
                                             cc(b, float(512 * t)))
            sch.g('vector', fn, reads=['lng', 'consts'], writes=['dkb'], tag=f"{pre}.dkb{t}.{c}")

            def fn(eng, c=c, t=t):
                return eng.activation(out=b['dk'][:, :], in_=b['iota512'][:, :],
                                      func=AF.Exp, scale=b['lng'][:, c:c + 1],
                                      bias=b['dkb'][:, 0:1])
            sch.g('scalar', fn, reads=['lng', 'iota512', 'dkb'], writes=['dk'], tag=f"{pre}.dk{t}.{c}")

            def fn(eng, c=c):
                return eng.tensor_mul(at[:, c * 512:(c + 1) * 512], PS[6][:, :], b['dk'][:, :])
            sch.g('vector', fn, reads=[('P', 6), 'dk'], writes=[('at', c)], tag=f"{pre}.ae{t}.{c}")

        for k in range(NCH):
            wload_chunk(sch, 'WoT', k, k % 2)

            def fn(eng, t=t, k=k):
                last = None
                for o in range(NCH):
                    last = eng.matmul(PS[o][:, :], wc[:, (k % 2) * 1024 + o * 128:(k % 2) * 1024 + (o + 1) * 128],
                                      at[:, k * 512:(k + 1) * 512],
                                      start=(k == 0), stop=(k == NCH - 1))
                return last
            sch.g('tensor', fn, reads=[('wc', k % 2), ('at', k)],
                  writes=[('P', o) for o in range(NCH)], tag=f"{pre}.wo{t}.{k}")

        for o in range(NCH):
            def fn(eng, o=o, t=t):
                Rs = b['R'](o)[:, t * 512:(t + 1) * 512]
                return eng.scalar_tensor_tensor(out=Rs, in0=PS[o][:, :],
                                                scalar=b['bo_cols'][:, o:o + 1],
                                                in1=Rs, op0=ALU.add, op1=ALU.add)
            sch.g('vector', fn, reads=[('P', o), 'bo_cols', ('R', o)], writes=[('R', o)],
                  tag=f"{pre}.x1.{o}.{t}")



def emit_shared_v2(sch, b, ld_sw, ld_sw3c, pre="sh"):
    xn = b['xnbf']
    sw12, gg, sa, wc3 = b['sw12'], b['gg'], b['sa'], b['wc3']
    PS = b['PS']

    for th in range(2):
        to = th * 1024
        for ih in range(2):
            ld_sw(sch, 'sw1T', ih, 0)
            ld_sw(sch, 'sw2T', ih, 1)
            for ic8 in range(8):
                ic = ih * 8 + ic8
                for tt in range(2):
                    base = (ic8 % 2) * 4
                    pa, pb = PS[base + tt * 2], PS[base + tt * 2 + 1]
                    ka, kb = ('P', base + tt * 2), ('P', base + tt * 2 + 1)

                    def fn(eng, ic8=ic8, tt=tt, to=to, pa=pa):
                        last = None
                        for k in range(NCH):
                            last = eng.matmul(pa[:, :], sw12[:, k * 1024 + ic8 * 128:k * 1024 + (ic8 + 1) * 128],
                                              xn(k)[:, to + tt * 512:to + (tt + 1) * 512],
                                              start=(k == 0), stop=(k == NCH - 1))
                        return last
                    sch.g('tensor', fn, reads=[('sw12', 0)] + [('xnbf', k) for k in range(NCH)],
                          writes=[ka], tag=f"{pre}.ua{th}.{ic}.{tt}")

                    def fn(eng, tt=tt, pa=pa):
                        return eng.activation(out=sa[:, tt * 512:(tt + 1) * 512], in_=pa[:, :], func=AF.Silu)
                    sch.g('scalar', fn, reads=[ka], writes=[('sa', tt)], tag=f"{pre}.uae{th}.{ic}.{tt}")

                    def fn(eng, ic8=ic8, tt=tt, to=to, pb=pb):
                        last = None
                        for k in range(NCH):
                            last = eng.matmul(pb[:, :], sw12[:, 8192 + k * 1024 + ic8 * 128:8192 + k * 1024 + (ic8 + 1) * 128],
                                              xn(k)[:, to + tt * 512:to + (tt + 1) * 512],
                                              start=(k == 0), stop=(k == NCH - 1))
                        return last
                    sch.g('tensor', fn, reads=[('sw12', 1)] + [('xnbf', k) for k in range(NCH)],
                          writes=[kb], tag=f"{pre}.ub{th}.{ic}.{tt}")

                    def fn(eng, ic=ic, tt=tt, pb=pb):
                        return eng.tensor_mul(gg[:, ic * 1024 + tt * 512:ic * 1024 + (tt + 1) * 512],
                                              pb[:, :], sa[:, tt * 512:(tt + 1) * 512])
                    sch.g('vector', fn, reads=[kb, ('sa', tt)], writes=[('gg', ic)],
                          tag=f"{pre}.gg{th}.{ic}.{tt}")

        for tt in range(2):
            for wcI in range(16):
                ld_sw3c(sch, wcI, wcI % 4)

                def fn(eng, wcI=wcI, tt=tt):
                    last = None
                    for oc in range(NCH):
                        last = eng.matmul(PS[oc][:, :],
                                          wc3[:, (wcI % 4) * 1024 + oc * 128:(wcI % 4) * 1024 + (oc + 1) * 128],
                                          gg[:, wcI * 1024 + tt * 512:wcI * 1024 + (tt + 1) * 512],
                                          start=(wcI == 0), stop=(wcI == 15))
                    return last
                sch.g('tensor', fn, reads=[('wc3', wcI % 4), ('gg', wcI)],
                      writes=[('P', oc) for oc in range(NCH)], tag=f"{pre}.z{th}.{tt}.{wcI}")

            for oc in range(NCH):
                def fn(eng, oc=oc, th=th, tt=tt):
                    Rs = b['R'](oc)[:, th * 1024 + tt * 512:th * 1024 + (tt + 1) * 512]
                    return eng.tensor_add(Rs, Rs, PS[oc][:, :])
                sch.g('vector', fn, reads=[('P', oc), ('R', oc)], writes=[('R', oc)],
                      tag=f"{pre}.ze{th}.{tt}.{oc}")



def emit_l1attn_v2(sch, b, pre="l1"):
    ND = T - 1
    xn = b['xnbf']
    df = b['df']
    wsq = b['wsq']
    PS = b['PS']
    ptb = b['l1_ptb']
    l1c = b['l1c']
    rwb, rwc = b['rwb'], b['rwc']
    onc, onr = b['onc'], b['onr']
    pd, pV, pB, pc = PS[5], PS[6], PS[7], PS[3]
    kpd, kpV, kpB, kpc = ('P', 5), ('P', 6), ('P', 7), ('P', 3)
    rslots = [(PS[0], 0, ('P', 0)), (PS[0], 32, ('P', 0)), (PS[0], 64, ('P', 0)),
              (PS[1], 0, ('P', 1))]

    for gc in range(4):
        def fn(eng, gc=gc):
            return eng.activation(out=wsq[:, 0:1024], in_=b['Gbf'][:, gc * 1024:(gc + 1) * 1024],
                                  func=AF.Square)
        sch.g('scalar', fn, reads=['Gbf'], writes=['l1sq'], tag=f"{pre}.gsq{gc}")

        def fn(eng, gc=gc):
            return eng.reduce_sum(l1c[:, 48 + gc:49 + gc], wsq[:, 0:1024], axis=AX.X)
        sch.g('vector', fn, reads=['l1sq'], writes=[('gn', gc)], tag=f"{pre}.gss{gc}")

    def fn(eng):
        return eng.tensor_scalar_max(l1c[:, 48:52], l1c[:, 48:52], cc(b, 1e-16))
    sch.g('vector', fn, reads=[('gn', g) for g in range(4)] + ['consts'], writes=['gnc'], tag=f"{pre}.gcl")

    def fn(eng):
        return eng.activation(out=l1c[:, 52:56], in_=l1c[:, 48:52], func=AF.Ln)
    sch.g('scalar', fn, reads=['gnc'], writes=['gnl'], tag=f"{pre}.gln")

    def fn(eng):
        return eng.activation(out=l1c[:, 48:52], in_=l1c[:, 52:56], func=AF.Exp, scale=cc(b, -0.5))
    sch.g('scalar', fn, reads=['gnl', 'consts'], writes=['gni'], tag=f"{pre}.gexp")

    for gc in range(4):
        def fn(eng, gc=gc):
            return eng.transpose(pV[0:1, 0:128], l1c[:, 48 + gc:49 + gc], b['id128'][:, :])
        sch.g('tensor', fn, reads=['gni', 'id128'], writes=[kpV], tag=f"{pre}.gtr{gc}")

        def fn(eng, gc=gc):
            return eng.tensor_copy(rwc[0:1, gc * 128:(gc + 1) * 128], pV[0:1, 0:128])
        sch.g('vector', fn, reads=[kpV], writes=['gnr'], tag=f"{pre}.gte{gc}")

    def fn(eng):
        return eng.matmul(pB[:, :], onr[0:1, :], rwc[0:1, 0:512], start=True, stop=True)
    sch.g('tensor', fn, reads=['gnr', 'ones'], writes=[kpB], tag=f"{pre}.gbc")

    def fn(eng):
        return eng.copy(out=b['invgn'][:, :], in_=pB[:, :])
    sch.g('scalar', fn, reads=[kpB], writes=['invgn'], tag=f"{pre}.gbce")

    for c in range(NCH):
        def fn(eng, c=c):
            eng.tensor_sub(df(c)[:, 0:ND], xn(c)[:, 1:T], xn(c)[:, 0:ND])
            return eng.memset(df(c)[:, ND:T], 0.0)
        sch.g('vector', fn, reads=[('xnbf', c)], writes=[('df', c)], tag=f"{pre}.df{c}")

    for c in range(NCH):
        def fn(eng, c=c):
            return eng.activation(out=wsq[:, :], in_=df(c)[:, :], func=AF.Square)
        sch.g('scalar', fn, reads=[('df', c), 'gnr'], writes=['l1sq'], tag=f"{pre}.sq{c}")

        def fn(eng, c=c):
            last = None
            for st in range(16):
                last = eng.matmul(pc[:, st:st + 1], wsq[:, st * 128:(st + 1) * 128], onc[:, :],
                                  start=(c == 0), stop=(c == NCH - 1))
            return last
        sch.g('tensor', fn, reads=['l1sq', 'ones'], writes=[kpc], tag=f"{pre}.cmm{c}")

        def fn(eng, c=c):
            last = None
            for t4 in range(4):
                ps, r, _ = rslots[t4]
                last = eng.matmul(ps[r:r + 1, :], onc[:, :], wsq[:, t4 * 512:(t4 + 1) * 512],
                                  start=(c == 0), stop=(c == NCH - 1))
            return last
        sch.g('tensor', fn, reads=['l1sq', 'ones'], writes=[('P', 0), ('P', 1)], tag=f"{pre}.rmm{c}")

    def fn(eng):
        return eng.tensor_copy(l1c[:, 0:16], pc[:, 0:16])
    sch.g('vector', fn, reads=[kpc], writes=['scol'], tag=f"{pre}.cev")

    def fn(eng):
        last = None
        for t4 in range(4):
            ps, r, _ = rslots[t4]
            last = eng.copy(out=rwb[0:1, t4 * 512:(t4 + 1) * 512], in_=ps[r:r + 1, :])
        return last
    sch.g('scalar', fn, reads=[('P', 0), ('P', 1)], writes=['rwb'], tag=f"{pre}.rev")

    def fn(eng):
        return eng.tensor_scalar_max(l1c[:, 0:16], l1c[:, 0:16], cc(b, 1e-12))
    sch.g('vector', fn, reads=['scol', 'consts'], writes=['scol'], tag=f"{pre}.ccl")

    def fn(eng):
        return eng.activation(out=l1c[:, 16:32], in_=l1c[:, 0:16], func=AF.Ln)
    sch.g('scalar', fn, reads=['scol'], writes=['lnc'], tag=f"{pre}.cln")

    def fn(eng):
        eng.activation(out=l1c[:, 0:16], in_=l1c[:, 16:32], func=AF.Exp, scale=cc(b, -0.5))
        return eng.activation(out=l1c[:, 32:48], in_=l1c[:, 16:32], func=AF.Exp, scale=cc(b, 0.5))
    sch.g('scalar', fn, reads=['lnc', 'consts'], writes=['invmc', 'magc'], tag=f"{pre}.cexp")

    def fn(eng):
        return eng.tensor_mul(l1c[:, 16:32], l1c[:, 0:16], l1c[:, 32:48])
    sch.g('vector', fn, reads=['invmc', 'magc'], writes=['anc'], tag=f"{pre}.can")

    def fn(eng):
        return eng.tensor_scalar_max(l1c[:, 16:32], l1c[:, 16:32], cc(b, 1e-8))
    sch.g('vector', fn, reads=['anc', 'consts'], writes=['anc'], tag=f"{pre}.crc1")

    def fn(eng):
        return eng.reciprocal(l1c[:, 16:32], l1c[:, 16:32])
    sch.g('vector', fn, reads=['anc'], writes=['anc'], tag=f"{pre}.crc2")

    def fn(eng):
        return eng.scalar_tensor_tensor(l1c[:, 16:32], l1c[:, 16:32], cc(b, 2.0), l1c[:, 0:16],
                                        op0=ALU.mult, op1=ALU.mult)
    sch.g('vector', fn, reads=['anc', 'invmc', 'consts'], writes=['rsc'], tag=f"{pre}.crs")

    def fn(eng):
        return eng.activation(out=l1c[:, 32:48], in_=l1c[:, 32:48], func=AF.Tanh, scale=cc(b, 2.0))
    sch.g('scalar', fn, reads=['magc', 'consts'], writes=['tnc'], tag=f"{pre}.ctn")

    def fn(eng):
        return eng.tensor_scalar_max(rwb[0:1, :], rwb[0:1, :], cc(b, 1e-12, 1))
    sch.g('vector', fn, reads=['rwb', 'consts'], writes=['rwb'], tag=f"{pre}.rcl")

    def fn(eng):
        return eng.activation(out=rwb[0:1, :], in_=rwb[0:1, :], func=AF.Ln)
    sch.g('scalar', fn, reads=['rwb'], writes=['rwb'], tag=f"{pre}.rln")

    def fn(eng):
        return eng.activation(out=rwc[0:1, :], in_=rwb[0:1, :], func=AF.Exp, scale=cc(b, 0.5, 1))
    sch.g('scalar', fn, reads=['rwb', 'consts', 'gnr'], writes=['rwc'], tag=f"{pre}.rmag")

    def fn(eng):
        return eng.activation(out=rwc[0:1, :], in_=rwc[0:1, :], func=AF.Tanh, scale=cc(b, 2.0, 1))
    sch.g('scalar', fn, reads=['rwc', 'consts'], writes=['rwc'], tag=f"{pre}.rtn")

    def fn(eng):
        return eng.activation(out=rwb[0:1, :], in_=rwb[0:1, :], func=AF.Exp, scale=cc(b, -0.5, 1))
    sch.g('scalar', fn, reads=['rwb', 'consts'], writes=['rwb'], tag=f"{pre}.rinv")

    def fn(eng):
        return eng.scalar_tensor_tensor(rwb[0:1, :], rwc[0:1, :], cc(b, 0.6, 1), rwb[0:1, :],
                                        op0=ALU.mult, op1=ALU.mult)
    sch.g('vector', fn, reads=['rwb', 'rwc', 'consts'], writes=['rwb'], tag=f"{pre}.rap")

    for st in range(16):
        def fn(eng, st=st):
            last = None
            for k in range(NCH):
                last = eng.matmul(pd[:, :], df(k)[:, st * 128:(st + 1) * 128],
                                  b['GTbf'][:, k * 512:(k + 1) * 512],
                                  start=(k == 0), stop=(k == NCH - 1))
            return last
        sch.g('tensor', fn, reads=['GTbf'] + [('df', k) for k in range(NCH)],
              writes=[kpd], tag=f"{pre}.dot{st}")

        def fn(eng, st=st):
            return eng.scalar_tensor_tensor(b['sims'][:, :], pd[:, :], l1c[:, 16 + st:17 + st],
                                            b['invgn'][:, :], op0=ALU.mult, op1=ALU.mult)
        sch.g('vector', fn, reads=[kpd, 'rsc', 'invgn'], writes=['sims'], tag=f"{pre}.sim{st}")

        def fn(eng, st=st):
            return eng.reduce_max(b['smx'][:, 0:1], b['sims'][:, :], axis=AX.X, negate=True)
        sch.g('vector', fn, reads=['sims'], writes=['smx'], tag=f"{pre}.smx{st}")

        def fn(eng, st=st):
            return eng.activation(out=b['sims'][:, :], in_=b['sims'][:, :], func=AF.Exp,
                                  bias=b['smx'][:, 0:1])
        sch.g('scalar', fn, reads=['sims', 'smx'], writes=['sims'], tag=f"{pre}.sex{st}")

        def fn(eng, st=st):
            return eng.reduce_sum(b['smx'][:, 1:2], b['sims'][:, :], axis=AX.X)
        sch.g('vector', fn, reads=['sims'], writes=['smx2'], tag=f"{pre}.ssm{st}")

        def fn(eng, st=st):
            return eng.reciprocal(b['smx'][:, 2:3], b['smx'][:, 1:2])
        sch.g('vector', fn, reads=['smx2'], writes=['smx3'], tag=f"{pre}.src{st}")

        def fn(eng, st=st):
            return eng.scalar_tensor_tensor(b['smx'][:, 3:4], b['smx'][:, 2:3], cc(b, 0.4),
                                            l1c[:, 32 + st:33 + st], op0=ALU.mult, op1=ALU.mult)
        sch.g('vector', fn, reads=['smx3', 'tnc', 'consts'], writes=['smx4'], tag=f"{pre}.snr{st}")

        def fn(eng, st=st):
            return eng.tensor_scalar(b['gwt'][:, :], b['sims'][:, :], b['smx'][:, 3:4], None,
                                     op0=ALU.mult)
        sch.g('vector', fn, reads=['sims', 'smx4'], writes=['gwt'], tag=f"{pre}.gw{st}")

        for j in range(4):
            def fn(eng, st=st, j=j):
                return eng.transpose(ptb[:, :], b['gwt'][:, j * 128:(j + 1) * 128], b['idbf'][:, :])
            sch.g('tensor', fn, reads=['gwt', 'idbf'], writes=['l1_ptb'], tag=f"{pre}.gt{st}.{j}")

            def fn(eng, st=st, j=j):
                return eng.tensor_copy(b['gwT'][:, j * T + st * 128:j * T + (st + 1) * 128], ptb[:, :])
            sch.g('vector', fn, reads=['l1_ptb'], writes=[('gwT', j)], tag=f"{pre}.gte{st}.{j}")

    for st2 in range(4):
        def fn(eng, st2=st2):
            return eng.matmul(pB[:, :], onr[0:1, :], rwb[0:1, st2 * 512:(st2 + 1) * 512],
                              start=True, stop=True)
        sch.g('tensor', fn, reads=['rwb', 'ones'], writes=[kpB], tag=f"{pre}.abc{st2}")

        def fn(eng, st2=st2):
            return eng.copy(out=b['apb'][:, :], in_=pB[:, :])
        sch.g('scalar', fn, reads=[kpB], writes=['apb'], tag=f"{pre}.abce{st2}")

        for c in range(NCH):
            pu = pd if c % 2 == 0 else pV
            puk = kpd if c % 2 == 0 else kpV

            def fn(eng, st2=st2, c=c, pu=pu):
                last = None
                for gc in range(4):
                    last = eng.matmul(pu[:, :], b['Gbf'][:, gc * 1024 + c * 128:gc * 1024 + (c + 1) * 128],
                                      b['gwT'][:, gc * T + st2 * 512:gc * T + (st2 + 1) * 512],
                                      start=(gc == 0), stop=(gc == 3))
                return last
            sch.g('tensor', fn, reads=['Gbf'] + [('gwT', gc) for gc in range(4)],
                  writes=[puk], tag=f"{pre}.inf{st2}.{c}")

            def fn(eng, st2=st2, c=c):
                return eng.tensor_mul(df(c)[:, st2 * 512:(st2 + 1) * 512],
                                      df(c)[:, st2 * 512:(st2 + 1) * 512], b['apb'][:, :])
            sch.g('vector', fn, reads=[('df', c), 'apb'], writes=[('df', c)], tag=f"{pre}.ua{st2}.{c}")

            def fn(eng, st2=st2, c=c, pu=pu):
                return eng.tensor_add(df(c)[:, st2 * 512:(st2 + 1) * 512],
                                      df(c)[:, st2 * 512:(st2 + 1) * 512], pu[:, :])
            sch.g('vector', fn, reads=[('df', c), puk], writes=[('df', c)], tag=f"{pre}.ub{st2}.{c}")

    def fn(eng):
        return eng.tensor_scalar_mul(rwb[0:1, 8:T], rwc[0:1, 0:T - 8], cc(b, TAPC[0], 1))
    sch.g('vector', fn, reads=['rwc', 'consts'], writes=['rwb'], tag=f"{pre}.dtap0")
    for k in range(1, 8):
        def fn(eng, k=k):
            return eng.scalar_tensor_tensor(rwb[0:1, 8:T], rwc[0:1, k:T - 8 + k], cc(b, TAPC[k], 1),
                                            rwb[0:1, 8:T], op0=ALU.mult, op1=ALU.add)
        sch.g('vector', fn, reads=['rwc', 'rwb', 'consts'], writes=['rwb'], tag=f"{pre}.dtap{k}")

    def fn(eng):
        return eng.matmul(pV[0:1, 0:8], l1c[0:8, 32:33], b['Cmat'][0:8, 0:8], start=True, stop=True)
    sch.g('tensor', fn, reads=['tnc', 'Cmat'], writes=[kpV], tag=f"{pre}.dsm")

    def fn(eng):
        return eng.tensor_copy(rwb[0:1, 0:8], pV[0:1, 0:8])
    sch.g('vector', fn, reads=[kpV], writes=['rwb'], tag=f"{pre}.dsme")

    def fn(eng):
        return eng.tensor_scalar_max(rwb[0:1, :], rwb[0:1, :], cc(b, 1e-8, 1))
    sch.g('vector', fn, reads=['rwb', 'consts'], writes=['rwb'], tag=f"{pre}.dcl")

    def fn(eng):
        return eng.activation(out=rwb[0:1, :], in_=rwb[0:1, :], func=AF.Ln)
    sch.g('scalar', fn, reads=['rwb'], writes=['rwb'], tag=f"{pre}.dinv1")

    def fn(eng):
        return eng.activation(out=rwb[0:1, :], in_=rwb[0:1, :], func=AF.Exp, scale=cc(b, -1.0, 1))
    sch.g('scalar', fn, reads=['rwb', 'consts'], writes=['rwb'], tag=f"{pre}.dinv2")

    for st2 in range(4):
        def fn(eng, st2=st2):
            return eng.matmul(pB[:, :], onr[0:1, :], rwb[0:1, st2 * 512:(st2 + 1) * 512],
                              start=True, stop=True)
        sch.g('tensor', fn, reads=['rwb', 'ones'], writes=[kpB], tag=f"{pre}.ibc{st2}")

        def fn(eng, st2=st2):
            return eng.copy(out=b['ivb'][:, st2 * 512:(st2 + 1) * 512], in_=pB[:, :])
        sch.g('scalar', fn, reads=[kpB], writes=['ivb'], tag=f"{pre}.ibce{st2}")

    for c in range(NCH):
        acc = b['accb']

        def fn(eng, c=c):
            return eng.tensor_scalar_mul(acc[:, 8:T], df(c)[:, 0:T - 8], cc(b, TAPC[0]))
        sch.g('vector', fn, reads=[('df', c), 'consts'], writes=['acc'],
              tag=f"{pre}.t0.{c}")
        for k in range(1, 8):
            def fn(eng, c=c, k=k):
                return eng.scalar_tensor_tensor(acc[:, 8:T], df(c)[:, k:T - 8 + k], cc(b, TAPC[k]),
                                                acc[:, 8:T], op0=ALU.mult, op1=ALU.add)
            sch.g('vector', fn, reads=[('df', c), 'acc', 'consts'], writes=['acc'], tag=f"{pre}.t{k}.{c}")

        def fn(eng, c=c):
            return eng.transpose(ptb[0:8, 0:128], df(c)[:, 0:8], b['idbf'][:, :])
        sch.g('tensor', fn, reads=[('df', c), 'idbf'], writes=['l1_ptb'], tag=f"{pre}.ut{c}")

        def fn(eng, c=c):
            return eng.tensor_copy(b['Usm'][0:8, :], ptb[0:8, 0:128])
        sch.g('vector', fn, reads=['l1_ptb'], writes=['Usm'], tag=f"{pre}.ute{c}")

        def fn(eng, c=c):
            return eng.matmul(pV[0:128, 0:8], b['Usm'][0:8, :], b['Cmbf'][0:8, 0:8],
                              start=True, stop=True)
        sch.g('tensor', fn, reads=['Usm', 'Cmbf'], writes=[kpV], tag=f"{pre}.usm{c}")

        def fn(eng, c=c):
            return eng.tensor_copy(acc[:, 0:8], pV[0:128, 0:8])
        sch.g('vector', fn, reads=[kpV], writes=['acc'], tag=f"{pre}.usme{c}")

        for st2 in range(4):
            def fn(eng, c=c, st2=st2):
                return eng.tensor_mul(b['tjb'][:, :], acc[:, st2 * 512:(st2 + 1) * 512],
                                      b['ivb'][:, st2 * 512:(st2 + 1) * 512])
            sch.g('gpsimd', fn, reads=['acc', 'ivb'], writes=['tjb'], tag=f"{pre}.tj{c}.{st2}")

            def fn(eng, c=c, st2=st2):
                Rs = b['R'](c)[:, st2 * 512:(st2 + 1) * 512]
                return eng.tensor_add(Rs, Rs, b['tjb'][:, :])
            sch.g('gpsimd', fn, reads=['tjb', ('R', c)], writes=[('R', c)], tag=f"{pre}.tja{c}.{st2}")



def emit_tree_v2(sch, b, ld_wout_chunk, pre="tf"):
    xn = b['xnbf']
    win = b['ti_win']
    ac, dc, ind, mT = b['ti_ac'], b['ti_dc'], b['ti_in'], b['ti_mT']
    PS = b['PS']
    ptb = b['tf_ptb']
    wc3 = b['wc3']

    for q in range(4):
        for ti in range(4):
            tt = q * 4 + ti
            for ws in range(4):
                ps, psk = (PS[ws % 2], ('P', ws % 2))

                def fn(eng, tt=tt, ws=ws, ps=ps):
                    for k in range(NCH):
                        eng.matmul(ps[:, :], xn(k)[:, tt * 128:(tt + 1) * 128],
                                   win[:, k * T + ws * 512:k * T + (ws + 1) * 512],
                                   start=(k == 0), stop=False)
                    return eng.matmul(ps[:, :], b['onrb'][0:1, :],
                                      b['ti_binr'][0:1, ws * 512:(ws + 1) * 512],
                                      start=False, stop=True)
                sch.g('tensor', fn, reads=['ti_win', 'ti_binr', 'onrb'] + [('xnbf', k) for k in range(NCH)],
                      writes=[psk], tag=f"{pre}.lg{tt}.{ws}")

                def fn(eng, ti=ti, ws=ws, ps=ps):
                    return eng.activation(out=ac[:, ti * T + ws * 512:ti * T + (ws + 1) * 512],
                                          in_=ps[:, :], func=AF.Silu)
                sch.g('scalar', fn, reads=[psk], writes=[('ac', ti)], tag=f"{pre}.si{tt}.{ws}")

                def fn(eng, ti=ti, ws=ws, ps=ps, bb=b):
                    return eng.tensor_single_scalar(dc[:, ti * T + ws * 512:ti * T + (ws + 1) * 512],
                                                    ps[:, :], cc(bb, 0.0), op=ALU.is_gt)
                sch.g('vector', fn, reads=[psk, 'consts', ('ac', ti)], writes=[('dc', ti)],
                      tag=f"{pre}.dc{tt}.{ws}")

        for ti in range(4):
            tt = q * 4 + ti
            io = ti * T
            ind3 = ind[:, io:io + 2040].rearrange("p (e n) -> p e n", n=255)
            dc3 = dc[:, io:io + 2040].rearrange("p (e n) -> p e n", n=255)

            def fn(eng, ti=ti, io=io):
                return eng.memset(ind[:, io:io + T], 0.0)
            sch.g('gpsimd', fn, reads=[], writes=[('ind', ti)], tag=f"{pre}.iz{tt}")

            def fn(eng, ti=ti, io=io):
                return eng.memset(ind[:, io:io + 2040].rearrange("p (e n) -> p e n", n=255)[:, :, 0:1], 1.0)
            sch.g('gpsimd', fn, reads=[], writes=[('ind', ti)], tag=f"{pre}.iz2{tt}")

            for d in range(7):
                a0 = (1 << d) - 1
                a1 = (1 << (d + 1)) - 1
                njd = 1 << d
                par = ind3[:, :, a0:a1]
                decp = dc3[:, :, a0:a1]
                chpair = ind3[:, :, a1:a1 + 2 * njd].rearrange("p e (i two) -> p e i two", two=2)
                right = chpair[:, :, :, 1:2].rearrange("p e i two -> p e (i two)")
                left = chpair[:, :, :, 0:1].rearrange("p e i two -> p e (i two)")

                def fn(eng, par=par, decp=decp, right=right):
                    return eng.tensor_mul(right, par, decp)
                sch.g('gpsimd', fn, reads=[('ind', ti), ('dc', ti)], writes=[('ind', ti)],
                      tag=f"{pre}.wka{tt}.{d}")

                def fn(eng, par=par, right=right, left=left):
                    return eng.tensor_sub(left, par, right)
                sch.g('gpsimd', fn, reads=[('ind', ti)], writes=[('ind', ti)],
                      tag=f"{pre}.wkb{tt}.{d}")

            def fn(eng, ti=ti, io=io):
                return eng.tensor_mul(ac[:, io:io + T], ac[:, io:io + T], ind[:, io:io + T])
            sch.g('gpsimd', fn, reads=[('ac', ti), ('ind', ti)], writes=[('ac', ti)],
                  tag=f"{pre}.mk{tt}")

            for wcn in range(16):
                def fn(eng, ti=ti, io=io, wcn=wcn):
                    return eng.transpose(ptb[:, :], ac[:, io + wcn * 128:io + (wcn + 1) * 128],
                                         b['idbf'][:, :])
                sch.g('tensor', fn, reads=[('ac', ti), 'idbf'], writes=['tf_ptb'],
                      tag=f"{pre}.tr{tt}.{wcn}")

                def fn(eng, ti=ti, wcn=wcn):
                    return eng.copy(out=mT[:, wcn * 512 + ti * 128:wcn * 512 + (ti + 1) * 128],
                                    in_=ptb[:, :])
                sch.g('scalar', fn, reads=['tf_ptb'], writes=[('mT', wcn)], tag=f"{pre}.te{tt}.{wcn}")

        for oh in range(2):
            for wcn in range(16):
                ld_wout_chunk(sch, oh, wcn, wcn % 4)

                def fn(eng, q=q, oh=oh, wcn=wcn):
                    last = None
                    for o4 in range(4):
                        oc = oh * 4 + o4
                        last = eng.matmul(PS[o4][:, :],
                                          wc3[:, (wcn % 4) * 512 + o4 * 128:(wcn % 4) * 512 + (o4 + 1) * 128],
                                          mT[:, wcn * 512:(wcn + 1) * 512],
                                          start=(wcn == 0), stop=(wcn == 15))
                    return last
                sch.g('tensor', fn, reads=[('wc3', wcn % 4), ('mT', wcn)],
                      writes=[('P', o4) for o4 in range(4)], tag=f"{pre}.out{q}.{oh}.{wcn}")

            for o4 in range(4):
                oc = oh * 4 + o4

                def fn(eng, q=q, oc=oc, o4=o4):
                    Rs = b['R'](oc)[:, q * 512:(q + 1) * 512]
                    return eng.tensor_add(Rs, Rs, PS[o4][:, :])
                sch.g('vector', fn, reads=[('P', o4), ('R', oc)], writes=[('R', oc)],
                      tag=f"{pre}.oe{q}.{oc}")



def emit_moe_v2(sch, b, load_w, pre="mo"):
    xn = b['xnbf']
    p0, p1, p2, p3 = b['mo_p0'], b['mo_p1'], b['mo_p2'], b['mo_p3']
    plg, ptr = b['mo_lg'], b['mo_tr']
    lgT, lgt = b['lgT'], b['lgt']
    sc_t, m_t = b['rt_sc'], b['rt_m']
    c1, c2, c3 = b['rt_c1'], b['rt_c2'], b['rt_c3']
    scT, mT = b['scT'], b['mT']
    h, x3m, t1 = b['h'], b['x3m'], b['t1']
    w1, w2, w3 = b['w1'], b['w2'], b['w3']
    idn = b['id128']

    for tt in range(4):
        def fn(eng, tt=tt):
            last = None
            for k in range(NCH):
                last = eng.matmul(plg[0:8, :], b['gw'][:, k * 8:(k + 1) * 8],
                                  xn(k)[:, tt * 512:(tt + 1) * 512],
                                  start=(k == 0), stop=(k == NCH - 1))
            return last
        sch.g('tensor', fn, reads=['gw'] + [('xnbf', k) for k in range(NCH)],
              writes=['mo_lg'], tag=f"{pre}.lg{tt}")

        def fn(eng, tt=tt):
            return eng.copy(out=lgT[0:8, tt * 512:(tt + 1) * 512], in_=plg[0:8, :])
        sch.g('scalar', fn, reads=['mo_lg'], writes=['lgT'], tag=f"{pre}.lge{tt}")

    for i in range(16):
        def fn(eng, i=i):
            return eng.transpose(ptr[0:128, 0:8], lgT[0:8, i * 128:(i + 1) * 128], idn[0:8, 0:8])
        sch.g('tensor', fn, reads=['lgT', 'id128'], writes=['mo_tr'], tag=f"{pre}.tr{i}")

        def fn(eng, i=i):
            return eng.tensor_copy(lgt[:, i * 8:(i + 1) * 8], ptr[0:128, 0:8])
        sch.g('vector', fn, reads=['mo_tr'], writes=[('lgt', i)], tag=f"{pre}.tre{i}")

    for i in range(16):
        lg_i = lgt[:, i * 8:(i + 1) * 8]
        sc_i = sc_t[:, i * 8:(i + 1) * 8]
        m_i = m_t[:, i * 8:(i + 1) * 8]
        c1_i, c2_i, c3_i = c1[:, i:i + 1], c2[:, i:i + 1], c3[:, i:i + 1]

        sch.g('vector', lambda e, a=c1_i, x=lg_i: e.reduce_max(a, x, axis=AX.X, negate=True),
              reads=[('lgt', i)], writes=[('c1', i)], tag=f"{pre}.r1.{i}")
        sch.g('scalar', lambda e, o=sc_i, x=lg_i, a=c1_i: e.activation(out=o, in_=x, func=AF.Exp, bias=a),
              reads=[('lgt', i), ('c1', i)], writes=[('sc', i)], tag=f"{pre}.exp.{i}")
        sch.g('vector', lambda e, a=c2_i, x=sc_i: e.reduce_sum(a, x, axis=AX.X),
              reads=[('sc', i)], writes=[('c2', i)], tag=f"{pre}.sum.{i}")
        sch.g('vector', lambda e, a=c3_i, x=c2_i: e.reciprocal(a, x),
              reads=[('c2', i)], writes=[('c3', i)], tag=f"{pre}.rcp.{i}")
        sch.g('vector', lambda e, o=sc_i, a=c3_i: e.tensor_scalar(o, o, a, None, op0=ALU.mult),
              reads=[('sc', i), ('c3', i)], writes=[('sc', i)], tag=f"{pre}.nrm.{i}")
        sch.g('vector', lambda e, a=c1_i, x=sc_i: e.reduce_max(a, x, axis=AX.X, negate=True),
              reads=[('sc', i)], writes=[('c1', i)], tag=f"{pre}.m1.{i}")
        sch.g('vector', lambda e, o=m_i, x=sc_i, a=c1_i: e.tensor_scalar(o, x, a, None, op0=ALU.add),
              reads=[('sc', i), ('c1', i)], writes=[('m', i)], tag=f"{pre}.d1.{i}")
        sch.g('vector', lambda e, o=m_i, bb=b: e.tensor_single_scalar(o, o, cc(bb, 0.0), op=ALU.is_equal),
              reads=[('m', i), 'consts'], writes=[('m', i)], tag=f"{pre}.eq.{i}")
        sch.g('vector', lambda e, o=m_i, x=sc_i, bb=b: e.scalar_tensor_tensor(o, o, cc(bb, -1e30), x, op0=ALU.mult, op1=ALU.add),
              reads=[('m', i), ('sc', i), 'consts'], writes=[('m', i)], tag=f"{pre}.msk.{i}")
        sch.g('vector', lambda e, a=c2_i, x=m_i: e.reduce_max(a, x, axis=AX.X, negate=True),
              reads=[('m', i)], writes=[('c2', i)], tag=f"{pre}.m2.{i}")
        sch.g('vector', lambda e, o=m_i, x=sc_i, a=c2_i: e.tensor_scalar(o, x, a, None, op0=ALU.add),
              reads=[('sc', i), ('c2', i)], writes=[('m', i)], tag=f"{pre}.d2.{i}")
        sch.g('vector', lambda e, o=m_i, bb=b: e.tensor_single_scalar(o, o, cc(bb, 0.0), op=ALU.is_ge),
              reads=[('m', i), 'consts'], writes=[('m', i)], tag=f"{pre}.ge.{i}")

    for i in range(16):
        def fn(eng, i=i):
            return eng.transpose(ptr[0:8, 0:128], sc_t[:, i * 8:(i + 1) * 8], idn[:, :])
        sch.g('tensor', fn, reads=[('sc', i), 'id128'], writes=['mo_tr'], tag=f"{pre}.ts{i}")
        sch.g('scalar', lambda e, i=i: e.copy(out=scT[0:8, i * 128:(i + 1) * 128], in_=ptr[0:8, 0:128]),
              reads=['mo_tr'], writes=['scT'], tag=f"{pre}.tse{i}")

        def fn(eng, i=i):
            return eng.transpose(ptr[0:8, 0:128], m_t[:, i * 8:(i + 1) * 8], idn[:, :])
        sch.g('tensor', fn, reads=[('m', i), 'id128'], writes=['mo_tr'], tag=f"{pre}.tm{i}")
        sch.g('scalar', lambda e, i=i: e.copy(out=mT[0:8, i * 128:(i + 1) * 128], in_=ptr[0:8, 0:128]),
              reads=['mo_tr'], writes=['mT'], tag=f"{pre}.tme{i}")

    def prep_bcast(e_, src, srck, dst, dstkey):
        for tt in range(4):
            ps = p2 if tt % 2 == 0 else p3
            psk = 'mo_p2' if tt % 2 == 0 else 'mo_p3'

            def fn(eng, e_=e_, tt=tt, src=src, ps=ps):
                return eng.matmul(ps[:, :], b['sel8'][0:8, e_ * 128:(e_ + 1) * 128],
                                  src[0:8, tt * 512:(tt + 1) * 512], start=True, stop=True)
            sch.g('tensor', fn, reads=[srck, 'sel8'], writes=[psk], tag=f"{pre}.bc{dstkey}{e_}.{tt}")

            def fn(eng, tt=tt, dst=dst, ps=ps):
                return eng.copy(out=dst[:, tt * 512:(tt + 1) * 512], in_=ps[:, :])
            sch.g('scalar', fn, reads=[psk], writes=[dstkey], tag=f"{pre}.bce{dstkey}{e_}.{tt}")

    def prep_wbmb(e_):
        prep_bcast(e_, scT, 'scT', b['wb'](e_ % 2), ('wb', e_ % 2))

    def prep_xs(e_, qf):
        par = qf % 2
        wbp = b['wb'](e_ % 2)
        qo = qf * 512
        for k in range(NCH):
            def fn(eng, k=k, qo=qo, wbp=wbp, par=par):
                return eng.tensor_mul(b['xs'](par)[:, k * 512:(k + 1) * 512],
                                      xn(k)[:, qo:qo + 512], wbp[:, qo:qo + 512])
            sch.g('gpsimd', fn, reads=[('xnbf', k), ('wb', e_ % 2)], writes=[('xs', par, k)],
                  tag=f"{pre}.xs{e_}.{qf}.{k}")

    prep_wbmb(0)
    prep_xs(0, 0)

    for e_ in range(8):
        load_w(sch, e_)
        mbp = b['mb']
        prep_bcast(e_, mT, 'mT', mbp, 'mb')

        for qf in range(4):
            par = qf % 2
            xsp = b['xs'](par)
            if qf < 3:
                prep_xs(e_, qf + 1)
            elif e_ < 7:
                pass
            qo = qf * 512

            for hc in range(NCH):
                ps = p0 if hc % 2 == 0 else p1
                psk = 'mo_p0' if hc % 2 == 0 else 'mo_p1'

                def fn(eng, hc=hc, ps=ps, xsp=xsp):
                    last = None
                    for k in range(NCH):
                        last = eng.matmul(ps[:, :], w1[:, k * 1024 + hc * 128:k * 1024 + (hc + 1) * 128],
                                          xsp[:, k * 512:(k + 1) * 512],
                                          start=(k == 0), stop=(k == NCH - 1))
                    return last
                sch.g('tensor', fn, reads=['w1sb'] + [('xs', par, k) for k in range(NCH)],
                      writes=[psk], tag=f"{pre}.u1{e_}.{qf}.{hc}")

                def fn(eng, hc=hc, e_=e_, ps=ps):
                    return eng.activation(out=h[:, hc * 512:(hc + 1) * 512],
                                          in_=ps[:, :], func=AF.Silu,
                                          bias=b['b1c'][:, hc * 8 + e_:hc * 8 + e_ + 1])
                sch.g('scalar', fn, reads=[psk, 'b1c'], writes=[('h', hc)],
                      tag=f"{pre}.u1e{e_}.{qf}.{hc}")

            if qf == 1 and e_ < 7:
                prep_wbmb(e_ + 1)

            for oc in range(NCH):
                psa, pska = (p0, 'mo_p0') if oc % 2 == 0 else (p2, 'mo_p2')
                psb, pskb = (p1, 'mo_p1') if oc % 2 == 0 else (p3, 'mo_p3')

                def fn(eng, oc=oc, psa=psa, xsp=xsp):
                    last = None
                    for k in range(NCH):
                        last = eng.matmul(psa[:, :], w3[:, k * 1024 + oc * 128:k * 1024 + (oc + 1) * 128],
                                          xsp[:, k * 512:(k + 1) * 512],
                                          start=(k == 0), stop=(k == NCH - 1))
                    return last
                sch.g('tensor', fn, reads=['w3sb'] + [('xs', par, k) for k in range(NCH)],
                      writes=[pska], tag=f"{pre}.x3{e_}.{qf}.{oc}")

                def fn(eng, oc=oc, e_=e_, qo=qo, psa=psa, mbp=mbp):
                    return eng.scalar_tensor_tensor(
                        out=x3m[:, 0:512], in0=psa[:, :],
                        scalar=b['b3c'][:, oc * 8 + e_:oc * 8 + e_ + 1],
                        in1=mbp[:, qo:qo + 512],
                        op0=ALU.add, op1=ALU.mult)
                sch.g('vector', fn, reads=[pska, 'b3c', 'mb'], writes=[('x3m', oc % 2)],
                      tag=f"{pre}.x3e{e_}.{qf}.{oc}")

                def fn(eng, oc=oc, psb=psb):
                    last = None
                    for k in range(NCH):
                        last = eng.matmul(psb[:, :], w2[:, k * 1024 + oc * 128:k * 1024 + (oc + 1) * 128],
                                          h[:, k * 512:(k + 1) * 512],
                                          start=(k == 0), stop=(k == NCH - 1))
                    return last
                sch.g('tensor', fn, reads=['w2sb'] + [('h', k) for k in range(NCH)],
                      writes=[pskb], tag=f"{pre}.x2{e_}.{qf}.{oc}")

                def fn(eng, oc=oc, e_=e_, psb=psb):
                    return eng.scalar_tensor_tensor(
                        out=t1[:, 0:512], in0=psb[:, :],
                        scalar=b['b2c'][:, oc * 8 + e_:oc * 8 + e_ + 1],
                        in1=x3m[:, 0:512], op0=ALU.add, op1=ALU.mult)
                sch.g('vector', fn, reads=[pskb, 'b2c', ('x3m', oc % 2)], writes=[('t1', oc % 2)],
                      tag=f"{pre}.x2e{e_}.{qf}.{oc}")

                def fn(eng, oc=oc, qo=qo):
                    yap = b['y'](oc)[:, qo:qo + 512]
                    return eng.tensor_add(yap, yap, t1[:, 0:512])
                sch.g('vector', fn, reads=[('t1', oc % 2), ('y', oc)], writes=[('y', oc)],
                      tag=f"{pre}.acc{e_}.{qf}.{oc}")

        if e_ < 7:
            prep_xs(e_ + 1, 0)


from contextlib import ExitStack
import ml_dtypes

BF16NP = ml_dtypes.bfloat16


def build_fused_program():
    nc = new_nc()

    def din(name, shape, dt=F32):
        return nc.dram_tensor(name, shape, dt, kind="ExternalInput").ap()

    x0T = din("x0T", [D, T])
    WqT = din("WqT", [D, D]); WkT = din("WkT", [D, D]); WvT = din("WvT", [D, D]); WoT = din("WoT", [D, D])
    bq_row_d = din("bq_row", [1, D]); bk_row_d = din("bk_row", [1, D])
    bv_row_d = din("bv_row", [1, D])
    bo_cols_d = din("bo_cols", [128, 8])
    gcols_d = din("gcols", [128, 8])
    iota_d = din("iota", [128, T])
    masks_d = din("masks", [128, 4 * 512], BF16)
    gwT = din("gwT", [D, 8])
    w1T = din("w1T", [8 * D, D]); w2T = din("w2T", [8 * D, D]); w3T = din("w3T", [8 * D, D])
    b1c_d = din("b1c_d", [128, 64]); b2c_d = din("b2c_d", [128, 64]); b3c_d = din("b3c_d", [128, 64])
    sel8_d = din("sel8_d", [8, 1024], BF16)
    sw1T = din("sw1T", [D, T]); sw2T = din("sw2T", [D, T]); sw3T = din("sw3T", [T, D])
    G_d = din("G_d", [512, D]); GT_d = din("GT_d", [D, 512])
    Cm_d = din("Cm_d", [8, 8]); Cmb_d = din("Cmb_d", [8, 8], BF16)
    WinT = din("WinT", [D, T]); WoutT = din("WoutT", [T, D])
    binr_d = din("binr_d", [1, T])
    id_d = din("id_d", [128, 128]); idb_d = din("idb_d", [128, 128], BF16)
    oc_d = din("oc_d", [128, 1]); or_d = din("or_d", [1, 128]); orb_d = din("orb_d", [1, 128], BF16)
    on512b_d = din("on512b_d", [1, 512], BF16)
    cc_d = din("cc_d", [128, 64])
    outT = nc.dram_tensor("outT", [D, T], F32, kind="ExternalOutput").ap()
    import os
    FDBG = os.environ.get('FDBG', '') == '1'
    dbg_t = {}
    if FDBG:
        for nm in ('dbg_p1R', 'dbg_p2R', 'dbg_p3R', 'dbg_p4R', 'dbg_p6R'):
            dbg_t[nm] = nc.dram_tensor(nm, [D, T], F32, kind="ExternalOutput").ap()
        for nm in ('dbg_p1xn', 'dbg_p5xn'):
            dbg_t[nm] = nc.dram_tensor(nm, [D, T], BF16, kind="ExternalOutput").ap()

    with ExitStack() as stack:
        sch = Sched(nc, stack)
        g = {}

        def sbp(name, shape, dt):
            return stack.enter_context(nc.sbuf_tensor(name, shape, dt))
        R_t = sbp("g_R", [128, NCH * T], F32)
        xn_t = sbp("g_xn", [128, NCH * T], BF16)
        g['R'] = lambda c, _t=R_t: _t[:, c * T:(c + 1) * T]
        g['xnbf'] = lambda c, _t=xn_t: _t[:, c * T:(c + 1) * T]
        g['cc'] = sbp("g_cc", [128, 64], F32)
        g['onc'] = sbp("g_onc", [128, 1], F32)
        g['onr'] = sbp("g_onr", [1, 128], F32)
        g['onrb'] = sbp("g_onrb", [1, 128], BF16)
        g['on512b'] = sbp("g_on512b", [1, 512], BF16)
        g['id128'] = sbp("g_id128", [128, 128], F32)
        g['idbf'] = sbp("g_idbf", [128, 128], BF16)
        g['sel8'] = sbp("g_sel8", [8, 1024], BF16)
        g['Cmat'] = sbp("g_Cmat", [8, 8], F32)
        g['Cmbf'] = sbp("g_Cmbf", [8, 8], BF16)
        g['bo_cols'] = sbp("g_bo", [128, 8], F32)
        g['gcols'] = sbp("g_gc", [128, 8], F32)
        g['b1c'] = sbp("g_b1c", [128, 64], F32)
        g['b2c'] = sbp("g_b2c", [128, 64], F32)
        g['b3c'] = sbp("g_b3c", [128, 64], F32)
        g['lng'] = sbp("g_lng", [128, 8], F32)

        def fn(eng, sem):
            for (t_, d_) in ((g['cc'], cc_d), (g['onc'], oc_d), (g['onr'], or_d),
                             (g['id128'], id_d), (g['idbf'], idb_d), (g['sel8'], sel8_d),
                             (g['Cmat'], Cm_d), (g['Cmbf'], Cmb_d), (g['bo_cols'], bo_cols_d),
                             (g['gcols'], gcols_d), (g['b1c'], b1c_d), (g['b2c'], b2c_d),
                             (g['b3c'], b3c_d), (g['onrb'], orb_d), (g['on512b'], on512b_d)):
                eng.dma_start(out=t_[:, :], in_=d_[:, :]).then_inc(sem, 16)
        sch.dma('sync', 15, fn,
                writes=['consts', 'ones', 'id128', 'idbf', 'sel8', 'Cmat', 'Cmbf',
                        'bo_cols', 'gcols', 'b1c', 'b2c', 'b3c', 'onrb', 'on512b'], tag="gconsts")

        def dump_R(nm):
            if not FDBG:
                return
            for c in range(NCH):
                def fn(eng, sem, c=c, nm=nm):
                    eng.dma_start(out=dbg_t[nm][c * 128:(c + 1) * 128, :], in_=g['R'](c)).then_inc(sem, 16)
                sch.dma('sync', 1, fn, reads=[('R', c)], tag=f"{nm}.{c}")

        def dump_xn(nm):
            if not FDBG:
                return
            for c in range(NCH):
                def fn(eng, sem, c=c, nm=nm):
                    eng.dma_start(out=dbg_t[nm][c * 128:(c + 1) * 128, :], in_=g['xnbf'](c)).then_inc(sem, 16)
                sch.dma('sync', 1, fn, reads=[('xnbf', c)], tag=f"{nm}.{c}")

        def _phase1():
            with ExitStack() as ph:
                def sb(name, shape, dt):
                    return ph.enter_context(nc.sbuf_tensor(name, shape, dt))
                b = dict(g)
                b['ln_sq'] = sb("p1_sq", [128, 2 * T], F32)
                b['ln_rows'] = sb("p1_rows", [1, 4 * T], F32)
                b['ln_mbc'] = sb("p1_mbc", [128, T], F32)
                b['ln_rbc'] = sb("p1_rbc", [128, T], F32)
                for k in ('ln_psA', 'ln_psB', 'ln_psC', 'ln_psM', 'ln_psR'):
                    b[k] = ph.enter_context(nc.psum_tensor("p1_" + k, [128, 512], F32))

                for c in range(NCH):
                    def fn(eng, sem, c=c):
                        eng.dma_start(out=g['R'](c), in_=x0T[c * 128:(c + 1) * 128, :]).then_inc(sem, 16)
                    sch.dma('sync', 1, fn, writes=[('R', c)], tag=f"p1.ldx{c}")

                emit_ln(sch, b, 'R', None, 'onc', 'onr', pre="ln0", xn_bf=g['xnbf'])
        _phase1()
        dump_xn('dbg_p1xn')
        sch.barrier_all()

        def _phase2():
            with ExitStack() as ph:
                def sb(name, shape, dt):
                    return ph.enter_context(nc.sbuf_tensor(name, shape, dt))
                b = dict(g)
                b['kT'] = sb("p2_kT", [128, NCH * T], BF16)
                b['v'] = sb("p2_v", [128, 16 * 1024], BF16)
                b['qt'] = sb("p2_qt", [128, 8 * 512], BF16)
                b['sc'] = sb("p2_sc", [128, 8 * 512], BF16)
                b['at'] = sb("p2_at", [128, 8 * 512], BF16)
                b['wc'] = sb("p2_wc", [128, 2 * 1024], BF16)
                b['iota512'] = sb("p2_iota", [128, 512], F32)
                b['dk'] = sb("p2_dk", [128, 512], F32)
                b['dkb'] = sb("p2_dkb", [128, 1], F32)
                b['bq_row'] = sb("p2_bq", [1, D], BF16)
                b['bk_row'] = sb("p2_bk", [1, D], BF16)
                b['bv_row'] = sb("p2_bv", [1, D], BF16)
                b['masks'] = sb("p2_masks", [128, 4 * 512], BF16)
                b['PS'] = [ph.enter_context(nc.psum_tensor(f"p2_P{i}", [128, 512], F32))
                           for i in range(8)]

                def fn(eng, sem):
                    eng.dma_start(out=b['iota512'][:, :], in_=iota_d[:, 0:512]).then_inc(sem, 16)
                    eng.dma_start(out=b['masks'][:, :], in_=masks_d[:, :]).then_inc(sem, 16)
                sch.dma('sync', 2, fn, writes=['iota512', 'masks'], tag="p2consts")

                def fn(eng, sem):
                    eng.dma_start(out=b['bq_row'][:, :], in_=bq_row_d[:, :]).then_inc(sem, 16)
                    eng.dma_start(out=b['bk_row'][:, :], in_=bk_row_d[:, :]).then_inc(sem, 16)
                    eng.dma_start(out=b['bv_row'][:, :], in_=bv_row_d[:, :]).then_inc(sem, 16)
                sch.dma('gpsimd', 3, fn, writes=['bq_row', 'bk_row', 'bv_row'], tag="p2bias")

                wd = {'WqT': WqT, 'WkT': WkT, 'WvT': WvT, 'WoT': WoT}

                def wload_chunk(sch, which, k, slot):
                    def fn(eng, sem, which=which, k=k, slot=slot):
                        eng.dma_start(out=b['wc'][:, slot * 1024:(slot + 1) * 1024],
                                      in_=wd[which][k * 128:(k + 1) * 128, :]).then_inc(sem, 16)
                    sch.dma('gpsimd', 1, fn, writes=[('wc', slot)], tag=f"p2.ld{which}{k}")

                emit_attn0_v2(sch, b, wload_chunk)
        _phase2()
        dump_R('dbg_p2R')
        sch.barrier_all()

        def _phase3():
            with ExitStack() as ph:
                def sb(name, shape, dt):
                    return ph.enter_context(nc.sbuf_tensor(name, shape, dt))
                b = dict(g)
                b['y'] = g['R']
                b['gw'] = sb("p3_gw", [128, 64], BF16)
                b['lgT'] = sb("p3_lgT", [8, T], F32)
                b['lgt'] = sb("p3_lgt", [128, 16 * 8], F32)
                b['rt_sc'] = sb("p3_rt_sc", [128, 16 * 8], F32)
                b['rt_m'] = sb("p3_rt_m", [128, 16 * 8], F32)
                b['rt_c1'] = sb("p3_c1", [128, 16], F32)
                b['rt_c2'] = sb("p3_c2", [128, 16], F32)
                b['rt_c3'] = sb("p3_c3", [128, 16], F32)
                b['rt_tmp'] = sb("p3_tmp", [128, 8], F32)
                b['scT'] = sb("p3_scT", [8, T], BF16)
                b['mT'] = sb("p3_mT", [8, T], BF16)
                wb_t = sb("p3_wb", [128, 2 * T], BF16)
                b['mb'] = sb("p3_mb", [128, T], BF16)
                xs_t = sb("p3_xs", [128, 2 * 8 * 512], BF16)
                b['wb'] = lambda p, _t=wb_t: _t[:, p * T:(p + 1) * T]
                b['xs'] = lambda p, _t=xs_t: _t[:, p * 8 * 512:(p + 1) * 8 * 512]
                b['h'] = sb("p3_h", [128, 8 * 512], BF16)
                b['x3m'] = sb("p3_x3m", [128, 512], F32)
                b['t1'] = sb("p3_t1", [128, 512], F32)
                b['w1'] = sb("p3_w1", [128, 8 * 1024], BF16)
                b['w2'] = sb("p3_w2", [128, 8 * 1024], BF16)
                b['w3'] = sb("p3_w3", [128, 8 * 1024], BF16)
                b['mo_p0'] = ph.enter_context(nc.psum_tensor("p3_p0", [128, 512], F32))
                b['mo_p1'] = ph.enter_context(nc.psum_tensor("p3_p1", [128, 512], F32))
                b['mo_p2'] = ph.enter_context(nc.psum_tensor("p3_p2", [128, 512], F32))
                b['mo_p3'] = ph.enter_context(nc.psum_tensor("p3_p3", [128, 512], F32))
                b['mo_lg'] = ph.enter_context(nc.psum_tensor("p3_lg", [8, 512], F32))
                b['mo_tr'] = ph.enter_context(nc.psum_tensor("p3_tr", [128, 512], F32))

                def fn(eng, sem):
                    for k in range(NCH):
                        eng.dma_start(out=b['gw'][:, k * 8:(k + 1) * 8],
                                      in_=gwT[k * 128:(k + 1) * 128, :]).then_inc(sem, 16)
                sch.dma('gpsimd', NCH, fn, writes=['gw'], tag="ldgw")

                def load_w(sch, e_):
                    for (key, sbuf, dram) in (('w1sb', b['w1'], w1T), ('w3sb', b['w3'], w3T),
                                              ('w2sb', b['w2'], w2T)):
                        def fn(eng, sem, sbuf=sbuf, dram=dram, e_=e_):
                            for k in range(NCH):
                                eng.dma_start(out=sbuf[:, k * 1024:(k + 1) * 1024],
                                              in_=dram[e_ * 1024 + k * 128:e_ * 1024 + (k + 1) * 128, :]).then_inc(sem, 16)
                        sch.dma('gpsimd', NCH, fn, writes=[key], tag=f"ldw{key}{e_}")

                emit_moe_v2(sch, b, load_w)
        _phase3()
        dump_R('dbg_p3R')
        sch.barrier_all()

        def _phase4():
            with ExitStack() as ph:
                def sb(name, shape, dt):
                    return ph.enter_context(nc.sbuf_tensor(name, shape, dt))
                b = dict(g)
                b['sw12'] = sb("p4_sw12", [128, 2 * 8 * 1024], BF16)
                b['gg'] = sb("p4_gg", [128, 16 * 1024], BF16)
                b['sa'] = sb("p4_sa", [128, 1024], BF16)
                b['wc3'] = sb("p4_wc3", [128, 4 * 1024], BF16)
                b['PS'] = [ph.enter_context(nc.psum_tensor(f"p4_P{i}", [128, 512], F32))
                           for i in range(8)]
                swd = {'sw1T': sw1T, 'sw2T': sw2T}

                def ld_sw(sch, which, ihalf, dstslot):
                    def fn(eng, sem, which=which, ihalf=ihalf, dstslot=dstslot):
                        for k in range(NCH):
                            eng.dma_start(out=b['sw12'][:, dstslot * 8192 + k * 1024:dstslot * 8192 + (k + 1) * 1024],
                                          in_=swd[which][k * 128:(k + 1) * 128, ihalf * 1024:(ihalf + 1) * 1024]).then_inc(sem, 16)
                    sch.dma('gpsimd', NCH, fn, writes=[('sw12', dstslot)], tag=f"p4.ld{which}{ihalf}")

                def ld_sw3c(sch, wcI, slot):
                    def fn(eng, sem, wcI=wcI, slot=slot):
                        eng.dma_start(out=b['wc3'][:, slot * 1024:(slot + 1) * 1024],
                                      in_=sw3T[wcI * 128:(wcI + 1) * 128, :]).then_inc(sem, 16)
                    sch.dma('gpsimd', 1, fn, writes=[('wc3', slot)], tag=f"p4.ldsw3{wcI}")

                emit_shared_v2(sch, b, ld_sw, ld_sw3c)
        _phase4()
        dump_R('dbg_p4R')
        sch.barrier_all()

        def _phase5():
            with ExitStack() as ph:
                def sb(name, shape, dt):
                    return ph.enter_context(nc.sbuf_tensor(name, shape, dt))
                b = dict(g)
                b['ln_sq'] = sb("p5_sq", [128, 2 * T], F32)
                b['ln_rows'] = sb("p5_rows", [1, 4 * T], F32)
                b['ln_mbc'] = sb("p5_mbc", [128, T], F32)
                b['ln_rbc'] = sb("p5_rbc", [128, T], F32)
                for k in ('ln_psA', 'ln_psB', 'ln_psC', 'ln_psM', 'ln_psR'):
                    b[k] = ph.enter_context(nc.psum_tensor("p5_" + k, [128, 512], F32))
                emit_ln(sch, b, 'R', None, 'onc', 'onr', pre="ln1", xn_bf=g['xnbf'])
        _phase5()
        dump_xn('dbg_p5xn')
        sch.barrier_all()

        def _phase6():
            with ExitStack() as ph:
                def sb(name, shape, dt):
                    return ph.enter_context(nc.sbuf_tensor(name, shape, dt))
                b = dict(g)
                df_t = sb("p6_df", [128, NCH * T], BF16)
                b['df'] = lambda c, _t=df_t: _t[:, c * T:(c + 1) * T]
                b['wsq'] = sb("p6_wsq", [128, T], F32)
                b['rwb'] = sb("p6_rwb", [1, T], F32)
                b['rwc'] = sb("p6_rwc", [1, T], F32)
                b['GTbf'] = sb("p6_GTbf", [128, 8 * 512], BF16)
                b['Gbf'] = sb("p6_Gbf", [128, 4 * 1024], BF16)
                b['gwT'] = sb("p6_gwT", [128, 4 * T], BF16)
                b['gwt'] = sb("p6_gwt", [128, 512], BF16)
                b['sims'] = sb("p6_sims", [128, 512], BF16)
                b['smx'] = sb("p6_smx", [128, 4], F32)
                b['invgn'] = sb("p6_invgn", [128, 512], BF16)
                b['apb'] = sb("p6_apb", [128, 512], BF16)
                b['l1c'] = sb("p6_l1c", [128, 64], F32)
                b['Usm'] = sb("p6_Usm", [8, 128], BF16)
                b['ivb'] = sb("p6_ivb", [128, T], BF16)
                b['accb'] = sb("p6_accb", [128, T], BF16)
                b['tjb'] = sb("p6_tjb", [128, 512], BF16)
                PS = [None] * 8
                for i in (0, 1, 3, 5, 6, 7):
                    PS[i] = ph.enter_context(nc.psum_tensor(f"p6_P{i}", [128, 512], F32))
                b['PS'] = PS
                b['l1_ptb'] = ph.enter_context(nc.psum_tensor("p6_ptb", [128, 128], BF16))

                def fn(eng, sem):
                    for k in range(NCH):
                        eng.dma_start(out=b['GTbf'][:, k * 512:(k + 1) * 512],
                                      in_=GT_d[k * 128:(k + 1) * 128, :]).then_inc(sem, 16)
                    for gc in range(4):
                        eng.dma_start(out=b['Gbf'][:, gc * 1024:(gc + 1) * 1024],
                                      in_=G_d[gc * 128:(gc + 1) * 128, :]).then_inc(sem, 16)
                sch.dma('gpsimd', 12, fn, writes=['GTbf', 'Gbf'], tag="ldG")

                emit_l1attn_v2(sch, b)
        _phase6()
        dump_R('dbg_p6R')
        sch.barrier_all()

        def _phase7():
            with ExitStack() as ph:
                def sb(name, shape, dt):
                    return ph.enter_context(nc.sbuf_tensor(name, shape, dt))
                b = dict(g)
                b['ti_win'] = sb("p7_win", [128, NCH * T], BF16)
                b['ti_binr'] = sb("p7_binr", [1, T], BF16)
                b['ti_ac'] = sb("p7_ac", [128, 4 * T], BF16)
                b['ti_dc'] = sb("p7_dc", [128, 4 * T], BF16)
                b['ti_in'] = sb("p7_in", [128, 4 * T], BF16)
                b['ti_mT'] = sb("p7_mT", [128, 16 * 512], BF16)
                b['wc3'] = sb("p7_wc3", [128, 4 * 512], BF16)
                b['PS'] = [ph.enter_context(nc.psum_tensor(f"p7_P{i}", [128, 512], F32))
                           for i in range(4)]
                b['tf_ptb'] = ph.enter_context(nc.psum_tensor("p7_ptb", [128, 128], BF16))

                def fn(eng, sem):
                    eng.dma_start(out=b['ti_binr'][:, :], in_=binr_d[:, :]).then_inc(sem, 16)
                sch.dma('gpsimd', 1, fn, writes=['ti_binr'], tag="p7consts")

                def fn(eng, sem):
                    for k in range(NCH):
                        eng.dma_start(out=b['ti_win'][:, k * T:(k + 1) * T],
                                      in_=WinT[k * 128:(k + 1) * 128, :]).then_inc(sem, 16)
                sch.dma('gpsimd', NCH, fn, writes=['ti_win'], tag="p7ldw")

                def ld_wout_chunk(sch, oh, wcn, slot):
                    def fn(eng, sem, oh=oh, wcn=wcn, slot=slot):
                        eng.dma_start(out=b['wc3'][:, slot * 512:(slot + 1) * 512],
                                      in_=WoutT[wcn * 128:(wcn + 1) * 128, oh * 512:(oh + 1) * 512]).then_inc(sem, 16)
                    sch.dma('gpsimd', 1, fn, writes=[('wc3', slot)], tag=f"p7.ldwo{oh}.{wcn}")

                emit_tree_v2(sch, b, ld_wout_chunk)
        _phase7()
        sch.barrier_all()

        for c in range(NCH):
            def fn(eng, sem, c=c):
                eng.dma_start(out=outT[c * 128:(c + 1) * 128, :], in_=g['R'](c)).then_inc(sem, 16)
            sch.dma('sync', 1, fn, reads=[('R', c)], tag=f"stout{c}")
        sch.out_wait('sync')

        with nc.Block() as block:
            sch.emit(block)
    return nc



_PROG_CACHE = {}


def _prog(name, builder):
    if name not in _PROG_CACHE:
        _PROG_CACHE[name] = builder()
    return _PROG_CACHE[name]


def kernel(**inputs):
    NCORE = 8
    x0 = np.asarray(inputs['L0_attn_x'], dtype=np.float32)
    gamma = np.asarray(inputs['L0_attn_gamma'], dtype=np.float32)

    nc = _prog('fused', build_fused_program)

    ones_col = np.ones((128, 1), np.float32)
    ones_row = np.ones((1, 128), np.float32)
    id128 = np.eye(128, dtype=np.float32)
    idbf = np.eye(128, dtype=np.float32).astype(BF16NP)
    iota = np.broadcast_to(np.arange(T, dtype=np.float32), (128, T)).copy()
    masks = np.zeros((128, 4, 512), np.float32)
    for r in range(4):
        for j in range(128):
            masks[j, r, :] = (np.arange(512) >= j + 128 * r)
    masks = masks.reshape(128, 4 * 512).astype(BF16NP)
    sel8 = np.zeros((8, 1024), np.float32)
    for e in range(8):
        sel8[e, e * 128:(e + 1) * 128] = 1.0
    sel8 = sel8.astype(BF16NP)
    gcols = np.zeros((128, 8), np.float32)
    for c in range(8):
        for p in range(128):
            gcols[p, c] = gamma[2 * c + p // 64]
    ccarr = const_host_array_padded()
    WinT = np.zeros((1024, 2048), np.float32)
    WinT[:, :2040] = np.asarray(inputs['L1_ffn_W_in']).T
    WoutT = np.zeros((2048, 1024), np.float32)
    WoutT[:2040, :] = np.asarray(inputs['L1_ffn_W_out']).T
    binr = np.zeros((1, 2048), np.float32)
    binr[0, :2040] = np.asarray(inputs['L1_ffn_b_in'])
    G = np.asarray(inputs['L1_attn_guidance_trajectories'], dtype=np.float32)

    shared = {
        "WqT": np.ascontiguousarray(np.asarray(inputs['L0_attn_Wq']).T),
        "WkT": np.ascontiguousarray(np.asarray(inputs['L0_attn_Wk']).T),
        "WvT": np.ascontiguousarray(np.asarray(inputs['L0_attn_Wv']).T),
        "WoT": np.ascontiguousarray(np.asarray(inputs['L0_attn_Wo']).T),
        "bq_row": np.asarray(inputs['L0_attn_bq'], dtype=np.float32).reshape(1, D),
        "bk_row": np.asarray(inputs['L0_attn_bk'], dtype=np.float32).reshape(1, D),
        "bv_row": np.asarray(inputs['L0_attn_bv'], dtype=np.float32).reshape(1, D),
        "bo_cols": np.ascontiguousarray(np.asarray(inputs['L0_attn_bo']).reshape(8, 128).T),
        "gcols": gcols, "iota": iota, "masks": masks,
        "gwT": np.ascontiguousarray(np.asarray(inputs['L0_ffn_gate_w']).T),
        "w1T": np.ascontiguousarray(np.asarray(inputs['L0_ffn_w1']).transpose(0, 2, 1).reshape(8 * D, D)),
        "w2T": np.ascontiguousarray(np.asarray(inputs['L0_ffn_w2']).transpose(0, 2, 1).reshape(8 * D, D)),
        "w3T": np.ascontiguousarray(np.asarray(inputs['L0_ffn_w3']).transpose(0, 2, 1).reshape(8 * D, D)),
        "b1c_d": np.ascontiguousarray(np.asarray(inputs['L0_ffn_b1']).reshape(8, 8, 128).transpose(2, 1, 0).reshape(128, 64)),
        "b2c_d": np.ascontiguousarray(np.asarray(inputs['L0_ffn_b2']).reshape(8, 8, 128).transpose(2, 1, 0).reshape(128, 64)),
        "b3c_d": np.ascontiguousarray(np.asarray(inputs['L0_ffn_b3']).reshape(8, 8, 128).transpose(2, 1, 0).reshape(128, 64)),
        "sel8_d": sel8,
        "sw1T": np.ascontiguousarray(np.asarray(inputs['L0_ffn_sw1']).T),
        "sw2T": np.ascontiguousarray(np.asarray(inputs['L0_ffn_sw2']).T),
        "sw3T": np.ascontiguousarray(np.asarray(inputs['L0_ffn_sw3']).T),
        "G_d": G, "GT_d": np.ascontiguousarray(G.T), "Cm_d": l1_cmat(),
        "Cmb_d": l1_cmat().astype(BF16NP),
        "WinT": WinT, "WoutT": WoutT, "binr_d": binr,
        "id_d": id128, "idb_d": idbf, "oc_d": ones_col, "or_d": ones_row,
        "orb_d": ones_row.astype(BF16NP),
        "on512b_d": np.ones((1, 512), np.float32).astype(BF16NP),
        "cc_d": ccarr,
    }
    in_maps = [dict(shared, x0T=np.ascontiguousarray(x0[i].T)) for i in range(NCORE)]
    res = run_bass_kernel_spmd(nc, in_maps, core_ids=list(range(NCORE)))
    out = np.stack([np.ascontiguousarray(res.results[i]["outT"].T) for i in range(NCORE)], axis=0)
    return out.astype(np.float32)



# version 12
# speedup vs baseline: 1.0519x; 1.0010x over previous
import numpy as np
import sys
sys.path.insert(0, '/opt/trn_rl_repo')
import concourse.bass as bass
import concourse.mybir as mybir
from concourse.bass_utils import run_bass_kernel_spmd

F32 = mybir.dt.float32
BF16 = mybir.dt.bfloat16
AF = mybir.ActivationFunctionType
ALU = mybir.AluOpType
AX = mybir.AxisListType

D = 1024
T = 2048
NCH = 8


def new_nc():
    return bass.Bass("TRN2", target_bir_lowering=False)


CONST_IDX = {}


def CONST(v):
    v = float(v)
    return CONST_IDX.setdefault(v, len(CONST_IDX))


def const_host_array():
    n = max(16, len(CONST_IDX))
    a = np.zeros((128, n), np.float32)
    for v, i in CONST_IDX.items():
        a[:, i] = v
    return a


def cc(b, v, p=128):
    return b['cc'][0:p, CONST(v):CONST(v) + 1]


class Group:
    __slots__ = ("eng", "fn", "waits", "inc", "tag")

    def __init__(self, eng, fn, waits, inc, tag):
        self.eng, self.fn, self.waits, self.inc, self.tag = eng, fn, waits, inc, tag


class Sched:
    ENGINES = ("sync", "tensor", "vector", "scalar", "gpsimd")

    NPOOL = 24

    def __init__(self, nc, stack):
        self.nc = nc
        self.sems = {e: stack.enter_context(nc.semaphore(f"S_{e}")) for e in self.ENGINES}
        for i in range(self.NPOOL):
            self.sems[f"D{i}"] = stack.enter_context(nc.semaphore(f"SD{i}"))
        self.count = {e: 0 for e in self.ENGINES}
        self.dcount = [0] * self.NPOOL
        self.dnext = 0
        self.groups = {e: [] for e in self.ENGINES}
        self.last_write = {}
        self.readers = {}

    def _deps(self, eng, reads, writes):
        need = {}

        def add(e2, c2):
            if e2 == eng and e2 == "tensor":
                return
            if need.get(e2, 0) < c2:
                need[e2] = c2
        self._add = add
        for k in reads:
            if k in self.last_write:
                add(*self.last_write[k])
        for k in writes:
            if k in self.last_write:
                add(*self.last_write[k])
            for e2, c2 in self.readers.get(k, {}).items():
                add(e2, c2)
        return need

    def _commit(self, eng, reads, writes, cnt_after):
        for k in reads:
            self.readers.setdefault(k, {})[eng] = cnt_after
        for k in writes:
            self.last_write[k] = (eng, cnt_after)
            self.readers[k] = {}

    def g(self, eng, fn, reads=(), writes=(), tag=""):
        need = self._deps(eng, reads, writes)
        self.count[eng] += 1
        self.groups[eng].append(Group(eng, fn, need, 1, tag))
        self._commit(eng, reads, writes, self.count[eng])
        return self.count[eng]

    def dma(self, eng, n, fn, reads=(), writes=(), tag=""):
        p = self.dnext
        self.dnext = (self.dnext + 1) % self.NPOOL
        need = self._deps(eng, reads, writes)
        dname = f"D{p}"
        gate = self.dcount[p]
        if gate > 0 and need.get(dname, 0) < gate:
            need[dname] = gate
        self.dcount[p] += 16 * n
        dcnt = self.dcount[p]
        sem = self.sems[dname]

        def wrapped(e, _unused, fn=fn, sem=sem):
            fn(e, sem)
        self.groups[eng].append(Group(eng, wrapped, need, 16 * n, tag))
        self._commit_dma(eng, dname, reads, writes, dcnt)
        return dcnt

    def _commit_dma(self, eng, dname, reads, writes, dcnt):
        for k in reads:
            self.readers.setdefault(k, {})[dname] = dcnt
        for k in writes:
            self.last_write[k] = (dname, dcnt)
            self.readers[k] = {}

    def barrier_all(self):
        snap = dict(self.count)
        dsnap = {f"D{i}": c for i, c in enumerate(self.dcount) if c > 0}
        for e in self.ENGINES:
            def fn(eng, _e=e):
                return eng.nop()
            need = {e2: c for e2, c in snap.items() if e2 != e and c > 0}
            need.update(dsnap)
            self.count[e] += 1
            self.groups[e].append(Group(e, fn, need, 1, "barrier"))

    def emit(self, block):
        sems = self.sems

        def run(eng_name):
            def body(eng):
                for g in self.groups[eng_name]:
                    for e2, c2 in sorted(g.waits.items()):
                        eng.wait_ge(sems[e2], c2)
                    if g.inc >= 16:
                        g.fn(eng, None)
                    else:
                        last = g.fn(eng)
                        if last is None:
                            raise RuntimeError(f"group {g.tag} on {eng_name} returned None")
                        last.then_inc(sems[eng_name], 1)
            return body
        block.sync(run("sync"))
        block.tensor(run("tensor"))
        block.vector(run("vector"))
        block.scalar(run("scalar"))
        block.gpsimd(run("gpsimd"))

    def drain_final(self):
        snap = dict(self.count)

        def fn(eng):
            return eng.nop()
        need = {e2: c for e2, c in snap.items() if e2 != "sync" and c > 0}
        self.count["sync"] += 1
        self.groups["sync"].append(Group("sync", fn, need, 1, "final"))

    def out_wait(self, eng="sync"):
        snap = dict(self.count)
        own = snap.pop(eng)

        def fn(e):
            if own > 0:
                e.wait_ge(self.sems[eng], own)
            return e.nop()
        need = {e2: c for e2, c in snap.items() if c > 0}
        need.update({f"D{i}": c for i, c in enumerate(self.dcount) if c > 0})
        self.count[eng] += 1
        self.groups[eng].append(Group(eng, fn, need, 1, "out_wait"))



def emit_ln(sch, b, xk, xak, onck, onrk, out_dram=None, src_dram=None,
            pre="ln", xn_bf=None):
    sq, rows = b['ln_sq'], b['ln_rows']
    mbc, rbc = b['ln_mbc'], b['ln_rbc']
    onc, onr = b[onck], b[onrk]
    psA, psB, psC, psM, psR = b['ln_psA'], b['ln_psB'], b['ln_psC'], b['ln_psM'], b['ln_psR']
    slots = [(psA, 0, 'ln_psA'), (psA, 32, 'ln_psA'), (psA, 64, 'ln_psA'), (psB, 0, 'ln_psB'),
             (psB, 32, 'ln_psB'), (psB, 64, 'ln_psB'), (psC, 0, 'ln_psC'), (psC, 32, 'ln_psC')]

    if src_dram is not None:
        for c in range(NCH):
            def fn(eng, sem, c=c):
                eng.dma_start(out=b[xk](c), in_=src_dram(c)).then_inc(sem, 16)
            sch.dma('sync', 1, fn, writes=[(xk, c)], tag=f"{pre}.ldx{c}")

    for c in range(NCH):
        def fn(eng, c=c):
            return eng.activation(out=sq[:, (c % 2) * T:(c % 2 + 1) * T], in_=b[xk](c), func=AF.Square)
        sch.g('scalar', fn, reads=[(xk, c)], writes=[('ln_sqb', c % 2)], tag=f"{pre}.sq{c}")

        def fn(eng, c=c):
            last = None
            for t in range(4):
                ps, r, _ = slots[t]
                last = eng.matmul(ps[r:r + 1, :], onc[:, :], b[xk](c)[:, t * 512:(t + 1) * 512],
                                  start=(c == 0), stop=(c == NCH - 1))
            return last
        sch.g('tensor', fn, reads=[(xk, c), 'ones'],
              writes=['ln_psA', 'ln_psB', 'ln_psC'], tag=f"{pre}.mmsum{c}")

        def fn(eng, c=c):
            last = None
            for t in range(4):
                ps, r, _ = slots[4 + t]
                last = eng.matmul(ps[r:r + 1, :], onc[:, :], sq[:, (c % 2) * T + t * 512:(c % 2) * T + (t + 1) * 512],
                                  start=(c == 0), stop=(c == NCH - 1))
            return last
        sch.g('tensor', fn, reads=[('ln_sqb', c % 2), 'ones'],
              writes=['ln_psA', 'ln_psB', 'ln_psC'], tag=f"{pre}.mmsq{c}")

    def fn(eng):
        last = None
        for t in range(4):
            ps, r, _ = slots[t]
            last = eng.mul(out=rows[0:1, t * 512:(t + 1) * 512], in_=ps[r:r + 1, :], mul=cc(b, 1.0 / 1024.0, 1))
        for t in range(4):
            ps, r, _ = slots[4 + t]
            last = eng.mul(out=rows[0:1, T + t * 512:T + (t + 1) * 512], in_=ps[r:r + 1, :], mul=cc(b, 1.0 / 1024.0, 1))
        return last
    sch.g('scalar', fn, reads=['ln_psA', 'ln_psB', 'ln_psC', 'consts'],
          writes=[('ln_rows', 'mean'), ('ln_rows', 'msq')], tag=f"{pre}.rows")

    def fn(eng):
        return eng.tensor_mul(rows[0:1, 2 * T:3 * T], rows[0:1, 0:T], rows[0:1, 0:T])
    sch.g('vector', fn, reads=[('ln_rows', 'mean')], writes=[('ln_rows', 'var')], tag=f"{pre}.var1")

    def fn(eng):
        return eng.tensor_sub(rows[0:1, 2 * T:3 * T], rows[0:1, T:2 * T], rows[0:1, 2 * T:3 * T])
    sch.g('vector', fn, reads=[('ln_rows', 'msq'), ('ln_rows', 'var')],
          writes=[('ln_rows', 'var')], tag=f"{pre}.var2")

    def fn(eng):
        return eng.activation(out=rows[0:1, 2 * T:3 * T], in_=rows[0:1, 2 * T:3 * T], func=AF.Ln, bias=cc(b, 1e-5, 1))
    sch.g('scalar', fn, reads=[('ln_rows', 'var'), 'consts'], writes=[('ln_rows', 'var')], tag=f"{pre}.rstd1")

    def fn(eng):
        return eng.activation(out=rows[0:1, 3 * T:4 * T], in_=rows[0:1, 2 * T:3 * T], func=AF.Exp, scale=cc(b, -0.5, 1))
    sch.g('scalar', fn, reads=[('ln_rows', 'var'), 'consts'], writes=[('ln_rows', 'rstd')], tag=f"{pre}.rstd2")

    for t in range(4):
        def fn(eng, t=t):
            eng.matmul(psM[:, :], onr[0:1, :], rows[0:1, t * 512:(t + 1) * 512], start=True, stop=True)
            return eng.matmul(psR[:, :], onr[0:1, :], rows[0:1, 3 * T + t * 512:3 * T + (t + 1) * 512],
                              start=True, stop=True)
        sch.g('tensor', fn, reads=[('ln_rows', 'mean'), ('ln_rows', 'rstd'), 'ones'],
              writes=['ln_psM', 'ln_psR'], tag=f"{pre}.bcmm{t}")

        def fn(eng, t=t):
            eng.tensor_copy(mbc[:, t * 512:(t + 1) * 512], psM[:, :])
            return eng.tensor_copy(rbc[:, t * 512:(t + 1) * 512], psR[:, :])
        sch.g('vector', fn, reads=['ln_psM', 'ln_psR'],
              writes=['ln_mbc', 'ln_rbc'], tag=f"{pre}.bcev{t}")

    for c in range(NCH):
        def fn(eng, c=c):
            return eng.tensor_sub(sq[:, 0:T], b[xk](c), mbc[:, :])
        sch.g('vector', fn, reads=[(xk, c), 'ln_mbc'], writes=[('ln_sqb', 0), ('ln_sqb', 1)],
              tag=f"{pre}.xns{c}")

        def fn(eng, c=c):
            last = None
            if xak is not None:
                last = eng.tensor_mul(b[xak](c), sq[:, 0:T], rbc[:, :])
            if xn_bf is not None:
                last = eng.tensor_mul(xn_bf(c), sq[:, 0:T], rbc[:, :])
            return last
        wr = []
        if xak is not None:
            wr.append((xak, c))
        if xn_bf is not None:
            wr.append(('xnbf', c))
        sch.g('vector', fn, reads=[('ln_sqb', 0), ('ln_sqb', 1), 'ln_rbc'], writes=wr,
              tag=f"{pre}.xn{c}")
        if out_dram is not None:
            def fn(eng, sem, c=c):
                eng.dma_start(out=out_dram(c), in_=b[xak](c)).then_inc(sem, 16)
            sch.dma('sync', 1, fn, reads=[(xak, c)], tag=f"{pre}.stx{c}")



def emit_attn0(sch, b, x0_tile, x1_store, pre="a0", load_wo=None, load_wq=None):
    wq, wk, wv, wo = b['wq'], b['wk'], b['wv'], b['wo']
    xn = b['xnbf']
    kT, v, qt, sc, at = b['kT'], b['v'], b['qt'], b['sc'], b['at']
    p0, p1 = b['a0_p0'], b['a0_p1']
    pS = [b['a0_s0'], b['a0_s1'], b['a0_s2'], b['a0_s3']]
    pav = b['a0_av']
    masks = b['masks']

    def fn(eng):
        return eng.activation(out=b['lng'][:, :], in_=b['gcols'][:, :], func=AF.Ln)
    sch.g('scalar', fn, reads=['gcols'], writes=['lng'], tag=f"{pre}.lng")

    for c in range(NCH):
        for tt in range(4):
            ps = p0 if (c * 4 + tt) % 2 == 0 else p1
            psk = 'a0_p0' if (c * 4 + tt) % 2 == 0 else 'a0_p1'

            def fn(eng, c=c, tt=tt, ps=ps):
                for k in range(NCH):
                    eng.matmul(ps[:, :], wk[:, k * 1024 + c * 128:k * 1024 + (c + 1) * 128],
                               xn(k)[:, tt * 512:(tt + 1) * 512],
                               start=(k == 0), stop=False)
                return eng.matmul(ps[:, :], b['bk_row'][0:1, c * 128:(c + 1) * 128],
                                  b['on512'][0:1, :], start=False, stop=True)
            sch.g('tensor', fn, reads=['wkT', 'bk_row', 'on512'] + [('xnbf', k) for k in range(NCH)],
                  writes=[psk], tag=f"{pre}.kp{c}.{tt}")

            def fn(eng, c=c, tt=tt, ps=ps):
                return eng.copy(out=kT[:, c * T + tt * 512:c * T + (tt + 1) * 512], in_=ps[:, :])
            sch.g('scalar', fn, reads=[psk], writes=[('kT', c)], tag=f"{pre}.ke{c}.{tt}")

    for jt in range(16):
        for h in range(2):
            ps = p0 if (jt * 2 + h) % 2 == 0 else p1
            psk = 'a0_p0' if (jt * 2 + h) % 2 == 0 else 'a0_p1'

            def fn(eng, jt=jt, h=h, ps=ps):
                for k in range(NCH):
                    eng.matmul(ps[:, :], xn(k)[:, jt * 128:(jt + 1) * 128],
                               wv[:, k * 1024 + h * 512:k * 1024 + (h + 1) * 512],
                               start=(k == 0), stop=False)
                return eng.matmul(ps[:, :], b['onr'][0:1, :], b['bv_row'][0:1, h * 512:(h + 1) * 512],
                                  start=False, stop=True)
            sch.g('tensor', fn, reads=['wvT', 'ones', 'bv_row'] + [('xnbf', k) for k in range(NCH)],
                  writes=[psk], tag=f"{pre}.vp{jt}.{h}")

            def fn(eng, jt=jt, h=h, ps=ps):
                return eng.copy(out=v[:, jt * 1024 + h * 512:jt * 1024 + (h + 1) * 512], in_=ps[:, :])
            sch.g('scalar', fn, reads=[psk], writes=[('v', jt)], tag=f"{pre}.ve{jt}.{h}")

    if load_wo is not None:
        load_wo(sch)
    if load_wq is not None:
        load_wq(sch)

    for t in range(4):
        nbj = 4 * t + 4
        for c in range(NCH):
            ps = p0 if c % 2 == 0 else p1
            psk = 'a0_p0' if c % 2 == 0 else 'a0_p1'

            def fn(eng, c=c, t=t, ps=ps):
                for k in range(NCH):
                    eng.matmul(ps[:, :], wq[:, k * 1024 + c * 128:k * 1024 + (c + 1) * 128],
                               xn(k)[:, t * 512:(t + 1) * 512],
                               start=(k == 0), stop=False)
                return eng.matmul(ps[:, :], b['bq_row'][0:1, c * 128:(c + 1) * 128],
                                  b['on512'][0:1, :], start=False, stop=True)
            sch.g('tensor', fn, reads=['wqT', 'bq_row', 'on512'] + [('xnbf', k) for k in range(NCH)],
                  writes=[psk], tag=f"{pre}.qp{t}.{c}")

            def fn(eng, c=c, ps=ps):
                return eng.copy(out=qt[:, c * 512:(c + 1) * 512], in_=ps[:, :])
            sch.g('scalar', fn, reads=[psk], writes=[('qt', c)], tag=f"{pre}.qe{t}.{c}")

        for c in range(NCH):
            for w0 in range(0, nbj, 8):
                wave = range(w0, min(w0 + 8, nbj))
                for bj in wave:
                    sA, sB = pS[bj % 2], pS[2 + bj % 2]
                    kA = ('a0_sA', bj % 2)
                    kB = ('a0_sB', bj % 2)
                    sl = bj % 8

                    def fn(eng, c=c, bj=bj, sA=sA, sB=sB):
                        eng.matmul(sA[:, :], kT[0:64, c * T + bj * 128:c * T + (bj + 1) * 128],
                                   qt[0:64, c * 512:(c + 1) * 512], start=True, stop=True,
                                   tile_position=(0, 0))
                        return eng.matmul(sB[:, :], kT[64:128, c * T + bj * 128:c * T + (bj + 1) * 128],
                                          qt[64:128, c * 512:(c + 1) * 512], start=True, stop=True,
                                          tile_position=(64, 0))
                    sch.g('tensor', fn, reads=[('kT', c), ('qt', c)], writes=[kA, kB],
                          tag=f"{pre}.sc{t}.{c}.{bj}")

                    r = bj - 4 * t
                    if r >= 0:
                        def fn(eng, sl=sl, r=r, sA=sA, sB=sB):
                            eng.tensor_mul(sc[:, (2 * sl) * 512:(2 * sl + 1) * 512], sA[:, :],
                                           masks[:, r * 512:(r + 1) * 512])
                            return eng.tensor_mul(sc[:, (2 * sl + 1) * 512:(2 * sl + 2) * 512], sB[:, :],
                                                  masks[:, r * 512:(r + 1) * 512])
                        sch.g('vector', fn, reads=[kA, kB, 'masks'],
                              writes=[('scb', sl)], tag=f"{pre}.se{t}.{c}.{bj}")
                    else:
                        def fn(eng, sl=sl, sA=sA, sB=sB):
                            eng.copy(out=sc[:, (2 * sl) * 512:(2 * sl + 1) * 512], in_=sA[:, :])
                            return eng.copy(out=sc[:, (2 * sl + 1) * 512:(2 * sl + 2) * 512], in_=sB[:, :])
                        sch.g('scalar', fn, reads=[kA, kB], writes=[('scb', sl)],
                              tag=f"{pre}.se{t}.{c}.{bj}")

                for bj in wave:
                    sl = bj % 8

                    def fn(eng, c=c, bj=bj, sl=sl, nbj=nbj):
                        eng.matmul(pav[0:64, :], v[:, bj * 1024 + c * 128:bj * 1024 + c * 128 + 64],
                                   sc[:, (2 * sl) * 512:(2 * sl + 1) * 512],
                                   start=(bj == 0), stop=(bj == nbj - 1), tile_position=(0, 0))
                        return eng.matmul(pav[64:128, :], v[:, bj * 1024 + c * 128 + 64:bj * 1024 + (c + 1) * 128],
                                          sc[:, (2 * sl + 1) * 512:(2 * sl + 2) * 512],
                                          start=(bj == 0), stop=(bj == nbj - 1), tile_position=(0, 64))
                    sch.g('tensor', fn, reads=[('scb', sl), ('v', bj)], writes=['a0_av'],
                          tag=f"{pre}.av{t}.{c}.{bj}")

            def fn(eng, c=c, t=t):
                return eng.tensor_scalar_mul(b['dkb'][:, 0:1], b['lng'][:, c:c + 1],
                                             cc(b, float(512 * t)))
            sch.g('vector', fn, reads=['lng', 'consts'], writes=['dkb'], tag=f"{pre}.dkb{t}.{c}")

            def fn(eng, c=c, t=t):
                return eng.activation(out=b['dk'][:, :], in_=b['iota512'][:, :],
                                      func=AF.Exp, scale=b['lng'][:, c:c + 1],
                                      bias=b['dkb'][:, 0:1])
            sch.g('scalar', fn, reads=['lng', 'iota512', 'dkb'], writes=['dk'], tag=f"{pre}.dk{t}.{c}")

            def fn(eng, c=c):
                return eng.tensor_mul(at[:, c * 512:(c + 1) * 512], pav[:, :], b['dk'][:, :])
            sch.g('vector', fn, reads=['a0_av', 'dk'], writes=[('at', c)], tag=f"{pre}.ae{t}.{c}")

        for o in range(NCH):
            ps = p0 if o % 2 == 0 else p1
            psk = 'a0_p0' if o % 2 == 0 else 'a0_p1'

            def fn(eng, o=o, ps=ps):
                last = None
                for k in range(NCH):
                    last = eng.matmul(ps[:, :], wo[:, k * 1024 + o * 128:k * 1024 + (o + 1) * 128],
                                      at[:, k * 512:(k + 1) * 512],
                                      start=(k == 0), stop=(k == NCH - 1))
                return last
            sch.g('tensor', fn, reads=['woT'] + [('at', k) for k in range(NCH)],
                  writes=[psk], tag=f"{pre}.wo{t}.{o}")

            x1_store(sch, o, t, ps, psk, b['bo_cols'][:, o:o + 1], x0_tile)


def const_host_array_padded(n=64):
    a = np.zeros((128, n), np.float32)
    s = const_host_array()
    a[:, :s.shape[1]] = s[:, :n]
    return a



def emit_moe(sch, b, load_w, first_write=True, pre="mo"):
    xn = b['xnbf']
    p0, p1, plg, ptr = b['mo_p0'], b['mo_p1'], b['mo_lg'], b['mo_tr']
    lgT, lgt = b['lgT'], b['lgt']
    sc_t, m_t = b['rt_sc'], b['rt_m']
    c1, c2, c3 = b['rt_c1'], b['rt_c2'], b['rt_c3']
    tmp = b['rt_tmp']
    scT, mT = b['scT'], b['mT']
    wb, mb = b['wb'], b['mb']
    xs, h, x3m, t1 = b['xs'], b['h'], b['x3m'], b['t1']
    w1, w2, w3 = b['w1'], b['w2'], b['w3']
    idn = b['id128']

    for tt in range(4):
        def fn(eng, tt=tt):
            last = None
            for k in range(NCH):
                last = eng.matmul(plg[0:8, :], b['gw'][:, k * 8:(k + 1) * 8],
                                  xn(k)[:, tt * 512:(tt + 1) * 512],
                                  start=(k == 0), stop=(k == NCH - 1))
            return last
        sch.g('tensor', fn, reads=['gw'] + [('xnbf', k) for k in range(NCH)],
              writes=['mo_lg'], tag=f"{pre}.lg{tt}")

        def fn(eng, tt=tt):
            return eng.copy(out=lgT[0:8, tt * 512:(tt + 1) * 512], in_=plg[0:8, :])
        sch.g('scalar', fn, reads=['mo_lg'], writes=['lgT'], tag=f"{pre}.lge{tt}")

    for i in range(16):
        def fn(eng, i=i):
            return eng.transpose(ptr[0:128, 0:8], lgT[0:8, i * 128:(i + 1) * 128], idn[0:8, 0:8])
        sch.g('tensor', fn, reads=['lgT', 'id128'], writes=['mo_tr'], tag=f"{pre}.tr{i}")

        def fn(eng, i=i):
            return eng.tensor_copy(lgt[:, i * 8:(i + 1) * 8], ptr[0:128, 0:8])
        sch.g('vector', fn, reads=['mo_tr'], writes=[('lgt', i)], tag=f"{pre}.tre{i}")

    for i in range(16):
        lg_i = lgt[:, i * 8:(i + 1) * 8]
        sc_i = sc_t[:, i * 8:(i + 1) * 8]
        m_i = m_t[:, i * 8:(i + 1) * 8]
        c1_i, c2_i, c3_i = c1[:, i:i + 1], c2[:, i:i + 1], c3[:, i:i + 1]

        sch.g('vector', lambda e, a=c1_i, x=lg_i: e.reduce_max(a, x, axis=AX.X, negate=True),
              reads=[('lgt', i)], writes=[('c1', i)], tag=f"{pre}.r1.{i}")
        sch.g('scalar', lambda e, o=sc_i, x=lg_i, a=c1_i: e.activation(out=o, in_=x, func=AF.Exp, bias=a),
              reads=[('lgt', i), ('c1', i)], writes=[('sc', i)], tag=f"{pre}.exp.{i}")
        sch.g('vector', lambda e, a=c2_i, x=sc_i: e.reduce_sum(a, x, axis=AX.X),
              reads=[('sc', i)], writes=[('c2', i)], tag=f"{pre}.sum.{i}")
        sch.g('vector', lambda e, a=c3_i, x=c2_i: e.reciprocal(a, x),
              reads=[('c2', i)], writes=[('c3', i)], tag=f"{pre}.rcp.{i}")
        sch.g('vector', lambda e, o=sc_i, a=c3_i: e.tensor_scalar(o, o, a, None, op0=ALU.mult),
              reads=[('sc', i), ('c3', i)], writes=[('sc', i)], tag=f"{pre}.nrm.{i}")
        sch.g('vector', lambda e, a=c1_i, x=sc_i: e.reduce_max(a, x, axis=AX.X, negate=True),
              reads=[('sc', i)], writes=[('c1', i)], tag=f"{pre}.m1.{i}")
        sch.g('vector', lambda e, o=m_i, x=sc_i, a=c1_i: e.tensor_scalar(o, x, a, None, op0=ALU.add),
              reads=[('sc', i), ('c1', i)], writes=[('m', i)], tag=f"{pre}.d1.{i}")
        sch.g('vector', lambda e, o=m_i, bb=b: e.tensor_single_scalar(o, o, cc(bb, 0.0), op=ALU.is_equal),
              reads=[('m', i), 'consts'], writes=[('m', i)], tag=f"{pre}.eq.{i}")
        sch.g('vector', lambda e, o=m_i, x=sc_i, bb=b: e.scalar_tensor_tensor(o, o, cc(bb, -1e30), x, op0=ALU.mult, op1=ALU.add),
              reads=[('m', i), ('sc', i), 'consts'], writes=[('m', i)], tag=f"{pre}.msk.{i}")
        sch.g('vector', lambda e, a=c2_i, x=m_i: e.reduce_max(a, x, axis=AX.X, negate=True),
              reads=[('m', i)], writes=[('c2', i)], tag=f"{pre}.m2.{i}")
        sch.g('vector', lambda e, o=m_i, x=sc_i, a=c2_i: e.tensor_scalar(o, x, a, None, op0=ALU.add),
              reads=[('sc', i), ('c2', i)], writes=[('m', i)], tag=f"{pre}.d2.{i}")
        sch.g('vector', lambda e, o=m_i, bb=b: e.tensor_single_scalar(o, o, cc(bb, 0.0), op=ALU.is_ge),
              reads=[('m', i), 'consts'], writes=[('m', i)], tag=f"{pre}.ge.{i}")

    for i in range(16):
        def fn(eng, i=i):
            return eng.transpose(ptr[0:8, 0:128], sc_t[:, i * 8:(i + 1) * 8], idn[:, :])
        sch.g('tensor', fn, reads=[('sc', i), 'id128'], writes=['mo_tr'], tag=f"{pre}.ts{i}")
        sch.g('scalar', lambda e, i=i: e.copy(out=scT[0:8, i * 128:(i + 1) * 128], in_=ptr[0:8, 0:128]),
              reads=['mo_tr'], writes=['scT'], tag=f"{pre}.tse{i}")

        def fn(eng, i=i):
            return eng.transpose(ptr[0:8, 0:128], m_t[:, i * 8:(i + 1) * 8], idn[:, :])
        sch.g('tensor', fn, reads=[('m', i), 'id128'], writes=['mo_tr'], tag=f"{pre}.tm{i}")
        sch.g('scalar', lambda e, i=i: e.copy(out=mT[0:8, i * 128:(i + 1) * 128], in_=ptr[0:8, 0:128]),
              reads=['mo_tr'], writes=['mT'], tag=f"{pre}.tme{i}")

    for e_ in range(8):
        load_w(sch, e_)

        for (src, srck, dst, dstk) in ((scT, 'scT', wb, 'wb'), (mT, 'mT', mb, 'mb')):
            for tt in range(4):
                ps = p0 if tt % 2 == 0 else p1
                psk = 'mo_p0' if tt % 2 == 0 else 'mo_p1'

                def fn(eng, e_=e_, tt=tt, src=src, ps=ps):
                    return eng.matmul(ps[:, :], b['sel8'][0:8, e_ * 128:(e_ + 1) * 128],
                                      src[0:8, tt * 512:(tt + 1) * 512], start=True, stop=True)
                sch.g('tensor', fn, reads=[srck, 'sel8'], writes=[psk], tag=f"{pre}.bc{dstk}{e_}.{tt}")

                def fn(eng, tt=tt, dst=dst, ps=ps):
                    return eng.copy(out=dst[:, tt * 512:(tt + 1) * 512], in_=ps[:, :])
                sch.g('scalar', fn, reads=[psk], writes=[dstk], tag=f"{pre}.bce{dstk}{e_}.{tt}")

        for qf in range(4):
            qo = qf * 512
            for k in range(NCH):
                def fn(eng, k=k, qo=qo):
                    return eng.tensor_mul(xs[:, k * 512:(k + 1) * 512],
                                          xn(k)[:, qo:qo + 512], wb[:, qo:qo + 512])
                sch.g('gpsimd', fn, reads=[('xnbf', k), 'wb'], writes=[('xs', k)],
                      tag=f"{pre}.xs{e_}.{qf}.{k}")

            for hc in range(NCH):
                ps = p0 if hc % 2 == 0 else p1
                psk = 'mo_p0' if hc % 2 == 0 else 'mo_p1'

                def fn(eng, hc=hc, ps=ps):
                    last = None
                    for k in range(NCH):
                        last = eng.matmul(ps[:, :], w1[:, k * 1024 + hc * 128:k * 1024 + (hc + 1) * 128],
                                          xs[:, k * 512:(k + 1) * 512],
                                          start=(k == 0), stop=(k == NCH - 1))
                    return last
                sch.g('tensor', fn, reads=['w1sb'] + [('xs', k) for k in range(NCH)],
                      writes=[psk], tag=f"{pre}.u1{e_}.{qf}.{hc}")

                def fn(eng, hc=hc, e_=e_, ps=ps):
                    return eng.activation(out=h[:, hc * 512:(hc + 1) * 512],
                                          in_=ps[:, :], func=AF.Silu,
                                          bias=b['b1c'][:, hc * 8 + e_:hc * 8 + e_ + 1])
                sch.g('scalar', fn, reads=[psk, 'b1c'], writes=[('h', hc)],
                      tag=f"{pre}.u1e{e_}.{qf}.{hc}")

            for oc in range(NCH):
                def fn(eng, oc=oc, ps=p0):
                    last = None
                    for k in range(NCH):
                        last = eng.matmul(ps[:, :], w3[:, k * 1024 + oc * 128:k * 1024 + (oc + 1) * 128],
                                          xs[:, k * 512:(k + 1) * 512],
                                          start=(k == 0), stop=(k == NCH - 1))
                    return last
                sch.g('tensor', fn, reads=['w3sb'] + [('xs', k) for k in range(NCH)],
                      writes=['mo_p0'], tag=f"{pre}.x3{e_}.{qf}.{oc}")

                def fn(eng, oc=oc, e_=e_, qo=qo):
                    return eng.scalar_tensor_tensor(
                        out=x3m[:, :], in0=p0[:, :],
                        scalar=b['b3c'][:, oc * 8 + e_:oc * 8 + e_ + 1],
                        in1=mb[:, qo:qo + 512],
                        op0=ALU.add, op1=ALU.mult)
                sch.g('vector', fn, reads=['mo_p0', 'b3c', 'mb'], writes=['x3m'],
                      tag=f"{pre}.x3e{e_}.{qf}.{oc}")

                def fn(eng, oc=oc, ps=p1):
                    last = None
                    for k in range(NCH):
                        last = eng.matmul(ps[:, :], w2[:, k * 1024 + oc * 128:k * 1024 + (oc + 1) * 128],
                                          h[:, k * 512:(k + 1) * 512],
                                          start=(k == 0), stop=(k == NCH - 1))
                    return last
                sch.g('tensor', fn, reads=['w2sb'] + [('h', k) for k in range(NCH)],
                      writes=['mo_p1'], tag=f"{pre}.x2{e_}.{qf}.{oc}")

                if e_ == 0 and first_write:
                    def fn(eng, oc=oc, e_=e_, qo=qo):
                        return eng.scalar_tensor_tensor(
                            out=b['y'](oc)[:, qo:qo + 512],
                            in0=p1[:, :], scalar=b['b2c'][:, oc * 8 + e_:oc * 8 + e_ + 1],
                            in1=x3m[:, :], op0=ALU.add, op1=ALU.mult)
                    sch.g('vector', fn, reads=['mo_p1', 'b2c', 'x3m'], writes=[('y', oc)],
                          tag=f"{pre}.x2e{e_}.{qf}.{oc}")
                else:
                    def fn(eng, oc=oc, e_=e_):
                        return eng.scalar_tensor_tensor(
                            out=t1[:, :], in0=p1[:, :],
                            scalar=b['b2c'][:, oc * 8 + e_:oc * 8 + e_ + 1],
                            in1=x3m[:, :], op0=ALU.add, op1=ALU.mult)
                    sch.g('vector', fn, reads=['mo_p1', 'b2c', 'x3m'], writes=['t1'],
                          tag=f"{pre}.x2e{e_}.{qf}.{oc}")

                    def fn(eng, oc=oc, qo=qo):
                        yap = b['y'](oc)[:, qo:qo + 512]
                        return eng.tensor_add(yap, yap, t1[:, :])
                    sch.g('gpsimd', fn, reads=['t1', ('y', oc)], writes=[('y', oc)],
                          tag=f"{pre}.acc{e_}.{qf}.{oc}")



def emit_shared(sch, b, z_store, pre="sh"):
    xn = b['xnbf']
    sw1, sw2, sw3 = b['sw1'], b['sw2'], b['sw3']
    sa, gg = b['sa'], b['gg']
    p0, p1 = b['sh_p0'], b['sh_p1']

    for hf in range(2):
        ho = hf * 1024
        for ic in range(16):
            for tt in range(2):
                ps, psk = (p0, 'sh_p0') if tt == 0 else (p1, 'sh_p1')

                def fn(eng, ic=ic, tt=tt, ho=ho, ps=ps):
                    last = None
                    for k in range(NCH):
                        last = eng.matmul(ps[:, :], sw1[:, k * T + ic * 128:k * T + (ic + 1) * 128],
                                          xn(k)[:, ho + tt * 512:ho + (tt + 1) * 512],
                                          start=(k == 0), stop=(k == NCH - 1))
                    return last
                sch.g('tensor', fn, reads=['sw1sb'] + [('xnbf', k) for k in range(NCH)],
                      writes=[psk], tag=f"{pre}.ua{hf}.{ic}.{tt}")

                def fn(eng, tt=tt, ps=ps):
                    return eng.activation(out=sa[:, tt * 512:(tt + 1) * 512], in_=ps[:, :], func=AF.Silu)
                sch.g('scalar', fn, reads=[psk], writes=[('sa', tt)], tag=f"{pre}.uae{hf}.{ic}.{tt}")

            for tt in range(2):
                ps, psk = (p0, 'sh_p0') if tt == 0 else (p1, 'sh_p1')

                def fn(eng, ic=ic, tt=tt, ho=ho, ps=ps):
                    last = None
                    for k in range(NCH):
                        last = eng.matmul(ps[:, :], sw2[:, k * T + ic * 128:k * T + (ic + 1) * 128],
                                          xn(k)[:, ho + tt * 512:ho + (tt + 1) * 512],
                                          start=(k == 0), stop=(k == NCH - 1))
                    return last
                sch.g('tensor', fn, reads=['sw2sb'] + [('xnbf', k) for k in range(NCH)],
                      writes=[psk], tag=f"{pre}.ub{hf}.{ic}.{tt}")

                def fn(eng, ic=ic, tt=tt, ps=ps):
                    return eng.tensor_mul(gg[:, ic * 1024 + tt * 512:ic * 1024 + (tt + 1) * 512],
                                          ps[:, :], sa[:, tt * 512:(tt + 1) * 512])
                sch.g('vector', fn, reads=[psk, ('sa', tt)], writes=[('gg', ic)],
                      tag=f"{pre}.gg{hf}.{ic}.{tt}")

        for oc in range(NCH):
            for tt in range(2):
                ps, psk = (p0, 'sh_p0') if tt == 0 else (p1, 'sh_p1')

                def fn(eng, oc=oc, tt=tt, ps=ps):
                    last = None
                    for ic in range(16):
                        last = eng.matmul(ps[:, :], sw3[:, ic * 1024 + oc * 128:ic * 1024 + (oc + 1) * 128],
                                          gg[:, ic * 1024 + tt * 512:ic * 1024 + (tt + 1) * 512],
                                          start=(ic == 0), stop=(ic == 15))
                    return last
                sch.g('tensor', fn, reads=['sw3sb'] + [('gg', ic) for ic in range(16)],
                      writes=[psk], tag=f"{pre}.z{hf}.{oc}.{tt}")

                z_store(sch, oc, hf, tt, psk, ps)



TAPC = [0.1 + 0.9 * k / 7.0 for k in range(8)]


def l1_cmat():
    C = np.zeros((8, 8), np.float32)
    for s in range(1, 8):
        w = min(s, 8)
        for dd in range(0, s):
            j = dd
            C[dd, s] = 0.1 + 0.9 * j / max(w - 1, 1) if w > 1 else 0.1
    return C


def emit_l1attn(sch, b, traj_store, g_dram, x_dram, pre="l1"):
    ND = T - 1
    df, x1c = b['df'], b['x1c']
    sq = b['l1sq']
    pd, pV, pB, pc, ptb = b['l1_pd'], b['l1_pV'], b['l1_pB'], b['l1_pc'], b['l1_ptb']
    pR0, pR1 = b['l1_pR0'], b['l1_pR1']
    l1c = b['l1c']
    rwa, rwb, rwc, rwd = b['rwa'], b['rwb'], b['rwc'], b['rwd']
    onc = b['onc']

    Gf = b['Gf32']

    def fn(eng, sem):
        for gc in range(4):
            eng.dma_start(out=Gf[:, gc * 1024:(gc + 1) * 1024],
                          in_=g_dram[gc * 128:(gc + 1) * 128, :]).then_inc(sem, 16)
    sch.dma('sync', 4, fn, writes=[('df', 0), ('df', 1), 'Gf32'], tag=f"{pre}.ldgf")

    for gc in range(4):
        def fn(eng, gc=gc):
            return eng.activation(out=sq[:, 0:1024], in_=Gf[:, gc * 1024:(gc + 1) * 1024], func=AF.Square)
        sch.g('scalar', fn, reads=['Gf32'], writes=['l1sq'], tag=f"{pre}.gsq{gc}")

        def fn(eng, gc=gc):
            return eng.reduce_sum(l1c[:, 48 + gc:49 + gc], sq[:, 0:1024], axis=AX.X)
        sch.g('vector', fn, reads=['l1sq'], writes=[('gn', gc)], tag=f"{pre}.gss{gc}")

    def fn(eng):
        return eng.tensor_scalar_max(l1c[:, 48:52], l1c[:, 48:52], cc(b, 1e-16))
    sch.g('vector', fn, reads=[('gn', g) for g in range(4)] + ['consts'], writes=['gnc'], tag=f"{pre}.gcl")

    def fn(eng):
        return eng.activation(out=l1c[:, 52:56], in_=l1c[:, 48:52], func=AF.Ln)
    sch.g('scalar', fn, reads=['gnc'], writes=['gnl'], tag=f"{pre}.gln")

    def fn(eng):
        return eng.activation(out=l1c[:, 48:52], in_=l1c[:, 52:56], func=AF.Exp, scale=cc(b, -0.5))
    sch.g('scalar', fn, reads=['gnl'], writes=['gni'], tag=f"{pre}.gexp")

    for gc in range(4):
        def fn(eng, gc=gc):
            return eng.transpose(pV[0:1, 0:128], l1c[:, 48 + gc:49 + gc], b['id128'][:, :])
        sch.g('tensor', fn, reads=['gni', 'id128'], writes=['l1_pV'], tag=f"{pre}.gtr{gc}")

        def fn(eng, gc=gc):
            return eng.tensor_copy(rwd[0:1, gc * 128:(gc + 1) * 128], pV[0:1, 0:128])
        sch.g('vector', fn, reads=['l1_pV'], writes=['gnr'], tag=f"{pre}.gte{gc}")

    def fn(eng):
        return eng.matmul(pB[:, :], b['onr'][0:1, :], rwd[0:1, 0:512], start=True, stop=True)
    sch.g('tensor', fn, reads=['gnr', 'ones'], writes=['l1_pB'], tag=f"{pre}.gbc")

    def fn(eng):
        return eng.copy(out=b['invgn'][:, :], in_=pB[:, :])
    sch.g('scalar', fn, reads=['l1_pB'], writes=['invgn'], tag=f"{pre}.gbce")

    for c in range(NCH):
        def fn(eng, sem, c=c):
            eng.dma_start(out=x1c(c), in_=x_dram(c)).then_inc(sem, 16)
        sch.dma('sync', 1, fn,
                reads=(['Gf32'] if c < 2 else []) + (['gnr'] if c == 7 else []),
                writes=[('x1c', c)] + (['rwd'] if c == 7 else []),
                tag=f"{pre}.ldx{c}")

    for c in range(NCH):
        def fn(eng, c=c):
            eng.tensor_sub(df(c)[:, 0:ND], x1c(c)[:, 1:T], x1c(c)[:, 0:ND])
            return eng.memset(df(c)[:, ND:T], 0.0)
        sch.g('vector', fn, reads=[('x1c', c)], writes=[('df', c)], tag=f"{pre}.df{c}")

    rslots = [(pR0, 0), (pR0, 32), (pR0, 64), (pR1, 0)]
    for c in range(NCH):
        def fn(eng, c=c):
            return eng.activation(out=sq[:, :], in_=df(c)[:, :], func=AF.Square)
        sch.g('scalar', fn, reads=[('df', c)], writes=['l1sq'], tag=f"{pre}.sq{c}")

        def fn(eng, c=c):
            last = None
            for st in range(16):
                last = eng.matmul(pc[:, st:st + 1], sq[:, st * 128:(st + 1) * 128], onc[:, :],
                                  start=(c == 0), stop=(c == NCH - 1))
            return last
        sch.g('tensor', fn, reads=['l1sq', 'ones'], writes=['l1_pc'], tag=f"{pre}.cmm{c}")

        def fn(eng, c=c):
            last = None
            for t4 in range(4):
                ps, r = rslots[t4]
                last = eng.matmul(ps[r:r + 1, :], onc[:, :], sq[:, t4 * 512:(t4 + 1) * 512],
                                  start=(c == 0), stop=(c == NCH - 1))
            return last
        sch.g('tensor', fn, reads=['l1sq', 'ones'], writes=['l1_pR'], tag=f"{pre}.rmm{c}")

    def fn(eng):
        return eng.tensor_copy(l1c[:, 0:16], pc[:, 0:16])
    sch.g('vector', fn, reads=['l1_pc'], writes=['scol'], tag=f"{pre}.cev")

    def fn(eng):
        last = None
        for t4 in range(4):
            ps, r = rslots[t4]
            last = eng.copy(out=rwa[0:1, t4 * 512:(t4 + 1) * 512], in_=ps[r:r + 1, :])
        return last
    sch.g('scalar', fn, reads=['l1_pR'] + [('x1c', c) for c in (4, 5, 6, 7)],
          writes=['rwa'], tag=f"{pre}.rev")

    def fn(eng):
        return eng.tensor_scalar_max(l1c[:, 0:16], l1c[:, 0:16], cc(b, 1e-12))
    sch.g('vector', fn, reads=['scol', 'consts'], writes=['scol'], tag=f"{pre}.ccl")

    def fn(eng):
        return eng.activation(out=l1c[:, 16:32], in_=l1c[:, 0:16], func=AF.Ln)
    sch.g('scalar', fn, reads=['scol'], writes=['lnc'], tag=f"{pre}.cln")

    def fn(eng):
        eng.activation(out=l1c[:, 0:16], in_=l1c[:, 16:32], func=AF.Exp, scale=cc(b, -0.5))
        return eng.activation(out=l1c[:, 32:48], in_=l1c[:, 16:32], func=AF.Exp, scale=cc(b, 0.5))
    sch.g('scalar', fn, reads=['lnc', 'consts'], writes=['invmc', 'magc'], tag=f"{pre}.cexp")

    def fn(eng):
        return eng.tensor_mul(l1c[:, 16:32], l1c[:, 0:16], l1c[:, 32:48])
    sch.g('vector', fn, reads=['invmc', 'magc'], writes=['anc'], tag=f"{pre}.can")

    def fn(eng):
        return eng.tensor_scalar_max(l1c[:, 16:32], l1c[:, 16:32], cc(b, 1e-8))
    sch.g('vector', fn, reads=['anc', 'consts'], writes=['anc'], tag=f"{pre}.crc1")

    def fn(eng):
        return eng.reciprocal(l1c[:, 16:32], l1c[:, 16:32])
    sch.g('vector', fn, reads=['anc'], writes=['anc'], tag=f"{pre}.crc2")

    def fn(eng):
        return eng.scalar_tensor_tensor(l1c[:, 16:32], l1c[:, 16:32], cc(b, 2.0), l1c[:, 0:16],
                                        op0=ALU.mult, op1=ALU.mult)
    sch.g('vector', fn, reads=['anc', 'invmc', 'consts'], writes=['rsc'], tag=f"{pre}.crs")

    def fn(eng):
        return eng.activation(out=l1c[:, 32:48], in_=l1c[:, 32:48], func=AF.Tanh, scale=cc(b, 2.0))
    sch.g('scalar', fn, reads=['magc', 'consts'], writes=['tnc'], tag=f"{pre}.ctn")

    def fn(eng):
        return eng.tensor_scalar_max(rwa[0:1, :], rwa[0:1, :], cc(b, 1e-12, 1))
    sch.g('vector', fn, reads=['rwa', 'consts'], writes=['rwa'], tag=f"{pre}.rcl")

    def fn(eng):
        return eng.activation(out=rwa[0:1, :], in_=rwa[0:1, :], func=AF.Ln)
    sch.g('scalar', fn, reads=['rwa'], writes=['rwa'], tag=f"{pre}.rln")

    def fn(eng):
        eng.activation(out=rwb[0:1, :], in_=rwa[0:1, :], func=AF.Exp, scale=cc(b, -0.5, 1))
        return eng.activation(out=rwc[0:1, :], in_=rwa[0:1, :], func=AF.Exp, scale=cc(b, 0.5, 1))
    sch.g('scalar', fn, reads=['rwa', 'consts'] + [('x1c', c) for c in (4, 5, 6, 7)],
          writes=['rwb', 'rwc'], tag=f"{pre}.rexp")

    def fn(eng):
        return eng.activation(out=rwc[0:1, :], in_=rwc[0:1, :], func=AF.Tanh, scale=cc(b, 2.0, 1))
    sch.g('scalar', fn, reads=['rwc', 'consts'], writes=['rwc'], tag=f"{pre}.rtn")

    def fn(eng):
        return eng.scalar_tensor_tensor(rwb[0:1, :], rwc[0:1, :], cc(b, 0.6, 1), rwb[0:1, :],
                                        op0=ALU.mult, op1=ALU.mult)
    sch.g('vector', fn, reads=['rwb', 'rwc', 'consts'], writes=['rwb'], tag=f"{pre}.rap")

    def fn(eng):
        return eng.tensor_scalar_mul(rwd[0:1, 8:T], rwc[0:1, 0:T - 8], cc(b, TAPC[0], 1))
    sch.g('vector', fn, reads=['rwc', 'consts', 'gnr'], writes=['rwd'], tag=f"{pre}.dtap0")
    for k in range(1, 8):
        def fn(eng, k=k):
            return eng.scalar_tensor_tensor(rwd[0:1, 8:T], rwc[0:1, k:T - 8 + k], cc(b, TAPC[k], 1),
                                            rwd[0:1, 8:T], op0=ALU.mult, op1=ALU.add)
        sch.g('vector', fn, reads=['rwc', 'rwd', 'consts'], writes=['rwd'], tag=f"{pre}.dtap{k}")

    def fn(eng):
        return eng.matmul(pV[0:1, 0:8], l1c[0:8, 32:33], b['Cmat'][0:8, 0:8], start=True, stop=True)
    sch.g('tensor', fn, reads=['tnc', 'Cmat'], writes=['l1_pV'], tag=f"{pre}.dsm")

    def fn(eng):
        return eng.tensor_copy(rwd[0:1, 0:8], pV[0:1, 0:8])
    sch.g('vector', fn, reads=['l1_pV'], writes=['rwd'], tag=f"{pre}.dsme")

    def fn(eng):
        return eng.tensor_scalar_max(rwa[0:1, :], rwd[0:1, :], cc(b, 1e-8, 1))
    sch.g('vector', fn, reads=['rwd', 'consts'], writes=['rwa'], tag=f"{pre}.dcl")

    def fn(eng):
        eng.activation(out=rwa[0:1, :], in_=rwa[0:1, :], func=AF.Ln)
        return eng.activation(out=rwa[0:1, :], in_=rwa[0:1, :], func=AF.Exp, scale=cc(b, -1.0, 1))
    sch.g('scalar', fn, reads=['rwa', 'consts'], writes=['rwa'], tag=f"{pre}.dinv")

    for st in range(16):
        def fn(eng, st=st):
            last = None
            for k in range(NCH):
                last = eng.tensor_copy(b['dfsl'][:, k * 128:(k + 1) * 128],
                                       df(k)[:, st * 128:(st + 1) * 128])
            return last
        sch.g('vector', fn, reads=[('df', k) for k in range(NCH)], writes=['dfsl'],
              tag=f"{pre}.cast{st}")

        def fn(eng, st=st):
            last = None
            for k in range(NCH):
                last = eng.matmul(pd[:, :], b['dfsl'][:, k * 128:(k + 1) * 128],
                                  b['GTbf'][:, k * 512:(k + 1) * 512],
                                  start=(k == 0), stop=(k == NCH - 1))
            return last
        sch.g('tensor', fn, reads=['dfsl', 'GTbf'], writes=['l1_pd'], tag=f"{pre}.dot{st}")

        def fn(eng, st=st):
            return eng.scalar_tensor_tensor(b['sims'][:, :], pd[:, :], l1c[:, 16 + st:17 + st],
                                            b['invgn'][:, :], op0=ALU.mult, op1=ALU.mult)
        sch.g('vector', fn, reads=['l1_pd', 'rsc', 'invgn'], writes=['sims'], tag=f"{pre}.sim{st}")

        def fn(eng, st=st):
            return eng.reduce_max(b['smx'][:, 0:1], b['sims'][:, :], axis=AX.X, negate=True)
        sch.g('vector', fn, reads=['sims'], writes=['smx'], tag=f"{pre}.smx{st}")

        def fn(eng, st=st):
            return eng.activation(out=b['sims'][:, :], in_=b['sims'][:, :], func=AF.Exp,
                                  bias=b['smx'][:, 0:1])
        sch.g('scalar', fn, reads=['sims', 'smx'], writes=['sims'], tag=f"{pre}.sex{st}")

        def fn(eng, st=st):
            return eng.reduce_sum(b['smx'][:, 1:2], b['sims'][:, :], axis=AX.X)
        sch.g('vector', fn, reads=['sims'], writes=['smx2'], tag=f"{pre}.ssm{st}")

        def fn(eng, st=st):
            return eng.reciprocal(b['smx'][:, 2:3], b['smx'][:, 1:2])
        sch.g('vector', fn, reads=['smx2'], writes=['smx3'], tag=f"{pre}.src{st}")

        def fn(eng, st=st):
            return eng.scalar_tensor_tensor(b['smx'][:, 3:4], b['smx'][:, 2:3], cc(b, 0.4),
                                            l1c[:, 32 + st:33 + st], op0=ALU.mult, op1=ALU.mult)
        sch.g('vector', fn, reads=['smx3', 'tnc', 'consts'], writes=['smx4'], tag=f"{pre}.snr{st}")

        def fn(eng, st=st):
            return eng.tensor_scalar(b['gwt'][:, :], b['sims'][:, :], b['smx'][:, 3:4], None,
                                     op0=ALU.mult)
        sch.g('vector', fn, reads=['sims', 'smx4'], writes=['gwt'], tag=f"{pre}.gw{st}")

        for j in range(4):
            def fn(eng, st=st, j=j):
                return eng.transpose(ptb[:, :], b['gwt'][:, j * 128:(j + 1) * 128], b['idbf'][:, :])
            sch.g('tensor', fn, reads=['gwt', 'idbf'], writes=['l1_ptb'], tag=f"{pre}.gt{st}.{j}")

            def fn(eng, st=st, j=j):
                return eng.tensor_copy(b['gwT'][:, j * T + st * 128:j * T + (st + 1) * 128], ptb[:, :])
            sch.g('vector', fn, reads=['l1_ptb'], writes=[('gwT', j)], tag=f"{pre}.gte{st}.{j}")

    for st2 in range(4):
        def fn(eng, st2=st2):
            return eng.matmul(pB[:, :], b['onr'][0:1, :], rwb[0:1, st2 * 512:(st2 + 1) * 512],
                              start=True, stop=True)
        sch.g('tensor', fn, reads=['rwb', 'ones'], writes=['l1_pB'], tag=f"{pre}.abc{st2}")

        def fn(eng, st2=st2):
            return eng.copy(out=b['apb'][:, :], in_=pB[:, :])
        sch.g('scalar', fn, reads=['l1_pB'], writes=['apb'], tag=f"{pre}.abce{st2}")

        for c in range(NCH):
            pu = pd if c % 2 == 0 else pV
            puk = 'l1_pd' if c % 2 == 0 else 'l1_pV'

            def fn(eng, st2=st2, c=c, pu=pu):
                last = None
                for gc in range(4):
                    last = eng.matmul(pu[:, :], b['Gbf'][:, gc * 1024 + c * 128:gc * 1024 + (c + 1) * 128],
                                      b['gwT'][:, gc * T + st2 * 512:gc * T + (st2 + 1) * 512],
                                      start=(gc == 0), stop=(gc == 3))
                return last
            sch.g('tensor', fn, reads=['Gbf'] + [('gwT', gc) for gc in range(4)],
                  writes=[puk], tag=f"{pre}.inf{st2}.{c}")

            def fn(eng, st2=st2, c=c):
                return eng.tensor_mul(df(c)[:, st2 * 512:(st2 + 1) * 512],
                                      df(c)[:, st2 * 512:(st2 + 1) * 512], b['apb'][:, :])
            sch.g('vector', fn, reads=[('df', c), 'apb'], writes=[('df', c)], tag=f"{pre}.ua{st2}.{c}")

            def fn(eng, st2=st2, c=c, pu=pu):
                return eng.tensor_add(df(c)[:, st2 * 512:(st2 + 1) * 512],
                                      df(c)[:, st2 * 512:(st2 + 1) * 512], pu[:, :])
            sch.g('vector', fn, reads=[('df', c), puk], writes=[('df', c)], tag=f"{pre}.ub{st2}.{c}")

    for st2 in range(4):
        def fn(eng, st2=st2):
            return eng.matmul(pB[:, :], b['onr'][0:1, :], rwa[0:1, st2 * 512:(st2 + 1) * 512],
                              start=True, stop=True)
        sch.g('tensor', fn, reads=['rwa', 'ones'], writes=['l1_pB'], tag=f"{pre}.ibc{st2}")

        def fn(eng, st2=st2):
            return eng.copy(out=b['ivb'][:, st2 * 512:(st2 + 1) * 512], in_=pB[:, :])
        sch.g('scalar', fn, reads=['l1_pB'] + ([('x1c', 1)] if st2 == 0 else []),
              writes=['ivb'], tag=f"{pre}.ibce{st2}")

    for c in range(NCH):
        ev = 'vector'
        acc = b['acc']

        def fn(eng, c=c):
            return eng.tensor_scalar_mul(acc[:, 8:T], df(c)[:, 0:T - 8], cc(b, TAPC[0]))
        sch.g(ev, fn, reads=[('df', c), 'consts', ('x1c', 0)], writes=['acc'], tag=f"{pre}.t0.{c}")
        for k in range(1, 8):
            def fn(eng, c=c, k=k):
                return eng.scalar_tensor_tensor(acc[:, 8:T], df(c)[:, k:T - 8 + k], cc(b, TAPC[k]),
                                                acc[:, 8:T], op0=ALU.mult, op1=ALU.add)
            sch.g(ev, fn, reads=[('df', c), 'acc', 'consts'], writes=['acc'], tag=f"{pre}.t{k}.{c}")

        def fn(eng, c=c):
            return eng.transpose(pV[0:8, 0:128], df(c)[:, 0:8], b['id128'][:, :])
        sch.g('tensor', fn, reads=[('df', c), 'id128'], writes=['l1_pV'], tag=f"{pre}.ut{c}")

        def fn(eng, c=c):
            return eng.tensor_copy(b['Usm'][0:8, :], pV[0:8, 0:128])
        sch.g('vector', fn, reads=['l1_pV'], writes=['Usm'], tag=f"{pre}.ute{c}")

        def fn(eng, c=c):
            return eng.matmul(pV[0:128, 0:8], b['Usm'][0:8, :], b['Cmat'][0:8, 0:8],
                              start=True, stop=True)
        sch.g('tensor', fn, reads=['Usm', 'Cmat'], writes=['l1_pV'], tag=f"{pre}.usm{c}")

        def fn(eng, c=c):
            return eng.tensor_copy(acc[:, 0:8], pV[0:128, 0:8])
        sch.g('vector', fn, reads=['l1_pV'], writes=['acc'], tag=f"{pre}.usme{c}")

        traj_store(sch, c)



def emit_tree(sch, b, out_store, pre="tf", skip=()):
    xn = b['xnbf']
    win, wout = b['ti_win'], b['ti_wout']
    ac, dc, ind, mT = b['ti_ac'], b['ti_dc'], b['ti_in'], b['ti_mT']
    p0, p1, ptb = b['tf_p0'], b['tf_p1'], b['tf_ptb']

    for q in range(4):
        for ti in range(4):
            tt = q * 4 + ti
            for ws in range(4):
                ps, psk = (p0, 'tf_p0') if ws % 2 == 0 else (p1, 'tf_p1')

                def fn(eng, tt=tt, ws=ws, ps=ps):
                    for k in range(NCH):
                        eng.matmul(ps[:, :], xn(k)[:, tt * 128:(tt + 1) * 128],
                                   win[:, k * T + ws * 512:k * T + (ws + 1) * 512],
                                   start=(k == 0), stop=False)
                    return eng.matmul(ps[:, :], b['onr'][0:1, :],
                                      b['ti_binr'][0:1, ws * 512:(ws + 1) * 512],
                                      start=False, stop=True)
                sch.g('tensor', fn, reads=['ti_win', 'ti_binr', 'ones'] + [('xnbf', k) for k in range(NCH)],
                      writes=[psk], tag=f"{pre}.lg{tt}.{ws}")

                def fn(eng, ti=ti, ws=ws, ps=ps):
                    return eng.activation(out=ac[:, ti * T + ws * 512:ti * T + (ws + 1) * 512],
                                          in_=ps[:, :], func=AF.Silu)
                sch.g('scalar', fn, reads=[psk], writes=[('ac', ti)], tag=f"{pre}.si{tt}.{ws}")

                def fn(eng, ti=ti, ws=ws, ps=ps, bb=b):
                    return eng.tensor_single_scalar(dc[:, ti * T + ws * 512:ti * T + (ws + 1) * 512],
                                                    ps[:, :], cc(bb, 0.0), op=ALU.is_gt)
                sch.g('vector', fn, reads=[psk, 'consts', ('ac', ti)], writes=[('dc', ti)], tag=f"{pre}.dc{tt}.{ws}")

        for ti in range(4):
            tt = q * 4 + ti
            io = ti * T
            ind3 = ind[:, io:io + 2040].rearrange("p (e n) -> p e n", n=255)
            dc3 = dc[:, io:io + 2040].rearrange("p (e n) -> p e n", n=255)

            if 'iz' in skip:
                pass
            else:
                def fn(eng, ti=ti, io=io):
                    return eng.memset(ind[:, io:io + T], 0.0)
                sch.g('vector', fn, reads=[], writes=[('ind', ti)], tag=f"{pre}.iz{tt}")

            if 'iz' not in skip:
                def fn(eng, ti=ti, io=io):
                    return eng.memset(ind[:, io:io + 2040].rearrange("p (e n) -> p e n", n=255)[:, :, 0:1], 1.0)
                sch.g('vector', fn, reads=[], writes=[('ind', ti)], tag=f"{pre}.iz2{tt}")

            for d in (() if 'walk' in skip else range(7)):
                a0 = (1 << d) - 1
                a1 = (1 << (d + 1)) - 1
                njd = 1 << d
                par = ind3[:, :, a0:a1]
                decp = dc3[:, :, a0:a1]
                chpair = ind3[:, :, a1:a1 + 2 * njd].rearrange("p e (i two) -> p e i two", two=2)
                right = chpair[:, :, :, 1:2].rearrange("p e i two -> p e (i two)")
                left = chpair[:, :, :, 0:1].rearrange("p e i two -> p e (i two)")

                def fn(eng, par=par, decp=decp, right=right):
                    return eng.tensor_mul(right, par, decp)
                sch.g('vector', fn, reads=[('ind', ti), ('dc', ti)], writes=[('ind', ti)],
                      tag=f"{pre}.wka{tt}.{d}")

                def fn(eng, par=par, right=right, left=left):
                    return eng.tensor_sub(left, par, right)
                sch.g('vector', fn, reads=[('ind', ti)], writes=[('ind', ti)],
                      tag=f"{pre}.wkb{tt}.{d}")

            if 'mk' not in skip:
                def fn(eng, ti=ti, io=io):
                    return eng.tensor_mul(ac[:, io:io + T], ac[:, io:io + T], ind[:, io:io + T])
                sch.g('vector', fn, reads=[('ac', ti), ('ind', ti)], writes=[('ac', ti)],
                      tag=f"{pre}.mk{tt}")

            for wc in (() if 'tr' in skip else range(16)):
                def fn(eng, ti=ti, io=io, wc=wc):
                    return eng.transpose(ptb[:, :], ac[:, io + wc * 128:io + (wc + 1) * 128],
                                         b['idbf'][:, :])
                sch.g('tensor', fn, reads=[('ac', ti), 'idbf'], writes=['tf_ptb'],
                      tag=f"{pre}.tr{tt}.{wc}")

                def fn(eng, ti=ti, wc=wc):
                    return eng.tensor_copy(mT[:, wc * 512 + ti * 128:wc * 512 + (ti + 1) * 128],
                                           ptb[:, :])
                sch.g('vector', fn, reads=['tf_ptb'], writes=[('mT', wc)], tag=f"{pre}.te{tt}.{wc}")

        for oc in range(NCH):
            ps, psk = (p0, 'tf_p0') if oc % 2 == 0 else (p1, 'tf_p1')

            def fn(eng, q=q, oc=oc, ps=ps):
                last = None
                for wc in range(16):
                    last = eng.matmul(ps[:, :], wout[:, wc * 1024 + oc * 128:wc * 1024 + (oc + 1) * 128],
                                      mT[:, wc * 512:(wc + 1) * 512],
                                      start=(wc == 0), stop=(wc == 15))
                return last
            sch.g('tensor', fn, reads=['ti_wout'] + [('mT', wc) for wc in range(16)],
                  writes=[psk], tag=f"{pre}.out{q}.{oc}")

            out_store(sch, q, oc, ps, psk)



def emit_attn0_v2(sch, b, wload_chunk, pre="a0"):
    xn = b['xnbf']
    kT, v, qt, sc, at = b['kT'], b['v'], b['qt'], b['sc'], b['at']
    PS = b['PS']
    wc = b['wc']
    masks = b['masks']

    def fn(eng):
        return eng.activation(out=b['lng'][:, :], in_=b['gcols'][:, :], func=AF.Ln)
    sch.g('scalar', fn, reads=['gcols'], writes=['lng'], tag=f"{pre}.lng")

    for tt in range(4):
        for k in range(NCH):
            wload_chunk(sch, 'WkT', k, k % 2)

            def fn(eng, tt=tt, k=k):
                last = None
                for c in range(NCH):
                    last = eng.matmul(PS[c][:, :], wc[:, (k % 2) * 1024 + c * 128:(k % 2) * 1024 + (c + 1) * 128],
                                      xn(k)[:, tt * 512:(tt + 1) * 512],
                                      start=(k == 0), stop=False)
                return last
            sch.g('tensor', fn, reads=[('wc', k % 2)] + [('xnbf', k)],
                  writes=[('P', c) for c in range(NCH)], tag=f"{pre}.kp{tt}.{k}")

        def fn(eng, tt=tt):
            last = None
            for c in range(NCH):
                last = eng.matmul(PS[c][:, :], b['bk_row'][0:1, c * 128:(c + 1) * 128],
                                  b['on512b'][0:1, :], start=False, stop=True)
            return last
        sch.g('tensor', fn, reads=['bk_row', 'on512b'], writes=[('P', c) for c in range(NCH)],
              tag=f"{pre}.kb{tt}")

        for c in range(NCH):
            def fn(eng, c=c, tt=tt):
                return eng.copy(out=kT[:, c * T + tt * 512:c * T + (tt + 1) * 512], in_=PS[c][:, :])
            sch.g('scalar', fn, reads=[('P', c)], writes=[('kT', c)], tag=f"{pre}.ke{tt}.{c}")

    for grp in range(4):
        for k in range(NCH):
            wload_chunk(sch, 'WvT', k, k % 2)

            def fn(eng, grp=grp, k=k):
                last = None
                for j in range(4):
                    jt = grp * 4 + j
                    for h in range(2):
                        last = eng.matmul(PS[j * 2 + h][:, :], xn(k)[:, jt * 128:(jt + 1) * 128],
                                          wc[:, (k % 2) * 1024 + h * 512:(k % 2) * 1024 + (h + 1) * 512],
                                          start=(k == 0), stop=False)
                return last
            sch.g('tensor', fn, reads=[('wc', k % 2), ('xnbf', k)],
                  writes=[('P', i) for i in range(8)], tag=f"{pre}.vp{grp}.{k}")

        def fn(eng, grp=grp):
            last = None
            for j in range(4):
                for h in range(2):
                    last = eng.matmul(PS[j * 2 + h][:, :], b['onrb'][0:1, :],
                                      b['bv_row'][0:1, h * 512:(h + 1) * 512], start=False, stop=True)
            return last
        sch.g('tensor', fn, reads=['onrb', 'bv_row'], writes=[('P', i) for i in range(8)],
              tag=f"{pre}.vb{grp}")

        for j in range(4):
            jt = grp * 4 + j
            for h in range(2):
                def fn(eng, jt=jt, j=j, h=h):
                    return eng.copy(out=v[:, jt * 1024 + h * 512:jt * 1024 + (h + 1) * 512],
                                    in_=PS[j * 2 + h][:, :])
                sch.g('scalar', fn, reads=[('P', j * 2 + h)], writes=[('v', jt)],
                      tag=f"{pre}.ve{jt}.{h}")

    for t in range(4):
        nbj = 4 * t + 4
        for k in range(NCH):
            wload_chunk(sch, 'WqT', k, k % 2)

            def fn(eng, t=t, k=k):
                last = None
                for c in range(NCH):
                    last = eng.matmul(PS[c][:, :], wc[:, (k % 2) * 1024 + c * 128:(k % 2) * 1024 + (c + 1) * 128],
                                      xn(k)[:, t * 512:(t + 1) * 512],
                                      start=(k == 0), stop=False)
                return last
            sch.g('tensor', fn, reads=[('wc', k % 2), ('xnbf', k)],
                  writes=[('P', c) for c in range(NCH)], tag=f"{pre}.qp{t}.{k}")

        def fn(eng, t=t):
            last = None
            for c in range(NCH):
                last = eng.matmul(PS[c][:, :], b['bq_row'][0:1, c * 128:(c + 1) * 128],
                                  b['on512b'][0:1, :], start=False, stop=True)
            return last
        sch.g('tensor', fn, reads=['bq_row', 'on512b'], writes=[('P', c) for c in range(NCH)],
              tag=f"{pre}.qb{t}")

        for c in range(NCH):
            def fn(eng, c=c):
                return eng.copy(out=qt[:, c * 512:(c + 1) * 512], in_=PS[c][:, :])
            sch.g('scalar', fn, reads=[('P', c)], writes=[('qt', c)], tag=f"{pre}.qe{t}.{c}")

        for c in range(NCH):
            for w0 in range(0, nbj, 4):
                wave = range(w0, min(w0 + 4, nbj))
                for bj in wave:
                    sA, sB = PS[bj % 3], PS[3 + bj % 3]
                    kA = ('P', bj % 3)
                    kB = ('P', 3 + bj % 3)
                    sl = bj % 4

                    def fn(eng, c=c, bj=bj, sA=sA, sB=sB):
                        eng.matmul(sA[:, :], kT[0:64, c * T + bj * 128:c * T + (bj + 1) * 128],
                                   qt[0:64, c * 512:(c + 1) * 512], start=True, stop=True,
                                   tile_position=(0, 0))
                        return eng.matmul(sB[:, :], kT[64:128, c * T + bj * 128:c * T + (bj + 1) * 128],
                                          qt[64:128, c * 512:(c + 1) * 512], start=True, stop=True,
                                          tile_position=(64, 0))
                    sch.g('tensor', fn, reads=[('kT', c), ('qt', c)], writes=[kA, kB],
                          tag=f"{pre}.sc{t}.{c}.{bj}")

                    r = bj - 4 * t
                    if r >= 0:
                        def fn(eng, sl=sl, r=r, sA=sA, sB=sB):
                            eng.tensor_mul(sc[:, (2 * sl) * 512:(2 * sl + 1) * 512], sA[:, :],
                                           masks[:, r * 512:(r + 1) * 512])
                            return eng.tensor_mul(sc[:, (2 * sl + 1) * 512:(2 * sl + 2) * 512], sB[:, :],
                                                  masks[:, r * 512:(r + 1) * 512])
                        sch.g('vector', fn, reads=[kA, kB, 'masks'],
                              writes=[('scb', sl)], tag=f"{pre}.se{t}.{c}.{bj}")
                    else:
                        def fn(eng, sl=sl, sA=sA, sB=sB):
                            eng.copy(out=sc[:, (2 * sl) * 512:(2 * sl + 1) * 512], in_=sA[:, :])
                            return eng.copy(out=sc[:, (2 * sl + 1) * 512:(2 * sl + 2) * 512], in_=sB[:, :])
                        sch.g('scalar', fn, reads=[kA, kB], writes=[('scb', sl)],
                              tag=f"{pre}.se{t}.{c}.{bj}")

                for bj in wave:
                    sl = bj % 4

                    def fn(eng, c=c, bj=bj, sl=sl, nbj=nbj):
                        eng.matmul(PS[6][0:64, :], v[:, bj * 1024 + c * 128:bj * 1024 + c * 128 + 64],
                                   sc[:, (2 * sl) * 512:(2 * sl + 1) * 512],
                                   start=(bj == 0), stop=(bj == nbj - 1), tile_position=(0, 0))
                        return eng.matmul(PS[6][64:128, :], v[:, bj * 1024 + c * 128 + 64:bj * 1024 + (c + 1) * 128],
                                          sc[:, (2 * sl + 1) * 512:(2 * sl + 2) * 512],
                                          start=(bj == 0), stop=(bj == nbj - 1), tile_position=(0, 64))
                    sch.g('tensor', fn, reads=[('scb', sl), ('v', bj)], writes=[('P', 6)],
                          tag=f"{pre}.av{t}.{c}.{bj}")

            def fn(eng, c=c, t=t):
                return eng.tensor_scalar_mul(b['dkb'][:, 0:1], b['lng'][:, c:c + 1],
                                             cc(b, float(512 * t)))
            sch.g('vector', fn, reads=['lng', 'consts'], writes=['dkb'], tag=f"{pre}.dkb{t}.{c}")

            def fn(eng, c=c, t=t):
                return eng.activation(out=b['dk'][:, :], in_=b['iota512'][:, :],
                                      func=AF.Exp, scale=b['lng'][:, c:c + 1],
                                      bias=b['dkb'][:, 0:1])
            sch.g('scalar', fn, reads=['lng', 'iota512', 'dkb'], writes=['dk'], tag=f"{pre}.dk{t}.{c}")

            def fn(eng, c=c):
                return eng.tensor_mul(at[:, c * 512:(c + 1) * 512], PS[6][:, :], b['dk'][:, :])
            sch.g('vector', fn, reads=[('P', 6), 'dk'], writes=[('at', c)], tag=f"{pre}.ae{t}.{c}")

        for k in range(NCH):
            wload_chunk(sch, 'WoT', k, k % 2)

            def fn(eng, t=t, k=k):
                last = None
                for o in range(NCH):
                    last = eng.matmul(PS[o][:, :], wc[:, (k % 2) * 1024 + o * 128:(k % 2) * 1024 + (o + 1) * 128],
                                      at[:, k * 512:(k + 1) * 512],
                                      start=(k == 0), stop=(k == NCH - 1))
                return last
            sch.g('tensor', fn, reads=[('wc', k % 2), ('at', k)],
                  writes=[('P', o) for o in range(NCH)], tag=f"{pre}.wo{t}.{k}")

        for o in range(NCH):
            def fn(eng, o=o, t=t):
                Rs = b['R'](o)[:, t * 512:(t + 1) * 512]
                return eng.scalar_tensor_tensor(out=Rs, in0=PS[o][:, :],
                                                scalar=b['bo_cols'][:, o:o + 1],
                                                in1=Rs, op0=ALU.add, op1=ALU.add)
            sch.g('vector', fn, reads=[('P', o), 'bo_cols', ('R', o)], writes=[('R', o)],
                  tag=f"{pre}.x1.{o}.{t}")



def emit_shared_v2(sch, b, ld_sw, ld_sw3c, pre="sh"):
    xn = b['xnbf']
    sw12, gg, sa, wc3 = b['sw12'], b['gg'], b['sa'], b['wc3']
    PS = b['PS']

    for th in range(2):
        to = th * 1024
        for ih in range(2):
            ld_sw(sch, 'sw1T', ih, 0)
            ld_sw(sch, 'sw2T', ih, 1)
            for ic8 in range(8):
                ic = ih * 8 + ic8
                for tt in range(2):
                    base = (ic8 % 2) * 4
                    pa, pb = PS[base + tt * 2], PS[base + tt * 2 + 1]
                    ka, kb = ('P', base + tt * 2), ('P', base + tt * 2 + 1)

                    def fn(eng, ic8=ic8, tt=tt, to=to, pa=pa):
                        last = None
                        for k in range(NCH):
                            last = eng.matmul(pa[:, :], sw12[:, k * 1024 + ic8 * 128:k * 1024 + (ic8 + 1) * 128],
                                              xn(k)[:, to + tt * 512:to + (tt + 1) * 512],
                                              start=(k == 0), stop=(k == NCH - 1))
                        return last
                    sch.g('tensor', fn, reads=[('sw12', 0)] + [('xnbf', k) for k in range(NCH)],
                          writes=[ka], tag=f"{pre}.ua{th}.{ic}.{tt}")

                    def fn(eng, tt=tt, pa=pa):
                        return eng.activation(out=sa[:, tt * 512:(tt + 1) * 512], in_=pa[:, :], func=AF.Silu)
                    sch.g('scalar', fn, reads=[ka], writes=[('sa', tt)], tag=f"{pre}.uae{th}.{ic}.{tt}")

                    def fn(eng, ic8=ic8, tt=tt, to=to, pb=pb):
                        last = None
                        for k in range(NCH):
                            last = eng.matmul(pb[:, :], sw12[:, 8192 + k * 1024 + ic8 * 128:8192 + k * 1024 + (ic8 + 1) * 128],
                                              xn(k)[:, to + tt * 512:to + (tt + 1) * 512],
                                              start=(k == 0), stop=(k == NCH - 1))
                        return last
                    sch.g('tensor', fn, reads=[('sw12', 1)] + [('xnbf', k) for k in range(NCH)],
                          writes=[kb], tag=f"{pre}.ub{th}.{ic}.{tt}")

                    def fn(eng, ic=ic, tt=tt, pb=pb):
                        return eng.tensor_mul(gg[:, ic * 1024 + tt * 512:ic * 1024 + (tt + 1) * 512],
                                              pb[:, :], sa[:, tt * 512:(tt + 1) * 512])
                    sch.g('vector', fn, reads=[kb, ('sa', tt)], writes=[('gg', ic)],
                          tag=f"{pre}.gg{th}.{ic}.{tt}")

        for tt in range(2):
            for wcI in range(16):
                ld_sw3c(sch, wcI, wcI % 4)

                def fn(eng, wcI=wcI, tt=tt):
                    last = None
                    for oc in range(NCH):
                        last = eng.matmul(PS[oc][:, :],
                                          wc3[:, (wcI % 4) * 1024 + oc * 128:(wcI % 4) * 1024 + (oc + 1) * 128],
                                          gg[:, wcI * 1024 + tt * 512:wcI * 1024 + (tt + 1) * 512],
                                          start=(wcI == 0), stop=(wcI == 15))
                    return last
                sch.g('tensor', fn, reads=[('wc3', wcI % 4), ('gg', wcI)],
                      writes=[('P', oc) for oc in range(NCH)], tag=f"{pre}.z{th}.{tt}.{wcI}")

            for oc in range(NCH):
                def fn(eng, oc=oc, th=th, tt=tt):
                    Rs = b['R'](oc)[:, th * 1024 + tt * 512:th * 1024 + (tt + 1) * 512]
                    return eng.tensor_add(Rs, Rs, PS[oc][:, :])
                sch.g('vector', fn, reads=[('P', oc), ('R', oc)], writes=[('R', oc)],
                      tag=f"{pre}.ze{th}.{tt}.{oc}")



def emit_l1attn_v2(sch, b, pre="l1"):
    ND = T - 1
    xn = b['xnbf']
    df = b['df']
    wsq = b['wsq']
    PS = b['PS']
    ptb = b['l1_ptb']
    l1c = b['l1c']
    rwb, rwc = b['rwb'], b['rwc']
    onc, onr = b['onc'], b['onr']
    pd, pV, pB, pc = PS[5], PS[6], PS[7], PS[3]
    kpd, kpV, kpB, kpc = ('P', 5), ('P', 6), ('P', 7), ('P', 3)
    rslots = [(PS[0], 0, ('P', 0)), (PS[0], 32, ('P', 0)), (PS[0], 64, ('P', 0)),
              (PS[1], 0, ('P', 1))]

    for c in range(NCH):
        def fn(eng, c=c):
            eng.tensor_sub(df(c)[:, 0:ND], xn(c)[:, 1:T], xn(c)[:, 0:ND])
            return eng.memset(df(c)[:, ND:T], 0.0)
        sch.g('vector', fn, reads=[('xnbf', c)], writes=[('df', c)], tag=f"{pre}.df{c}")

    for gc in range(4):
        def fn(eng, gc=gc):
            return eng.activation(out=wsq[:, 0:1024], in_=b['Gbf'][:, gc * 1024:(gc + 1) * 1024],
                                  func=AF.Square)
        sch.g('scalar', fn, reads=['Gbf'], writes=['l1sq'], tag=f"{pre}.gsq{gc}")

        def fn(eng, gc=gc):
            return eng.reduce_sum(l1c[:, 48 + gc:49 + gc], wsq[:, 0:1024], axis=AX.X)
        sch.g('vector', fn, reads=['l1sq'], writes=[('gn', gc)], tag=f"{pre}.gss{gc}")

    def fn(eng):
        return eng.tensor_scalar_max(l1c[:, 48:52], l1c[:, 48:52], cc(b, 1e-16))
    sch.g('vector', fn, reads=[('gn', g) for g in range(4)] + ['consts'], writes=['gnc'], tag=f"{pre}.gcl")

    def fn(eng):
        return eng.activation(out=l1c[:, 52:56], in_=l1c[:, 48:52], func=AF.Ln)
    sch.g('scalar', fn, reads=['gnc'], writes=['gnl'], tag=f"{pre}.gln")

    def fn(eng):
        return eng.activation(out=l1c[:, 48:52], in_=l1c[:, 52:56], func=AF.Exp, scale=cc(b, -0.5))
    sch.g('scalar', fn, reads=['gnl', 'consts'], writes=['gni'], tag=f"{pre}.gexp")

    for gc in range(4):
        def fn(eng, gc=gc):
            return eng.transpose(pV[0:1, 0:128], l1c[:, 48 + gc:49 + gc], b['id128'][:, :])
        sch.g('tensor', fn, reads=['gni', 'id128'], writes=[kpV], tag=f"{pre}.gtr{gc}")

        def fn(eng, gc=gc):
            return eng.tensor_copy(rwc[0:1, gc * 128:(gc + 1) * 128], pV[0:1, 0:128])
        sch.g('vector', fn, reads=[kpV], writes=['gnr'], tag=f"{pre}.gte{gc}")

    def fn(eng):
        return eng.matmul(pB[:, :], onr[0:1, :], rwc[0:1, 0:512], start=True, stop=True)
    sch.g('tensor', fn, reads=['gnr', 'ones'], writes=[kpB], tag=f"{pre}.gbc")

    def fn(eng):
        return eng.copy(out=b['invgn'][:, :], in_=pB[:, :])
    sch.g('scalar', fn, reads=[kpB], writes=['invgn'], tag=f"{pre}.gbce")

    for c in range(NCH):
        def fn(eng, c=c):
            return eng.activation(out=wsq[:, :], in_=df(c)[:, :], func=AF.Square)
        sch.g('scalar', fn, reads=[('df', c), 'gnr'], writes=['l1sq'], tag=f"{pre}.sq{c}")

        def fn(eng, c=c):
            last = None
            for st in range(16):
                last = eng.matmul(pc[:, st:st + 1], wsq[:, st * 128:(st + 1) * 128], onc[:, :],
                                  start=(c == 0), stop=(c == NCH - 1))
            return last
        sch.g('tensor', fn, reads=['l1sq', 'ones'], writes=[kpc], tag=f"{pre}.cmm{c}")

        def fn(eng, c=c):
            last = None
            for t4 in range(4):
                ps, r, _ = rslots[t4]
                last = eng.matmul(ps[r:r + 1, :], onc[:, :], wsq[:, t4 * 512:(t4 + 1) * 512],
                                  start=(c == 0), stop=(c == NCH - 1))
            return last
        sch.g('tensor', fn, reads=['l1sq', 'ones'], writes=[('P', 0), ('P', 1)], tag=f"{pre}.rmm{c}")

    def fn(eng):
        return eng.tensor_copy(l1c[:, 0:16], pc[:, 0:16])
    sch.g('vector', fn, reads=[kpc], writes=['scol'], tag=f"{pre}.cev")

    def fn(eng):
        last = None
        for t4 in range(4):
            ps, r, _ = rslots[t4]
            last = eng.copy(out=rwb[0:1, t4 * 512:(t4 + 1) * 512], in_=ps[r:r + 1, :])
        return last
    sch.g('scalar', fn, reads=[('P', 0), ('P', 1)], writes=['rwb'], tag=f"{pre}.rev")

    def fn(eng):
        return eng.tensor_scalar_max(l1c[:, 0:16], l1c[:, 0:16], cc(b, 1e-12))
    sch.g('vector', fn, reads=['scol', 'consts'], writes=['scol'], tag=f"{pre}.ccl")

    def fn(eng):
        return eng.activation(out=l1c[:, 16:32], in_=l1c[:, 0:16], func=AF.Ln)
    sch.g('scalar', fn, reads=['scol'], writes=['lnc'], tag=f"{pre}.cln")

    def fn(eng):
        eng.activation(out=l1c[:, 0:16], in_=l1c[:, 16:32], func=AF.Exp, scale=cc(b, -0.5))
        return eng.activation(out=l1c[:, 32:48], in_=l1c[:, 16:32], func=AF.Exp, scale=cc(b, 0.5))
    sch.g('scalar', fn, reads=['lnc', 'consts'], writes=['invmc', 'magc'], tag=f"{pre}.cexp")

    def fn(eng):
        return eng.tensor_mul(l1c[:, 16:32], l1c[:, 0:16], l1c[:, 32:48])
    sch.g('vector', fn, reads=['invmc', 'magc'], writes=['anc'], tag=f"{pre}.can")

    def fn(eng):
        return eng.tensor_scalar_max(l1c[:, 16:32], l1c[:, 16:32], cc(b, 1e-8))
    sch.g('vector', fn, reads=['anc', 'consts'], writes=['anc'], tag=f"{pre}.crc1")

    def fn(eng):
        return eng.reciprocal(l1c[:, 16:32], l1c[:, 16:32])
    sch.g('vector', fn, reads=['anc'], writes=['anc'], tag=f"{pre}.crc2")

    def fn(eng):
        return eng.scalar_tensor_tensor(l1c[:, 16:32], l1c[:, 16:32], cc(b, 2.0), l1c[:, 0:16],
                                        op0=ALU.mult, op1=ALU.mult)
    sch.g('vector', fn, reads=['anc', 'invmc', 'consts'], writes=['rsc'], tag=f"{pre}.crs")

    def fn(eng):
        return eng.activation(out=l1c[:, 32:48], in_=l1c[:, 32:48], func=AF.Tanh, scale=cc(b, 2.0))
    sch.g('scalar', fn, reads=['magc', 'consts'], writes=['tnc'], tag=f"{pre}.ctn")

    def fn(eng):
        return eng.tensor_scalar_max(rwb[0:1, :], rwb[0:1, :], cc(b, 1e-12, 1))
    sch.g('vector', fn, reads=['rwb', 'consts'], writes=['rwb'], tag=f"{pre}.rcl")

    def fn(eng):
        return eng.activation(out=rwb[0:1, :], in_=rwb[0:1, :], func=AF.Ln)
    sch.g('scalar', fn, reads=['rwb'], writes=['rwb'], tag=f"{pre}.rln")

    def fn(eng):
        return eng.activation(out=rwc[0:1, :], in_=rwb[0:1, :], func=AF.Exp, scale=cc(b, 0.5, 1))
    sch.g('scalar', fn, reads=['rwb', 'consts', 'gnr'], writes=['rwc'], tag=f"{pre}.rmag")

    def fn(eng):
        return eng.activation(out=rwc[0:1, :], in_=rwc[0:1, :], func=AF.Tanh, scale=cc(b, 2.0, 1))
    sch.g('scalar', fn, reads=['rwc', 'consts'], writes=['rwc'], tag=f"{pre}.rtn")

    def fn(eng):
        return eng.activation(out=rwb[0:1, :], in_=rwb[0:1, :], func=AF.Exp, scale=cc(b, -0.5, 1))
    sch.g('scalar', fn, reads=['rwb', 'consts'], writes=['rwb'], tag=f"{pre}.rinv")

    def fn(eng):
        return eng.scalar_tensor_tensor(rwb[0:1, :], rwc[0:1, :], cc(b, 0.6, 1), rwb[0:1, :],
                                        op0=ALU.mult, op1=ALU.mult)
    sch.g('vector', fn, reads=['rwb', 'rwc', 'consts'], writes=['rwb'], tag=f"{pre}.rap")

    for st in range(16):
        def fn(eng, st=st):
            last = None
            for k in range(NCH):
                last = eng.matmul(pd[:, :], df(k)[:, st * 128:(st + 1) * 128],
                                  b['GTbf'][:, k * 512:(k + 1) * 512],
                                  start=(k == 0), stop=(k == NCH - 1))
            return last
        sch.g('tensor', fn, reads=['GTbf'] + [('df', k) for k in range(NCH)],
              writes=[kpd], tag=f"{pre}.dot{st}")

        def fn(eng, st=st):
            return eng.scalar_tensor_tensor(b['sims'][:, :], pd[:, :], l1c[:, 16 + st:17 + st],
                                            b['invgn'][:, :], op0=ALU.mult, op1=ALU.mult)
        sch.g('vector', fn, reads=[kpd, 'rsc', 'invgn'], writes=['sims'], tag=f"{pre}.sim{st}")

        def fn(eng, st=st):
            return eng.reduce_max(b['smx'][:, 0:1], b['sims'][:, :], axis=AX.X, negate=True)
        sch.g('vector', fn, reads=['sims'], writes=['smx'], tag=f"{pre}.smx{st}")

        def fn(eng, st=st):
            return eng.activation(out=b['sims'][:, :], in_=b['sims'][:, :], func=AF.Exp,
                                  bias=b['smx'][:, 0:1])
        sch.g('scalar', fn, reads=['sims', 'smx'], writes=['sims'], tag=f"{pre}.sex{st}")

        def fn(eng, st=st):
            return eng.reduce_sum(b['smx'][:, 1:2], b['sims'][:, :], axis=AX.X)
        sch.g('vector', fn, reads=['sims'], writes=['smx2'], tag=f"{pre}.ssm{st}")

        def fn(eng, st=st):
            return eng.reciprocal(b['smx'][:, 2:3], b['smx'][:, 1:2])
        sch.g('vector', fn, reads=['smx2'], writes=['smx3'], tag=f"{pre}.src{st}")

        def fn(eng, st=st):
            return eng.scalar_tensor_tensor(b['smx'][:, 3:4], b['smx'][:, 2:3], cc(b, 0.4),
                                            l1c[:, 32 + st:33 + st], op0=ALU.mult, op1=ALU.mult)
        sch.g('vector', fn, reads=['smx3', 'tnc', 'consts'], writes=['smx4'], tag=f"{pre}.snr{st}")

        def fn(eng, st=st):
            return eng.tensor_scalar(b['gwt'][:, :], b['sims'][:, :], b['smx'][:, 3:4], None,
                                     op0=ALU.mult)
        sch.g('vector', fn, reads=['sims', 'smx4'], writes=['gwt'], tag=f"{pre}.gw{st}")

        for j in range(4):
            def fn(eng, st=st, j=j):
                return eng.transpose(ptb[:, :], b['gwt'][:, j * 128:(j + 1) * 128], b['idbf'][:, :])
            sch.g('tensor', fn, reads=['gwt', 'idbf'], writes=['l1_ptb'], tag=f"{pre}.gt{st}.{j}")

            def fn(eng, st=st, j=j):
                return eng.tensor_copy(b['gwT'][:, j * T + st * 128:j * T + (st + 1) * 128], ptb[:, :])
            sch.g('vector', fn, reads=['l1_ptb'], writes=[('gwT', j)], tag=f"{pre}.gte{st}.{j}")

    for st2 in range(4):
        def fn(eng, st2=st2):
            return eng.matmul(pB[:, :], onr[0:1, :], rwb[0:1, st2 * 512:(st2 + 1) * 512],
                              start=True, stop=True)
        sch.g('tensor', fn, reads=['rwb', 'ones'], writes=[kpB], tag=f"{pre}.abc{st2}")

        def fn(eng, st2=st2):
            return eng.copy(out=b['apb'][:, :], in_=pB[:, :])
        sch.g('scalar', fn, reads=[kpB], writes=['apb'], tag=f"{pre}.abce{st2}")

        for c in range(NCH):
            pu = pd if c % 2 == 0 else pV
            puk = kpd if c % 2 == 0 else kpV

            def fn(eng, st2=st2, c=c, pu=pu):
                last = None
                for gc in range(4):
                    last = eng.matmul(pu[:, :], b['Gbf'][:, gc * 1024 + c * 128:gc * 1024 + (c + 1) * 128],
                                      b['gwT'][:, gc * T + st2 * 512:gc * T + (st2 + 1) * 512],
                                      start=(gc == 0), stop=(gc == 3))
                return last
            sch.g('tensor', fn, reads=['Gbf'] + [('gwT', gc) for gc in range(4)],
                  writes=[puk], tag=f"{pre}.inf{st2}.{c}")

            def fn(eng, st2=st2, c=c):
                return eng.tensor_mul(df(c)[:, st2 * 512:(st2 + 1) * 512],
                                      df(c)[:, st2 * 512:(st2 + 1) * 512], b['apb'][:, :])
            sch.g('vector', fn, reads=[('df', c), 'apb'], writes=[('df', c)], tag=f"{pre}.ua{st2}.{c}")

            def fn(eng, st2=st2, c=c, pu=pu):
                return eng.tensor_add(df(c)[:, st2 * 512:(st2 + 1) * 512],
                                      df(c)[:, st2 * 512:(st2 + 1) * 512], pu[:, :])
            sch.g('vector', fn, reads=[('df', c), puk], writes=[('df', c)], tag=f"{pre}.ub{st2}.{c}")

    def fn(eng):
        return eng.tensor_scalar_mul(rwb[0:1, 8:T], rwc[0:1, 0:T - 8], cc(b, TAPC[0], 1))
    sch.g('vector', fn, reads=['rwc', 'consts'], writes=['rwb'], tag=f"{pre}.dtap0")
    for k in range(1, 8):
        def fn(eng, k=k):
            return eng.scalar_tensor_tensor(rwb[0:1, 8:T], rwc[0:1, k:T - 8 + k], cc(b, TAPC[k], 1),
                                            rwb[0:1, 8:T], op0=ALU.mult, op1=ALU.add)
        sch.g('vector', fn, reads=['rwc', 'rwb', 'consts'], writes=['rwb'], tag=f"{pre}.dtap{k}")

    def fn(eng):
        return eng.matmul(pV[0:1, 0:8], l1c[0:8, 32:33], b['Cmat'][0:8, 0:8], start=True, stop=True)
    sch.g('tensor', fn, reads=['tnc', 'Cmat'], writes=[kpV], tag=f"{pre}.dsm")

    def fn(eng):
        return eng.tensor_copy(rwb[0:1, 0:8], pV[0:1, 0:8])
    sch.g('vector', fn, reads=[kpV], writes=['rwb'], tag=f"{pre}.dsme")

    def fn(eng):
        return eng.tensor_scalar_max(rwb[0:1, :], rwb[0:1, :], cc(b, 1e-8, 1))
    sch.g('vector', fn, reads=['rwb', 'consts'], writes=['rwb'], tag=f"{pre}.dcl")

    def fn(eng):
        return eng.activation(out=rwb[0:1, :], in_=rwb[0:1, :], func=AF.Ln)
    sch.g('scalar', fn, reads=['rwb'], writes=['rwb'], tag=f"{pre}.dinv1")

    def fn(eng):
        return eng.activation(out=rwb[0:1, :], in_=rwb[0:1, :], func=AF.Exp, scale=cc(b, -1.0, 1))
    sch.g('scalar', fn, reads=['rwb', 'consts'], writes=['rwb'], tag=f"{pre}.dinv2")

    for st2 in range(4):
        def fn(eng, st2=st2):
            return eng.matmul(pB[:, :], onr[0:1, :], rwb[0:1, st2 * 512:(st2 + 1) * 512],
                              start=True, stop=True)
        sch.g('tensor', fn, reads=['rwb', 'ones'], writes=[kpB], tag=f"{pre}.ibc{st2}")

        def fn(eng, st2=st2):
            return eng.copy(out=b['ivb'][:, st2 * 512:(st2 + 1) * 512], in_=pB[:, :])
        sch.g('scalar', fn, reads=[kpB], writes=['ivb'], tag=f"{pre}.ibce{st2}")

    for c in range(NCH):
        acc = b['accb']

        def fn(eng, c=c):
            return eng.tensor_scalar_mul(acc[:, 8:T], df(c)[:, 0:T - 8], cc(b, TAPC[0]))
        sch.g('vector', fn, reads=[('df', c), 'consts'], writes=['acc'],
              tag=f"{pre}.t0.{c}")
        for k in range(1, 8):
            def fn(eng, c=c, k=k):
                return eng.scalar_tensor_tensor(acc[:, 8:T], df(c)[:, k:T - 8 + k], cc(b, TAPC[k]),
                                                acc[:, 8:T], op0=ALU.mult, op1=ALU.add)
            sch.g('vector', fn, reads=[('df', c), 'acc', 'consts'], writes=['acc'], tag=f"{pre}.t{k}.{c}")

        def fn(eng, c=c):
            return eng.transpose(ptb[0:8, 0:128], df(c)[:, 0:8], b['idbf'][:, :])
        sch.g('tensor', fn, reads=[('df', c), 'idbf'], writes=['l1_ptb'], tag=f"{pre}.ut{c}")

        def fn(eng, c=c):
            return eng.tensor_copy(b['Usm'][0:8, :], ptb[0:8, 0:128])
        sch.g('vector', fn, reads=['l1_ptb'], writes=['Usm'], tag=f"{pre}.ute{c}")

        def fn(eng, c=c):
            return eng.matmul(pV[0:128, 0:8], b['Usm'][0:8, :], b['Cmbf'][0:8, 0:8],
                              start=True, stop=True)
        sch.g('tensor', fn, reads=['Usm', 'Cmbf'], writes=[kpV], tag=f"{pre}.usm{c}")

        def fn(eng, c=c):
            return eng.tensor_copy(acc[:, 0:8], pV[0:128, 0:8])
        sch.g('vector', fn, reads=[kpV], writes=['acc'], tag=f"{pre}.usme{c}")

        for st2 in range(4):
            def fn(eng, c=c, st2=st2):
                return eng.tensor_mul(b['tjb'][:, :], acc[:, st2 * 512:(st2 + 1) * 512],
                                      b['ivb'][:, st2 * 512:(st2 + 1) * 512])
            sch.g('gpsimd', fn, reads=['acc', 'ivb'], writes=['tjb'], tag=f"{pre}.tj{c}.{st2}")

            def fn(eng, c=c, st2=st2):
                Rs = b['R'](c)[:, st2 * 512:(st2 + 1) * 512]
                return eng.tensor_add(Rs, Rs, b['tjb'][:, :])
            sch.g('gpsimd', fn, reads=['tjb', ('R', c)], writes=[('R', c)], tag=f"{pre}.tja{c}.{st2}")



def emit_tree_v2(sch, b, ld_wout_chunk, pre="tf"):
    xn = b['xnbf']
    win = b['ti_win']
    ac, dc, ind, mT = b['ti_ac'], b['ti_dc'], b['ti_in'], b['ti_mT']
    PS = b['PS']
    ptb = b['tf_ptb']
    wc3 = b['wc3']

    for q in range(4):
        for ti in range(4):
            tt = q * 4 + ti
            for ws in range(4):
                ps, psk = (PS[ws % 2], ('P', ws % 2))

                def fn(eng, tt=tt, ws=ws, ps=ps):
                    for k in range(NCH):
                        eng.matmul(ps[:, :], xn(k)[:, tt * 128:(tt + 1) * 128],
                                   win[:, k * T + ws * 512:k * T + (ws + 1) * 512],
                                   start=(k == 0), stop=False)
                    return eng.matmul(ps[:, :], b['onrb'][0:1, :],
                                      b['ti_binr'][0:1, ws * 512:(ws + 1) * 512],
                                      start=False, stop=True)
                sch.g('tensor', fn, reads=['ti_win', 'ti_binr', 'onrb'] + [('xnbf', k) for k in range(NCH)],
                      writes=[psk], tag=f"{pre}.lg{tt}.{ws}")

                def fn(eng, ti=ti, ws=ws, ps=ps):
                    return eng.activation(out=ac[:, ti * T + ws * 512:ti * T + (ws + 1) * 512],
                                          in_=ps[:, :], func=AF.Silu)
                sch.g('scalar', fn, reads=[psk], writes=[('ac', ti)], tag=f"{pre}.si{tt}.{ws}")

                def fn(eng, ti=ti, ws=ws, ps=ps, bb=b):
                    return eng.tensor_single_scalar(dc[:, ti * T + ws * 512:ti * T + (ws + 1) * 512],
                                                    ps[:, :], cc(bb, 0.0), op=ALU.is_gt)
                sch.g('vector', fn, reads=[psk, 'consts', ('ac', ti)], writes=[('dc', ti)],
                      tag=f"{pre}.dc{tt}.{ws}")

        for ti in range(4):
            tt = q * 4 + ti
            io = ti * T
            ind3 = ind[:, io:io + 2040].rearrange("p (e n) -> p e n", n=255)
            dc3 = dc[:, io:io + 2040].rearrange("p (e n) -> p e n", n=255)

            def fn(eng, ti=ti, io=io):
                return eng.memset(ind[:, io:io + T], 0.0)
            sch.g('gpsimd', fn, reads=[], writes=[('ind', ti)], tag=f"{pre}.iz{tt}")

            def fn(eng, ti=ti, io=io):
                return eng.memset(ind[:, io:io + 2040].rearrange("p (e n) -> p e n", n=255)[:, :, 0:1], 1.0)
            sch.g('gpsimd', fn, reads=[], writes=[('ind', ti)], tag=f"{pre}.iz2{tt}")

            for d in range(7):
                a0 = (1 << d) - 1
                a1 = (1 << (d + 1)) - 1
                njd = 1 << d
                par = ind3[:, :, a0:a1]
                decp = dc3[:, :, a0:a1]
                chpair = ind3[:, :, a1:a1 + 2 * njd].rearrange("p e (i two) -> p e i two", two=2)
                right = chpair[:, :, :, 1:2].rearrange("p e i two -> p e (i two)")
                left = chpair[:, :, :, 0:1].rearrange("p e i two -> p e (i two)")

                def fn(eng, par=par, decp=decp, right=right):
                    return eng.tensor_mul(right, par, decp)
                sch.g('gpsimd', fn, reads=[('ind', ti), ('dc', ti)], writes=[('ind', ti)],
                      tag=f"{pre}.wka{tt}.{d}")

                def fn(eng, par=par, right=right, left=left):
                    return eng.tensor_sub(left, par, right)
                sch.g('gpsimd', fn, reads=[('ind', ti)], writes=[('ind', ti)],
                      tag=f"{pre}.wkb{tt}.{d}")

            def fn(eng, ti=ti, io=io):
                return eng.tensor_mul(ac[:, io:io + T], ac[:, io:io + T], ind[:, io:io + T])
            sch.g('gpsimd', fn, reads=[('ac', ti), ('ind', ti)], writes=[('ac', ti)],
                  tag=f"{pre}.mk{tt}")

            for wcn in range(16):
                def fn(eng, ti=ti, io=io, wcn=wcn):
                    return eng.transpose(ptb[:, :], ac[:, io + wcn * 128:io + (wcn + 1) * 128],
                                         b['idbf'][:, :])
                sch.g('tensor', fn, reads=[('ac', ti), 'idbf'], writes=['tf_ptb'],
                      tag=f"{pre}.tr{tt}.{wcn}")

                def fn(eng, ti=ti, wcn=wcn):
                    return eng.copy(out=mT[:, wcn * 512 + ti * 128:wcn * 512 + (ti + 1) * 128],
                                    in_=ptb[:, :])
                sch.g('scalar', fn, reads=['tf_ptb'], writes=[('mT', wcn)], tag=f"{pre}.te{tt}.{wcn}")

        for oh in range(2):
            for wcn in range(16):
                ld_wout_chunk(sch, oh, wcn, wcn % 4)

                def fn(eng, q=q, oh=oh, wcn=wcn):
                    last = None
                    for o4 in range(4):
                        oc = oh * 4 + o4
                        last = eng.matmul(PS[o4][:, :],
                                          wc3[:, (wcn % 4) * 512 + o4 * 128:(wcn % 4) * 512 + (o4 + 1) * 128],
                                          mT[:, wcn * 512:(wcn + 1) * 512],
                                          start=(wcn == 0), stop=(wcn == 15))
                    return last
                sch.g('tensor', fn, reads=[('wc3', wcn % 4), ('mT', wcn)],
                      writes=[('P', o4) for o4 in range(4)], tag=f"{pre}.out{q}.{oh}.{wcn}")

            for o4 in range(4):
                oc = oh * 4 + o4

                def fn(eng, q=q, oc=oc, o4=o4):
                    Rs = b['R'](oc)[:, q * 512:(q + 1) * 512]
                    return eng.tensor_add(Rs, Rs, PS[o4][:, :])
                sch.g('vector', fn, reads=[('P', o4), ('R', oc)], writes=[('R', oc)],
                      tag=f"{pre}.oe{q}.{oc}")



def emit_moe_v2(sch, b, load_w, pre="mo"):
    xn = b['xnbf']
    p0, p1, p2, p3 = b['mo_p0'], b['mo_p1'], b['mo_p2'], b['mo_p3']
    plg, ptr = b['mo_lg'], b['mo_tr']
    lgT, lgt = b['lgT'], b['lgt']
    sc_t, m_t = b['rt_sc'], b['rt_m']
    c1, c2, c3 = b['rt_c1'], b['rt_c2'], b['rt_c3']
    scT, mT = b['scT'], b['mT']
    h, x3m, t1 = b['h'], b['x3m'], b['t1']
    w1, w2, w3 = b['w1'], b['w2'], b['w3']
    idn = b['id128']

    for tt in range(4):
        def fn(eng, tt=tt):
            last = None
            for k in range(NCH):
                last = eng.matmul(plg[0:8, :], b['gw'][:, k * 8:(k + 1) * 8],
                                  xn(k)[:, tt * 512:(tt + 1) * 512],
                                  start=(k == 0), stop=(k == NCH - 1))
            return last
        sch.g('tensor', fn, reads=['gw'] + [('xnbf', k) for k in range(NCH)],
              writes=['mo_lg'], tag=f"{pre}.lg{tt}")

        def fn(eng, tt=tt):
            return eng.copy(out=lgT[0:8, tt * 512:(tt + 1) * 512], in_=plg[0:8, :])
        sch.g('scalar', fn, reads=['mo_lg'], writes=['lgT'], tag=f"{pre}.lge{tt}")

    for i in range(16):
        def fn(eng, i=i):
            return eng.transpose(ptr[0:128, 0:8], lgT[0:8, i * 128:(i + 1) * 128], idn[0:8, 0:8])
        sch.g('tensor', fn, reads=['lgT', 'id128'], writes=['mo_tr'], tag=f"{pre}.tr{i}")

        def fn(eng, i=i):
            return eng.tensor_copy(lgt[:, i * 8:(i + 1) * 8], ptr[0:128, 0:8])
        sch.g('vector', fn, reads=['mo_tr'], writes=[('lgt', i)], tag=f"{pre}.tre{i}")

    for i in range(16):
        lg_i = lgt[:, i * 8:(i + 1) * 8]
        sc_i = sc_t[:, i * 8:(i + 1) * 8]
        m_i = m_t[:, i * 8:(i + 1) * 8]
        c1_i, c2_i, c3_i = c1[:, i:i + 1], c2[:, i:i + 1], c3[:, i:i + 1]

        sch.g('vector', lambda e, a=c1_i, x=lg_i: e.reduce_max(a, x, axis=AX.X, negate=True),
              reads=[('lgt', i)], writes=[('c1', i)], tag=f"{pre}.r1.{i}")
        sch.g('scalar', lambda e, o=sc_i, x=lg_i, a=c1_i: e.activation(out=o, in_=x, func=AF.Exp, bias=a),
              reads=[('lgt', i), ('c1', i)], writes=[('sc', i)], tag=f"{pre}.exp.{i}")
        sch.g('vector', lambda e, a=c2_i, x=sc_i: e.reduce_sum(a, x, axis=AX.X),
              reads=[('sc', i)], writes=[('c2', i)], tag=f"{pre}.sum.{i}")
        sch.g('vector', lambda e, a=c3_i, x=c2_i: e.reciprocal(a, x),
              reads=[('c2', i)], writes=[('c3', i)], tag=f"{pre}.rcp.{i}")
        sch.g('vector', lambda e, o=sc_i, a=c3_i: e.tensor_scalar(o, o, a, None, op0=ALU.mult),
              reads=[('sc', i), ('c3', i)], writes=[('sc', i)], tag=f"{pre}.nrm.{i}")
        sch.g('vector', lambda e, a=c1_i, x=sc_i: e.reduce_max(a, x, axis=AX.X, negate=True),
              reads=[('sc', i)], writes=[('c1', i)], tag=f"{pre}.m1.{i}")
        sch.g('vector', lambda e, o=m_i, x=sc_i, a=c1_i: e.tensor_scalar(o, x, a, None, op0=ALU.add),
              reads=[('sc', i), ('c1', i)], writes=[('m', i)], tag=f"{pre}.d1.{i}")
        sch.g('vector', lambda e, o=m_i, bb=b: e.tensor_single_scalar(o, o, cc(bb, 0.0), op=ALU.is_equal),
              reads=[('m', i), 'consts'], writes=[('m', i)], tag=f"{pre}.eq.{i}")
        sch.g('vector', lambda e, o=m_i, x=sc_i, bb=b: e.scalar_tensor_tensor(o, o, cc(bb, -1e30), x, op0=ALU.mult, op1=ALU.add),
              reads=[('m', i), ('sc', i), 'consts'], writes=[('m', i)], tag=f"{pre}.msk.{i}")
        sch.g('vector', lambda e, a=c2_i, x=m_i: e.reduce_max(a, x, axis=AX.X, negate=True),
              reads=[('m', i)], writes=[('c2', i)], tag=f"{pre}.m2.{i}")
        sch.g('vector', lambda e, o=m_i, x=sc_i, a=c2_i: e.tensor_scalar(o, x, a, None, op0=ALU.add),
              reads=[('sc', i), ('c2', i)], writes=[('m', i)], tag=f"{pre}.d2.{i}")
        sch.g('vector', lambda e, o=m_i, bb=b: e.tensor_single_scalar(o, o, cc(bb, 0.0), op=ALU.is_ge),
              reads=[('m', i), 'consts'], writes=[('m', i)], tag=f"{pre}.ge.{i}")

    for i in range(16):
        def fn(eng, i=i):
            return eng.transpose(ptr[0:8, 0:128], sc_t[:, i * 8:(i + 1) * 8], idn[:, :])
        sch.g('tensor', fn, reads=[('sc', i), 'id128'], writes=['mo_tr'], tag=f"{pre}.ts{i}")
        sch.g('scalar', lambda e, i=i: e.copy(out=scT[0:8, i * 128:(i + 1) * 128], in_=ptr[0:8, 0:128]),
              reads=['mo_tr'], writes=['scT'], tag=f"{pre}.tse{i}")

        def fn(eng, i=i):
            return eng.transpose(ptr[0:8, 0:128], m_t[:, i * 8:(i + 1) * 8], idn[:, :])
        sch.g('tensor', fn, reads=[('m', i), 'id128'], writes=['mo_tr'], tag=f"{pre}.tm{i}")
        sch.g('scalar', lambda e, i=i: e.copy(out=mT[0:8, i * 128:(i + 1) * 128], in_=ptr[0:8, 0:128]),
              reads=['mo_tr'], writes=['mT'], tag=f"{pre}.tme{i}")

    def prep_bcast(e_, src, srck, dst, dstkey):
        for tt in range(4):
            ps = p2 if tt % 2 == 0 else p3
            psk = 'mo_p2' if tt % 2 == 0 else 'mo_p3'

            def fn(eng, e_=e_, tt=tt, src=src, ps=ps):
                return eng.matmul(ps[:, :], b['sel8'][0:8, e_ * 128:(e_ + 1) * 128],
                                  src[0:8, tt * 512:(tt + 1) * 512], start=True, stop=True)
            sch.g('tensor', fn, reads=[srck, 'sel8'], writes=[psk], tag=f"{pre}.bc{dstkey}{e_}.{tt}")

            def fn(eng, tt=tt, dst=dst, ps=ps):
                return eng.copy(out=dst[:, tt * 512:(tt + 1) * 512], in_=ps[:, :])
            sch.g('scalar', fn, reads=[psk], writes=[dstkey], tag=f"{pre}.bce{dstkey}{e_}.{tt}")

    def prep_wbmb(e_):
        prep_bcast(e_, scT, 'scT', b['wb'](e_ % 2), ('wb', e_ % 2))

    def prep_xs(e_, qf):
        par = qf % 2
        wbp = b['wb'](e_ % 2)
        qo = qf * 512
        for k in range(NCH):
            def fn(eng, k=k, qo=qo, wbp=wbp, par=par):
                return eng.tensor_mul(b['xs'](par)[:, k * 512:(k + 1) * 512],
                                      xn(k)[:, qo:qo + 512], wbp[:, qo:qo + 512])
            sch.g('gpsimd', fn, reads=[('xnbf', k), ('wb', e_ % 2)], writes=[('xs', par, k)],
                  tag=f"{pre}.xs{e_}.{qf}.{k}")

    prep_wbmb(0)
    prep_xs(0, 0)

    for e_ in range(8):
        load_w(sch, e_)
        mbp = b['mb']
        prep_bcast(e_, mT, 'mT', mbp, 'mb')

        for qf in range(4):
            par = qf % 2
            xsp = b['xs'](par)
            if qf < 3:
                prep_xs(e_, qf + 1)
            elif e_ < 7:
                pass
            qo = qf * 512

            for hc in range(NCH):
                ps = p0 if hc % 2 == 0 else p1
                psk = 'mo_p0' if hc % 2 == 0 else 'mo_p1'

                def fn(eng, hc=hc, ps=ps, xsp=xsp):
                    last = None
                    for k in range(NCH):
                        last = eng.matmul(ps[:, :], w1[:, k * 1024 + hc * 128:k * 1024 + (hc + 1) * 128],
                                          xsp[:, k * 512:(k + 1) * 512],
                                          start=(k == 0), stop=(k == NCH - 1))
                    return last
                sch.g('tensor', fn, reads=['w1sb'] + [('xs', par, k) for k in range(NCH)],
                      writes=[psk], tag=f"{pre}.u1{e_}.{qf}.{hc}")

                def fn(eng, hc=hc, e_=e_, ps=ps):
                    return eng.activation(out=h[:, hc * 512:(hc + 1) * 512],
                                          in_=ps[:, :], func=AF.Silu,
                                          bias=b['b1c'][:, hc * 8 + e_:hc * 8 + e_ + 1])
                sch.g('scalar', fn, reads=[psk, 'b1c'], writes=[('h', hc)],
                      tag=f"{pre}.u1e{e_}.{qf}.{hc}")

            if qf == 1 and e_ < 7:
                prep_wbmb(e_ + 1)

            for oc in range(NCH):
                psa, pska = (p0, 'mo_p0') if oc % 2 == 0 else (p2, 'mo_p2')
                psb, pskb = (p1, 'mo_p1') if oc % 2 == 0 else (p3, 'mo_p3')

                def fn(eng, oc=oc, psa=psa, xsp=xsp):
                    last = None
                    for k in range(NCH):
                        last = eng.matmul(psa[:, :], w3[:, k * 1024 + oc * 128:k * 1024 + (oc + 1) * 128],
                                          xsp[:, k * 512:(k + 1) * 512],
                                          start=(k == 0), stop=(k == NCH - 1))
                    return last
                sch.g('tensor', fn, reads=['w3sb'] + [('xs', par, k) for k in range(NCH)],
                      writes=[pska], tag=f"{pre}.x3{e_}.{qf}.{oc}")

                def fn(eng, oc=oc, e_=e_, qo=qo, psa=psa, mbp=mbp):
                    return eng.scalar_tensor_tensor(
                        out=x3m[:, 0:512], in0=psa[:, :],
                        scalar=b['b3c'][:, oc * 8 + e_:oc * 8 + e_ + 1],
                        in1=mbp[:, qo:qo + 512],
                        op0=ALU.add, op1=ALU.mult)
                sch.g('vector', fn, reads=[pska, 'b3c', 'mb'], writes=[('x3m', oc % 2)],
                      tag=f"{pre}.x3e{e_}.{qf}.{oc}")

                def fn(eng, oc=oc, psb=psb):
                    last = None
                    for k in range(NCH):
                        last = eng.matmul(psb[:, :], w2[:, k * 1024 + oc * 128:k * 1024 + (oc + 1) * 128],
                                          h[:, k * 512:(k + 1) * 512],
                                          start=(k == 0), stop=(k == NCH - 1))
                    return last
                sch.g('tensor', fn, reads=['w2sb'] + [('h', k) for k in range(NCH)],
                      writes=[pskb], tag=f"{pre}.x2{e_}.{qf}.{oc}")

                def fn(eng, oc=oc, e_=e_, psb=psb):
                    return eng.scalar_tensor_tensor(
                        out=t1[:, 0:512], in0=psb[:, :],
                        scalar=b['b2c'][:, oc * 8 + e_:oc * 8 + e_ + 1],
                        in1=x3m[:, 0:512], op0=ALU.add, op1=ALU.mult)
                sch.g('vector', fn, reads=[pskb, 'b2c', ('x3m', oc % 2)], writes=[('t1', oc % 2)],
                      tag=f"{pre}.x2e{e_}.{qf}.{oc}")

                def fn(eng, oc=oc, qo=qo):
                    yap = b['y'](oc)[:, qo:qo + 512]
                    return eng.tensor_add(yap, yap, t1[:, 0:512])
                sch.g('vector', fn, reads=[('t1', oc % 2), ('y', oc)], writes=[('y', oc)],
                      tag=f"{pre}.acc{e_}.{qf}.{oc}")

        if e_ < 7:
            prep_xs(e_ + 1, 0)


from contextlib import ExitStack
import ml_dtypes

BF16NP = ml_dtypes.bfloat16


def build_fused_program():
    nc = new_nc()

    def din(name, shape, dt=F32):
        return nc.dram_tensor(name, shape, dt, kind="ExternalInput").ap()

    x0T = din("x0T", [D, T])
    WqT = din("WqT", [D, D]); WkT = din("WkT", [D, D]); WvT = din("WvT", [D, D]); WoT = din("WoT", [D, D])
    bq_row_d = din("bq_row", [1, D]); bk_row_d = din("bk_row", [1, D])
    bv_row_d = din("bv_row", [1, D])
    bo_cols_d = din("bo_cols", [128, 8])
    gcols_d = din("gcols", [128, 8])
    iota_d = din("iota", [128, T])
    masks_d = din("masks", [128, 4 * 512], BF16)
    gwT = din("gwT", [D, 8])
    w1T = din("w1T", [8 * D, D]); w2T = din("w2T", [8 * D, D]); w3T = din("w3T", [8 * D, D])
    b1c_d = din("b1c_d", [128, 64]); b2c_d = din("b2c_d", [128, 64]); b3c_d = din("b3c_d", [128, 64])
    sel8_d = din("sel8_d", [8, 1024], BF16)
    sw1T = din("sw1T", [D, T]); sw2T = din("sw2T", [D, T]); sw3T = din("sw3T", [T, D])
    G_d = din("G_d", [512, D]); GT_d = din("GT_d", [D, 512])
    Cm_d = din("Cm_d", [8, 8]); Cmb_d = din("Cmb_d", [8, 8], BF16)
    WinT = din("WinT", [D, T]); WoutT = din("WoutT", [T, D])
    binr_d = din("binr_d", [1, T])
    id_d = din("id_d", [128, 128]); idb_d = din("idb_d", [128, 128], BF16)
    oc_d = din("oc_d", [128, 1]); or_d = din("or_d", [1, 128]); orb_d = din("orb_d", [1, 128], BF16)
    on512b_d = din("on512b_d", [1, 512], BF16)
    cc_d = din("cc_d", [128, 64])
    outT = nc.dram_tensor("outT", [D, T], F32, kind="ExternalOutput").ap()
    import os
    FDBG = os.environ.get('FDBG', '') == '1'
    dbg_t = {}
    if FDBG:
        for nm in ('dbg_p1R', 'dbg_p2R', 'dbg_p3R', 'dbg_p4R', 'dbg_p6R'):
            dbg_t[nm] = nc.dram_tensor(nm, [D, T], F32, kind="ExternalOutput").ap()
        for nm in ('dbg_p1xn', 'dbg_p5xn'):
            dbg_t[nm] = nc.dram_tensor(nm, [D, T], BF16, kind="ExternalOutput").ap()

    with ExitStack() as stack:
        sch = Sched(nc, stack)
        g = {}

        def sbp(name, shape, dt):
            return stack.enter_context(nc.sbuf_tensor(name, shape, dt))
        R_t = sbp("g_R", [128, NCH * T], F32)
        xn_t = sbp("g_xn", [128, NCH * T], BF16)
        g['R'] = lambda c, _t=R_t: _t[:, c * T:(c + 1) * T]
        g['xnbf'] = lambda c, _t=xn_t: _t[:, c * T:(c + 1) * T]
        g['cc'] = sbp("g_cc", [128, 64], F32)
        g['onc'] = sbp("g_onc", [128, 1], F32)
        g['onr'] = sbp("g_onr", [1, 128], F32)
        g['onrb'] = sbp("g_onrb", [1, 128], BF16)
        g['on512b'] = sbp("g_on512b", [1, 512], BF16)
        g['id128'] = sbp("g_id128", [128, 128], F32)
        g['idbf'] = sbp("g_idbf", [128, 128], BF16)
        g['sel8'] = sbp("g_sel8", [8, 1024], BF16)
        g['Cmat'] = sbp("g_Cmat", [8, 8], F32)
        g['Cmbf'] = sbp("g_Cmbf", [8, 8], BF16)
        g['bo_cols'] = sbp("g_bo", [128, 8], F32)
        g['gcols'] = sbp("g_gc", [128, 8], F32)
        g['b1c'] = sbp("g_b1c", [128, 64], F32)
        g['b2c'] = sbp("g_b2c", [128, 64], F32)
        g['b3c'] = sbp("g_b3c", [128, 64], F32)
        g['lng'] = sbp("g_lng", [128, 8], F32)

        def fn(eng, sem):
            for (t_, d_) in ((g['cc'], cc_d), (g['onc'], oc_d), (g['onr'], or_d),
                             (g['id128'], id_d), (g['idbf'], idb_d), (g['sel8'], sel8_d),
                             (g['Cmat'], Cm_d), (g['Cmbf'], Cmb_d), (g['bo_cols'], bo_cols_d),
                             (g['gcols'], gcols_d), (g['b1c'], b1c_d), (g['b2c'], b2c_d),
                             (g['b3c'], b3c_d), (g['onrb'], orb_d), (g['on512b'], on512b_d)):
                eng.dma_start(out=t_[:, :], in_=d_[:, :]).then_inc(sem, 16)
        sch.dma('sync', 15, fn,
                writes=['consts', 'ones', 'id128', 'idbf', 'sel8', 'Cmat', 'Cmbf',
                        'bo_cols', 'gcols', 'b1c', 'b2c', 'b3c', 'onrb', 'on512b'], tag="gconsts")

        def dump_R(nm):
            if not FDBG:
                return
            for c in range(NCH):
                def fn(eng, sem, c=c, nm=nm):
                    eng.dma_start(out=dbg_t[nm][c * 128:(c + 1) * 128, :], in_=g['R'](c)).then_inc(sem, 16)
                sch.dma('sync', 1, fn, reads=[('R', c)], tag=f"{nm}.{c}")

        def dump_xn(nm):
            if not FDBG:
                return
            for c in range(NCH):
                def fn(eng, sem, c=c, nm=nm):
                    eng.dma_start(out=dbg_t[nm][c * 128:(c + 1) * 128, :], in_=g['xnbf'](c)).then_inc(sem, 16)
                sch.dma('sync', 1, fn, reads=[('xnbf', c)], tag=f"{nm}.{c}")

        def _phase1():
            with ExitStack() as ph:
                def sb(name, shape, dt):
                    return ph.enter_context(nc.sbuf_tensor(name, shape, dt))
                b = dict(g)
                b['ln_sq'] = sb("p1_sq", [128, 2 * T], F32)
                b['ln_rows'] = sb("p1_rows", [1, 4 * T], F32)
                b['ln_mbc'] = sb("p1_mbc", [128, T], F32)
                b['ln_rbc'] = sb("p1_rbc", [128, T], F32)
                for k in ('ln_psA', 'ln_psB', 'ln_psC', 'ln_psM', 'ln_psR'):
                    b[k] = ph.enter_context(nc.psum_tensor("p1_" + k, [128, 512], F32))

                for c in range(NCH):
                    def fn(eng, sem, c=c):
                        eng.dma_start(out=g['R'](c), in_=x0T[c * 128:(c + 1) * 128, :]).then_inc(sem, 16)
                    sch.dma('sync', 1, fn, writes=[('R', c)], tag=f"p1.ldx{c}")

                emit_ln(sch, b, 'R', None, 'onc', 'onr', pre="ln0", xn_bf=g['xnbf'])
        _phase1()
        dump_xn('dbg_p1xn')
        sch.barrier_all()

        def _phase2():
            with ExitStack() as ph:
                def sb(name, shape, dt):
                    return ph.enter_context(nc.sbuf_tensor(name, shape, dt))
                b = dict(g)
                b['kT'] = sb("p2_kT", [128, NCH * T], BF16)
                b['v'] = sb("p2_v", [128, 16 * 1024], BF16)
                b['qt'] = sb("p2_qt", [128, 8 * 512], BF16)
                b['sc'] = sb("p2_sc", [128, 8 * 512], BF16)
                b['at'] = sb("p2_at", [128, 8 * 512], BF16)
                b['wc'] = sb("p2_wc", [128, 2 * 1024], BF16)
                b['iota512'] = sb("p2_iota", [128, 512], F32)
                b['dk'] = sb("p2_dk", [128, 512], F32)
                b['dkb'] = sb("p2_dkb", [128, 1], F32)
                b['bq_row'] = sb("p2_bq", [1, D], BF16)
                b['bk_row'] = sb("p2_bk", [1, D], BF16)
                b['bv_row'] = sb("p2_bv", [1, D], BF16)
                b['masks'] = sb("p2_masks", [128, 4 * 512], BF16)
                b['PS'] = [ph.enter_context(nc.psum_tensor(f"p2_P{i}", [128, 512], F32))
                           for i in range(8)]

                def fn(eng, sem):
                    eng.dma_start(out=b['iota512'][:, :], in_=iota_d[:, 0:512]).then_inc(sem, 16)
                    eng.dma_start(out=b['masks'][:, :], in_=masks_d[:, :]).then_inc(sem, 16)
                sch.dma('sync', 2, fn, writes=['iota512', 'masks'], tag="p2consts")

                def fn(eng, sem):
                    eng.dma_start(out=b['bq_row'][:, :], in_=bq_row_d[:, :]).then_inc(sem, 16)
                    eng.dma_start(out=b['bk_row'][:, :], in_=bk_row_d[:, :]).then_inc(sem, 16)
                    eng.dma_start(out=b['bv_row'][:, :], in_=bv_row_d[:, :]).then_inc(sem, 16)
                sch.dma('gpsimd', 3, fn, writes=['bq_row', 'bk_row', 'bv_row'], tag="p2bias")

                wd = {'WqT': WqT, 'WkT': WkT, 'WvT': WvT, 'WoT': WoT}

                def wload_chunk(sch, which, k, slot):
                    def fn(eng, sem, which=which, k=k, slot=slot):
                        eng.dma_start(out=b['wc'][:, slot * 1024:(slot + 1) * 1024],
                                      in_=wd[which][k * 128:(k + 1) * 128, :]).then_inc(sem, 16)
                    sch.dma('gpsimd', 1, fn, writes=[('wc', slot)], tag=f"p2.ld{which}{k}")

                emit_attn0_v2(sch, b, wload_chunk)
        _phase2()
        dump_R('dbg_p2R')
        sch.barrier_all()

        def _phase3():
            with ExitStack() as ph:
                def sb(name, shape, dt):
                    return ph.enter_context(nc.sbuf_tensor(name, shape, dt))
                b = dict(g)
                b['y'] = g['R']
                b['gw'] = sb("p3_gw", [128, 64], BF16)
                b['lgT'] = sb("p3_lgT", [8, T], F32)
                b['lgt'] = sb("p3_lgt", [128, 16 * 8], F32)
                b['rt_sc'] = sb("p3_rt_sc", [128, 16 * 8], F32)
                b['rt_m'] = sb("p3_rt_m", [128, 16 * 8], F32)
                b['rt_c1'] = sb("p3_c1", [128, 16], F32)
                b['rt_c2'] = sb("p3_c2", [128, 16], F32)
                b['rt_c3'] = sb("p3_c3", [128, 16], F32)
                b['rt_tmp'] = sb("p3_tmp", [128, 8], F32)
                b['scT'] = sb("p3_scT", [8, T], BF16)
                b['mT'] = sb("p3_mT", [8, T], BF16)
                wb_t = sb("p3_wb", [128, 2 * T], BF16)
                b['mb'] = sb("p3_mb", [128, T], BF16)
                xs_t = sb("p3_xs", [128, 2 * 8 * 512], BF16)
                b['wb'] = lambda p, _t=wb_t: _t[:, p * T:(p + 1) * T]
                b['xs'] = lambda p, _t=xs_t: _t[:, p * 8 * 512:(p + 1) * 8 * 512]
                b['h'] = sb("p3_h", [128, 8 * 512], BF16)
                b['x3m'] = sb("p3_x3m", [128, 512], F32)
                b['t1'] = sb("p3_t1", [128, 512], F32)
                b['w1'] = sb("p3_w1", [128, 8 * 1024], BF16)
                b['w2'] = sb("p3_w2", [128, 8 * 1024], BF16)
                b['w3'] = sb("p3_w3", [128, 8 * 1024], BF16)
                b['mo_p0'] = ph.enter_context(nc.psum_tensor("p3_p0", [128, 512], F32))
                b['mo_p1'] = ph.enter_context(nc.psum_tensor("p3_p1", [128, 512], F32))
                b['mo_p2'] = ph.enter_context(nc.psum_tensor("p3_p2", [128, 512], F32))
                b['mo_p3'] = ph.enter_context(nc.psum_tensor("p3_p3", [128, 512], F32))
                b['mo_lg'] = ph.enter_context(nc.psum_tensor("p3_lg", [8, 512], F32))
                b['mo_tr'] = ph.enter_context(nc.psum_tensor("p3_tr", [128, 512], F32))

                def fn(eng, sem):
                    for k in range(NCH):
                        eng.dma_start(out=b['gw'][:, k * 8:(k + 1) * 8],
                                      in_=gwT[k * 128:(k + 1) * 128, :]).then_inc(sem, 16)
                sch.dma('gpsimd', NCH, fn, writes=['gw'], tag="ldgw")

                def load_w(sch, e_):
                    for (key, sbuf, dram) in (('w1sb', b['w1'], w1T), ('w3sb', b['w3'], w3T),
                                              ('w2sb', b['w2'], w2T)):
                        def fn(eng, sem, sbuf=sbuf, dram=dram, e_=e_):
                            for k in range(NCH):
                                eng.dma_start(out=sbuf[:, k * 1024:(k + 1) * 1024],
                                              in_=dram[e_ * 1024 + k * 128:e_ * 1024 + (k + 1) * 128, :]).then_inc(sem, 16)
                        sch.dma('gpsimd', NCH, fn, writes=[key], tag=f"ldw{key}{e_}")

                emit_moe_v2(sch, b, load_w)
        _phase3()
        dump_R('dbg_p3R')
        sch.barrier_all()

        def _phase4():
            with ExitStack() as ph:
                def sb(name, shape, dt):
                    return ph.enter_context(nc.sbuf_tensor(name, shape, dt))
                b = dict(g)
                b['sw12'] = sb("p4_sw12", [128, 2 * 8 * 1024], BF16)
                b['gg'] = sb("p4_gg", [128, 16 * 1024], BF16)
                b['sa'] = sb("p4_sa", [128, 1024], BF16)
                b['wc3'] = sb("p4_wc3", [128, 4 * 1024], BF16)
                b['PS'] = [ph.enter_context(nc.psum_tensor(f"p4_P{i}", [128, 512], F32))
                           for i in range(8)]
                swd = {'sw1T': sw1T, 'sw2T': sw2T}

                def ld_sw(sch, which, ihalf, dstslot):
                    def fn(eng, sem, which=which, ihalf=ihalf, dstslot=dstslot):
                        for k in range(NCH):
                            eng.dma_start(out=b['sw12'][:, dstslot * 8192 + k * 1024:dstslot * 8192 + (k + 1) * 1024],
                                          in_=swd[which][k * 128:(k + 1) * 128, ihalf * 1024:(ihalf + 1) * 1024]).then_inc(sem, 16)
                    sch.dma('gpsimd', NCH, fn, writes=[('sw12', dstslot)], tag=f"p4.ld{which}{ihalf}")

                def ld_sw3c(sch, wcI, slot):
                    def fn(eng, sem, wcI=wcI, slot=slot):
                        eng.dma_start(out=b['wc3'][:, slot * 1024:(slot + 1) * 1024],
                                      in_=sw3T[wcI * 128:(wcI + 1) * 128, :]).then_inc(sem, 16)
                    sch.dma('gpsimd', 1, fn, writes=[('wc3', slot)], tag=f"p4.ldsw3{wcI}")

                emit_shared_v2(sch, b, ld_sw, ld_sw3c)
        _phase4()
        dump_R('dbg_p4R')
        sch.barrier_all()

        def _phase5():
            with ExitStack() as ph:
                def sb(name, shape, dt):
                    return ph.enter_context(nc.sbuf_tensor(name, shape, dt))
                b = dict(g)
                b['ln_sq'] = sb("p5_sq", [128, 2 * T], F32)
                b['ln_rows'] = sb("p5_rows", [1, 4 * T], F32)
                b['ln_mbc'] = sb("p5_mbc", [128, T], F32)
                b['ln_rbc'] = sb("p5_rbc", [128, T], F32)
                for k in ('ln_psA', 'ln_psB', 'ln_psC', 'ln_psM', 'ln_psR'):
                    b[k] = ph.enter_context(nc.psum_tensor("p5_" + k, [128, 512], F32))
                emit_ln(sch, b, 'R', None, 'onc', 'onr', pre="ln1", xn_bf=g['xnbf'])
        _phase5()
        dump_xn('dbg_p5xn')
        sch.barrier_all()

        def _phase6():
            with ExitStack() as ph:
                def sb(name, shape, dt):
                    return ph.enter_context(nc.sbuf_tensor(name, shape, dt))
                b = dict(g)
                df_t = sb("p6_df", [128, NCH * T], BF16)
                b['df'] = lambda c, _t=df_t: _t[:, c * T:(c + 1) * T]
                b['wsq'] = sb("p6_wsq", [128, T], F32)
                b['rwb'] = sb("p6_rwb", [1, T], F32)
                b['rwc'] = sb("p6_rwc", [1, T], F32)
                b['GTbf'] = sb("p6_GTbf", [128, 8 * 512], BF16)
                b['Gbf'] = sb("p6_Gbf", [128, 4 * 1024], BF16)
                b['gwT'] = sb("p6_gwT", [128, 4 * T], BF16)
                b['gwt'] = sb("p6_gwt", [128, 512], BF16)
                b['sims'] = sb("p6_sims", [128, 512], BF16)
                b['smx'] = sb("p6_smx", [128, 4], F32)
                b['invgn'] = sb("p6_invgn", [128, 512], BF16)
                b['apb'] = sb("p6_apb", [128, 512], BF16)
                b['l1c'] = sb("p6_l1c", [128, 64], F32)
                b['Usm'] = sb("p6_Usm", [8, 128], BF16)
                b['ivb'] = sb("p6_ivb", [128, T], BF16)
                b['accb'] = sb("p6_accb", [128, T], BF16)
                b['tjb'] = sb("p6_tjb", [128, 512], BF16)
                PS = [None] * 8
                for i in (0, 1, 3, 5, 6, 7):
                    PS[i] = ph.enter_context(nc.psum_tensor(f"p6_P{i}", [128, 512], F32))
                b['PS'] = PS
                b['l1_ptb'] = ph.enter_context(nc.psum_tensor("p6_ptb", [128, 128], BF16))

                def fn(eng, sem):
                    for k in range(NCH):
                        eng.dma_start(out=b['GTbf'][:, k * 512:(k + 1) * 512],
                                      in_=GT_d[k * 128:(k + 1) * 128, :]).then_inc(sem, 16)
                    for gc in range(4):
                        eng.dma_start(out=b['Gbf'][:, gc * 1024:(gc + 1) * 1024],
                                      in_=G_d[gc * 128:(gc + 1) * 128, :]).then_inc(sem, 16)
                sch.dma('gpsimd', 12, fn, writes=['GTbf', 'Gbf'], tag="ldG")

                emit_l1attn_v2(sch, b)
        _phase6()
        dump_R('dbg_p6R')
        sch.barrier_all()

        def _phase7():
            with ExitStack() as ph:
                def sb(name, shape, dt):
                    return ph.enter_context(nc.sbuf_tensor(name, shape, dt))
                b = dict(g)
                b['ti_win'] = sb("p7_win", [128, NCH * T], BF16)
                b['ti_binr'] = sb("p7_binr", [1, T], BF16)
                b['ti_ac'] = sb("p7_ac", [128, 4 * T], BF16)
                b['ti_dc'] = sb("p7_dc", [128, 4 * T], BF16)
                b['ti_in'] = sb("p7_in", [128, 4 * T], BF16)
                b['ti_mT'] = sb("p7_mT", [128, 16 * 512], BF16)
                b['wc3'] = sb("p7_wc3", [128, 4 * 512], BF16)
                b['PS'] = [ph.enter_context(nc.psum_tensor(f"p7_P{i}", [128, 512], F32))
                           for i in range(4)]
                b['tf_ptb'] = ph.enter_context(nc.psum_tensor("p7_ptb", [128, 128], BF16))

                def fn(eng, sem):
                    eng.dma_start(out=b['ti_binr'][:, :], in_=binr_d[:, :]).then_inc(sem, 16)
                sch.dma('gpsimd', 1, fn, writes=['ti_binr'], tag="p7consts")

                def fn(eng, sem):
                    for k in range(NCH):
                        eng.dma_start(out=b['ti_win'][:, k * T:(k + 1) * T],
                                      in_=WinT[k * 128:(k + 1) * 128, :]).then_inc(sem, 16)
                sch.dma('gpsimd', NCH, fn, writes=['ti_win'], tag="p7ldw")

                def ld_wout_chunk(sch, oh, wcn, slot):
                    def fn(eng, sem, oh=oh, wcn=wcn, slot=slot):
                        eng.dma_start(out=b['wc3'][:, slot * 512:(slot + 1) * 512],
                                      in_=WoutT[wcn * 128:(wcn + 1) * 128, oh * 512:(oh + 1) * 512]).then_inc(sem, 16)
                    sch.dma('gpsimd', 1, fn, writes=[('wc3', slot)], tag=f"p7.ldwo{oh}.{wcn}")

                emit_tree_v2(sch, b, ld_wout_chunk)
        _phase7()
        sch.barrier_all()

        for c in range(NCH):
            def fn(eng, sem, c=c):
                eng.dma_start(out=outT[c * 128:(c + 1) * 128, :], in_=g['R'](c)).then_inc(sem, 16)
            sch.dma('sync', 1, fn, reads=[('R', c)], tag=f"stout{c}")
        sch.out_wait('sync')

        with nc.Block() as block:
            sch.emit(block)
    return nc



_PROG_CACHE = {}


def _prog(name, builder):
    if name not in _PROG_CACHE:
        _PROG_CACHE[name] = builder()
    return _PROG_CACHE[name]


def kernel(**inputs):
    NCORE = 8
    x0 = np.asarray(inputs['L0_attn_x'], dtype=np.float32)
    gamma = np.asarray(inputs['L0_attn_gamma'], dtype=np.float32)

    nc = _prog('fused', build_fused_program)

    ones_col = np.ones((128, 1), np.float32)
    ones_row = np.ones((1, 128), np.float32)
    id128 = np.eye(128, dtype=np.float32)
    idbf = np.eye(128, dtype=np.float32).astype(BF16NP)
    iota = np.broadcast_to(np.arange(T, dtype=np.float32), (128, T)).copy()
    masks = np.zeros((128, 4, 512), np.float32)
    for r in range(4):
        for j in range(128):
            masks[j, r, :] = (np.arange(512) >= j + 128 * r)
    masks = masks.reshape(128, 4 * 512).astype(BF16NP)
    sel8 = np.zeros((8, 1024), np.float32)
    for e in range(8):
        sel8[e, e * 128:(e + 1) * 128] = 1.0
    sel8 = sel8.astype(BF16NP)
    gcols = np.zeros((128, 8), np.float32)
    for c in range(8):
        for p in range(128):
            gcols[p, c] = gamma[2 * c + p // 64]
    ccarr = const_host_array_padded()
    WinT = np.zeros((1024, 2048), np.float32)
    WinT[:, :2040] = np.asarray(inputs['L1_ffn_W_in']).T
    WoutT = np.zeros((2048, 1024), np.float32)
    WoutT[:2040, :] = np.asarray(inputs['L1_ffn_W_out']).T
    binr = np.zeros((1, 2048), np.float32)
    binr[0, :2040] = np.asarray(inputs['L1_ffn_b_in'])
    G = np.asarray(inputs['L1_attn_guidance_trajectories'], dtype=np.float32)

    shared = {
        "WqT": np.ascontiguousarray(np.asarray(inputs['L0_attn_Wq']).T),
        "WkT": np.ascontiguousarray(np.asarray(inputs['L0_attn_Wk']).T),
        "WvT": np.ascontiguousarray(np.asarray(inputs['L0_attn_Wv']).T),
        "WoT": np.ascontiguousarray(np.asarray(inputs['L0_attn_Wo']).T),
        "bq_row": np.asarray(inputs['L0_attn_bq'], dtype=np.float32).reshape(1, D),
        "bk_row": np.asarray(inputs['L0_attn_bk'], dtype=np.float32).reshape(1, D),
        "bv_row": np.asarray(inputs['L0_attn_bv'], dtype=np.float32).reshape(1, D),
        "bo_cols": np.ascontiguousarray(np.asarray(inputs['L0_attn_bo']).reshape(8, 128).T),
        "gcols": gcols, "iota": iota, "masks": masks,
        "gwT": np.ascontiguousarray(np.asarray(inputs['L0_ffn_gate_w']).T),
        "w1T": np.ascontiguousarray(np.asarray(inputs['L0_ffn_w1']).transpose(0, 2, 1).reshape(8 * D, D)),
        "w2T": np.ascontiguousarray(np.asarray(inputs['L0_ffn_w2']).transpose(0, 2, 1).reshape(8 * D, D)),
        "w3T": np.ascontiguousarray(np.asarray(inputs['L0_ffn_w3']).transpose(0, 2, 1).reshape(8 * D, D)),
        "b1c_d": np.ascontiguousarray(np.asarray(inputs['L0_ffn_b1']).reshape(8, 8, 128).transpose(2, 1, 0).reshape(128, 64)),
        "b2c_d": np.ascontiguousarray(np.asarray(inputs['L0_ffn_b2']).reshape(8, 8, 128).transpose(2, 1, 0).reshape(128, 64)),
        "b3c_d": np.ascontiguousarray(np.asarray(inputs['L0_ffn_b3']).reshape(8, 8, 128).transpose(2, 1, 0).reshape(128, 64)),
        "sel8_d": sel8,
        "sw1T": np.ascontiguousarray(np.asarray(inputs['L0_ffn_sw1']).T),
        "sw2T": np.ascontiguousarray(np.asarray(inputs['L0_ffn_sw2']).T),
        "sw3T": np.ascontiguousarray(np.asarray(inputs['L0_ffn_sw3']).T),
        "G_d": G, "GT_d": np.ascontiguousarray(G.T), "Cm_d": l1_cmat(),
        "Cmb_d": l1_cmat().astype(BF16NP),
        "WinT": WinT, "WoutT": WoutT, "binr_d": binr,
        "id_d": id128, "idb_d": idbf, "oc_d": ones_col, "or_d": ones_row,
        "orb_d": ones_row.astype(BF16NP),
        "on512b_d": np.ones((1, 512), np.float32).astype(BF16NP),
        "cc_d": ccarr,
    }
    in_maps = [dict(shared, x0T=np.ascontiguousarray(x0[i].T)) for i in range(NCORE)]
    res = run_bass_kernel_spmd(nc, in_maps, core_ids=list(range(NCORE)))
    out = np.stack([np.ascontiguousarray(res.results[i]["outT"].T) for i in range(NCORE)], axis=0)
    return out.astype(np.float32)

